# version 12
# baseline (speedup 1.0000x reference)
import contextlib
import numpy as np
import ml_dtypes
import concourse.bass as bass
import concourse.mybir as mybir
from concourse.bass_utils import run_bass_kernel_spmd

F32 = mybir.dt.float32
BF16 = mybir.dt.bfloat16
AF = mybir.ActivationFunctionType
ALU = mybir.AluOpType
AXC = mybir.AxisListType.C

N_CORES = 8
D = 1024
HID = 4096
NH = 16
WIN = 256
TSEQ = 2048
TT = 512
EPS = 1e-5



def build_stage_mats(W):
    SA = np.zeros((7, 32, 128, 128), np.float32)
    l = np.arange(128)
    for st in range(7):
        s = 1 << st
        q = l >> (st + 1)
        half = (l >> st) & 1
        r = l & (s - 1)
        kloc = q * s + r
        part = l + np.where(half == 0, s, -s)
        for B in range(32):
            k = B * 64 + kloc
            SA[st, B, l, l] = W[st, k, half, half]
            SA[st, B, l, part] = W[st, k, half, 1 - half]
    SQ = np.zeros((5, 32, 128, 128), np.float32)
    p = np.arange(128)
    Bp = p >> 2
    jj = p & 3
    for st in range(7, 12):
        s = 1 << st
        sb = 1 << (st - 7)
        for g in range(32):
            f = Bp * 128 + jj * 32 + g
            half = (f >> st) & 1
            k = (f >> (st + 1)) * s + (f & (s - 1))
            partner = p + np.where(half == 0, sb * 4, -sb * 4)
            SQ[st - 7, g, p, p] = W[st, k, half, half]
            SQ[st - 7, g, p, partner] = W[st, k, half, 1 - half]
    return SA, SQ


def f_of_hprime():
    hp = np.arange(HID)
    g = hp >> 7
    p = hp & 127
    return (p >> 2) * 128 + (p & 3) * 32 + g



ENGS = ("sync", "tensor", "vector", "scalar", "gpsimd")


class SemCtr:
    __slots__ = ("sem", "step", "n")

    def __init__(self, sem, step):
        self.sem = sem
        self.step = step
        self.n = 0


class Plan:
    def __init__(self):
        self.q = {e: [] for e in ENGS}

    def op(self, eng, build, waits=(), inc=None, inck=1):
        wl = tuple((sc.sem, v) for sc, v in waits if v > 0)
        amt = None
        sem = None
        if inc is not None:
            amt = inck * inc.step
            inc.n += amt
            sem = inc.sem

        def fn(e, wl=wl, build=build, amt=amt, sem=sem):
            for s, v in wl:
                e.wait_ge(s, v)
            inst = build(e)
            if amt is not None:
                inst.then_inc(sem, amt)
        self.q[eng].append(fn)

    def wait(self, eng, *waits):
        wl = tuple((sc.sem, v) for sc, v in waits if v > 0)

        def fn(e, wl=wl):
            for s, v in wl:
                e.wait_ge(s, v)
        self.q[eng].append(fn)



def build_program(n_seq=4, phases="paf", attn_tiles=None, attn_stage=9, win_stage=9, skip_prep=False, win_heads=4, win_wins=None, prep_stage=9):
    nc = bass.Bass("TRN2", target_bir_lowering=False)
    NTOK = n_seq * TSEQ
    NTIL = NTOK // TT
    TPS = TSEQ // TT
    WPT = TT // WIN

    dt = nc.dram_tensor
    x_fm = dt("x_fm", [n_seq, D, TSEQ], F32, kind="ExternalInput").ap()
    w_attnT = dt("w_attnT", [D, 3 * D], F32, kind="ExternalInput").ap()
    w_projT = dt("w_projT", [D, D], F32, kind="ExternalInput").ap()
    w_expT = dt("w_expT", [HID, D], F32, kind="ExternalInput").ap()
    w_con = dt("w_con", [HID, D], F32, kind="ExternalInput").ap()
    b_exp = dt("b_exp", [HID], F32, kind="ExternalInput").ap()
    ub_perm = dt("ub_perm", [HID], F32, kind="ExternalInput").ap()
    b_con = dt("b_con", [D], F32, kind="ExternalInput").ap()
    d_bias = dt("d_bias", [D], F32, kind="ExternalInput").ap()
    SAu = dt("SAu", [7, 32, 128, 128], F32, kind="ExternalInput").ap()
    SQu = dt("SQu", [5, 32, 128, 128], F32, kind="ExternalInput").ap()
    SAdT = dt("SAdT", [7, 32, 128, 128], F32, kind="ExternalInput").ap()
    SQdT = dt("SQdT", [5, 32, 128, 128], F32, kind="ExternalInput").ap()
    cf_in = dt("cf_in", [128, 132], F32, kind="ExternalInput").ap()
    cb_in = dt("cb_in", [128, 132], BF16, kind="ExternalInput").ap()
    cb2_in = dt("cb2_in", [128, 4], BF16, kind="ExternalInput").ap()
    rf_in = dt("rf_in", [4, 128], F32, kind="ExternalInput").ap()
    ef_in = dt("ef_in", [128, 128], F32, kind="ExternalInput").ap()
    rb_in = dt("rb_in", [2, 128], BF16, kind="ExternalInput").ap()

    out_fm = dt("out_fm", [n_seq, D, TSEQ], F32, kind="ExternalOutput").ap()

    mid_scr = dt("mid_scr", [n_seq, D, TSEQ], F32, kind="Internal").ap()
    p_scr = dt("p_scr", [HID, 1028], F32, kind="Internal").ap()
    wexp_scr = dt("wexp_scr", [D, HID], BF16, kind="Internal").ap()
    wcon_scr = dt("wcon_scr", [HID, D], BF16, kind="Internal").ap()
    stats_scr = dt("stats_scr", [2, NTOK], F32, kind="Internal").ap()

    es = contextlib.ExitStack()
    sb = lambda name, shape, dtp: es.enter_context(nc.sbuf_tensor(name, shape, dtp))
    sem = lambda name, step: SemCtr(es.enter_context(nc.semaphore(name)), step)

    cf = sb("cf", [128, 132], F32)
    cbc = sb("cbc", [128, 4], BF16)
    orow = sb("orow", [33, 128], F32)
    e2f = sb("e2f", [128, 128], F32)
    biasg = sb("biasg", [128, 32], F32)
    bex = sb("bex", [128, 32], F32)
    bco = sb("bco", [128, 8], F32)

    qc = sem("qc", 16)
    qx0 = sem("qx0", 16)
    qx1 = sem("qx1", 16)
    qp = sem("qp", 16)
    qp2 = sem("qp2", 16)
    gso = sem("gso", 16)
    qf = sem("qf", 16)
    qf2 = sem("qf2", 16)
    gs = sem("gs", 16)
    gstat = sem("gstat", 16)
    gmid = sem("gmid", 16)
    gout = sem("gout", 16)
    sp = sem("sp", 1)
    sv = sem("sv", 1)
    sa = sem("sa", 1)
    sg = sem("sg", 1)

    P = Plan()

    P.op("sync", lambda e: e.dma_start(out=cf[:], in_=cf_in[:, :]), inc=qc)
    P.op("sync", lambda e: e.dma_start(out=cbc[:], in_=cb2_in[:, :]), inc=qc)
    P.op("sync", lambda e: e.dma_start(out=orow[0:1, :], in_=rf_in[0:1, :]), inc=qc)
    P.op("sync", lambda e: e.dma_start(out=orow[32:33, :], in_=rf_in[0:1, :]), inc=qc)
    P.op("sync", lambda e: e.dma_start(out=e2f[:], in_=ef_in[:, :]), inc=qc)
    QC0 = qc.n
    ID = cf[:, 0:128]
    EPSC = cf[:, 128:129]
    ONEB = cbc[:, 0:1]

    with contextlib.ExitStack() as pes:
      if not skip_prep:
        psb = lambda n, s, d: pes.enter_context(nc.sbuf_tensor(n, s, d))
        pps = lambda n, s, d: pes.enter_context(nc.psum_tensor(n, s, d))
        sall = [psb(f"sall{i}", [128, 32 * 128], F32) for i in range(2)]
        gall = [psb(f"gall{i}", [128, 32 * 128], F32) for i in range(2)]
        svin2 = [psb(f"svin{i}", [128, 1028], F32) for i in range(2)]
        svout2 = [psb(f"svout{i}", [128, 1028], F32) for i in range(2)]
        sv2b2 = [psb(f"sv2b{i}", [128, 1024], BF16) for i in range(2)]
        swx2 = [psb(f"swx{i}", [128, 1024], BF16) for i in range(2)]
        bx = psb("bx", [128, 96], F32)
        idbp = psb("idbp", [128, 128], BF16)
        P.op("sync", lambda e: e.dma_start(out=idbp[:], in_=cb_in[:, 0:128]), inc=qc)
        QC0b = qc.n
        QC0 = QC0b
        PP = [pps(f"PP{i}", [128, 512], F32) for i in range(3)]
        PQ2 = [pps(f"PQ2{i}", [128, 512], F32) for i in range(3)]
        PPT = [pps(f"PPT{i}", [128, 512], BF16) for i in range(2)]

        state = {"sall_mark": [0, 0], "gall_guard": 0, "ppmark": {},
                 "chain_end": 0}

        def batched_loop(stages_ap, n_st, reverse, emit_load, ncols, emit_out,
                         gslot_final_holder):
            order = list(range(n_st))
            if reverse:
                order = order[::-1]
            for i, j in enumerate(order):
                sslot = i % 2
                gsl = i % 2
                sv_ = stages_ap[j].rearrange("b p c -> p b c")
                qsl = (qp, qp2)[sslot]
                for part in range(8):
                    P.op("sync", lambda e, sv_=sv_, sslot=sslot, part=part: e.dma_start(
                        out=sall[sslot][:, :].rearrange("p (b c) -> p b c", b=32)[
                            :, part * 4:(part + 1) * 4, :],
                        in_=sv_[:, part * 4:(part + 1) * 4, :]),
                        waits=([(sp, state["sall_mark"][sslot]), (qc, QC0)]
                               if part == 0 else []), inc=qsl)
                ld = (qsl, qsl.n)
                for b in range(32):
                    tile = PP[b % 3][:, 0:128]
                    w_ = []
                    if b == 0:
                        w_ = [ld]
                        if i == 0:
                            w_.append((sv, sv.n))
                            w_.append((sa, sa.n))
                            w_.append((sp, state["gall_guard"]))
                            w_.append((qc, QC0))
                    pm = state["ppmark"].get(b % 3)
                    if pm is not None:
                        w_ = w_ + [pm]
                    if i == 0:
                        P.op("tensor", lambda e, b=b, tile=tile, sslot=sslot: e.matmul(
                            tile, sall[sslot][:, b * 128:(b + 1) * 128], ID,
                            start=True, stop=True), waits=w_, inc=sp)
                    else:
                        P.op("tensor", lambda e, b=b, tile=tile, sslot=sslot, gsl=gsl: e.matmul(
                            tile, sall[sslot][:, b * 128:(b + 1) * 128],
                            gall[1 - gsl][:, b * 128:(b + 1) * 128],
                            start=True, stop=True), waits=w_, inc=sp)
                    if b % 2 == 0:
                        P.op("vector", lambda e, b=b, tile=tile, gsl=gsl: e.tensor_copy(
                            gall[gsl][:, b * 128:(b + 1) * 128], tile),
                            waits=[(sp, sp.n)], inc=sv)
                        state["ppmark"][b % 3] = (sv, sv.n)
                    else:
                        P.op("scalar", lambda e, b=b, tile=tile, gsl=gsl: e.copy(
                            out=gall[gsl][:, b * 128:(b + 1) * 128], in_=tile),
                            waits=[(sp, sp.n)], inc=sa)
                        state["ppmark"][b % 3] = (sa, sa.n)
                state["sall_mark"][sslot] = sp.n
            gfin = (n_st - 1) % 2
            SV_CH, SA_CH = sv.n, sa.n
            state["chain_end"] = sp.n
            state.pop("apply_mm0", None)
            state.pop("apply_mm1", None)
            for b in range(32 if prep_stage >= 2 else 0):
                slot = b % 2
                qsl = (qp, qp2)[slot]
                emit_load(b, slot, qsl,
                          [(sp, state.get(f"apply_mm{slot}", state["chain_end"]))])
                ld = (qsl, qsl.n)
                splits = []
                c0 = 0
                for pi, pq in enumerate(PQ2):
                    if c0 >= ncols:
                        break
                    n = min(512, ncols - c0)
                    splits.append((pq, c0, n, slot))
                    c0 += n
                for k, (pq, c0, n, sl) in enumerate(splits):
                    w_ = []
                    if k == 0:
                        w_ = [ld, (sv, SV_CH), (sa, SA_CH)]
                        if state.get("pqmark"):
                            w_.append(state["pqmark"])
                    P.op("tensor", lambda e, pq=pq, c0=c0, n=n, b=b, sl=sl, gfin=gfin: e.matmul(
                        pq[:, (sl) * 0: n] if False else pq[:, 0:n],
                        gall[gfin][:, b * 128:(b + 1) * 128],
                        svin2[sl][:, c0:c0 + n], start=True, stop=True),
                        waits=w_, inc=(sp if k == len(splits) - 1 else None))
                state[f"apply_mm{slot}"] = sp.n
                for ki, (pq, c0, n, sl) in enumerate(splits):
                    w_ = [(sp, sp.n)]
                    if ki == 0 and state.get(f"outst{slot}"):
                        w_.append(state[f"outst{slot}"])
                    P.op("vector", lambda e, pq=pq, c0=c0, n=n, sl=sl: e.tensor_copy(
                        svout2[sl][:, c0:c0 + n], pq[:, 0:n]),
                        waits=w_, inc=sv)
                state["pqmark"] = (sv, sv.n)
                state[f"apply_ev{slot}"] = sv.n
                if prep_stage >= 3:
                    emit_out(b, slot)
            state["gall_guard"] = sp.n
            state["sall_mark"] = [sp.n, sp.n]

        def ld_upA(b, slot, qsl, w_):
            P.op("sync", lambda e, b=b, slot=slot: e.dma_start(
                out=svin2[slot][:, 0:1024], in_=w_expT[b * 128:(b + 1) * 128, :]),
                waits=w_, inc=qsl)
            P.op("sync", lambda e, b=b, slot=slot: e.dma_start(
                out=svin2[slot][:, 1024:1025],
                in_=b_exp[b * 128:(b + 1) * 128].unsqueeze(1)), inc=qsl)

        def out_upA(b, slot):
            gq = (gs, gso)[slot]
            P.op("scalar", lambda e, b=b, slot=slot: e.dma_start(
                out=p_scr[b * 128:(b + 1) * 128, 0:1025], in_=svout2[slot][:, 0:1025]),
                waits=[(sv, sv.n)], inc=gq)
            state[f"outst{slot}"] = (gq, gq.n)

        batched_loop(SAu, 7, True, ld_upA, 1025, out_upA, None)
        P.wait("sync", (gs, gs.n), (gso, gso.n))

        p_view = p_scr.rearrange("(B jj s) c -> (B jj) s c", B=32, jj=4, s=32)

        def ld_upQ(g, slot, qsl, w_):
            P.op("sync", lambda e, g=g, slot=slot: e.dma_start(
                out=svin2[slot][:, 0:1025], in_=p_view[:, g, 0:1025]),
                waits=w_, inc=qsl)

        def out_upQ(g, slot):
            P.op("vector", lambda e, g=g, slot=slot: e.tensor_copy(
                bex[:, g:g + 1], svout2[slot][:, 1024:1025]),
                waits=[(sv, sv.n)], inc=sv)
            P.op("vector", lambda e, slot=slot: e.tensor_copy(
                sv2b2[slot][:], svout2[slot][:, 0:1024]),
                waits=[(sv, sv.n), (sp, state.get(f"tp{slot}", 0))], inc=sv)
            SV_CAST = sv.n
            for ci in range(8):
                pt = PPT[ci % 2][:, 0:128]
                w_ = [(sv, SV_CAST)] if ci == 0 else []
                ptm = state.setdefault("ptmark", {}).get(ci % 2)
                if ptm is not None:
                    w_ = w_ + [ptm]
                else:
                    w_ = w_ + [(qc, QC0b)]
                P.op("tensor", lambda e, ci=ci, slot=slot, pt=pt: e.transpose(
                    pt, sv2b2[slot][:, ci * 128:(ci + 1) * 128], idbp[:]),
                    waits=w_, inc=sp)
                sw = state.get(f"swxst{slot}")
                P.op("scalar", lambda e, ci=ci, slot=slot, pt=pt: e.copy(
                    out=swx2[slot][:, ci * 128:(ci + 1) * 128], in_=pt),
                    waits=[(sp, sp.n)] + ([sw] if ci == 0 and sw else []), inc=sa)
                state["ptmark"][ci % 2] = (sa, sa.n)
            state[f"tp{slot}"] = sp.n
            SA_SWX = sa.n
            P.op("scalar", lambda e, g=g, slot=slot: e.dma_start(
                out=wexp_scr.rearrange("(cc p) f -> p (cc f)", p=128)[
                    :, 0:0] if False else wexp_scr[0:128, 0:128],
                in_=swx2[slot][:, 0:128]), waits=[], inc=gs) if False else None
            gq = (gs, gso)[slot]
            for ci in range(8):
                P.op("scalar", lambda e, ci=ci, g=g, slot=slot: e.dma_start(
                    out=wexp_scr[ci * 128:(ci + 1) * 128, g * 128:(g + 1) * 128],
                    in_=swx2[slot][:, ci * 128:(ci + 1) * 128]),
                    waits=[(sa, SA_SWX)] if ci == 0 else [], inc=gq)
            state[f"swxst{slot}"] = (gq, gq.n)
            state[f"outst{slot}"] = (sv, SV_CAST)

        batched_loop(SQu, 5, True, ld_upQ, 1025, out_upQ, None)
        P.wait("sync", (gs, gs.n), (gso, gso.n))

        w_view = w_con.rearrange("(B jj s) c -> (B jj) s c", B=32, jj=4, s=32)
        pv2 = p_scr.rearrange("(B jj s) c -> (B jj) s c", B=32, jj=4, s=32)

        def ld_dnQ(g, slot, qsl, w_):
            P.op("sync", lambda e, g=g, slot=slot: e.dma_start(
                out=svin2[slot][:, 0:1024], in_=w_view[:, g, :]),
                waits=w_, inc=qsl)

        def out_dnQ(g, slot):
            gq = (gs, gso)[slot]
            P.op("scalar", lambda e, g=g, slot=slot: e.dma_start(
                out=pv2[:, g, 0:1024], in_=svout2[slot][:, 0:1024]),
                waits=[(sv, sv.n)], inc=gq)
            state[f"outst{slot}"] = (gq, gq.n)

        batched_loop(SQdT, 5, False, ld_dnQ, 1024, out_dnQ, None)
        P.wait("sync", (gs, gs.n), (gso, gso.n))

        def ld_dnA(b, slot, qsl, w_):
            P.op("sync", lambda e, b=b, slot=slot: e.dma_start(
                out=svin2[slot][:, 0:1024], in_=p_scr[b * 128:(b + 1) * 128, 0:1024]),
                waits=w_, inc=qsl)

        def out_dnA(b, slot):
            wc = state.get(f"wcst{slot}")
            P.op("vector", lambda e, slot=slot: e.tensor_copy(
                sv2b2[slot][:], svout2[slot][:, 0:1024]),
                waits=[(sv, sv.n)] + ([wc] if wc else []), inc=sv)
            gq = (gs, gso)[slot]
            P.op("scalar", lambda e, b=b, slot=slot: e.dma_start(
                out=wcon_scr[b * 128:(b + 1) * 128, :], in_=sv2b2[slot][:]),
                waits=[(sv, sv.n)], inc=gq)
            state[f"wcst{slot}"] = (gq, gq.n)
            state[f"outst{slot}"] = (sv, sv.n)

        batched_loop(SAdT, 7, False, ld_dnA, 1024, out_dnA, None)
        P.wait("sync", (gs, gs.n), (gso, gso.n))

        uview = ub_perm.rearrange("(c p) -> c p", p=128)
        for c in range(32):
            P.op("sync", lambda e, c=c: e.dma_start(
                out=bx[:, 32 + c:33 + c], in_=uview[c, :].unsqueeze(1)), inc=qp)
        bcv = b_con.rearrange("(c p) -> c p", p=128)
        dbv = d_bias.rearrange("(c p) -> c p", p=128)
        for c in range(8):
            P.op("sync", lambda e, c=c: e.dma_start(
                out=bx[:, 64 + c:65 + c], in_=bcv[c, :].unsqueeze(1)), inc=qp)
            P.op("sync", lambda e, c=c: e.dma_start(
                out=bx[:, 72 + c:73 + c], in_=dbv[c, :].unsqueeze(1)), inc=qp)
        P.op("vector", lambda e: e.tensor_tensor(
            out=biasg[:], in0=bex[:], in1=bx[:, 32:64], op=ALU.add),
            waits=[(qp, qp.n), (sv, sv.n)], inc=sv)
        P.op("vector", lambda e: e.tensor_tensor(
            out=bco[:], in0=bx[:, 64:72], in1=bx[:, 72:80], op=ALU.add), inc=sv)

    SP_PREP, SV_PREP, SA_PREP, GS_PREP = sp.n, sv.n, sa.n, gs.n

    if "p" == phases:
        with contextlib.ExitStack() as des:
            db = des.enter_context(nc.sbuf_tensor("db", [128, 1024], BF16))
            df = des.enter_context(nc.sbuf_tensor("df", [128, 1024], F32))
            qd = SemCtr(des.enter_context(nc.semaphore("qd")), 16)
            for i in range(16):
                P.op("sync", lambda e, i=i: e.dma_start(
                    out=db[:], in_=wcon_scr[i * 128:(i + 1) * 128, :]),
                    waits=[(gs, gs.n), (sv, sv.n)], inc=qd)
                P.op("vector", lambda e, i=i: e.tensor_copy(df[:], db[:]),
                     waits=[(qd, qd.n), (gout, gout.n)], inc=sv)
                P.op("scalar", lambda e, i=i: e.dma_start(
                    out=out_fm[0, (i % 8) * 128:(i % 8 + 1) * 128,
                               (i // 8) * 1024:(i // 8 + 1) * 1024],
                    in_=df[:]), waits=[(sv, sv.n)], inc=gout)
                P.wait("sync", (gout, gout.n))
            P.wait("scalar", (gout, gout.n))
    attn_on = "a" in phases
    with contextlib.ExitStack() as aes:
      if attn_on:
          asb = lambda n, s, d: aes.enter_context(nc.sbuf_tensor(n, s, d))
          aps = lambda n, s, d: aes.enter_context(nc.psum_tensor(n, s, d))
          waT = asb("waT", [128, 8 * 3072], BF16)
          wpT = asb("wpT", [128, 8 * 1024], BF16)
          with nc.sbuf_tensor("wstg0", [128, 3072], F32) as wstg0, \
               nc.sbuf_tensor("wstg1", [128, 3072], F32) as wstg1:
              wst = (wstg0, wstg1)
              qw = (qx0, qx1)
              cast_mark = [0, 0]
              for c in range(8):
                  sl = c % 2
                  P.op("sync", lambda e, c=c, sl=sl: e.dma_start(
                      out=wst[sl][:, :], in_=w_attnT[c * 128:(c + 1) * 128, :]),
                      waits=[(sv, cast_mark[sl])], inc=qw[sl])
                  P.op("vector", lambda e, c=c, sl=sl: e.tensor_copy(
                      waT[:, c * 3072:(c + 1) * 3072], wst[sl][:, :]),
                      waits=[(qw[sl], qw[sl].n)], inc=sv)
                  cast_mark[sl] = sv.n
              for c in range(8):
                  sl = c % 2
                  P.op("sync", lambda e, c=c, sl=sl: e.dma_start(
                      out=wst[sl][:, 0:1024], in_=w_projT[c * 128:(c + 1) * 128, :]),
                      waits=[(sv, cast_mark[sl])], inc=qw[sl])
                  P.op("vector", lambda e, c=c, sl=sl: e.tensor_copy(
                      wpT[:, c * 1024:(c + 1) * 1024], wst[sl][:, 0:1024]),
                      waits=[(qw[sl], qw[sl].n)], inc=sv)
                  cast_mark[sl] = sv.n
              P.wait("sync", (sv, sv.n))
          SV_W = sv.n

          x_sb = [asb(f"x{i}", [128, 8 * TT], F32) for i in range(2)]
          xbt = asb("xbt", [128, 8 * TT], BF16)
          x2t = asb("x2t", [128, 8 * TT], BF16)
          xnt = asb("xnt", [128, 8 * TT], BF16)
          qkt = asb("qkt", [128, 16 * TT], BF16)
          v_sb = asb("v_sb", [128, 8 * TT], BF16)
          y_sb = asb("y_sb", [128, 8 * TT], BF16)
          ett = asb("ett", [128, 2 * 1024], BF16)
          midt = asb("midt", [128, 8 * TT], F32)
          bgt = asb("bgt", [128, 512], F32)
          mrow = asb("mrow", [1, 4 * TT], F32)
          zz = asb("zz", [128, 1024], F32)
          P.op("vector", lambda e: e.memset(zz[:], 0.0), inc=sv)
          mbt = asb("mbt", [128, TT], BF16)
          rbt = asb("rbt", [128, TT], BF16)
          PA = aps("PA", [128, 1024], F32)
          PB = aps("PB", [128, 1024], F32)
          PC = aps("PC", [128, 512], F32)
          PD = aps("PD", [128, 512], F32)
          PE_ = aps("PE_", [128, 512], F32)
          PF = aps("PF", [128, 512], F32)

          prev = dict(sp_st2=SP_PREP, sa_c2=SA_PREP, gs_mid=0,
                      gs_stat=0, sv_v2=SV_PREP, sp_vend=SP_PREP,
                      sv_mid=SV_W, sv_mid2=SV_W, sp_bc_done=SP_PREP,
                      sp_stat1=SP_PREP)

          for t in range(NTIL):
              seq = t // TPS
              t0 = (t % TPS) * TT
              qx = (qx0, qx1)[t % 2]
              xs = x_sb[t % 2]

              for c in range(8):
                  P.op("sync", lambda e, c=c, seq=seq, t0=t0, xs=xs: e.dma_start(
                      out=xs[:, c * TT:(c + 1) * TT],
                      in_=x_fm[seq, c * 128:(c + 1) * 128, t0:t0 + TT]),
                      waits=[(sv, prev["sv_mid2"])] if c == 0 else [], inc=qx)
              LD_X = qx.n

              for c in range(8):
                  P.op("scalar", lambda e, c=c, xs=xs: e.copy(
                      out=xbt[:, c * TT:(c + 1) * TT], in_=xs[:, c * TT:(c + 1) * TT]),
                      waits=([(qx, LD_X), (sp, prev["sp_stat1"])] if c == 0 else []),
                      inc=sa)
              for c in range(8):
                  P.op("scalar", lambda e, c=c, xs=xs: e.activation(
                      out=x2t[:, c * TT:(c + 1) * TT], in_=xs[:, c * TT:(c + 1) * TT],
                      func=AF.Square),
                      waits=([(sp, prev["sp_st2"])] if c == 0 else []), inc=sa)
              SA_CAST = sa.n

              if attn_stage < 2:
                prev = dict(prev, sv_mid=sv.n, sa_c2=sa.n, sp_st2=sp.n,
                            sv_v2=sv.n, sp_vend=sp.n)
                continue
              for c in range(8):
                  P.op("tensor", lambda e, c=c: e.matmul(
                      PC[0:1, 0:TT], ONEB, xbt[:, c * TT:(c + 1) * TT],
                      start=(c == 0), stop=(c == 7)),
                      waits=([(sa, SA_CAST), (sv, prev["sv_v2"])] if c == 0 else []),
                      inc=(sp if c == 7 else None))
              for c in range(8):
                  P.op("tensor", lambda e, c=c: e.matmul(
                      PD[0:1, 0:TT], ONEB, x2t[:, c * TT:(c + 1) * TT],
                      start=(c == 0), stop=(c == 7)), inc=(sp if c == 7 else None))
              SP_STAT = sp.n

              P.op("vector", lambda e: e.tensor_scalar(
                  out=mrow[:, 0:TT], in0=PC[0:1, 0:TT], scalar1=1.0 / D,
                  scalar2=None, op0=ALU.mult),
                  waits=[(sp, SP_STAT), (gstat, prev["gs_stat"])], inc=sv)
              P.op("vector", lambda e: e.tensor_scalar(
                  out=mrow[:, 2 * TT:3 * TT], in0=PD[0:1, 0:TT], scalar1=1.0 / D,
                  scalar2=None, op0=ALU.mult), inc=sv)
              P.op("vector", lambda e: e.tensor_tensor(
                  out=mrow[:, 3 * TT:4 * TT], in0=mrow[:, 0:TT],
                  in1=mrow[:, 0:TT], op=ALU.mult), waits=[(sv, sv.n)], inc=sv)
              P.op("vector", lambda e: e.tensor_tensor(
                  out=mrow[:, 2 * TT:3 * TT], in0=mrow[:, 2 * TT:3 * TT],
                  in1=mrow[:, 3 * TT:4 * TT], op=ALU.subtract),
                  waits=[(sv, sv.n)], inc=sv)
              SV_VAR = sv.n
              P.op("scalar", lambda e: e.activation(
                  out=mrow[:, 3 * TT:4 * TT], in_=mrow[:, 2 * TT:3 * TT],
                  func=AF.Ln, bias=EPSC[0:1, :]), waits=[(sv, SV_VAR)], inc=sa)
              P.op("scalar", lambda e: e.activation(
                  out=mrow[:, TT:2 * TT], in_=mrow[:, 3 * TT:4 * TT],
                  func=AF.Exp, scale=-0.5), waits=[(sa, sa.n)], inc=sa)
              SA_LN = sa.n

              if attn_stage < 3:
                prev = dict(prev, sv_mid=sv.n, sa_c2=sa.n, sp_st2=sp.n,
                            sv_v2=sv.n, sp_vend=sp.n)
                continue
              P.op("tensor", lambda e: e.matmul(
                  PE_[:, 0:TT], orow[0:1, :], mrow[0:1, 0:TT], start=True, stop=True),
                  waits=[(sa, SA_LN), (sv, prev["sv_mid"])], inc=sp)
              P.op("tensor", lambda e: e.matmul(
                  PF[:, 0:TT], orow[0:1, :], mrow[0:1, TT:2 * TT], start=True, stop=True),
                  inc=sp)
              SP_BC = sp.n
              P.op("vector", lambda e: e.tensor_copy(mbt[:], PE_[:, 0:TT]),
                   waits=[(sp, SP_BC)], inc=sv)
              P.op("vector", lambda e: e.tensor_copy(rbt[:], PF[:, 0:TT]), inc=sv)
              SV_BC = sv.n

              for c in range(8):
                  P.op("vector", lambda e, c=c: e.tensor_tensor(
                      out=x2t[:, c * TT:(c + 1) * TT], in0=xbt[:, c * TT:(c + 1) * TT],
                      in1=mbt[:], op=ALU.subtract),
                      waits=([(sv, SV_BC)] if c == 0 else [(sv, sv.n)]), inc=sv)
                  P.op("vector", lambda e, c=c: e.tensor_tensor(
                      out=xnt[:, c * TT:(c + 1) * TT], in0=x2t[:, c * TT:(c + 1) * TT],
                      in1=rbt[:], op=ALU.mult),
                      waits=[(sv, sv.n)] + ([(sp, prev["sp_vend"]),
                                             (sp, prev["sp_st2"])] if c == 0 else []),
                      inc=sv)
              SV_XN = sv.n

              if attn_stage < 4:
                prev = dict(prev, sv_mid=sv.n, sa_c2=sa.n, sp_st2=sp.n,
                            sv_v2=sv.n, sp_vend=sp.n)
                continue
              pq = (PC, PD)
              for ei in range(16):
                  pp = pq[ei % 2]
                  for c in range(8):
                      w_ = []
                      if ei == 0 and c == 0:
                          w_ = [(sv, SV_XN)]
                      elif c == 0 and ei >= 2:
                          w_ = [(sv, sv.n)]
                      P.op("tensor", lambda e, ei=ei, c=c, pp=pp: e.matmul(
                          pp[:, 0:TT],
                          waT[:, c * 3072 + ei * 128: c * 3072 + (ei + 1) * 128],
                          xnt[:, c * TT:(c + 1) * TT],
                          start=(c == 0), stop=(c == 7)),
                          waits=w_, inc=(sp if c == 7 else None))
                  P.op("vector", lambda e, ei=ei, pp=pp: e.tensor_copy(
                      qkt[:, ei * TT:(ei + 1) * TT], pp[:, 0:TT]),
                      waits=[(sp, sp.n)], inc=sv)
              for ts4 in range(TT // 128):
                  for eh in range(2):
                      pp = pq[(ts4 * 2 + eh) % 2]
                      for c in range(8):
                          P.op("tensor", lambda e, ts4=ts4, eh=eh, c=c, pp=pp: e.matmul(
                              pp[:, :],
                              xnt[:, c * TT + ts4 * 128: c * TT + (ts4 + 1) * 128],
                              waT[:, c * 3072 + 2048 + eh * 512: c * 3072 + 2048 + (eh + 1) * 512],
                              start=(c == 0), stop=(c == 7)),
                              waits=([(sv, sv.n)] if c == 0 else []),
                              inc=(sp if c == 7 else None))
                      P.op("vector", lambda e, ts4=ts4, eh=eh, pp=pp: e.tensor_copy(
                          v_sb[:, ts4 * 1024 + eh * 512: ts4 * 1024 + (eh + 1) * 512],
                          pp[:, :]), waits=[(sp, sp.n)], inc=sv)
              SV_QKV = sv.n
              SP_VEND = sp.n

              if attn_stage < 5:
                  prev = dict(prev, sv_mid=sv.n, sa_c2=sa.n, sp_st2=sp.n,
                              sv_v2=sv.n, sp_vend=SP_VEND)
                  continue
              CS2 = {(0, 0): 0, (0, 1): 256, (1, 0): 512, (1, 1): 768}
              NIT = WPT * 8
              mk = {"st": {}, "et": {-2: prev["sa_c2"], -1: prev["sa_c2"]},
                    "z": {}, "rc": {-2: prev["sa_c2"], -1: prev["sa_c2"]},
                    "av": {}, "bg": {}, "bge": {-2: prev["sa_c2"], -1: prev["sa_c2"]},
                    "y": {-2: SV_BC, -1: SV_BC, -3: SV_BC, -4: SV_BC}}

              def emit_st(k):
                  w, gi = divmod(k, 8)
                  pp = (PA, PB)[k % 2]
                  first = (k == 0)
                  for idx, (hp, jh) in enumerate(((0, 0), (0, 1), (1, 0), (1, 1))):
                      h = 2 * gi + hp
                      qc_ = h // 2
                      qb = (h % 2) * 64
                      w_ = []
                      if idx == 0:
                          w_ = [(sa, mk["et"][k - 2]), (sa, mk["rc"][k - 2])]
                          if first:
                              w_.append((sv, SV_QKV))
                      P.op("tensor", lambda e, pp=pp, qc_=qc_, qb=qb, jh=jh, w=w, hp=hp: e.matmul(
                          pp[:, CS2[(hp, jh)]:CS2[(hp, jh)] + 256],
                          qkt[qb:qb + 64, (8 + qc_) * TT + w * 256 + jh * 128:
                              (8 + qc_) * TT + w * 256 + (jh + 1) * 128],
                          qkt[qb:qb + 64, qc_ * TT + w * 256: qc_ * TT + (w + 1) * 256],
                          start=True, stop=True),
                          waits=w_, inc=(sp if idx == 3 else None))
                  mk["st"][k] = sp.n

              def emit_exp(k):
                  pp = (PA, PB)[k % 2]
                  eb = (k % 2) * 1024
                  P.op("scalar", lambda e, pp=pp, eb=eb: e.activation(
                      out=ett[:, eb:eb + 1024], in_=pp[:, :], func=AF.Exp, scale=0.125),
                      waits=[(sp, mk["st"][k]), (sp, mk["av"].get(k - 2, 0))], inc=sa)
                  mk["et"][k] = sa.n

              def emit_z(k):
                  zp = (PA, PB)[k % 2]
                  eb = (k % 2) * 1024
                  for idx, (hp, col) in enumerate(((0, 0), (1, 256))):
                      for jh in range(2):
                          w_ = []
                          if idx == 0 and jh == 0:
                              w_ = [(sa, mk["et"][k]), (sa, mk["rc"][k - 2])]
                          P.op("tensor", lambda e, zp=zp, col=col, eb=eb, hp=hp, jh=jh: e.matmul(
                              zp[0:1, col:col + 256], ONEB,
                              ett[:, eb + CS2[(hp, jh)]: eb + CS2[(hp, jh)] + 256],
                              start=(jh == 0), stop=(jh == 1)),
                              waits=w_, inc=(sp if (idx == 1 and jh == 1) else None))
                  mk["z"][k] = sp.n

              def emit_recip(k):
                  zp = (PA, PB)[k % 2]
                  sc = (k % 2) * 512
                  P.op("scalar", lambda e, zp=zp, sc=sc: e.activation(
                      out=zz[0:1, sc:sc + 256], in_=zp[0:1, 0:256], func=AF.Ln),
                      waits=[(sp, mk["z"][k]), (sp, mk["bg"].get(k - 2, 0))], inc=sa)
                  P.op("scalar", lambda e, zp=zp, sc=sc: e.activation(
                      out=zz[64:65, sc:sc + 256], in_=zp[0:1, 256:512], func=AF.Ln),
                      inc=sa)
                  mk["rc"][k] = sa.n

              def emit_av(k):
                  w, gi = divmod(k, 8)
                  eb = (k % 2) * 1024
                  yp = (PD, PE_)[k % 2]
                  for idx, hp in enumerate((0, 1)):
                      h = 2 * gi + hp
                      ob = hp * 64
                      for jh in range(2):
                          vt = w * 2 + jh
                          w_ = []
                          if idx == 0 and jh == 0:
                              w_ = [(sa, mk["et"][k]), (sv, mk["y"][k - 2])]
                          P.op("tensor", lambda e, ob=ob, h=h, jh=jh, vt=vt, eb=eb, yp=yp, hp=hp: e.matmul(
                              yp[ob:ob + 64, 0:256],
                              v_sb[:, vt * 1024 + h * 64: vt * 1024 + (h + 1) * 64],
                              ett[:, eb + CS2[(hp, jh)]: eb + CS2[(hp, jh)] + 256],
                              start=(jh == 0), stop=(jh == 1)),
                              waits=w_, inc=(sp if (idx == 1 and jh == 1) else None))
                  mk["av"][k] = sp.n

              def emit_bg(k):
                  sc = (k % 2) * 512
                  bp = (PC, PF)[k % 2]
                  P.op("tensor", lambda e, sc=sc, bp=bp: e.matmul(
                      bp[:, 0:256], e2f[0:65, :], zz[0:65, sc:sc + 256],
                      start=True, stop=True),
                      waits=[(sa, mk["rc"][k]), (sa, mk["bge"][k - 2])], inc=sp)
                  mk["bg"][k] = sp.n

              def emit_bge(k):
                  bc = (k % 2) * 256
                  bp = (PC, PF)[k % 2]
                  P.op("scalar", lambda e, bc=bc, bp=bp: e.activation(
                      out=bgt[:, bc:bc + 256], in_=bp[:, 0:256], func=AF.Exp,
                      scale=-1.0),
                      waits=[(sp, mk["bg"][k]), (sv, mk["y"][k - 2])], inc=sa)
                  mk["bge"][k] = sa.n

              def emit_y(k):
                  w, gi = divmod(k, 8)
                  yc = (k % 2) * 256
                  yp = (PD, PE_)[k % 2]
                  P.op("vector", lambda e, yc=yc, gi=gi, w=w, yp=yp: e.tensor_tensor(
                      out=y_sb[:, gi * TT + w * 256: gi * TT + (w + 1) * 256],
                      in0=yp[:, 0:256], in1=bgt[:, yc:yc + 256], op=ALU.mult),
                      waits=[(sp, mk["av"][k]), (sa, mk["bge"][k])], inc=sv)
                  mk["y"][k] = sv.n

              for k in range(NIT + 2):
                  if k < NIT:
                      emit_st(k)
                  if 1 <= k <= NIT:
                      emit_z(k - 1)
                  if k >= 2:
                      emit_av(k - 2)
                      emit_bg(k - 2)
                  if k < NIT:
                      emit_exp(k)
                  if 1 <= k <= NIT:
                      emit_recip(k - 1)
                  if k >= 2:
                      emit_bge(k - 2)
                      emit_y(k - 2)
              SV_Y = sv.n


              if attn_stage < 6:
                prev = dict(prev, sv_mid=sv.n, sa_c2=sa.n, sp_st2=sp.n,
                            sv_v2=sv.n, sp_vend=SP_VEND)
                continue
              for ot in range(8):
                  pp = (PE_, PF)[ot % 2]
                  for c in range(8):
                      w_ = []
                      if ot == 0 and c == 0:
                          w_ = [(sv, SV_Y)]
                      elif c == 0 and ot >= 2:
                          w_ = [(sv, sv.n)]
                      P.op("tensor", lambda e, ot=ot, c=c, pp=pp: e.matmul(
                          pp[:, :],
                          wpT[:, c * 1024 + ot * 128: c * 1024 + (ot + 1) * 128],
                          y_sb[:, c * TT:(c + 1) * TT],
                          start=(c == 0), stop=(c == 7)),
                          waits=w_, inc=(sp if c == 7 else None))
                  P.op("vector", lambda e, ot=ot, pp=pp, xs=xs: e.tensor_tensor(
                      out=midt[:, ot * TT:(ot + 1) * TT], in0=pp[:, :],
                      in1=xs[:, ot * TT:(ot + 1) * TT], op=ALU.add),
                      waits=[(sp, sp.n)] + ([(gmid, prev["gs_mid"]),
                                             (sa, prev["sa_c2"])] if ot == 0 else []),
                      inc=sv)
                  SV_EV = sv.n
                  P.op("scalar", lambda e, ot=ot: e.copy(
                      out=x2t[:, ot * TT:(ot + 1) * TT],
                      in_=midt[:, ot * TT:(ot + 1) * TT]),
                      waits=[(sv, SV_EV)] + ([(sp, prev["sp_st2"])] if ot == 0 else []),
                      inc=sa)
                  P.op("scalar", lambda e, ot=ot: e.activation(
                      out=xnt[:, ot * TT:(ot + 1) * TT],
                      in_=midt[:, ot * TT:(ot + 1) * TT], func=AF.Square), inc=sa)
                  SA_LC = sa.n
                  P.op("tensor", lambda e, ot=ot: e.matmul(
                      PC[0:1, 0:TT], ONEB, x2t[:, ot * TT:(ot + 1) * TT],
                      start=(ot == 0), stop=(ot == 7)),
                      waits=[(sa, SA_LC)], inc=(sp if ot == 7 else None))
                  P.op("tensor", lambda e, ot=ot: e.matmul(
                      PD[0:1, 0:TT], ONEB, xnt[:, ot * TT:(ot + 1) * TT],
                      start=(ot == 0), stop=(ot == 7)),
                      inc=(sp if ot == 7 else None))
              SV_MID = sv.n
              SA_C2 = sa.n
              SP_ST2 = sp.n
              P.op("vector", lambda e: e.tensor_scalar(
                  out=mrow[:, 0:TT], in0=PC[0:1, 0:TT], scalar1=1.0 / D,
                  scalar2=None, op0=ALU.mult), waits=[(sp, SP_ST2)], inc=sv)
              P.op("vector", lambda e: e.tensor_scalar(
                  out=mrow[:, 2 * TT:3 * TT], in0=PD[0:1, 0:TT], scalar1=1.0 / D,
                  scalar2=None, op0=ALU.mult), inc=sv)
              P.op("vector", lambda e: e.tensor_tensor(
                  out=mrow[:, 3 * TT:4 * TT], in0=mrow[:, 0:TT],
                  in1=mrow[:, 0:TT], op=ALU.mult), waits=[(sv, sv.n)], inc=sv)
              P.op("vector", lambda e: e.tensor_tensor(
                  out=mrow[:, 2 * TT:3 * TT], in0=mrow[:, 2 * TT:3 * TT],
                  in1=mrow[:, 3 * TT:4 * TT], op=ALU.subtract),
                  waits=[(sv, sv.n)], inc=sv)
              SV_V2 = sv.n
              P.op("scalar", lambda e: e.activation(
                  out=mrow[:, 3 * TT:4 * TT], in_=mrow[:, 2 * TT:3 * TT],
                  func=AF.Ln, bias=EPSC[0:1, :]), waits=[(sv, SV_V2)], inc=sa)
              P.op("scalar", lambda e: e.activation(
                  out=mrow[:, TT:2 * TT], in_=mrow[:, 3 * TT:4 * TT],
                  func=AF.Exp, scale=-0.5), waits=[(sa, sa.n)], inc=sa)
              SA_LN2 = sa.n

              tok0 = t * TT
              P.op("scalar", lambda e, tok0=tok0: e.dma_start(
                  out=stats_scr[0, tok0:tok0 + TT].unsqueeze(0), in_=mrow[:, 0:TT]),
                  waits=[(sa, SA_LN2)], inc=gstat)
              P.op("scalar", lambda e, tok0=tok0: e.dma_start(
                  out=stats_scr[1, tok0:tok0 + TT].unsqueeze(0), in_=mrow[:, TT:2 * TT]),
                  inc=gstat)
              GS_STAT = gstat.n
              for c in range(8):
                  P.op("scalar", lambda e, c=c, seq=seq, t0=t0: e.dma_start(
                      out=mid_scr[seq, c * 128:(c + 1) * 128, t0:t0 + TT],
                      in_=midt[:, c * TT:(c + 1) * TT]),
                      waits=[(sv, SV_MID)] if c == 0 else [], inc=gmid)
              GS_MID = gmid.n

              prev = dict(sp_st2=SP_ST2, sa_c2=SA_C2, gs_mid=GS_MID,
                          gs_stat=GS_STAT, sv_v2=SV_V2, sp_vend=SP_VEND,
                          sv_mid=SV_MID, sv_mid2=prev["sv_mid"],
                          sp_bc_done=SP_BC, sp_stat1=SP_STAT)

    SP_ATTN, SV_ATTN, SA_ATTN = sp.n, sv.n, sa.n
    GMID_ATTN, GSTAT_ATTN, GS_ATTN = gmid.n, gstat.n, gs.n


    if phases == "pa":
        with contextlib.ExitStack() as des:
            df = des.enter_context(nc.sbuf_tensor("dfa", [128, 2048], F32))
            qd = SemCtr(des.enter_context(nc.semaphore("qda")), 16)
            for s_ in range(n_seq):
                for c in range(8):
                    P.op("sync", lambda e, s_=s_, c=c: e.dma_start(
                        out=df[:], in_=mid_scr[s_, c * 128:(c + 1) * 128, :]),
                        waits=[(gmid, gmid.n), (gout, gout.n)], inc=qd)
                    P.op("scalar", lambda e, s_=s_, c=c: e.dma_start(
                        out=out_fm[s_, c * 128:(c + 1) * 128, :], in_=df[:]),
                        waits=[(qd, qd.n)], inc=gout)
                    P.wait("sync", (gout, gout.n))
            P.wait("scalar", (gout, gout.n))

    with contextlib.ExitStack() as fes:
      if phases == "paf":
          fsb = lambda n, s, d: fes.enter_context(nc.sbuf_tensor(n, s, d))
          fps = lambda n, s, d: fes.enter_context(nc.psum_tensor(n, s, d))
          wex = fsb("wex", [128, 8 * HID], BF16)
          wcn = fsb("wcn", [128, 32 * 1024], BF16)
          mid2_ = [fsb(f"mid2{i}", [128, 8 * TT], F32) for i in range(2)]
          xn2 = fsb("xn2", [128, 8 * TT], BF16)
          g_sb = fsb("g_sb", [128, 32 * TT], BF16)
          o_sb = fsb("o_sb", [128, 2 * TT], F32)
          mb2 = fsb("mb2", [128, TT], BF16)
          rb2 = fsb("rb2", [128, TT], BF16)
          HP = [fps(f"HP{i}", [128, 512], F32) for i in range(2)]
          PO = [fps(f"PO{i}", [128, 512], F32) for i in range(4)]
          PBC = [fps(f"PBC{i}", [128, 512], F32) for i in range(2)]

          P.wait("sync", (gs, GS_ATTN), (gmid, GMID_ATTN), (gstat, GSTAT_ATTN))
          for c in range(8):
              P.op("sync", lambda e, c=c: e.dma_start(
                  out=wex[:, c * HID:(c + 1) * HID],
                  in_=wexp_scr[c * 128:(c + 1) * 128, :]), inc=qf)
          wcv = wcon_scr.rearrange("(B jj s) c -> (B jj) s c", B=32, jj=4, s=32)
          for g in range(32):
              P.op("sync", lambda e, g=g: e.dma_start(
                  out=wcn[:, g * 1024:(g + 1) * 1024], in_=wcv[:, g, :]), inc=qf)
          QF_W = qf.n

          sv_o_prev = SV_ATTN
          sv_o_prev2 = SV_ATTN
          sp_bc2_prev = SP_ATTN
          sv_xn2_prev = SV_ATTN
          gs_out_prev = 0
          for t in range(NTIL):
              seq = t // TPS
              t0 = (t % TPS) * TT
              tok0 = t * TT
              qft = (qf, qf2)[t % 2]
              mid2 = mid2_[t % 2]
              for c in range(8):
                  P.op("sync", lambda e, c=c, seq=seq, t0=t0, mid2=mid2: e.dma_start(
                      out=mid2[:, c * TT:(c + 1) * TT],
                      in_=mid_scr[seq, c * 128:(c + 1) * 128, t0:t0 + TT]),
                      waits=[(sv, sv_o_prev2)] if c == 0 else [], inc=qft)
              P.op("sync", lambda e, tok0=tok0: e.dma_start(
                  out=o_sb[0:1, TT:2 * TT],
                  in_=stats_scr[0, tok0:tok0 + TT].unsqueeze(0)),
                  waits=[(sp, sp_bc2_prev), (gout, gs_out_prev)], inc=qft)
              P.op("sync", lambda e, tok0=tok0: e.dma_start(
                  out=o_sb[32:33, TT:2 * TT],
                  in_=stats_scr[1, tok0:tok0 + TT].unsqueeze(0)),
                  inc=qft)
              QF_T = qft.n

              w_ = [(qft, QF_T), (sv, sv_xn2_prev)]
              if t == 0:
                  w_.append((sa, SA_ATTN))
              P.op("tensor", lambda e: e.matmul(
                  PBC[0][:, :], orow[0:1, :], o_sb[0:1, TT:2 * TT],
                  start=True, stop=True), waits=w_, inc=sp)
              P.op("tensor", lambda e: e.matmul(
                  PBC[1][:, :], orow[32:33, :], o_sb[32:33, TT:2 * TT],
                  start=True, stop=True), inc=sp)
              SP_BC2 = sp.n
              sp_bc2_prev = SP_BC2
              P.op("vector", lambda e: e.tensor_copy(mb2[:], PBC[0][:, :]),
                   waits=[(sp, SP_BC2), (sv, sv_xn2_prev)], inc=sv)
              P.op("vector", lambda e: e.tensor_copy(rb2[:], PBC[1][:, :]), inc=sv)
              SV_BC2 = sv.n
              for c in range(8):
                  w_ = [(sv, SV_BC2)] if c == 0 else [(sv, sv.n)]
                  P.op("vector", lambda e, c=c, mid2=mid2: e.tensor_tensor(
                      out=PBC[1][:, :], in0=mid2[:, c * TT:(c + 1) * TT],
                      in1=mb2[:], op=ALU.subtract),
                      waits=w_, inc=sv)
                  P.op("vector", lambda e, c=c: e.tensor_tensor(
                      out=xn2[:, c * TT:(c + 1) * TT], in0=PBC[1][:, :],
                      in1=rb2[:], op=ALU.mult), waits=[(sv, sv.n)], inc=sv)
              SV_XN2 = sv.n
              sv_xn2_prev = SV_XN2

              for fc in range(32):
                  pp = HP[fc % 2]
                  for c in range(8):
                      w_ = []
                      if fc == 0 and c == 0:
                          w_ = [(qft, QF_T), (sv, SV_XN2)]
                      elif c == 0 and fc >= 2:
                          w_ = [(sa, sa.n)]
                      P.op("tensor", lambda e, fc=fc, c=c, pp=pp: e.matmul(
                          pp[:, :],
                          wex[:, c * HID + fc * 128: c * HID + (fc + 1) * 128],
                          xn2[:, c * TT:(c + 1) * TT],
                          start=(c == 0), stop=(c == 7)),
                          waits=w_, inc=(sp if c == 7 else None))
                  P.op("scalar", lambda e, fc=fc, pp=pp: e.activation(
                      out=g_sb[:, fc * TT:(fc + 1) * TT], in_=pp[:, :], func=AF.Gelu,
                      bias=biasg[:, fc:fc + 1]), waits=[(sp, sp.n)], inc=sa)
              SA_G = sa.n

              for oh in range(2):
                  for fc in range(32):
                      for osub in range(4):
                          ot = oh * 4 + osub
                          w_ = []
                          if fc == 0 and osub == 0:
                              w_ = [(sa, SA_G)] if oh == 0 else [(sv, sv.n)]
                          P.op("tensor", lambda e, fc=fc, ot=ot, osub=osub: e.matmul(
                              PO[osub][:, :],
                              wcn[:, fc * 1024 + ot * 128: fc * 1024 + (ot + 1) * 128],
                              g_sb[:, fc * TT:(fc + 1) * TT],
                              start=(fc == 0), stop=(fc == 31)),
                              waits=w_, inc=(sp if (fc == 31 and osub == 3) else None))
                  SP_O = sp.n
                  for osub in range(4):
                      ot = oh * 4 + osub
                      w_ = [(gout, gs_out_prev)] if osub % 2 == 0 else [(sv, sv.n)]
                      if osub == 0:
                          w_.append((sp, SP_O))
                      P.op("vector", lambda e, osub=osub, ot=ot, mid2=mid2: e.scalar_tensor_tensor(
                          out=o_sb[:, (osub % 2) * TT:((osub % 2) + 1) * TT],
                          in0=PO[osub][:, :],
                          scalar=bco[:, ot:ot + 1], in1=mid2[:, ot * TT:(ot + 1) * TT],
                          op0=ALU.add, op1=ALU.add),
                          waits=w_, inc=sv)
                      if osub % 2 == 1:
                          SV_O = sv.n
                          for oo in (osub - 1, osub):
                              ot2 = oh * 4 + oo
                              P.op("scalar", lambda e, oo=oo, ot2=ot2, seq=seq, t0=t0: e.dma_start(
                                  out=out_fm[seq, ot2 * 128:(ot2 + 1) * 128, t0:t0 + TT],
                                  in_=o_sb[:, (oo % 2) * TT:((oo % 2) + 1) * TT]),
                                  waits=[(sv, SV_O)] if oo == osub - 1 else [], inc=gout)
                          gs_out_prev = gout.n
              sv_o_prev2 = sv_o_prev
              sv_o_prev = sv.n

          P.wait("scalar", (gout, gout.n))

    P.wait("sync", (qc, qc.n), (qx0, qx0.n), (qx1, qx1.n), (qp, qp.n),
           (qp2, qp2.n), (qf, qf.n), (qf2, qf2.n))
    P.wait("scalar", (gs, gs.n), (gso, gso.n), (gstat, gstat.n), (gmid, gmid.n),
           (gout, gout.n))

    with nc.allow_low_precision(
            reason="intentional bf16 operand rounding (matmul inputs / 1/Z "
                   "scale); all accumulation stays fp32 in PSUM"), \
         nc.Block() as block:
        @block.sync
        def _(e):
            for fn in P.q["sync"]:
                fn(e)

        @block.tensor
        def _(e):
            for fn in P.q["tensor"]:
                fn(e)

        @block.vector
        def _(e):
            for fn in P.q["vector"]:
                fn(e)

        @block.scalar
        def _(e):
            for fn in P.q["scalar"]:
                fn(e)

        @block.gpsimd
        def _(e):
            for fn in P.q["gpsimd"]:
                fn(e)

    es.close()
    return nc



def make_consts():
    cf = np.zeros((128, 132), np.float32)
    cf[:, 0:128] = np.eye(128, dtype=np.float32)
    cf[:, 128] = EPS
    cbm = np.zeros((128, 132), np.float32)
    cbm[:, 0:128] = np.eye(128)
    cbm[:, 128] = 1.0
    rf = np.zeros((4, 128), np.float32)
    rf[0, :] = 1.0
    rf[1, 0:64] = 1.0
    rf[2, 64:128] = 1.0
    rb2 = np.zeros((2, 128), np.float32)
    rb2[0, 0:64] = 1.0
    rb2[1, 64:128] = 1.0
    ef = np.zeros((128, 128), np.float32)
    ef[0, 0:64] = 1.0
    ef[64, 64:128] = 1.0
    cb2 = np.zeros((128, 4), np.float32)
    cb2[:, 0] = 1.0
    return (cf, cbm.astype(ml_dtypes.bfloat16), rf,
            rb2.astype(ml_dtypes.bfloat16), ef, cb2.astype(ml_dtypes.bfloat16))


def make_inputs(L0_attn_w_attn, L0_ffn_W_exp, L0_ffn_b_exp, L0_ffn_up_weights,
                L0_ffn_up_bias, L0_ffn_down_weights, L0_ffn_W_con, L0_ffn_b_con,
                L0_ffn_down_bias, L0_attn_w_proj):
    SA_u, SQ_u = build_stage_mats(np.asarray(L0_ffn_up_weights, np.float32))
    SA_d, SQ_d = build_stage_mats(np.asarray(L0_ffn_down_weights, np.float32))
    cfc, cbb, rf, rb2, ef, cb2 = make_consts()
    fmap = f_of_hprime()
    return {
        "w_attnT": np.ascontiguousarray(np.asarray(L0_attn_w_attn, np.float32).T),
        "w_projT": np.ascontiguousarray(np.asarray(L0_attn_w_proj, np.float32).T),
        "w_expT": np.ascontiguousarray(np.asarray(L0_ffn_W_exp, np.float32).T),
        "w_con": np.ascontiguousarray(np.asarray(L0_ffn_W_con, np.float32)),
        "b_exp": np.asarray(L0_ffn_b_exp, np.float32),
        "ub_perm": np.ascontiguousarray(np.asarray(L0_ffn_up_bias, np.float32)[fmap]),
        "b_con": np.asarray(L0_ffn_b_con, np.float32),
        "d_bias": np.asarray(L0_ffn_down_bias, np.float32),
        "SAu": np.ascontiguousarray(SA_u), "SQu": np.ascontiguousarray(SQ_u),
        "SAdT": np.ascontiguousarray(SA_d.transpose(0, 1, 3, 2)),
        "SQdT": np.ascontiguousarray(SQ_d.transpose(0, 1, 3, 2)),
        "cf_in": cfc, "cb_in": cbb, "rf_in": rf, "rb_in": rb2, "ef_in": ef,
        "cb2_in": cb2,
    }


def kernel(L0_attn_x, L0_attn_w_attn, L0_attn_w_proj, L0_ffn_W_exp, L0_ffn_b_exp,
           L0_ffn_up_weights, L0_ffn_up_bias, L0_ffn_down_weights, L0_ffn_W_con,
           L0_ffn_b_con, L0_ffn_down_bias):
    x = np.asarray(L0_attn_x, dtype=np.float32)
    B = x.shape[0]
    n_seq = B // N_CORES
    shared = make_inputs(L0_attn_w_attn, L0_ffn_W_exp, L0_ffn_b_exp,
                         L0_ffn_up_weights, L0_ffn_up_bias, L0_ffn_down_weights,
                         L0_ffn_W_con, L0_ffn_b_con, L0_ffn_down_bias,
                         L0_attn_w_proj)
    nc = build_program(n_seq=n_seq)
    in_maps = []
    for core in range(N_CORES):
        m = dict(shared)
        m["x_fm"] = np.ascontiguousarray(
            x[core * n_seq:(core + 1) * n_seq].transpose(0, 2, 1))
        in_maps.append(m)
    res = run_bass_kernel_spmd(nc, in_maps, core_ids=list(range(N_CORES)))
    outs = [np.ascontiguousarray(res.results[c]["out_fm"].transpose(0, 2, 1))
            for c in range(N_CORES)]
    return np.concatenate(outs, axis=0)


# version 13
# speedup vs baseline: 1.0078x; 1.0078x over previous
import contextlib
import numpy as np
import ml_dtypes
import concourse.bass as bass
import concourse.mybir as mybir
from concourse.bass_utils import run_bass_kernel_spmd

F32 = mybir.dt.float32
BF16 = mybir.dt.bfloat16
AF = mybir.ActivationFunctionType
ALU = mybir.AluOpType
AXC = mybir.AxisListType.C

N_CORES = 8
D = 1024
HID = 4096
NH = 16
WIN = 256
TSEQ = 2048
TT = 512
EPS = 1e-5



def build_stage_mats(W):
    SA = np.zeros((7, 32, 128, 128), np.float32)
    l = np.arange(128)
    for st in range(7):
        s = 1 << st
        q = l >> (st + 1)
        half = (l >> st) & 1
        r = l & (s - 1)
        kloc = q * s + r
        part = l + np.where(half == 0, s, -s)
        for B in range(32):
            k = B * 64 + kloc
            SA[st, B, l, l] = W[st, k, half, half]
            SA[st, B, l, part] = W[st, k, half, 1 - half]
    SQ = np.zeros((5, 32, 128, 128), np.float32)
    p = np.arange(128)
    Bp = p >> 2
    jj = p & 3
    for st in range(7, 12):
        s = 1 << st
        sb = 1 << (st - 7)
        for g in range(32):
            f = Bp * 128 + jj * 32 + g
            half = (f >> st) & 1
            k = (f >> (st + 1)) * s + (f & (s - 1))
            partner = p + np.where(half == 0, sb * 4, -sb * 4)
            SQ[st - 7, g, p, p] = W[st, k, half, half]
            SQ[st - 7, g, p, partner] = W[st, k, half, 1 - half]
    return SA, SQ


def f_of_hprime():
    hp = np.arange(HID)
    g = hp >> 7
    p = hp & 127
    return (p >> 2) * 128 + (p & 3) * 32 + g



ENGS = ("sync", "tensor", "vector", "scalar", "gpsimd")


class SemCtr:
    __slots__ = ("sem", "step", "n")

    def __init__(self, sem, step):
        self.sem = sem
        self.step = step
        self.n = 0


class Plan:
    def __init__(self):
        self.q = {e: [] for e in ENGS}

    def op(self, eng, build, waits=(), inc=None, inck=1):
        wl = tuple((sc.sem, v) for sc, v in waits if v > 0)
        amt = None
        sem = None
        if inc is not None:
            amt = inck * inc.step
            inc.n += amt
            sem = inc.sem

        def fn(e, wl=wl, build=build, amt=amt, sem=sem):
            for s, v in wl:
                e.wait_ge(s, v)
            inst = build(e)
            if amt is not None:
                inst.then_inc(sem, amt)
        self.q[eng].append(fn)

    def wait(self, eng, *waits):
        wl = tuple((sc.sem, v) for sc, v in waits if v > 0)

        def fn(e, wl=wl):
            for s, v in wl:
                e.wait_ge(s, v)
        self.q[eng].append(fn)



def build_program(n_seq=4, phases="paf", attn_tiles=None, attn_stage=9, win_stage=9, skip_prep=False, win_heads=4, win_wins=None, prep_stage=9):
    nc = bass.Bass("TRN2", target_bir_lowering=False)
    NTOK = n_seq * TSEQ
    NTIL = NTOK // TT
    TPS = TSEQ // TT
    WPT = TT // WIN

    dt = nc.dram_tensor
    x_fm = dt("x_fm", [n_seq, D, TSEQ], F32, kind="ExternalInput").ap()
    w_attnT = dt("w_attnT", [D, 3 * D], F32, kind="ExternalInput").ap()
    w_projT = dt("w_projT", [D, D], F32, kind="ExternalInput").ap()
    w_expT = dt("w_expT", [HID, D], F32, kind="ExternalInput").ap()
    w_con = dt("w_con", [HID, D], F32, kind="ExternalInput").ap()
    b_exp = dt("b_exp", [HID], F32, kind="ExternalInput").ap()
    ub_perm = dt("ub_perm", [HID], F32, kind="ExternalInput").ap()
    b_con = dt("b_con", [D], F32, kind="ExternalInput").ap()
    d_bias = dt("d_bias", [D], F32, kind="ExternalInput").ap()
    SAu = dt("SAu", [7, 32, 128, 128], F32, kind="ExternalInput").ap()
    SQu = dt("SQu", [5, 32, 128, 128], F32, kind="ExternalInput").ap()
    SAdT = dt("SAdT", [7, 32, 128, 128], F32, kind="ExternalInput").ap()
    SQdT = dt("SQdT", [5, 32, 128, 128], F32, kind="ExternalInput").ap()
    cf_in = dt("cf_in", [128, 132], F32, kind="ExternalInput").ap()
    cb_in = dt("cb_in", [128, 132], BF16, kind="ExternalInput").ap()
    cb2_in = dt("cb2_in", [128, 4], BF16, kind="ExternalInput").ap()
    rf_in = dt("rf_in", [4, 128], F32, kind="ExternalInput").ap()
    ef_in = dt("ef_in", [128, 128], F32, kind="ExternalInput").ap()
    rb_in = dt("rb_in", [2, 128], BF16, kind="ExternalInput").ap()

    out_fm = dt("out_fm", [n_seq, D, TSEQ], F32, kind="ExternalOutput").ap()

    mid_scr = dt("mid_scr", [n_seq, D, TSEQ], F32, kind="Internal").ap()
    p_scr = dt("p_scr", [HID, 1028], F32, kind="Internal").ap()
    wexp_scr = dt("wexp_scr", [D, HID], BF16, kind="Internal").ap()
    wcon_scr = dt("wcon_scr", [HID, D], BF16, kind="Internal").ap()
    stats_scr = dt("stats_scr", [2, NTOK], F32, kind="Internal").ap()

    es = contextlib.ExitStack()
    sb = lambda name, shape, dtp: es.enter_context(nc.sbuf_tensor(name, shape, dtp))
    sem = lambda name, step: SemCtr(es.enter_context(nc.semaphore(name)), step)

    cf = sb("cf", [128, 132], F32)
    cbc = sb("cbc", [128, 4], BF16)
    orow = sb("orow", [33, 128], F32)
    e2f = sb("e2f", [128, 128], F32)
    biasg = sb("biasg", [128, 32], F32)
    bex = sb("bex", [128, 32], F32)
    bco = sb("bco", [128, 8], F32)

    qc = sem("qc", 16)
    qx0 = sem("qx0", 16)
    qx1 = sem("qx1", 16)
    qp = sem("qp", 16)
    qp2 = sem("qp2", 16)
    gso = sem("gso", 16)
    qf = sem("qf", 16)
    qf2 = sem("qf2", 16)
    gs = sem("gs", 16)
    gstat = sem("gstat", 16)
    gmid = sem("gmid", 16)
    gout = sem("gout", 16)
    sp = sem("sp", 1)
    sv = sem("sv", 1)
    sa = sem("sa", 1)
    sg = sem("sg", 1)

    P = Plan()

    P.op("sync", lambda e: e.dma_start(out=cf[:], in_=cf_in[:, :]), inc=qc)
    P.op("sync", lambda e: e.dma_start(out=cbc[:], in_=cb2_in[:, :]), inc=qc)
    P.op("sync", lambda e: e.dma_start(out=orow[0:1, :], in_=rf_in[0:1, :]), inc=qc)
    P.op("sync", lambda e: e.dma_start(out=orow[32:33, :], in_=rf_in[0:1, :]), inc=qc)
    P.op("sync", lambda e: e.dma_start(out=e2f[:], in_=ef_in[:, :]), inc=qc)
    QC0 = qc.n
    ID = cf[:, 0:128]
    EPSC = cf[:, 128:129]
    ONEB = cbc[:, 0:1]

    with contextlib.ExitStack() as pes:
      if not skip_prep:
        psb = lambda n, s, d: pes.enter_context(nc.sbuf_tensor(n, s, d))
        pps = lambda n, s, d: pes.enter_context(nc.psum_tensor(n, s, d))
        sall = [psb(f"sall{i}", [128, 32 * 128], F32) for i in range(2)]
        gall = [psb(f"gall{i}", [128, 32 * 128], F32) for i in range(2)]
        svin2 = [psb(f"svin{i}", [128, 1028], F32) for i in range(2)]
        svout2 = [psb(f"svout{i}", [128, 1028], F32) for i in range(2)]
        sv2b2 = [psb(f"sv2b{i}", [128, 1024], BF16) for i in range(2)]
        swx2 = [psb(f"swx{i}", [128, 1024], BF16) for i in range(2)]
        bx = psb("bx", [128, 96], F32)
        idbp = psb("idbp", [128, 128], BF16)
        P.op("sync", lambda e: e.dma_start(out=idbp[:], in_=cb_in[:, 0:128]), inc=qc)
        QC0b = qc.n
        QC0 = QC0b
        PP = [pps(f"PP{i}", [128, 512], F32) for i in range(3)]
        PQ2 = [pps(f"PQ2{i}", [128, 512], F32) for i in range(3)]
        PPT = [pps(f"PPT{i}", [128, 512], BF16) for i in range(2)]

        state = {"sall_mark": [0, 0], "gall_guard": 0, "ppmark": {},
                 "chain_end": 0}

        def batched_loop(stages_ap, n_st, reverse, emit_load, ncols, emit_out,
                         gslot_final_holder):
            order = list(range(n_st))
            if reverse:
                order = order[::-1]
            for i, j in enumerate(order):
                sslot = i % 2
                gsl = i % 2
                sv_ = stages_ap[j].rearrange("b p c -> p b c")
                qsl = (qp, qp2)[sslot]
                for part in range(8):
                    P.op("sync", lambda e, sv_=sv_, sslot=sslot, part=part: e.dma_start(
                        out=sall[sslot][:, :].rearrange("p (b c) -> p b c", b=32)[
                            :, part * 4:(part + 1) * 4, :],
                        in_=sv_[:, part * 4:(part + 1) * 4, :]),
                        waits=([(sp, state["sall_mark"][sslot]), (qc, QC0)]
                               if part == 0 else []), inc=qsl)
                ld = (qsl, qsl.n)
                for b in range(32):
                    tile = PP[b % 3][:, 0:128]
                    w_ = []
                    if b == 0:
                        w_ = [ld]
                        if i == 0:
                            w_.append((sv, sv.n))
                            w_.append((sa, sa.n))
                            w_.append((sp, state["gall_guard"]))
                            w_.append((qc, QC0))
                    pm = state["ppmark"].get(b % 3)
                    if pm is not None:
                        w_ = w_ + [pm]
                    if i == 0:
                        P.op("tensor", lambda e, b=b, tile=tile, sslot=sslot: e.matmul(
                            tile, sall[sslot][:, b * 128:(b + 1) * 128], ID,
                            start=True, stop=True), waits=w_, inc=sp)
                    else:
                        P.op("tensor", lambda e, b=b, tile=tile, sslot=sslot, gsl=gsl: e.matmul(
                            tile, sall[sslot][:, b * 128:(b + 1) * 128],
                            gall[1 - gsl][:, b * 128:(b + 1) * 128],
                            start=True, stop=True), waits=w_, inc=sp)
                    if b % 2 == 0:
                        P.op("vector", lambda e, b=b, tile=tile, gsl=gsl: e.tensor_copy(
                            gall[gsl][:, b * 128:(b + 1) * 128], tile),
                            waits=[(sp, sp.n)], inc=sv)
                        state["ppmark"][b % 3] = (sv, sv.n)
                    else:
                        P.op("scalar", lambda e, b=b, tile=tile, gsl=gsl: e.copy(
                            out=gall[gsl][:, b * 128:(b + 1) * 128], in_=tile),
                            waits=[(sp, sp.n)], inc=sa)
                        state["ppmark"][b % 3] = (sa, sa.n)
                state["sall_mark"][sslot] = sp.n
            gfin = (n_st - 1) % 2
            SV_CH, SA_CH = sv.n, sa.n
            state["chain_end"] = sp.n
            state.pop("apply_mm0", None)
            state.pop("apply_mm1", None)
            for b in range(32 if prep_stage >= 2 else 0):
                slot = b % 2
                qsl = (qp, qp2)[slot]
                emit_load(b, slot, qsl,
                          [(sp, state.get(f"apply_mm{slot}", state["chain_end"]))])
                ld = (qsl, qsl.n)
                splits = []
                c0 = 0
                for pi, pq in enumerate(PQ2):
                    if c0 >= ncols:
                        break
                    n = min(512, ncols - c0)
                    splits.append((pq, c0, n, slot))
                    c0 += n
                for k, (pq, c0, n, sl) in enumerate(splits):
                    w_ = []
                    if k == 0:
                        w_ = [ld, (sv, SV_CH), (sa, SA_CH)]
                        if state.get("pqmark"):
                            w_.append(state["pqmark"])
                    P.op("tensor", lambda e, pq=pq, c0=c0, n=n, b=b, sl=sl, gfin=gfin: e.matmul(
                        pq[:, (sl) * 0: n] if False else pq[:, 0:n],
                        gall[gfin][:, b * 128:(b + 1) * 128],
                        svin2[sl][:, c0:c0 + n], start=True, stop=True),
                        waits=w_, inc=(sp if k == len(splits) - 1 else None))
                state[f"apply_mm{slot}"] = sp.n
                for ki, (pq, c0, n, sl) in enumerate(splits):
                    w_ = [(sp, sp.n)]
                    if ki == 0 and state.get(f"outst{slot}"):
                        w_.append(state[f"outst{slot}"])
                    P.op("vector", lambda e, pq=pq, c0=c0, n=n, sl=sl: e.tensor_copy(
                        svout2[sl][:, c0:c0 + n], pq[:, 0:n]),
                        waits=w_, inc=sv)
                state["pqmark"] = (sv, sv.n)
                state[f"apply_ev{slot}"] = sv.n
                if prep_stage >= 3:
                    emit_out(b, slot)
            state["gall_guard"] = sp.n
            state["sall_mark"] = [sp.n, sp.n]

        def ld_upA(b, slot, qsl, w_):
            P.op("sync", lambda e, b=b, slot=slot: e.dma_start(
                out=svin2[slot][:, 0:1024], in_=w_expT[b * 128:(b + 1) * 128, :]),
                waits=w_, inc=qsl)
            P.op("sync", lambda e, b=b, slot=slot: e.dma_start(
                out=svin2[slot][:, 1024:1025],
                in_=b_exp[b * 128:(b + 1) * 128].unsqueeze(1)), inc=qsl)

        def out_upA(b, slot):
            gq = (gs, gso)[slot]
            P.op("scalar", lambda e, b=b, slot=slot: e.dma_start(
                out=p_scr[b * 128:(b + 1) * 128, 0:1025], in_=svout2[slot][:, 0:1025]),
                waits=[(sv, sv.n)], inc=gq)
            state[f"outst{slot}"] = (gq, gq.n)

        batched_loop(SAu, 7, True, ld_upA, 1025, out_upA, None)
        P.wait("sync", (gs, gs.n), (gso, gso.n))

        p_view = p_scr.rearrange("(B jj s) c -> (B jj) s c", B=32, jj=4, s=32)

        def ld_upQ(g, slot, qsl, w_):
            P.op("sync", lambda e, g=g, slot=slot: e.dma_start(
                out=svin2[slot][:, 0:1025], in_=p_view[:, g, 0:1025]),
                waits=w_, inc=qsl)

        def out_upQ(g, slot):
            P.op("vector", lambda e, g=g, slot=slot: e.tensor_copy(
                bex[:, g:g + 1], svout2[slot][:, 1024:1025]),
                waits=[(sv, sv.n)], inc=sv)
            P.op("vector", lambda e, slot=slot: e.tensor_copy(
                sv2b2[slot][:], svout2[slot][:, 0:1024]),
                waits=[(sv, sv.n), (sp, state.get(f"tp{slot}", 0))], inc=sv)
            SV_CAST = sv.n
            for ci in range(8):
                pt = PPT[ci % 2][:, 0:128]
                w_ = [(sv, SV_CAST)] if ci == 0 else []
                ptm = state.setdefault("ptmark", {}).get(ci % 2)
                if ptm is not None:
                    w_ = w_ + [ptm]
                else:
                    w_ = w_ + [(qc, QC0b)]
                P.op("tensor", lambda e, ci=ci, slot=slot, pt=pt: e.transpose(
                    pt, sv2b2[slot][:, ci * 128:(ci + 1) * 128], idbp[:]),
                    waits=w_, inc=sp)
                sw = state.get(f"swxst{slot}")
                P.op("scalar", lambda e, ci=ci, slot=slot, pt=pt: e.copy(
                    out=swx2[slot][:, ci * 128:(ci + 1) * 128], in_=pt),
                    waits=[(sp, sp.n)] + ([sw] if ci == 0 and sw else []), inc=sa)
                state["ptmark"][ci % 2] = (sa, sa.n)
            state[f"tp{slot}"] = sp.n
            SA_SWX = sa.n
            P.op("scalar", lambda e, g=g, slot=slot: e.dma_start(
                out=wexp_scr.rearrange("(cc p) f -> p (cc f)", p=128)[
                    :, 0:0] if False else wexp_scr[0:128, 0:128],
                in_=swx2[slot][:, 0:128]), waits=[], inc=gs) if False else None
            gq = (gs, gso)[slot]
            for ci in range(8):
                P.op("scalar", lambda e, ci=ci, g=g, slot=slot: e.dma_start(
                    out=wexp_scr[ci * 128:(ci + 1) * 128, g * 128:(g + 1) * 128],
                    in_=swx2[slot][:, ci * 128:(ci + 1) * 128]),
                    waits=[(sa, SA_SWX)] if ci == 0 else [], inc=gq)
            state[f"swxst{slot}"] = (gq, gq.n)
            state[f"outst{slot}"] = (sv, SV_CAST)

        batched_loop(SQu, 5, True, ld_upQ, 1025, out_upQ, None)
        P.wait("sync", (gs, gs.n), (gso, gso.n))

        w_view = w_con.rearrange("(B jj s) c -> (B jj) s c", B=32, jj=4, s=32)
        pv2 = p_scr.rearrange("(B jj s) c -> (B jj) s c", B=32, jj=4, s=32)

        def ld_dnQ(g, slot, qsl, w_):
            P.op("sync", lambda e, g=g, slot=slot: e.dma_start(
                out=svin2[slot][:, 0:1024], in_=w_view[:, g, :]),
                waits=w_, inc=qsl)

        def out_dnQ(g, slot):
            gq = (gs, gso)[slot]
            P.op("scalar", lambda e, g=g, slot=slot: e.dma_start(
                out=pv2[:, g, 0:1024], in_=svout2[slot][:, 0:1024]),
                waits=[(sv, sv.n)], inc=gq)
            state[f"outst{slot}"] = (gq, gq.n)

        batched_loop(SQdT, 5, False, ld_dnQ, 1024, out_dnQ, None)
        P.wait("sync", (gs, gs.n), (gso, gso.n))

        def ld_dnA(b, slot, qsl, w_):
            P.op("sync", lambda e, b=b, slot=slot: e.dma_start(
                out=svin2[slot][:, 0:1024], in_=p_scr[b * 128:(b + 1) * 128, 0:1024]),
                waits=w_, inc=qsl)

        def out_dnA(b, slot):
            wc = state.get(f"wcst{slot}")
            P.op("vector", lambda e, slot=slot: e.tensor_copy(
                sv2b2[slot][:], svout2[slot][:, 0:1024]),
                waits=[(sv, sv.n)] + ([wc] if wc else []), inc=sv)
            gq = (gs, gso)[slot]
            P.op("scalar", lambda e, b=b, slot=slot: e.dma_start(
                out=wcon_scr[b * 128:(b + 1) * 128, :], in_=sv2b2[slot][:]),
                waits=[(sv, sv.n)], inc=gq)
            state[f"wcst{slot}"] = (gq, gq.n)
            state[f"outst{slot}"] = (sv, sv.n)

        batched_loop(SAdT, 7, False, ld_dnA, 1024, out_dnA, None)
        P.wait("sync", (gs, gs.n), (gso, gso.n))

        uview = ub_perm.rearrange("(c p) -> c p", p=128)
        for c in range(32):
            P.op("sync", lambda e, c=c: e.dma_start(
                out=bx[:, 32 + c:33 + c], in_=uview[c, :].unsqueeze(1)), inc=qp)
        bcv = b_con.rearrange("(c p) -> c p", p=128)
        dbv = d_bias.rearrange("(c p) -> c p", p=128)
        for c in range(8):
            P.op("sync", lambda e, c=c: e.dma_start(
                out=bx[:, 64 + c:65 + c], in_=bcv[c, :].unsqueeze(1)), inc=qp)
            P.op("sync", lambda e, c=c: e.dma_start(
                out=bx[:, 72 + c:73 + c], in_=dbv[c, :].unsqueeze(1)), inc=qp)
        P.op("vector", lambda e: e.tensor_tensor(
            out=biasg[:], in0=bex[:], in1=bx[:, 32:64], op=ALU.add),
            waits=[(qp, qp.n), (sv, sv.n)], inc=sv)
        P.op("vector", lambda e: e.tensor_tensor(
            out=bco[:], in0=bx[:, 64:72], in1=bx[:, 72:80], op=ALU.add), inc=sv)

    SP_PREP, SV_PREP, SA_PREP, GS_PREP = sp.n, sv.n, sa.n, gs.n

    if "p" == phases:
        with contextlib.ExitStack() as des:
            db = des.enter_context(nc.sbuf_tensor("db", [128, 1024], BF16))
            df = des.enter_context(nc.sbuf_tensor("df", [128, 1024], F32))
            qd = SemCtr(des.enter_context(nc.semaphore("qd")), 16)
            for i in range(16):
                P.op("sync", lambda e, i=i: e.dma_start(
                    out=db[:], in_=wcon_scr[i * 128:(i + 1) * 128, :]),
                    waits=[(gs, gs.n), (sv, sv.n)], inc=qd)
                P.op("vector", lambda e, i=i: e.tensor_copy(df[:], db[:]),
                     waits=[(qd, qd.n), (gout, gout.n)], inc=sv)
                P.op("scalar", lambda e, i=i: e.dma_start(
                    out=out_fm[0, (i % 8) * 128:(i % 8 + 1) * 128,
                               (i // 8) * 1024:(i // 8 + 1) * 1024],
                    in_=df[:]), waits=[(sv, sv.n)], inc=gout)
                P.wait("sync", (gout, gout.n))
            P.wait("scalar", (gout, gout.n))
    attn_on = "a" in phases
    with contextlib.ExitStack() as aes:
      if attn_on:
          asb = lambda n, s, d: aes.enter_context(nc.sbuf_tensor(n, s, d))
          aps = lambda n, s, d: aes.enter_context(nc.psum_tensor(n, s, d))
          waT = asb("waT", [128, 8 * 3072], BF16)
          wpT = asb("wpT", [128, 8 * 1024], BF16)
          with nc.sbuf_tensor("wstg0", [128, 3072], F32) as wstg0, \
               nc.sbuf_tensor("wstg1", [128, 3072], F32) as wstg1:
              wst = (wstg0, wstg1)
              qw = (qx0, qx1)
              cast_mark = [0, 0]
              for c in range(8):
                  sl = c % 2
                  P.op("sync", lambda e, c=c, sl=sl: e.dma_start(
                      out=wst[sl][:, :], in_=w_attnT[c * 128:(c + 1) * 128, :]),
                      waits=[(sv, cast_mark[sl])], inc=qw[sl])
                  P.op("vector", lambda e, c=c, sl=sl: e.tensor_copy(
                      waT[:, c * 3072:(c + 1) * 3072], wst[sl][:, :]),
                      waits=[(qw[sl], qw[sl].n)], inc=sv)
                  cast_mark[sl] = sv.n
              for c in range(8):
                  sl = c % 2
                  P.op("sync", lambda e, c=c, sl=sl: e.dma_start(
                      out=wst[sl][:, 0:1024], in_=w_projT[c * 128:(c + 1) * 128, :]),
                      waits=[(sv, cast_mark[sl])], inc=qw[sl])
                  P.op("vector", lambda e, c=c, sl=sl: e.tensor_copy(
                      wpT[:, c * 1024:(c + 1) * 1024], wst[sl][:, 0:1024]),
                      waits=[(qw[sl], qw[sl].n)], inc=sv)
                  cast_mark[sl] = sv.n
              P.wait("sync", (sv, sv.n))
          SV_W = sv.n

          x_sb = [asb(f"x{i}", [128, 8 * TT], F32) for i in range(2)]
          xbt = asb("xbt", [128, 8 * TT], BF16)
          x2t = asb("x2t", [128, 8 * TT], BF16)
          xnt = asb("xnt", [128, 8 * TT], BF16)
          qkt = asb("qkt", [128, 16 * TT], BF16)
          v_sb = asb("v_sb", [128, 8 * TT], BF16)
          y_sb = asb("y_sb", [128, 8 * TT], BF16)
          ett = asb("ett", [128, 2 * 1024], BF16)
          midt = asb("midt", [128, 8 * TT], F32)
          bgt = asb("bgt", [128, 512], F32)
          mrow = asb("mrow", [1, 4 * TT], F32)
          zz = asb("zz", [128, 1024], F32)
          P.op("vector", lambda e: e.memset(zz[:], 0.0), inc=sv)
          mbt = asb("mbt", [128, TT], BF16)
          rbt = asb("rbt", [128, TT], BF16)
          PA = aps("PA", [128, 1024], F32)
          PB = aps("PB", [128, 1024], F32)
          PC = aps("PC", [128, 512], F32)
          PD = aps("PD", [128, 512], F32)
          PE_ = aps("PE_", [128, 512], F32)
          PF = aps("PF", [128, 512], F32)

          prev = dict(sp_st2=SP_PREP, sa_c2=SA_PREP, gs_mid=0,
                      gs_stat=0, sv_v2=SV_PREP, sp_vend=SP_PREP,
                      sv_mid=SV_W, sv_mid2=SV_W, sp_bc_done=SP_PREP,
                      sp_stat1=SP_PREP)

          for t in range(NTIL):
              seq = t // TPS
              t0 = (t % TPS) * TT
              qx = (qx0, qx1)[t % 2]
              xs = x_sb[t % 2]

              for c in range(8):
                  P.op("sync", lambda e, c=c, seq=seq, t0=t0, xs=xs: e.dma_start(
                      out=xs[:, c * TT:(c + 1) * TT],
                      in_=x_fm[seq, c * 128:(c + 1) * 128, t0:t0 + TT]),
                      waits=[(sv, prev["sv_mid2"])] if c == 0 else [], inc=qx)
              LD_X = qx.n

              for c in range(8):
                  P.op("scalar", lambda e, c=c, xs=xs: e.copy(
                      out=xbt[:, c * TT:(c + 1) * TT], in_=xs[:, c * TT:(c + 1) * TT]),
                      waits=([(qx, LD_X), (sp, prev["sp_stat1"])] if c == 0 else []),
                      inc=sa)
              for c in range(8):
                  P.op("scalar", lambda e, c=c, xs=xs: e.activation(
                      out=x2t[:, c * TT:(c + 1) * TT], in_=xs[:, c * TT:(c + 1) * TT],
                      func=AF.Square),
                      waits=([(sp, prev["sp_st2"])] if c == 0 else []), inc=sa)
              SA_CAST = sa.n

              if attn_stage < 2:
                prev = dict(prev, sv_mid=sv.n, sa_c2=sa.n, sp_st2=sp.n,
                            sv_v2=sv.n, sp_vend=sp.n)
                continue
              for c in range(8):
                  P.op("tensor", lambda e, c=c: e.matmul(
                      PC[0:1, 0:TT], ONEB, xbt[:, c * TT:(c + 1) * TT],
                      start=(c == 0), stop=(c == 7)),
                      waits=([(sa, SA_CAST), (sv, prev["sv_v2"])] if c == 0 else []),
                      inc=(sp if c == 7 else None))
              for c in range(8):
                  P.op("tensor", lambda e, c=c: e.matmul(
                      PD[0:1, 0:TT], ONEB, x2t[:, c * TT:(c + 1) * TT],
                      start=(c == 0), stop=(c == 7)), inc=(sp if c == 7 else None))
              SP_STAT = sp.n

              P.op("vector", lambda e: e.tensor_scalar(
                  out=mrow[:, 0:TT], in0=PC[0:1, 0:TT], scalar1=1.0 / D,
                  scalar2=None, op0=ALU.mult),
                  waits=[(sp, SP_STAT), (gstat, prev["gs_stat"])], inc=sv)
              P.op("vector", lambda e: e.tensor_scalar(
                  out=mrow[:, 2 * TT:3 * TT], in0=PD[0:1, 0:TT], scalar1=1.0 / D,
                  scalar2=None, op0=ALU.mult), inc=sv)
              P.op("vector", lambda e: e.tensor_tensor(
                  out=mrow[:, 3 * TT:4 * TT], in0=mrow[:, 0:TT],
                  in1=mrow[:, 0:TT], op=ALU.mult), waits=[(sv, sv.n)], inc=sv)
              P.op("vector", lambda e: e.tensor_tensor(
                  out=mrow[:, 2 * TT:3 * TT], in0=mrow[:, 2 * TT:3 * TT],
                  in1=mrow[:, 3 * TT:4 * TT], op=ALU.subtract),
                  waits=[(sv, sv.n)], inc=sv)
              SV_VAR = sv.n
              P.op("scalar", lambda e: e.activation(
                  out=mrow[:, 3 * TT:4 * TT], in_=mrow[:, 2 * TT:3 * TT],
                  func=AF.Ln, bias=EPSC[0:1, :]), waits=[(sv, SV_VAR)], inc=sa)
              P.op("scalar", lambda e: e.activation(
                  out=mrow[:, TT:2 * TT], in_=mrow[:, 3 * TT:4 * TT],
                  func=AF.Exp, scale=-0.5), waits=[(sa, sa.n)], inc=sa)
              SA_LN = sa.n

              if attn_stage < 3:
                prev = dict(prev, sv_mid=sv.n, sa_c2=sa.n, sp_st2=sp.n,
                            sv_v2=sv.n, sp_vend=sp.n)
                continue
              P.op("tensor", lambda e: e.matmul(
                  PE_[:, 0:TT], orow[0:1, :], mrow[0:1, 0:TT], start=True, stop=True),
                  waits=[(sa, SA_LN), (sv, prev["sv_mid"])], inc=sp)
              P.op("tensor", lambda e: e.matmul(
                  PF[:, 0:TT], orow[0:1, :], mrow[0:1, TT:2 * TT], start=True, stop=True),
                  inc=sp)
              SP_BC = sp.n
              P.op("vector", lambda e: e.tensor_copy(mbt[:], PE_[:, 0:TT]),
                   waits=[(sp, SP_BC)], inc=sv)
              P.op("vector", lambda e: e.tensor_copy(rbt[:], PF[:, 0:TT]), inc=sv)
              SV_BC = sv.n

              for c in range(8):
                  P.op("vector", lambda e, c=c: e.tensor_tensor(
                      out=x2t[:, c * TT:(c + 1) * TT], in0=xbt[:, c * TT:(c + 1) * TT],
                      in1=mbt[:], op=ALU.subtract),
                      waits=([(sv, SV_BC)] if c == 0 else [(sv, sv.n)]), inc=sv)
                  P.op("vector", lambda e, c=c: e.tensor_tensor(
                      out=xnt[:, c * TT:(c + 1) * TT], in0=x2t[:, c * TT:(c + 1) * TT],
                      in1=rbt[:], op=ALU.mult),
                      waits=[(sv, sv.n)] + ([(sp, prev["sp_vend"]),
                                             (sp, prev["sp_st2"])] if c == 0 else []),
                      inc=sv)
              SV_XN = sv.n

              if attn_stage < 4:
                prev = dict(prev, sv_mid=sv.n, sa_c2=sa.n, sp_st2=sp.n,
                            sv_v2=sv.n, sp_vend=sp.n)
                continue
              pq = (PC, PD)
              for ei in range(16):
                  pp = pq[ei % 2]
                  for c in range(8):
                      w_ = []
                      if ei == 0 and c == 0:
                          w_ = [(sv, SV_XN)]
                      elif c == 0 and ei >= 2:
                          w_ = [(sv, sv.n)]
                      P.op("tensor", lambda e, ei=ei, c=c, pp=pp: e.matmul(
                          pp[:, 0:TT],
                          waT[:, c * 3072 + ei * 128: c * 3072 + (ei + 1) * 128],
                          xnt[:, c * TT:(c + 1) * TT],
                          start=(c == 0), stop=(c == 7)),
                          waits=w_, inc=(sp if c == 7 else None))
                  P.op("vector", lambda e, ei=ei, pp=pp: e.tensor_copy(
                      qkt[:, ei * TT:(ei + 1) * TT], pp[:, 0:TT]),
                      waits=[(sp, sp.n)], inc=sv)
              for ts4 in range(TT // 128):
                  for eh in range(2):
                      pp = pq[(ts4 * 2 + eh) % 2]
                      for c in range(8):
                          P.op("tensor", lambda e, ts4=ts4, eh=eh, c=c, pp=pp: e.matmul(
                              pp[:, :],
                              xnt[:, c * TT + ts4 * 128: c * TT + (ts4 + 1) * 128],
                              waT[:, c * 3072 + 2048 + eh * 512: c * 3072 + 2048 + (eh + 1) * 512],
                              start=(c == 0), stop=(c == 7)),
                              waits=([(sv, sv.n)] if c == 0 else []),
                              inc=(sp if c == 7 else None))
                      P.op("vector", lambda e, ts4=ts4, eh=eh, pp=pp: e.tensor_copy(
                          v_sb[:, ts4 * 1024 + eh * 512: ts4 * 1024 + (eh + 1) * 512],
                          pp[:, :]), waits=[(sp, sp.n)], inc=sv)
              SV_QKV = sv.n
              SP_VEND = sp.n

              if attn_stage < 5:
                  prev = dict(prev, sv_mid=sv.n, sa_c2=sa.n, sp_st2=sp.n,
                              sv_v2=sv.n, sp_vend=SP_VEND)
                  continue
              CS2 = {(0, 0): 0, (0, 1): 256, (1, 0): 512, (1, 1): 768}
              NIT = WPT * 8
              mk = {"st": {}, "et": {-2: prev["sa_c2"], -1: prev["sa_c2"]},
                    "z": {}, "rc": {-2: prev["sa_c2"], -1: prev["sa_c2"]},
                    "av": {}, "bg": {}, "bge": {-2: prev["sa_c2"], -1: prev["sa_c2"]},
                    "y": {-2: SV_BC, -1: SV_BC, -3: SV_BC, -4: SV_BC}}

              def emit_st(k):
                  w, gi = divmod(k, 8)
                  pp = (PA, PB)[k % 2]
                  first = (k == 0)
                  for idx, (hp, jh) in enumerate(((0, 0), (0, 1), (1, 0), (1, 1))):
                      h = 2 * gi + hp
                      qc_ = h // 2
                      qb = (h % 2) * 64
                      w_ = []
                      if idx == 0:
                          w_ = [(sa, mk["et"][k - 2]), (sa, mk["rc"][k - 2])]
                          if first:
                              w_.append((sv, SV_QKV))
                      P.op("tensor", lambda e, pp=pp, qc_=qc_, qb=qb, jh=jh, w=w, hp=hp: e.matmul(
                          pp[:, CS2[(hp, jh)]:CS2[(hp, jh)] + 256],
                          qkt[qb:qb + 64, (8 + qc_) * TT + w * 256 + jh * 128:
                              (8 + qc_) * TT + w * 256 + (jh + 1) * 128],
                          qkt[qb:qb + 64, qc_ * TT + w * 256: qc_ * TT + (w + 1) * 256],
                          start=True, stop=True),
                          waits=w_, inc=(sp if idx == 3 else None))
                  mk["st"][k] = sp.n

              def emit_exp(k):
                  pp = (PA, PB)[k % 2]
                  eb = (k % 2) * 1024
                  P.op("scalar", lambda e, pp=pp, eb=eb: e.activation(
                      out=ett[:, eb:eb + 1024], in_=pp[:, :], func=AF.Exp, scale=0.125),
                      waits=[(sp, mk["st"][k]), (sp, mk["av"].get(k - 2, 0))], inc=sa)
                  mk["et"][k] = sa.n

              def emit_z(k):
                  zp = (PA, PB)[k % 2]
                  eb = (k % 2) * 1024
                  for idx, (hp, col) in enumerate(((0, 0), (1, 256))):
                      for jh in range(2):
                          w_ = []
                          if idx == 0 and jh == 0:
                              w_ = [(sa, mk["et"][k]), (sa, mk["rc"][k - 2])]
                          P.op("tensor", lambda e, zp=zp, col=col, eb=eb, hp=hp, jh=jh: e.matmul(
                              zp[0:1, col:col + 256], ONEB,
                              ett[:, eb + CS2[(hp, jh)]: eb + CS2[(hp, jh)] + 256],
                              start=(jh == 0), stop=(jh == 1)),
                              waits=w_, inc=(sp if (idx == 1 and jh == 1) else None))
                  mk["z"][k] = sp.n

              def emit_recip(k):
                  zp = (PA, PB)[k % 2]
                  sc = (k % 2) * 512
                  P.op("scalar", lambda e, zp=zp, sc=sc: e.activation(
                      out=zz[0:1, sc:sc + 512], in_=zp[0:1, 0:512], func=AF.Ln),
                      waits=[(sp, mk["z"][k]), (sp, mk["bg"].get(k - 2, 0))], inc=sa)
                  mk["rc"][k] = sa.n

              def emit_av(k):
                  w, gi = divmod(k, 8)
                  eb = (k % 2) * 1024
                  yp = (PD, PE_)[k % 2]
                  for idx, hp in enumerate((0, 1)):
                      h = 2 * gi + hp
                      ob = hp * 64
                      for jh in range(2):
                          vt = w * 2 + jh
                          w_ = []
                          if idx == 0 and jh == 0:
                              w_ = [(sa, mk["et"][k]), (sv, mk["y"][k - 2])]
                          P.op("tensor", lambda e, ob=ob, h=h, jh=jh, vt=vt, eb=eb, yp=yp, hp=hp: e.matmul(
                              yp[ob:ob + 64, 0:256],
                              v_sb[:, vt * 1024 + h * 64: vt * 1024 + (h + 1) * 64],
                              ett[:, eb + CS2[(hp, jh)]: eb + CS2[(hp, jh)] + 256],
                              start=(jh == 0), stop=(jh == 1)),
                              waits=w_, inc=(sp if (idx == 1 and jh == 1) else None))
                  mk["av"][k] = sp.n

              def emit_bg(k):
                  sc = (k % 2) * 512
                  bp = (PC, PF)[k % 2]
                  P.op("tensor", lambda e, sc=sc, bp=bp: e.matmul(
                      bp[0:64, 0:256], orow[0:1, 0:64], zz[0:1, sc:sc + 256],
                      start=True, stop=True),
                      waits=[(sa, mk["rc"][k]), (sa, mk["bge"][k - 2])])
                  P.op("tensor", lambda e, sc=sc, bp=bp: e.matmul(
                      bp[64:128, 0:256], orow[0:1, 64:128], zz[0:1, sc + 256:sc + 512],
                      start=True, stop=True), inc=sp)
                  mk["bg"][k] = sp.n

              def emit_bge(k):
                  bc = (k % 2) * 256
                  bp = (PC, PF)[k % 2]
                  P.op("scalar", lambda e, bc=bc, bp=bp: e.activation(
                      out=bgt[:, bc:bc + 256], in_=bp[:, 0:256], func=AF.Exp,
                      scale=-1.0),
                      waits=[(sp, mk["bg"][k]), (sv, mk["y"][k - 2])], inc=sa)
                  mk["bge"][k] = sa.n

              def emit_y(k):
                  w, gi = divmod(k, 8)
                  yc = (k % 2) * 256
                  yp = (PD, PE_)[k % 2]
                  P.op("vector", lambda e, yc=yc, gi=gi, w=w, yp=yp: e.tensor_tensor(
                      out=y_sb[:, gi * TT + w * 256: gi * TT + (w + 1) * 256],
                      in0=yp[:, 0:256], in1=bgt[:, yc:yc + 256], op=ALU.mult),
                      waits=[(sp, mk["av"][k]), (sa, mk["bge"][k])], inc=sv)
                  mk["y"][k] = sv.n

              for k in range(NIT + 2):
                  if k < NIT:
                      emit_st(k)
                  if 1 <= k <= NIT:
                      emit_z(k - 1)
                  if k >= 2:
                      emit_av(k - 2)
                      emit_bg(k - 2)
                  if k < NIT:
                      emit_exp(k)
                  if 1 <= k <= NIT:
                      emit_recip(k - 1)
                  if k >= 2:
                      emit_bge(k - 2)
                      emit_y(k - 2)
              SV_Y = sv.n


              if attn_stage < 6:
                prev = dict(prev, sv_mid=sv.n, sa_c2=sa.n, sp_st2=sp.n,
                            sv_v2=sv.n, sp_vend=SP_VEND)
                continue
              for ot in range(8):
                  pp = (PE_, PF)[ot % 2]
                  for c in range(8):
                      w_ = []
                      if ot == 0 and c == 0:
                          w_ = [(sv, SV_Y)]
                      elif c == 0 and ot >= 2:
                          w_ = [(sv, sv.n)]
                      P.op("tensor", lambda e, ot=ot, c=c, pp=pp: e.matmul(
                          pp[:, :],
                          wpT[:, c * 1024 + ot * 128: c * 1024 + (ot + 1) * 128],
                          y_sb[:, c * TT:(c + 1) * TT],
                          start=(c == 0), stop=(c == 7)),
                          waits=w_, inc=(sp if c == 7 else None))
                  P.op("vector", lambda e, ot=ot, pp=pp, xs=xs: e.tensor_tensor(
                      out=midt[:, ot * TT:(ot + 1) * TT], in0=pp[:, :],
                      in1=xs[:, ot * TT:(ot + 1) * TT], op=ALU.add),
                      waits=[(sp, sp.n)] + ([(gmid, prev["gs_mid"]),
                                             (sa, prev["sa_c2"])] if ot == 0 else []),
                      inc=sv)
                  SV_EV = sv.n
                  P.op("scalar", lambda e, ot=ot: e.copy(
                      out=x2t[:, ot * TT:(ot + 1) * TT],
                      in_=midt[:, ot * TT:(ot + 1) * TT]),
                      waits=[(sv, SV_EV)] + ([(sp, prev["sp_st2"])] if ot == 0 else []),
                      inc=sa)
                  P.op("scalar", lambda e, ot=ot: e.activation(
                      out=xnt[:, ot * TT:(ot + 1) * TT],
                      in_=midt[:, ot * TT:(ot + 1) * TT], func=AF.Square), inc=sa)
                  SA_LC = sa.n
                  P.op("tensor", lambda e, ot=ot: e.matmul(
                      PC[0:1, 0:TT], ONEB, x2t[:, ot * TT:(ot + 1) * TT],
                      start=(ot == 0), stop=(ot == 7)),
                      waits=[(sa, SA_LC)], inc=(sp if ot == 7 else None))
                  P.op("tensor", lambda e, ot=ot: e.matmul(
                      PD[0:1, 0:TT], ONEB, xnt[:, ot * TT:(ot + 1) * TT],
                      start=(ot == 0), stop=(ot == 7)),
                      inc=(sp if ot == 7 else None))
              SV_MID = sv.n
              SA_C2 = sa.n
              SP_ST2 = sp.n
              P.op("vector", lambda e: e.tensor_scalar(
                  out=mrow[:, 0:TT], in0=PC[0:1, 0:TT], scalar1=1.0 / D,
                  scalar2=None, op0=ALU.mult), waits=[(sp, SP_ST2)], inc=sv)
              P.op("vector", lambda e: e.tensor_scalar(
                  out=mrow[:, 2 * TT:3 * TT], in0=PD[0:1, 0:TT], scalar1=1.0 / D,
                  scalar2=None, op0=ALU.mult), inc=sv)
              P.op("vector", lambda e: e.tensor_tensor(
                  out=mrow[:, 3 * TT:4 * TT], in0=mrow[:, 0:TT],
                  in1=mrow[:, 0:TT], op=ALU.mult), waits=[(sv, sv.n)], inc=sv)
              P.op("vector", lambda e: e.tensor_tensor(
                  out=mrow[:, 2 * TT:3 * TT], in0=mrow[:, 2 * TT:3 * TT],
                  in1=mrow[:, 3 * TT:4 * TT], op=ALU.subtract),
                  waits=[(sv, sv.n)], inc=sv)
              SV_V2 = sv.n
              P.op("scalar", lambda e: e.activation(
                  out=mrow[:, 3 * TT:4 * TT], in_=mrow[:, 2 * TT:3 * TT],
                  func=AF.Ln, bias=EPSC[0:1, :]), waits=[(sv, SV_V2)], inc=sa)
              P.op("scalar", lambda e: e.activation(
                  out=mrow[:, TT:2 * TT], in_=mrow[:, 3 * TT:4 * TT],
                  func=AF.Exp, scale=-0.5), waits=[(sa, sa.n)], inc=sa)
              SA_LN2 = sa.n

              tok0 = t * TT
              P.op("scalar", lambda e, tok0=tok0: e.dma_start(
                  out=stats_scr[0, tok0:tok0 + TT].unsqueeze(0), in_=mrow[:, 0:TT]),
                  waits=[(sa, SA_LN2)], inc=gstat)
              P.op("scalar", lambda e, tok0=tok0: e.dma_start(
                  out=stats_scr[1, tok0:tok0 + TT].unsqueeze(0), in_=mrow[:, TT:2 * TT]),
                  inc=gstat)
              GS_STAT = gstat.n
              for c in range(8):
                  P.op("scalar", lambda e, c=c, seq=seq, t0=t0: e.dma_start(
                      out=mid_scr[seq, c * 128:(c + 1) * 128, t0:t0 + TT],
                      in_=midt[:, c * TT:(c + 1) * TT]),
                      waits=[(sv, SV_MID)] if c == 0 else [], inc=gmid)
              GS_MID = gmid.n

              prev = dict(sp_st2=SP_ST2, sa_c2=SA_C2, gs_mid=GS_MID,
                          gs_stat=GS_STAT, sv_v2=SV_V2, sp_vend=SP_VEND,
                          sv_mid=SV_MID, sv_mid2=prev["sv_mid"],
                          sp_bc_done=SP_BC, sp_stat1=SP_STAT)

    SP_ATTN, SV_ATTN, SA_ATTN = sp.n, sv.n, sa.n
    GMID_ATTN, GSTAT_ATTN, GS_ATTN = gmid.n, gstat.n, gs.n


    if phases == "pa":
        with contextlib.ExitStack() as des:
            df = des.enter_context(nc.sbuf_tensor("dfa", [128, 2048], F32))
            qd = SemCtr(des.enter_context(nc.semaphore("qda")), 16)
            for s_ in range(n_seq):
                for c in range(8):
                    P.op("sync", lambda e, s_=s_, c=c: e.dma_start(
                        out=df[:], in_=mid_scr[s_, c * 128:(c + 1) * 128, :]),
                        waits=[(gmid, gmid.n), (gout, gout.n)], inc=qd)
                    P.op("scalar", lambda e, s_=s_, c=c: e.dma_start(
                        out=out_fm[s_, c * 128:(c + 1) * 128, :], in_=df[:]),
                        waits=[(qd, qd.n)], inc=gout)
                    P.wait("sync", (gout, gout.n))
            P.wait("scalar", (gout, gout.n))

    with contextlib.ExitStack() as fes:
      if phases == "paf":
          fsb = lambda n, s, d: fes.enter_context(nc.sbuf_tensor(n, s, d))
          fps = lambda n, s, d: fes.enter_context(nc.psum_tensor(n, s, d))
          wex = fsb("wex", [128, 8 * HID], BF16)
          wcn = fsb("wcn", [128, 32 * 1024], BF16)
          mid2_ = [fsb(f"mid2{i}", [128, 8 * TT], F32) for i in range(2)]
          xn2 = fsb("xn2", [128, 8 * TT], BF16)
          g_sb = fsb("g_sb", [128, 32 * TT], BF16)
          o_sb = fsb("o_sb", [128, 2 * TT], F32)
          mb2 = fsb("mb2", [128, TT], BF16)
          rb2 = fsb("rb2", [128, TT], BF16)
          HP = [fps(f"HP{i}", [128, 512], F32) for i in range(2)]
          PO = [fps(f"PO{i}", [128, 512], F32) for i in range(4)]
          PBC = [fps(f"PBC{i}", [128, 512], F32) for i in range(2)]

          P.wait("sync", (gs, GS_ATTN), (gmid, GMID_ATTN), (gstat, GSTAT_ATTN))
          for c in range(8):
              P.op("sync", lambda e, c=c: e.dma_start(
                  out=wex[:, c * HID:(c + 1) * HID],
                  in_=wexp_scr[c * 128:(c + 1) * 128, :]), inc=qf)
          wcv = wcon_scr.rearrange("(B jj s) c -> (B jj) s c", B=32, jj=4, s=32)
          for g in range(32):
              P.op("sync", lambda e, g=g: e.dma_start(
                  out=wcn[:, g * 1024:(g + 1) * 1024], in_=wcv[:, g, :]), inc=qf)
          QF_W = qf.n

          sv_o_prev = SV_ATTN
          sv_o_prev2 = SV_ATTN
          sp_bc2_prev = SP_ATTN
          sv_xn2_prev = SV_ATTN
          gs_out_prev = 0
          for t in range(NTIL):
              seq = t // TPS
              t0 = (t % TPS) * TT
              tok0 = t * TT
              qft = (qf, qf2)[t % 2]
              mid2 = mid2_[t % 2]
              for c in range(8):
                  P.op("sync", lambda e, c=c, seq=seq, t0=t0, mid2=mid2: e.dma_start(
                      out=mid2[:, c * TT:(c + 1) * TT],
                      in_=mid_scr[seq, c * 128:(c + 1) * 128, t0:t0 + TT]),
                      waits=[(sv, sv_o_prev2)] if c == 0 else [], inc=qft)
              P.op("sync", lambda e, tok0=tok0: e.dma_start(
                  out=o_sb[0:1, TT:2 * TT],
                  in_=stats_scr[0, tok0:tok0 + TT].unsqueeze(0)),
                  waits=[(sp, sp_bc2_prev), (gout, gs_out_prev)], inc=qft)
              P.op("sync", lambda e, tok0=tok0: e.dma_start(
                  out=o_sb[32:33, TT:2 * TT],
                  in_=stats_scr[1, tok0:tok0 + TT].unsqueeze(0)),
                  inc=qft)
              QF_T = qft.n

              w_ = [(qft, QF_T), (sv, sv_xn2_prev)]
              if t == 0:
                  w_.append((sa, SA_ATTN))
              P.op("tensor", lambda e: e.matmul(
                  PBC[0][:, :], orow[0:1, :], o_sb[0:1, TT:2 * TT],
                  start=True, stop=True), waits=w_, inc=sp)
              P.op("tensor", lambda e: e.matmul(
                  PBC[1][:, :], orow[32:33, :], o_sb[32:33, TT:2 * TT],
                  start=True, stop=True), inc=sp)
              SP_BC2 = sp.n
              sp_bc2_prev = SP_BC2
              P.op("vector", lambda e: e.tensor_copy(mb2[:], PBC[0][:, :]),
                   waits=[(sp, SP_BC2), (sv, sv_xn2_prev)], inc=sv)
              P.op("vector", lambda e: e.tensor_copy(rb2[:], PBC[1][:, :]), inc=sv)
              SV_BC2 = sv.n
              for c in range(8):
                  w_ = [(sv, SV_BC2)] if c == 0 else [(sv, sv.n)]
                  P.op("vector", lambda e, c=c, mid2=mid2: e.tensor_tensor(
                      out=PBC[1][:, :], in0=mid2[:, c * TT:(c + 1) * TT],
                      in1=mb2[:], op=ALU.subtract),
                      waits=w_, inc=sv)
                  P.op("vector", lambda e, c=c: e.tensor_tensor(
                      out=xn2[:, c * TT:(c + 1) * TT], in0=PBC[1][:, :],
                      in1=rb2[:], op=ALU.mult), waits=[(sv, sv.n)], inc=sv)
              SV_XN2 = sv.n
              sv_xn2_prev = SV_XN2

              for fc in range(32):
                  pp = HP[fc % 2]
                  for c in range(8):
                      w_ = []
                      if fc == 0 and c == 0:
                          w_ = [(qft, QF_T), (sv, SV_XN2)]
                      elif c == 0 and fc >= 2:
                          w_ = [(sa, sa.n)]
                      P.op("tensor", lambda e, fc=fc, c=c, pp=pp: e.matmul(
                          pp[:, :],
                          wex[:, c * HID + fc * 128: c * HID + (fc + 1) * 128],
                          xn2[:, c * TT:(c + 1) * TT],
                          start=(c == 0), stop=(c == 7)),
                          waits=w_, inc=(sp if c == 7 else None))
                  P.op("scalar", lambda e, fc=fc, pp=pp: e.activation(
                      out=g_sb[:, fc * TT:(fc + 1) * TT], in_=pp[:, :], func=AF.Gelu,
                      bias=biasg[:, fc:fc + 1]), waits=[(sp, sp.n)], inc=sa)
              SA_G = sa.n

              for oh in range(2):
                  for fc in range(32):
                      for osub in range(4):
                          ot = oh * 4 + osub
                          w_ = []
                          if fc == 0 and osub == 0:
                              w_ = [(sa, SA_G)] if oh == 0 else [(sv, sv.n)]
                          P.op("tensor", lambda e, fc=fc, ot=ot, osub=osub: e.matmul(
                              PO[osub][:, :],
                              wcn[:, fc * 1024 + ot * 128: fc * 1024 + (ot + 1) * 128],
                              g_sb[:, fc * TT:(fc + 1) * TT],
                              start=(fc == 0), stop=(fc == 31)),
                              waits=w_, inc=(sp if (fc == 31 and osub == 3) else None))
                  SP_O = sp.n
                  for osub in range(4):
                      ot = oh * 4 + osub
                      w_ = [(gout, gs_out_prev)] if osub % 2 == 0 else [(sv, sv.n)]
                      if osub == 0:
                          w_.append((sp, SP_O))
                      P.op("vector", lambda e, osub=osub, ot=ot, mid2=mid2: e.scalar_tensor_tensor(
                          out=o_sb[:, (osub % 2) * TT:((osub % 2) + 1) * TT],
                          in0=PO[osub][:, :],
                          scalar=bco[:, ot:ot + 1], in1=mid2[:, ot * TT:(ot + 1) * TT],
                          op0=ALU.add, op1=ALU.add),
                          waits=w_, inc=sv)
                      if osub % 2 == 1:
                          SV_O = sv.n
                          for oo in (osub - 1, osub):
                              ot2 = oh * 4 + oo
                              P.op("scalar", lambda e, oo=oo, ot2=ot2, seq=seq, t0=t0: e.dma_start(
                                  out=out_fm[seq, ot2 * 128:(ot2 + 1) * 128, t0:t0 + TT],
                                  in_=o_sb[:, (oo % 2) * TT:((oo % 2) + 1) * TT]),
                                  waits=[(sv, SV_O)] if oo == osub - 1 else [], inc=gout)
                          gs_out_prev = gout.n
              sv_o_prev2 = sv_o_prev
              sv_o_prev = sv.n

          P.wait("scalar", (gout, gout.n))

    P.wait("sync", (qc, qc.n), (qx0, qx0.n), (qx1, qx1.n), (qp, qp.n),
           (qp2, qp2.n), (qf, qf.n), (qf2, qf2.n))
    P.wait("scalar", (gs, gs.n), (gso, gso.n), (gstat, gstat.n), (gmid, gmid.n),
           (gout, gout.n))

    with nc.allow_low_precision(
            reason="intentional bf16 operand rounding (matmul inputs / 1/Z "
                   "scale); all accumulation stays fp32 in PSUM"), \
         nc.Block() as block:
        @block.sync
        def _(e):
            for fn in P.q["sync"]:
                fn(e)

        @block.tensor
        def _(e):
            for fn in P.q["tensor"]:
                fn(e)

        @block.vector
        def _(e):
            for fn in P.q["vector"]:
                fn(e)

        @block.scalar
        def _(e):
            for fn in P.q["scalar"]:
                fn(e)

        @block.gpsimd
        def _(e):
            for fn in P.q["gpsimd"]:
                fn(e)

    es.close()
    return nc



def make_consts():
    cf = np.zeros((128, 132), np.float32)
    cf[:, 0:128] = np.eye(128, dtype=np.float32)
    cf[:, 128] = EPS
    cbm = np.zeros((128, 132), np.float32)
    cbm[:, 0:128] = np.eye(128)
    cbm[:, 128] = 1.0
    rf = np.zeros((4, 128), np.float32)
    rf[0, :] = 1.0
    rf[1, 0:64] = 1.0
    rf[2, 64:128] = 1.0
    rb2 = np.zeros((2, 128), np.float32)
    rb2[0, 0:64] = 1.0
    rb2[1, 64:128] = 1.0
    ef = np.zeros((128, 128), np.float32)
    ef[0, 0:64] = 1.0
    ef[64, 64:128] = 1.0
    cb2 = np.zeros((128, 4), np.float32)
    cb2[:, 0] = 1.0
    return (cf, cbm.astype(ml_dtypes.bfloat16), rf,
            rb2.astype(ml_dtypes.bfloat16), ef, cb2.astype(ml_dtypes.bfloat16))


def make_inputs(L0_attn_w_attn, L0_ffn_W_exp, L0_ffn_b_exp, L0_ffn_up_weights,
                L0_ffn_up_bias, L0_ffn_down_weights, L0_ffn_W_con, L0_ffn_b_con,
                L0_ffn_down_bias, L0_attn_w_proj):
    SA_u, SQ_u = build_stage_mats(np.asarray(L0_ffn_up_weights, np.float32))
    SA_d, SQ_d = build_stage_mats(np.asarray(L0_ffn_down_weights, np.float32))
    cfc, cbb, rf, rb2, ef, cb2 = make_consts()
    fmap = f_of_hprime()
    return {
        "w_attnT": np.ascontiguousarray(np.asarray(L0_attn_w_attn, np.float32).T),
        "w_projT": np.ascontiguousarray(np.asarray(L0_attn_w_proj, np.float32).T),
        "w_expT": np.ascontiguousarray(np.asarray(L0_ffn_W_exp, np.float32).T),
        "w_con": np.ascontiguousarray(np.asarray(L0_ffn_W_con, np.float32)),
        "b_exp": np.asarray(L0_ffn_b_exp, np.float32),
        "ub_perm": np.ascontiguousarray(np.asarray(L0_ffn_up_bias, np.float32)[fmap]),
        "b_con": np.asarray(L0_ffn_b_con, np.float32),
        "d_bias": np.asarray(L0_ffn_down_bias, np.float32),
        "SAu": np.ascontiguousarray(SA_u), "SQu": np.ascontiguousarray(SQ_u),
        "SAdT": np.ascontiguousarray(SA_d.transpose(0, 1, 3, 2)),
        "SQdT": np.ascontiguousarray(SQ_d.transpose(0, 1, 3, 2)),
        "cf_in": cfc, "cb_in": cbb, "rf_in": rf, "rb_in": rb2, "ef_in": ef,
        "cb2_in": cb2,
    }


def kernel(L0_attn_x, L0_attn_w_attn, L0_attn_w_proj, L0_ffn_W_exp, L0_ffn_b_exp,
           L0_ffn_up_weights, L0_ffn_up_bias, L0_ffn_down_weights, L0_ffn_W_con,
           L0_ffn_b_con, L0_ffn_down_bias):
    x = np.asarray(L0_attn_x, dtype=np.float32)
    B = x.shape[0]
    n_seq = B // N_CORES
    shared = make_inputs(L0_attn_w_attn, L0_ffn_W_exp, L0_ffn_b_exp,
                         L0_ffn_up_weights, L0_ffn_up_bias, L0_ffn_down_weights,
                         L0_ffn_W_con, L0_ffn_b_con, L0_ffn_down_bias,
                         L0_attn_w_proj)
    nc = build_program(n_seq=n_seq)
    in_maps = []
    for core in range(N_CORES):
        m = dict(shared)
        m["x_fm"] = np.ascontiguousarray(
            x[core * n_seq:(core + 1) * n_seq].transpose(0, 2, 1))
        in_maps.append(m)
    res = run_bass_kernel_spmd(nc, in_maps, core_ids=list(range(N_CORES)))
    outs = [np.ascontiguousarray(res.results[c]["out_fm"].transpose(0, 2, 1))
            for c in range(N_CORES)]
    return np.concatenate(outs, axis=0)


# version 14
# speedup vs baseline: 1.0277x; 1.0197x over previous
import contextlib
import numpy as np
import ml_dtypes
import concourse.bass as bass
import concourse.mybir as mybir
from concourse.bass_utils import run_bass_kernel_spmd

F32 = mybir.dt.float32
BF16 = mybir.dt.bfloat16
AF = mybir.ActivationFunctionType
ALU = mybir.AluOpType
AXC = mybir.AxisListType.C

N_CORES = 8
D = 1024
HID = 4096
NH = 16
WIN = 256
TSEQ = 2048
TT = 512
EPS = 1e-5



def build_stage_mats(W):
    SA = np.zeros((7, 32, 128, 128), np.float32)
    l = np.arange(128)
    for st in range(7):
        s = 1 << st
        q = l >> (st + 1)
        half = (l >> st) & 1
        r = l & (s - 1)
        kloc = q * s + r
        part = l + np.where(half == 0, s, -s)
        for B in range(32):
            k = B * 64 + kloc
            SA[st, B, l, l] = W[st, k, half, half]
            SA[st, B, l, part] = W[st, k, half, 1 - half]
    SQ = np.zeros((5, 32, 128, 128), np.float32)
    p = np.arange(128)
    Bp = p >> 2
    jj = p & 3
    for st in range(7, 12):
        s = 1 << st
        sb = 1 << (st - 7)
        for g in range(32):
            f = Bp * 128 + jj * 32 + g
            half = (f >> st) & 1
            k = (f >> (st + 1)) * s + (f & (s - 1))
            partner = p + np.where(half == 0, sb * 4, -sb * 4)
            SQ[st - 7, g, p, p] = W[st, k, half, half]
            SQ[st - 7, g, p, partner] = W[st, k, half, 1 - half]
    return SA, SQ


def f_of_hprime():
    hp = np.arange(HID)
    g = hp >> 7
    p = hp & 127
    return (p >> 2) * 128 + (p & 3) * 32 + g



ENGS = ("sync", "tensor", "vector", "scalar", "gpsimd")


class SemCtr:
    __slots__ = ("sem", "step", "n")

    def __init__(self, sem, step):
        self.sem = sem
        self.step = step
        self.n = 0


class Plan:
    def __init__(self):
        self.q = {e: [] for e in ENGS}

    def op(self, eng, build, waits=(), inc=None, inck=1):
        wl = tuple((sc.sem, v) for sc, v in waits if v > 0)
        amt = None
        sem = None
        if inc is not None:
            amt = inck * inc.step
            inc.n += amt
            sem = inc.sem

        def fn(e, wl=wl, build=build, amt=amt, sem=sem):
            for s, v in wl:
                e.wait_ge(s, v)
            inst = build(e)
            if amt is not None:
                inst.then_inc(sem, amt)
        self.q[eng].append(fn)

    def wait(self, eng, *waits):
        wl = tuple((sc.sem, v) for sc, v in waits if v > 0)

        def fn(e, wl=wl):
            for s, v in wl:
                e.wait_ge(s, v)
        self.q[eng].append(fn)



def build_program(n_seq=4, phases="paf", attn_tiles=None, attn_stage=9, win_stage=9, skip_prep=False, win_heads=4, win_wins=None, prep_stage=9):
    nc = bass.Bass("TRN2", target_bir_lowering=False)
    NTOK = n_seq * TSEQ
    NTIL = NTOK // TT
    TPS = TSEQ // TT
    WPT = TT // WIN

    dt = nc.dram_tensor
    x_fm = dt("x_fm", [n_seq, D, TSEQ], F32, kind="ExternalInput").ap()
    w_attnT = dt("w_attnT", [D, 3 * D], F32, kind="ExternalInput").ap()
    w_projT = dt("w_projT", [D, D], F32, kind="ExternalInput").ap()
    w_expT = dt("w_expT", [HID, D], F32, kind="ExternalInput").ap()
    w_con = dt("w_con", [HID, D], F32, kind="ExternalInput").ap()
    b_exp = dt("b_exp", [HID], F32, kind="ExternalInput").ap()
    ub_perm = dt("ub_perm", [HID], F32, kind="ExternalInput").ap()
    b_con = dt("b_con", [D], F32, kind="ExternalInput").ap()
    d_bias = dt("d_bias", [D], F32, kind="ExternalInput").ap()
    SAu = dt("SAu", [7, 32, 128, 128], F32, kind="ExternalInput").ap()
    SQu = dt("SQu", [5, 32, 128, 128], F32, kind="ExternalInput").ap()
    SAdT = dt("SAdT", [7, 32, 128, 128], F32, kind="ExternalInput").ap()
    SQdT = dt("SQdT", [5, 32, 128, 128], F32, kind="ExternalInput").ap()
    cf_in = dt("cf_in", [128, 132], F32, kind="ExternalInput").ap()
    cb_in = dt("cb_in", [128, 132], BF16, kind="ExternalInput").ap()
    cb2_in = dt("cb2_in", [128, 4], BF16, kind="ExternalInput").ap()
    rf_in = dt("rf_in", [4, 128], F32, kind="ExternalInput").ap()
    ef_in = dt("ef_in", [128, 128], F32, kind="ExternalInput").ap()
    rb_in = dt("rb_in", [2, 128], BF16, kind="ExternalInput").ap()

    out_fm = dt("out_fm", [n_seq, D, TSEQ], F32, kind="ExternalOutput").ap()

    mid_scr = dt("mid_scr", [n_seq, D, TSEQ], F32, kind="Internal").ap()
    p_scr = dt("p_scr", [HID, 1028], F32, kind="Internal").ap()
    wexp_scr = dt("wexp_scr", [D, HID], BF16, kind="Internal").ap()
    wcon_scr = dt("wcon_scr", [HID, D], BF16, kind="Internal").ap()
    stats_scr = dt("stats_scr", [2, NTOK], F32, kind="Internal").ap()

    es = contextlib.ExitStack()
    sb = lambda name, shape, dtp: es.enter_context(nc.sbuf_tensor(name, shape, dtp))
    sem = lambda name, step: SemCtr(es.enter_context(nc.semaphore(name)), step)

    cf = sb("cf", [128, 132], F32)
    cbc = sb("cbc", [128, 4], BF16)
    orow = sb("orow", [33, 128], F32)
    e2f = sb("e2f", [128, 128], F32)
    biasg = sb("biasg", [128, 32], F32)
    bex = sb("bex", [128, 32], F32)
    bco = sb("bco", [128, 8], F32)

    qc = sem("qc", 16)
    qx0 = sem("qx0", 16)
    qx1 = sem("qx1", 16)
    qp = sem("qp", 16)
    qp2 = sem("qp2", 16)
    gso = sem("gso", 16)
    qf = sem("qf", 16)
    qf2 = sem("qf2", 16)
    gs = sem("gs", 16)
    gstat = sem("gstat", 16)
    gmid = sem("gmid", 16)
    gout = sem("gout", 16)
    sp = sem("sp", 1)
    sv = sem("sv", 1)
    sa = sem("sa", 1)
    sg = sem("sg", 1)

    P = Plan()

    P.op("sync", lambda e: e.dma_start(out=cf[:], in_=cf_in[:, :]), inc=qc)
    P.op("sync", lambda e: e.dma_start(out=cbc[:], in_=cb2_in[:, :]), inc=qc)
    P.op("sync", lambda e: e.dma_start(out=orow[0:1, :], in_=rf_in[0:1, :]), inc=qc)
    P.op("sync", lambda e: e.dma_start(out=orow[32:33, :], in_=rf_in[0:1, :]), inc=qc)
    P.op("sync", lambda e: e.dma_start(out=e2f[:], in_=ef_in[:, :]), inc=qc)
    QC0 = qc.n
    ID = cf[:, 0:128]
    EPSC = cf[:, 128:129]
    ONEB = cbc[:, 0:1]

    with contextlib.ExitStack() as pes:
      if not skip_prep:
        psb = lambda n, s, d: pes.enter_context(nc.sbuf_tensor(n, s, d))
        pps = lambda n, s, d: pes.enter_context(nc.psum_tensor(n, s, d))
        sall = [psb(f"sall{i}", [128, 32 * 128], F32) for i in range(2)]
        gall = [psb(f"gall{i}", [128, 32 * 128], F32) for i in range(2)]
        svin2 = [psb(f"svin{i}", [128, 1028], F32) for i in range(2)]
        svinb2 = [psb(f"svinb{i}", [128, 1028], BF16) for i in range(2)]
        gallb = psb("gallb", [128, 32 * 128], BF16)
        P.op("vector", lambda e: e.memset(svin2[0][:], 0.0), inc=sv)
        P.op("vector", lambda e: e.memset(svin2[1][:], 0.0), inc=sv)
        svout2 = [psb(f"svout{i}", [128, 1028], F32) for i in range(2)]
        sv2b2 = [psb(f"sv2b{i}", [128, 1024], BF16) for i in range(2)]
        swx2 = [psb(f"swx{i}", [128, 1024], BF16) for i in range(2)]
        bx = psb("bx", [128, 96], F32)
        idbp = psb("idbp", [128, 128], BF16)
        P.op("sync", lambda e: e.dma_start(out=idbp[:], in_=cb_in[:, 0:128]), inc=qc)
        QC0b = qc.n
        QC0 = QC0b
        PP = [pps(f"PP{i}", [128, 512], F32) for i in range(3)]
        PQ2 = [pps(f"PQ2{i}", [128, 512], F32) for i in range(3)]
        PPT = [pps(f"PPT{i}", [128, 512], BF16) for i in range(2)]

        state = {"sall_mark": [0, 0], "gall_guard": 0, "ppmark": {},
                 "chain_end": 0}

        def batched_loop(stages_ap, n_st, reverse, emit_load, ncols, emit_out,
                         gslot_final_holder):
            order = list(range(n_st))
            if reverse:
                order = order[::-1]
            for i, j in enumerate(order):
                sslot = i % 2
                gsl = i % 2
                sv_ = stages_ap[j].rearrange("b p c -> p b c")
                qsl = (qp, qp2)[sslot]
                for part in range(8):
                    P.op("sync", lambda e, sv_=sv_, sslot=sslot, part=part: e.dma_start(
                        out=sall[sslot][:, :].rearrange("p (b c) -> p b c", b=32)[
                            :, part * 4:(part + 1) * 4, :],
                        in_=sv_[:, part * 4:(part + 1) * 4, :]),
                        waits=([(sp, state["sall_mark"][sslot]), (qc, QC0)]
                               if part == 0 else []), inc=qsl)
                ld = (qsl, qsl.n)
                for b in range(32):
                    tile = PP[b % 3][:, 0:128]
                    w_ = []
                    if b == 0:
                        w_ = [ld]
                        if i == 0:
                            w_.append((sv, sv.n))
                            w_.append((sa, sa.n))
                            w_.append((sp, state["gall_guard"]))
                            w_.append((qc, QC0))
                    pm = state["ppmark"].get(b % 3)
                    if pm is not None:
                        w_ = w_ + [pm]
                    if i == 0:
                        P.op("tensor", lambda e, b=b, tile=tile, sslot=sslot: e.matmul(
                            tile, sall[sslot][:, b * 128:(b + 1) * 128], ID,
                            start=True, stop=True), waits=w_, inc=sp)
                    else:
                        P.op("tensor", lambda e, b=b, tile=tile, sslot=sslot, gsl=gsl: e.matmul(
                            tile, sall[sslot][:, b * 128:(b + 1) * 128],
                            gall[1 - gsl][:, b * 128:(b + 1) * 128],
                            start=True, stop=True), waits=w_, inc=sp)
                    if b % 2 == 0:
                        P.op("vector", lambda e, b=b, tile=tile, gsl=gsl: e.tensor_copy(
                            gall[gsl][:, b * 128:(b + 1) * 128], tile),
                            waits=[(sp, sp.n)], inc=sv)
                        state["ppmark"][b % 3] = (sv, sv.n)
                    else:
                        P.op("scalar", lambda e, b=b, tile=tile, gsl=gsl: e.copy(
                            out=gall[gsl][:, b * 128:(b + 1) * 128], in_=tile),
                            waits=[(sp, sp.n)], inc=sa)
                        state["ppmark"][b % 3] = (sa, sa.n)
                state["sall_mark"][sslot] = sp.n
            gfin = (n_st - 1) % 2
            SV_CH, SA_CH = sv.n, sa.n
            state["chain_end"] = sp.n
            state.pop("apply_mm0", None)
            state.pop("apply_mm1", None)
            for b in range(32 if prep_stage >= 2 else 0):
                slot = b % 2
                qsl = (qp, qp2)[slot]
                emit_load(b, slot, qsl,
                          [(sp, state.get(f"apply_mm{slot}", state["chain_end"]))])
                ld = (qsl, qsl.n)
                P.op("vector", lambda e, b=b, gfin=gfin: e.tensor_copy(
                    gallb[:, b * 128:(b + 1) * 128],
                    gall[gfin][:, b * 128:(b + 1) * 128]),
                    waits=[(sv, SV_CH), (sa, SA_CH),
                           (sp, state.get(f"apply_mm{slot}", state["chain_end"]))],
                    inc=sv)
                P.op("vector", lambda e, slot=slot: e.tensor_copy(
                    svinb2[slot][:], svin2[slot][:]),
                    waits=[ld, (sv, sv.n)], inc=sv)
                SV_CASTB = sv.n
                splits = []
                c0 = 0
                for pi, pq in enumerate(PQ2):
                    if c0 >= ncols:
                        break
                    n = min(512, ncols - c0)
                    splits.append((pq, c0, n, slot))
                    c0 += n
                for k, (pq, c0, n, sl) in enumerate(splits):
                    w_ = []
                    if k == 0:
                        w_ = [(sv, SV_CASTB)]
                        if state.get("pqmark"):
                            w_.append(state["pqmark"])
                    P.op("tensor", lambda e, pq=pq, c0=c0, n=n, b=b, sl=sl: e.matmul(
                        pq[:, 0:n],
                        gallb[:, b * 128:(b + 1) * 128],
                        svinb2[sl][:, c0:c0 + n], start=True, stop=True),
                        waits=w_, inc=(sp if k == len(splits) - 1 else None))
                state[f"apply_mm{slot}"] = sp.n
                for ki, (pq, c0, n, sl) in enumerate(splits):
                    w_ = [(sp, sp.n)]
                    if ki == 0 and state.get(f"outst{slot}"):
                        w_.append(state[f"outst{slot}"])
                    P.op("vector", lambda e, pq=pq, c0=c0, n=n, sl=sl: e.tensor_copy(
                        svout2[sl][:, c0:c0 + n], pq[:, 0:n]),
                        waits=w_, inc=sv)
                state["pqmark"] = (sv, sv.n)
                state[f"apply_ev{slot}"] = sv.n
                if prep_stage >= 3:
                    emit_out(b, slot)
            state["gall_guard"] = sp.n
            state["sall_mark"] = [sp.n, sp.n]

        def ld_upA(b, slot, qsl, w_):
            P.op("sync", lambda e, b=b, slot=slot: e.dma_start(
                out=svin2[slot][:, 0:1024], in_=w_expT[b * 128:(b + 1) * 128, :]),
                waits=w_, inc=qsl)
            P.op("sync", lambda e, b=b, slot=slot: e.dma_start(
                out=svin2[slot][:, 1024:1025],
                in_=b_exp[b * 128:(b + 1) * 128].unsqueeze(1)), inc=qsl)

        def out_upA(b, slot):
            gq = (gs, gso)[slot]
            P.op("scalar", lambda e, b=b, slot=slot: e.dma_start(
                out=p_scr[b * 128:(b + 1) * 128, 0:1025], in_=svout2[slot][:, 0:1025]),
                waits=[(sv, sv.n)], inc=gq)
            state[f"outst{slot}"] = (gq, gq.n)

        batched_loop(SAu, 7, True, ld_upA, 1025, out_upA, None)
        P.wait("sync", (gs, gs.n), (gso, gso.n))

        p_view = p_scr.rearrange("(B jj s) c -> (B jj) s c", B=32, jj=4, s=32)

        def ld_upQ(g, slot, qsl, w_):
            P.op("sync", lambda e, g=g, slot=slot: e.dma_start(
                out=svin2[slot][:, 0:1025], in_=p_view[:, g, 0:1025]),
                waits=w_, inc=qsl)

        def out_upQ(g, slot):
            P.op("vector", lambda e, g=g, slot=slot: e.tensor_copy(
                bex[:, g:g + 1], svout2[slot][:, 1024:1025]),
                waits=[(sv, sv.n)], inc=sv)
            P.op("vector", lambda e, slot=slot: e.tensor_copy(
                sv2b2[slot][:], svout2[slot][:, 0:1024]),
                waits=[(sv, sv.n), (sp, state.get(f"tp{slot}", 0))], inc=sv)
            SV_CAST = sv.n
            for ci in range(8):
                pt = PPT[ci % 2][:, 0:128]
                w_ = [(sv, SV_CAST)] if ci == 0 else []
                ptm = state.setdefault("ptmark", {}).get(ci % 2)
                if ptm is not None:
                    w_ = w_ + [ptm]
                else:
                    w_ = w_ + [(qc, QC0b)]
                P.op("tensor", lambda e, ci=ci, slot=slot, pt=pt: e.transpose(
                    pt, sv2b2[slot][:, ci * 128:(ci + 1) * 128], idbp[:]),
                    waits=w_, inc=sp)
                sw = state.get(f"swxst{slot}")
                P.op("scalar", lambda e, ci=ci, slot=slot, pt=pt: e.copy(
                    out=swx2[slot][:, ci * 128:(ci + 1) * 128], in_=pt),
                    waits=[(sp, sp.n)] + ([sw] if ci == 0 and sw else []), inc=sa)
                state["ptmark"][ci % 2] = (sa, sa.n)
            state[f"tp{slot}"] = sp.n
            SA_SWX = sa.n
            P.op("scalar", lambda e, g=g, slot=slot: e.dma_start(
                out=wexp_scr.rearrange("(cc p) f -> p (cc f)", p=128)[
                    :, 0:0] if False else wexp_scr[0:128, 0:128],
                in_=swx2[slot][:, 0:128]), waits=[], inc=gs) if False else None
            gq = (gs, gso)[slot]
            for ci in range(8):
                P.op("scalar", lambda e, ci=ci, g=g, slot=slot: e.dma_start(
                    out=wexp_scr[ci * 128:(ci + 1) * 128, g * 128:(g + 1) * 128],
                    in_=swx2[slot][:, ci * 128:(ci + 1) * 128]),
                    waits=[(sa, SA_SWX)] if ci == 0 else [], inc=gq)
            state[f"swxst{slot}"] = (gq, gq.n)
            state[f"outst{slot}"] = (sv, SV_CAST)

        batched_loop(SQu, 5, True, ld_upQ, 1025, out_upQ, None)
        P.wait("sync", (gs, gs.n), (gso, gso.n))

        w_view = w_con.rearrange("(B jj s) c -> (B jj) s c", B=32, jj=4, s=32)
        pv2 = p_scr.rearrange("(B jj s) c -> (B jj) s c", B=32, jj=4, s=32)

        def ld_dnQ(g, slot, qsl, w_):
            P.op("sync", lambda e, g=g, slot=slot: e.dma_start(
                out=svin2[slot][:, 0:1024], in_=w_view[:, g, :]),
                waits=w_, inc=qsl)

        def out_dnQ(g, slot):
            gq = (gs, gso)[slot]
            P.op("scalar", lambda e, g=g, slot=slot: e.dma_start(
                out=pv2[:, g, 0:1024], in_=svout2[slot][:, 0:1024]),
                waits=[(sv, sv.n)], inc=gq)
            state[f"outst{slot}"] = (gq, gq.n)

        batched_loop(SQdT, 5, False, ld_dnQ, 1024, out_dnQ, None)
        P.wait("sync", (gs, gs.n), (gso, gso.n))

        def ld_dnA(b, slot, qsl, w_):
            P.op("sync", lambda e, b=b, slot=slot: e.dma_start(
                out=svin2[slot][:, 0:1024], in_=p_scr[b * 128:(b + 1) * 128, 0:1024]),
                waits=w_, inc=qsl)

        def out_dnA(b, slot):
            wc = state.get(f"wcst{slot}")
            P.op("vector", lambda e, slot=slot: e.tensor_copy(
                sv2b2[slot][:], svout2[slot][:, 0:1024]),
                waits=[(sv, sv.n)] + ([wc] if wc else []), inc=sv)
            gq = (gs, gso)[slot]
            P.op("scalar", lambda e, b=b, slot=slot: e.dma_start(
                out=wcon_scr[b * 128:(b + 1) * 128, :], in_=sv2b2[slot][:]),
                waits=[(sv, sv.n)], inc=gq)
            state[f"wcst{slot}"] = (gq, gq.n)
            state[f"outst{slot}"] = (sv, sv.n)

        batched_loop(SAdT, 7, False, ld_dnA, 1024, out_dnA, None)
        P.wait("sync", (gs, gs.n), (gso, gso.n))

        uview = ub_perm.rearrange("(c p) -> c p", p=128)
        for c in range(32):
            P.op("sync", lambda e, c=c: e.dma_start(
                out=bx[:, 32 + c:33 + c], in_=uview[c, :].unsqueeze(1)), inc=qp)
        bcv = b_con.rearrange("(c p) -> c p", p=128)
        dbv = d_bias.rearrange("(c p) -> c p", p=128)
        for c in range(8):
            P.op("sync", lambda e, c=c: e.dma_start(
                out=bx[:, 64 + c:65 + c], in_=bcv[c, :].unsqueeze(1)), inc=qp)
            P.op("sync", lambda e, c=c: e.dma_start(
                out=bx[:, 72 + c:73 + c], in_=dbv[c, :].unsqueeze(1)), inc=qp)
        P.op("vector", lambda e: e.tensor_tensor(
            out=biasg[:], in0=bex[:], in1=bx[:, 32:64], op=ALU.add),
            waits=[(qp, qp.n), (sv, sv.n)], inc=sv)
        P.op("vector", lambda e: e.tensor_tensor(
            out=bco[:], in0=bx[:, 64:72], in1=bx[:, 72:80], op=ALU.add), inc=sv)

    SP_PREP, SV_PREP, SA_PREP, GS_PREP = sp.n, sv.n, sa.n, gs.n

    if "p" == phases:
        with contextlib.ExitStack() as des:
            db = des.enter_context(nc.sbuf_tensor("db", [128, 1024], BF16))
            df = des.enter_context(nc.sbuf_tensor("df", [128, 1024], F32))
            qd = SemCtr(des.enter_context(nc.semaphore("qd")), 16)
            for i in range(16):
                P.op("sync", lambda e, i=i: e.dma_start(
                    out=db[:], in_=wcon_scr[i * 128:(i + 1) * 128, :]),
                    waits=[(gs, gs.n), (sv, sv.n)], inc=qd)
                P.op("vector", lambda e, i=i: e.tensor_copy(df[:], db[:]),
                     waits=[(qd, qd.n), (gout, gout.n)], inc=sv)
                P.op("scalar", lambda e, i=i: e.dma_start(
                    out=out_fm[0, (i % 8) * 128:(i % 8 + 1) * 128,
                               (i // 8) * 1024:(i // 8 + 1) * 1024],
                    in_=df[:]), waits=[(sv, sv.n)], inc=gout)
                P.wait("sync", (gout, gout.n))
            P.wait("scalar", (gout, gout.n))
    attn_on = "a" in phases
    with contextlib.ExitStack() as aes:
      if attn_on:
          asb = lambda n, s, d: aes.enter_context(nc.sbuf_tensor(n, s, d))
          aps = lambda n, s, d: aes.enter_context(nc.psum_tensor(n, s, d))
          waT = asb("waT", [128, 8 * 3072], BF16)
          wpT = asb("wpT", [128, 8 * 1024], BF16)
          with nc.sbuf_tensor("wstg0", [128, 3072], F32) as wstg0, \
               nc.sbuf_tensor("wstg1", [128, 3072], F32) as wstg1:
              wst = (wstg0, wstg1)
              qw = (qx0, qx1)
              cast_mark = [0, 0]
              for c in range(8):
                  sl = c % 2
                  P.op("sync", lambda e, c=c, sl=sl: e.dma_start(
                      out=wst[sl][:, :], in_=w_attnT[c * 128:(c + 1) * 128, :]),
                      waits=[(sv, cast_mark[sl])], inc=qw[sl])
                  P.op("vector", lambda e, c=c, sl=sl: e.tensor_copy(
                      waT[:, c * 3072:(c + 1) * 3072], wst[sl][:, :]),
                      waits=[(qw[sl], qw[sl].n)], inc=sv)
                  cast_mark[sl] = sv.n
              for c in range(8):
                  sl = c % 2
                  P.op("sync", lambda e, c=c, sl=sl: e.dma_start(
                      out=wst[sl][:, 0:1024], in_=w_projT[c * 128:(c + 1) * 128, :]),
                      waits=[(sv, cast_mark[sl])], inc=qw[sl])
                  P.op("vector", lambda e, c=c, sl=sl: e.tensor_copy(
                      wpT[:, c * 1024:(c + 1) * 1024], wst[sl][:, 0:1024]),
                      waits=[(qw[sl], qw[sl].n)], inc=sv)
                  cast_mark[sl] = sv.n
              P.wait("sync", (sv, sv.n))
          SV_W = sv.n

          x_sb = [asb(f"x{i}", [128, 8 * TT], F32) for i in range(2)]
          xbt = asb("xbt", [128, 8 * TT], BF16)
          x2t = asb("x2t", [128, 8 * TT], BF16)
          xnt = asb("xnt", [128, 8 * TT], BF16)
          qkt = asb("qkt", [128, 16 * TT], BF16)
          v_sb = asb("v_sb", [128, 8 * TT], BF16)
          y_sb = asb("y_sb", [128, 8 * TT], BF16)
          ett = asb("ett", [128, 2 * 1024], BF16)
          midt = asb("midt", [128, 8 * TT], F32)
          bgt = asb("bgt", [128, 512], F32)
          mrow = asb("mrow", [1, 4 * TT], F32)
          zz = asb("zz", [128, 1024], F32)
          P.op("vector", lambda e: e.memset(zz[:], 0.0), inc=sv)
          mbt = asb("mbt", [128, TT], BF16)
          rbt = asb("rbt", [128, TT], BF16)
          PA = aps("PA", [128, 1024], F32)
          PB = aps("PB", [128, 1024], F32)
          PC = aps("PC", [128, 512], F32)
          PD = aps("PD", [128, 512], F32)
          PE_ = aps("PE_", [128, 512], F32)
          PF = aps("PF", [128, 512], F32)

          prev = dict(sp_st2=SP_PREP, sa_c2=SA_PREP, gs_mid=0,
                      gs_stat=0, sv_v2=SV_PREP, sp_vend=SP_PREP,
                      sv_mid=SV_W, sv_mid2=SV_W, sp_bc_done=SP_PREP)

          for t in range(NTIL):
              seq = t // TPS
              t0 = (t % TPS) * TT
              qx = (qx0, qx1)[t % 2]
              xs = x_sb[t % 2]

              for c in range(8):
                  P.op("sync", lambda e, c=c, seq=seq, t0=t0, xs=xs: e.dma_start(
                      out=xs[:, c * TT:(c + 1) * TT],
                      in_=x_fm[seq, c * 128:(c + 1) * 128, t0:t0 + TT]),
                      waits=[(sv, prev["sv_mid2"])] if c == 0 else [], inc=qx)
              LD_X = qx.n

              for c in range(8):
                  P.op("scalar", lambda e, c=c, xs=xs: e.copy(
                      out=xbt[:, c * TT:(c + 1) * TT], in_=xs[:, c * TT:(c + 1) * TT]),
                      waits=([(qx, LD_X), (sp, prev["sp_st2"])] if c == 0 else []),
                      inc=sa)
              for c in range(8):
                  P.op("scalar", lambda e, c=c, xs=xs: e.activation(
                      out=x2t[:, c * TT:(c + 1) * TT], in_=xs[:, c * TT:(c + 1) * TT],
                      func=AF.Square), inc=sa)
              SA_CAST = sa.n

              if attn_stage < 2:
                prev = dict(prev, sv_mid=sv.n, sa_c2=sa.n, sp_st2=sp.n,
                            sv_v2=sv.n, sp_vend=sp.n)
                continue
              for c in range(8):
                  P.op("tensor", lambda e, c=c: e.matmul(
                      PC[0:1, 0:TT], ONEB, xbt[:, c * TT:(c + 1) * TT],
                      start=(c == 0), stop=(c == 7)),
                      waits=([(sa, SA_CAST), (sv, prev["sv_v2"])] if c == 0 else []),
                      inc=(sp if c == 7 else None))
              for c in range(8):
                  P.op("tensor", lambda e, c=c: e.matmul(
                      PD[0:1, 0:TT], ONEB, x2t[:, c * TT:(c + 1) * TT],
                      start=(c == 0), stop=(c == 7)), inc=(sp if c == 7 else None))
              SP_STAT = sp.n

              P.op("vector", lambda e: e.tensor_scalar(
                  out=mrow[:, 0:TT], in0=PC[0:1, 0:TT], scalar1=1.0 / D,
                  scalar2=None, op0=ALU.mult),
                  waits=[(sp, SP_STAT), (gstat, prev["gs_stat"])], inc=sv)
              P.op("vector", lambda e: e.tensor_scalar(
                  out=mrow[:, 2 * TT:3 * TT], in0=PD[0:1, 0:TT], scalar1=1.0 / D,
                  scalar2=None, op0=ALU.mult), inc=sv)
              P.op("vector", lambda e: e.tensor_tensor(
                  out=mrow[:, 3 * TT:4 * TT], in0=mrow[:, 0:TT],
                  in1=mrow[:, 0:TT], op=ALU.mult), waits=[(sv, sv.n)], inc=sv)
              P.op("vector", lambda e: e.tensor_tensor(
                  out=mrow[:, 2 * TT:3 * TT], in0=mrow[:, 2 * TT:3 * TT],
                  in1=mrow[:, 3 * TT:4 * TT], op=ALU.subtract),
                  waits=[(sv, sv.n)], inc=sv)
              SV_VAR = sv.n
              P.op("scalar", lambda e: e.activation(
                  out=mrow[:, 3 * TT:4 * TT], in_=mrow[:, 2 * TT:3 * TT],
                  func=AF.Ln, bias=EPSC[0:1, :]), waits=[(sv, SV_VAR)], inc=sa)
              P.op("scalar", lambda e: e.activation(
                  out=mrow[:, TT:2 * TT], in_=mrow[:, 3 * TT:4 * TT],
                  func=AF.Exp, scale=-0.5), waits=[(sa, sa.n)], inc=sa)
              SA_LN = sa.n

              if attn_stage < 3:
                prev = dict(prev, sv_mid=sv.n, sa_c2=sa.n, sp_st2=sp.n,
                            sv_v2=sv.n, sp_vend=sp.n)
                continue
              P.op("tensor", lambda e: e.matmul(
                  PE_[:, 0:TT], orow[0:1, :], mrow[0:1, 0:TT], start=True, stop=True),
                  waits=[(sa, SA_LN), (sv, prev["sv_mid"])], inc=sp)
              P.op("tensor", lambda e: e.matmul(
                  PF[:, 0:TT], orow[0:1, :], mrow[0:1, TT:2 * TT], start=True, stop=True),
                  inc=sp)
              SP_BC = sp.n
              P.op("vector", lambda e: e.tensor_copy(mbt[:], PE_[:, 0:TT]),
                   waits=[(sp, SP_BC)], inc=sv)
              P.op("vector", lambda e: e.tensor_copy(rbt[:], PF[:, 0:TT]), inc=sv)
              SV_BC = sv.n

              for c in range(8):
                  P.op("vector", lambda e, c=c: e.tensor_tensor(
                      out=x2t[:, c * TT:(c + 1) * TT], in0=xbt[:, c * TT:(c + 1) * TT],
                      in1=mbt[:], op=ALU.subtract),
                      waits=([(sv, SV_BC)] if c == 0 else [(sv, sv.n)]), inc=sv)
                  P.op("vector", lambda e, c=c: e.tensor_tensor(
                      out=xnt[:, c * TT:(c + 1) * TT], in0=x2t[:, c * TT:(c + 1) * TT],
                      in1=rbt[:], op=ALU.mult),
                      waits=[(sv, sv.n)] + ([(sp, prev["sp_vend"]),
                                             (sp, prev["sp_st2"])] if c == 0 else []),
                      inc=sv)
              SV_XN = sv.n

              if attn_stage < 4:
                prev = dict(prev, sv_mid=sv.n, sa_c2=sa.n, sp_st2=sp.n,
                            sv_v2=sv.n, sp_vend=sp.n)
                continue
              pq = (PC, PD)
              for ei in range(16):
                  pp = pq[ei % 2]
                  for c in range(8):
                      w_ = []
                      if ei == 0 and c == 0:
                          w_ = [(sv, SV_XN)]
                      elif c == 0 and ei >= 2:
                          w_ = [(sv, sv.n)]
                      P.op("tensor", lambda e, ei=ei, c=c, pp=pp: e.matmul(
                          pp[:, 0:TT],
                          waT[:, c * 3072 + ei * 128: c * 3072 + (ei + 1) * 128],
                          xnt[:, c * TT:(c + 1) * TT],
                          start=(c == 0), stop=(c == 7)),
                          waits=w_, inc=(sp if c == 7 else None))
                  P.op("vector", lambda e, ei=ei, pp=pp: e.tensor_copy(
                      qkt[:, ei * TT:(ei + 1) * TT], pp[:, 0:TT]),
                      waits=[(sp, sp.n)], inc=sv)
              for ts4 in range(TT // 128):
                  for eh in range(2):
                      pp = pq[(ts4 * 2 + eh) % 2]
                      for c in range(8):
                          P.op("tensor", lambda e, ts4=ts4, eh=eh, c=c, pp=pp: e.matmul(
                              pp[:, :],
                              xnt[:, c * TT + ts4 * 128: c * TT + (ts4 + 1) * 128],
                              waT[:, c * 3072 + 2048 + eh * 512: c * 3072 + 2048 + (eh + 1) * 512],
                              start=(c == 0), stop=(c == 7)),
                              waits=([(sv, sv.n)] if c == 0 else []),
                              inc=(sp if c == 7 else None))
                      P.op("vector", lambda e, ts4=ts4, eh=eh, pp=pp: e.tensor_copy(
                          v_sb[:, ts4 * 1024 + eh * 512: ts4 * 1024 + (eh + 1) * 512],
                          pp[:, :]), waits=[(sp, sp.n)], inc=sv)
              SV_QKV = sv.n
              SP_VEND = sp.n

              if attn_stage < 5:
                  prev = dict(prev, sv_mid=sv.n, sa_c2=sa.n, sp_st2=sp.n,
                              sv_v2=sv.n, sp_vend=SP_VEND)
                  continue
              CS2 = {(0, 0): 0, (0, 1): 256, (1, 0): 512, (1, 1): 768}
              NIT = WPT * 8
              mk = {"st": {}, "et": {-2: prev["sa_c2"], -1: prev["sa_c2"]},
                    "z": {}, "rc": {-2: prev["sa_c2"], -1: prev["sa_c2"]},
                    "av": {}, "bg": {}, "bge": {-2: prev["sa_c2"], -1: prev["sa_c2"]},
                    "y": {-2: SV_BC, -1: SV_BC, -3: SV_BC, -4: SV_BC}}

              def emit_st(k):
                  w, gi = divmod(k, 8)
                  pp = (PA, PB)[k % 2]
                  first = (k == 0)
                  for idx, (hp, jh) in enumerate(((0, 0), (0, 1), (1, 0), (1, 1))):
                      h = 2 * gi + hp
                      qc_ = h // 2
                      qb = (h % 2) * 64
                      w_ = []
                      if idx == 0:
                          w_ = [(sa, mk["et"][k - 2]), (sa, mk["rc"][k - 2])]
                          if first:
                              w_.append((sv, SV_QKV))
                      P.op("tensor", lambda e, pp=pp, qc_=qc_, qb=qb, jh=jh, w=w, hp=hp: e.matmul(
                          pp[:, CS2[(hp, jh)]:CS2[(hp, jh)] + 256],
                          qkt[qb:qb + 64, (8 + qc_) * TT + w * 256 + jh * 128:
                              (8 + qc_) * TT + w * 256 + (jh + 1) * 128],
                          qkt[qb:qb + 64, qc_ * TT + w * 256: qc_ * TT + (w + 1) * 256],
                          start=True, stop=True),
                          waits=w_, inc=(sp if idx == 3 else None))
                  mk["st"][k] = sp.n

              def emit_exp(k):
                  pp = (PA, PB)[k % 2]
                  eb = (k % 2) * 1024
                  P.op("scalar", lambda e, pp=pp, eb=eb: e.activation(
                      out=ett[:, eb:eb + 1024], in_=pp[:, :], func=AF.Exp, scale=0.125),
                      waits=[(sp, mk["st"][k]), (sp, mk["av"].get(k - 2, 0))], inc=sa)
                  mk["et"][k] = sa.n

              def emit_z(k):
                  zp = (PA, PB)[k % 2]
                  eb = (k % 2) * 1024
                  for idx, (hp, col) in enumerate(((0, 0), (1, 256))):
                      for jh in range(2):
                          w_ = []
                          if idx == 0 and jh == 0:
                              w_ = [(sa, mk["et"][k]), (sa, mk["rc"][k - 2])]
                          P.op("tensor", lambda e, zp=zp, col=col, eb=eb, hp=hp, jh=jh: e.matmul(
                              zp[0:1, col:col + 256], ONEB,
                              ett[:, eb + CS2[(hp, jh)]: eb + CS2[(hp, jh)] + 256],
                              start=(jh == 0), stop=(jh == 1)),
                              waits=w_, inc=(sp if (idx == 1 and jh == 1) else None))
                  mk["z"][k] = sp.n

              def emit_recip(k):
                  zp = (PA, PB)[k % 2]
                  sc = (k % 2) * 512
                  P.op("scalar", lambda e, zp=zp, sc=sc: e.activation(
                      out=zz[0:1, sc:sc + 256], in_=zp[0:1, 0:256], func=AF.Ln),
                      waits=[(sp, mk["z"][k]), (sp, mk["bg"].get(k - 2, 0))], inc=sa)
                  P.op("scalar", lambda e, zp=zp, sc=sc: e.activation(
                      out=zz[64:65, sc:sc + 256], in_=zp[0:1, 256:512], func=AF.Ln),
                      inc=sa)
                  mk["rc"][k] = sa.n

              def emit_av(k):
                  w, gi = divmod(k, 8)
                  eb = (k % 2) * 1024
                  yp = (PD, PE_)[k % 2]
                  for idx, hp in enumerate((0, 1)):
                      h = 2 * gi + hp
                      ob = hp * 64
                      for jh in range(2):
                          vt = w * 2 + jh
                          w_ = []
                          if idx == 0 and jh == 0:
                              w_ = [(sa, mk["et"][k]), (sv, mk["y"][k - 2])]
                          P.op("tensor", lambda e, ob=ob, h=h, jh=jh, vt=vt, eb=eb, yp=yp, hp=hp: e.matmul(
                              yp[ob:ob + 64, 0:256],
                              v_sb[:, vt * 1024 + h * 64: vt * 1024 + (h + 1) * 64],
                              ett[:, eb + CS2[(hp, jh)]: eb + CS2[(hp, jh)] + 256],
                              start=(jh == 0), stop=(jh == 1)),
                              waits=w_, inc=(sp if (idx == 1 and jh == 1) else None))
                  mk["av"][k] = sp.n

              def emit_bg(k):
                  sc = (k % 2) * 512
                  bp = (PC, PF)[k % 2]
                  P.op("tensor", lambda e, sc=sc, bp=bp: e.matmul(
                      bp[:, 0:256], e2f[0:65, :], zz[0:65, sc:sc + 256],
                      start=True, stop=True),
                      waits=[(sa, mk["rc"][k]), (sa, mk["bge"][k - 2])], inc=sp)
                  mk["bg"][k] = sp.n

              def emit_bge(k):
                  bc = (k % 2) * 256
                  bp = (PC, PF)[k % 2]
                  P.op("scalar", lambda e, bc=bc, bp=bp: e.activation(
                      out=bgt[:, bc:bc + 256], in_=bp[:, 0:256], func=AF.Exp,
                      scale=-1.0),
                      waits=[(sp, mk["bg"][k]), (sv, mk["y"][k - 2])], inc=sa)
                  mk["bge"][k] = sa.n

              def emit_y(k):
                  w, gi = divmod(k, 8)
                  yc = (k % 2) * 256
                  yp = (PD, PE_)[k % 2]
                  P.op("vector", lambda e, yc=yc, gi=gi, w=w, yp=yp: e.tensor_tensor(
                      out=y_sb[:, gi * TT + w * 256: gi * TT + (w + 1) * 256],
                      in0=yp[:, 0:256], in1=bgt[:, yc:yc + 256], op=ALU.mult),
                      waits=[(sp, mk["av"][k]), (sa, mk["bge"][k])], inc=sv)
                  mk["y"][k] = sv.n

              for k in range(NIT + 2):
                  if k < NIT:
                      emit_st(k)
                  if 1 <= k <= NIT:
                      emit_z(k - 1)
                  if k >= 2:
                      emit_av(k - 2)
                      emit_bg(k - 2)
                  if k < NIT:
                      emit_exp(k)
                  if 1 <= k <= NIT:
                      emit_recip(k - 1)
                  if k >= 2:
                      emit_bge(k - 2)
                      emit_y(k - 2)
              SV_Y = sv.n


              if attn_stage < 6:
                prev = dict(prev, sv_mid=sv.n, sa_c2=sa.n, sp_st2=sp.n,
                            sv_v2=sv.n, sp_vend=SP_VEND)
                continue
              for ot in range(8):
                  pp = (PE_, PF)[ot % 2]
                  for c in range(8):
                      w_ = []
                      if ot == 0 and c == 0:
                          w_ = [(sv, SV_Y)]
                      elif c == 0 and ot >= 2:
                          w_ = [(sv, sv.n)]
                      P.op("tensor", lambda e, ot=ot, c=c, pp=pp: e.matmul(
                          pp[:, :],
                          wpT[:, c * 1024 + ot * 128: c * 1024 + (ot + 1) * 128],
                          y_sb[:, c * TT:(c + 1) * TT],
                          start=(c == 0), stop=(c == 7)),
                          waits=w_, inc=(sp if c == 7 else None))
                  P.op("vector", lambda e, ot=ot, pp=pp, xs=xs: e.tensor_tensor(
                      out=midt[:, ot * TT:(ot + 1) * TT], in0=pp[:, :],
                      in1=xs[:, ot * TT:(ot + 1) * TT], op=ALU.add),
                      waits=[(sp, sp.n)] + ([(gmid, prev["gs_mid"]),
                                             (sa, prev["sa_c2"])] if ot == 0 else []),
                      inc=sv)
              SV_MID = sv.n

              for c in range(8):
                  P.op("scalar", lambda e, c=c: e.copy(
                      out=xbt[:, c * TT:(c + 1) * TT], in_=midt[:, c * TT:(c + 1) * TT]),
                      waits=([(sv, SV_MID)] if c == 0 else []), inc=sa)
              for c in range(8):
                  P.op("scalar", lambda e, c=c: e.activation(
                      out=x2t[:, c * TT:(c + 1) * TT], in_=midt[:, c * TT:(c + 1) * TT],
                      func=AF.Square), inc=sa)
              SA_C2 = sa.n
              for c in range(8):
                  P.op("tensor", lambda e, c=c: e.matmul(
                      PC[0:1, 0:TT], ONEB, xbt[:, c * TT:(c + 1) * TT],
                      start=(c == 0), stop=(c == 7)),
                      waits=([(sa, SA_C2), (sv, sv.n)] if c == 0 else []),
                      inc=(sp if c == 7 else None))
              for c in range(8):
                  P.op("tensor", lambda e, c=c: e.matmul(
                      PD[0:1, 0:TT], ONEB, x2t[:, c * TT:(c + 1) * TT],
                      start=(c == 0), stop=(c == 7)), inc=(sp if c == 7 else None))
              SP_ST2 = sp.n
              P.op("vector", lambda e: e.tensor_scalar(
                  out=mrow[:, 0:TT], in0=PC[0:1, 0:TT], scalar1=1.0 / D,
                  scalar2=None, op0=ALU.mult), waits=[(sp, SP_ST2)], inc=sv)
              P.op("vector", lambda e: e.tensor_scalar(
                  out=mrow[:, 2 * TT:3 * TT], in0=PD[0:1, 0:TT], scalar1=1.0 / D,
                  scalar2=None, op0=ALU.mult), inc=sv)
              P.op("vector", lambda e: e.tensor_tensor(
                  out=mrow[:, 3 * TT:4 * TT], in0=mrow[:, 0:TT],
                  in1=mrow[:, 0:TT], op=ALU.mult), waits=[(sv, sv.n)], inc=sv)
              P.op("vector", lambda e: e.tensor_tensor(
                  out=mrow[:, 2 * TT:3 * TT], in0=mrow[:, 2 * TT:3 * TT],
                  in1=mrow[:, 3 * TT:4 * TT], op=ALU.subtract),
                  waits=[(sv, sv.n)], inc=sv)
              SV_V2 = sv.n
              P.op("scalar", lambda e: e.activation(
                  out=mrow[:, 3 * TT:4 * TT], in_=mrow[:, 2 * TT:3 * TT],
                  func=AF.Ln, bias=EPSC[0:1, :]), waits=[(sv, SV_V2)], inc=sa)
              P.op("scalar", lambda e: e.activation(
                  out=mrow[:, TT:2 * TT], in_=mrow[:, 3 * TT:4 * TT],
                  func=AF.Exp, scale=-0.5), waits=[(sa, sa.n)], inc=sa)
              SA_LN2 = sa.n

              tok0 = t * TT
              P.op("scalar", lambda e, tok0=tok0: e.dma_start(
                  out=stats_scr[0, tok0:tok0 + TT].unsqueeze(0), in_=mrow[:, 0:TT]),
                  waits=[(sa, SA_LN2)], inc=gstat)
              P.op("scalar", lambda e, tok0=tok0: e.dma_start(
                  out=stats_scr[1, tok0:tok0 + TT].unsqueeze(0), in_=mrow[:, TT:2 * TT]),
                  inc=gstat)
              GS_STAT = gstat.n
              for c in range(8):
                  P.op("scalar", lambda e, c=c, seq=seq, t0=t0: e.dma_start(
                      out=mid_scr[seq, c * 128:(c + 1) * 128, t0:t0 + TT],
                      in_=midt[:, c * TT:(c + 1) * TT]),
                      waits=[(sv, SV_MID)] if c == 0 else [], inc=gmid)
              GS_MID = gmid.n

              prev = dict(sp_st2=SP_ST2, sa_c2=SA_C2, gs_mid=GS_MID,
                          gs_stat=GS_STAT, sv_v2=SV_V2, sp_vend=SP_VEND,
                          sv_mid=SV_MID, sv_mid2=prev["sv_mid"],
                          sp_bc_done=SP_BC)

    SP_ATTN, SV_ATTN, SA_ATTN = sp.n, sv.n, sa.n
    GMID_ATTN, GSTAT_ATTN, GS_ATTN = gmid.n, gstat.n, gs.n


    if phases == "pa":
        with contextlib.ExitStack() as des:
            df = des.enter_context(nc.sbuf_tensor("dfa", [128, 2048], F32))
            qd = SemCtr(des.enter_context(nc.semaphore("qda")), 16)
            for s_ in range(n_seq):
                for c in range(8):
                    P.op("sync", lambda e, s_=s_, c=c: e.dma_start(
                        out=df[:], in_=mid_scr[s_, c * 128:(c + 1) * 128, :]),
                        waits=[(gmid, gmid.n), (gout, gout.n)], inc=qd)
                    P.op("scalar", lambda e, s_=s_, c=c: e.dma_start(
                        out=out_fm[s_, c * 128:(c + 1) * 128, :], in_=df[:]),
                        waits=[(qd, qd.n)], inc=gout)
                    P.wait("sync", (gout, gout.n))
            P.wait("scalar", (gout, gout.n))

    with contextlib.ExitStack() as fes:
      if phases == "paf":
          fsb = lambda n, s, d: fes.enter_context(nc.sbuf_tensor(n, s, d))
          fps = lambda n, s, d: fes.enter_context(nc.psum_tensor(n, s, d))
          wex = fsb("wex", [128, 8 * HID], BF16)
          wcn = fsb("wcn", [128, 32 * 1024], BF16)
          mid2_ = [fsb(f"mid2{i}", [128, 8 * TT], F32) for i in range(2)]
          xn2 = fsb("xn2", [128, 8 * TT], BF16)
          g_sb = fsb("g_sb", [128, 32 * TT], BF16)
          o_sb = fsb("o_sb", [128, 2 * TT], F32)
          mb2 = fsb("mb2", [128, TT], BF16)
          rb2 = fsb("rb2", [128, TT], BF16)
          HP = [fps(f"HP{i}", [128, 512], F32) for i in range(2)]
          PO = [fps(f"PO{i}", [128, 512], F32) for i in range(4)]
          PBC = [fps(f"PBC{i}", [128, 512], F32) for i in range(2)]

          P.wait("sync", (gs, GS_ATTN), (gmid, GMID_ATTN), (gstat, GSTAT_ATTN))
          for c in range(8):
              P.op("sync", lambda e, c=c: e.dma_start(
                  out=wex[:, c * HID:(c + 1) * HID],
                  in_=wexp_scr[c * 128:(c + 1) * 128, :]), inc=qf)
          wcv = wcon_scr.rearrange("(B jj s) c -> (B jj) s c", B=32, jj=4, s=32)
          for g in range(32):
              P.op("sync", lambda e, g=g: e.dma_start(
                  out=wcn[:, g * 1024:(g + 1) * 1024], in_=wcv[:, g, :]), inc=qf)
          QF_W = qf.n

          sv_o_prev = SV_ATTN
          sv_o_prev2 = SV_ATTN
          sp_bc2_prev = SP_ATTN
          sv_xn2_prev = SV_ATTN
          gs_out_prev = 0
          for t in range(NTIL):
              seq = t // TPS
              t0 = (t % TPS) * TT
              tok0 = t * TT
              qft = (qf, qf2)[t % 2]
              mid2 = mid2_[t % 2]
              for c in range(8):
                  P.op("sync", lambda e, c=c, seq=seq, t0=t0, mid2=mid2: e.dma_start(
                      out=mid2[:, c * TT:(c + 1) * TT],
                      in_=mid_scr[seq, c * 128:(c + 1) * 128, t0:t0 + TT]),
                      waits=[(sv, sv_o_prev2)] if c == 0 else [], inc=qft)
              P.op("sync", lambda e, tok0=tok0: e.dma_start(
                  out=o_sb[0:1, TT:2 * TT],
                  in_=stats_scr[0, tok0:tok0 + TT].unsqueeze(0)),
                  waits=[(sp, sp_bc2_prev), (gout, gs_out_prev)], inc=qft)
              P.op("sync", lambda e, tok0=tok0: e.dma_start(
                  out=o_sb[32:33, TT:2 * TT],
                  in_=stats_scr[1, tok0:tok0 + TT].unsqueeze(0)),
                  inc=qft)
              QF_T = qft.n

              w_ = [(qft, QF_T), (sv, sv_xn2_prev)]
              if t == 0:
                  w_.append((sa, SA_ATTN))
              P.op("tensor", lambda e: e.matmul(
                  PBC[0][:, :], orow[0:1, :], o_sb[0:1, TT:2 * TT],
                  start=True, stop=True), waits=w_, inc=sp)
              P.op("tensor", lambda e: e.matmul(
                  PBC[1][:, :], orow[32:33, :], o_sb[32:33, TT:2 * TT],
                  start=True, stop=True), inc=sp)
              SP_BC2 = sp.n
              sp_bc2_prev = SP_BC2
              P.op("vector", lambda e: e.tensor_copy(mb2[:], PBC[0][:, :]),
                   waits=[(sp, SP_BC2), (sv, sv_xn2_prev)], inc=sv)
              P.op("vector", lambda e: e.tensor_copy(rb2[:], PBC[1][:, :]), inc=sv)
              SV_BC2 = sv.n
              for c in range(8):
                  w_ = [(sv, SV_BC2)] if c == 0 else [(sv, sv.n)]
                  P.op("vector", lambda e, c=c, mid2=mid2: e.tensor_tensor(
                      out=PBC[1][:, :], in0=mid2[:, c * TT:(c + 1) * TT],
                      in1=mb2[:], op=ALU.subtract),
                      waits=w_, inc=sv)
                  P.op("vector", lambda e, c=c: e.tensor_tensor(
                      out=xn2[:, c * TT:(c + 1) * TT], in0=PBC[1][:, :],
                      in1=rb2[:], op=ALU.mult), waits=[(sv, sv.n)], inc=sv)
              SV_XN2 = sv.n
              sv_xn2_prev = SV_XN2

              for fc in range(32):
                  pp = HP[fc % 2]
                  for c in range(8):
                      w_ = []
                      if fc == 0 and c == 0:
                          w_ = [(qft, QF_T), (sv, SV_XN2)]
                      elif c == 0 and fc >= 2:
                          w_ = [(sa, sa.n)]
                      P.op("tensor", lambda e, fc=fc, c=c, pp=pp: e.matmul(
                          pp[:, :],
                          wex[:, c * HID + fc * 128: c * HID + (fc + 1) * 128],
                          xn2[:, c * TT:(c + 1) * TT],
                          start=(c == 0), stop=(c == 7)),
                          waits=w_, inc=(sp if c == 7 else None))
                  P.op("scalar", lambda e, fc=fc, pp=pp: e.activation(
                      out=g_sb[:, fc * TT:(fc + 1) * TT], in_=pp[:, :], func=AF.Gelu,
                      bias=biasg[:, fc:fc + 1]), waits=[(sp, sp.n)], inc=sa)
              SA_G = sa.n

              for oh in range(2):
                  for fc in range(32):
                      for osub in range(4):
                          ot = oh * 4 + osub
                          w_ = []
                          if fc == 0 and osub == 0:
                              w_ = [(sa, SA_G)] if oh == 0 else [(sv, sv.n)]
                          P.op("tensor", lambda e, fc=fc, ot=ot, osub=osub: e.matmul(
                              PO[osub][:, :],
                              wcn[:, fc * 1024 + ot * 128: fc * 1024 + (ot + 1) * 128],
                              g_sb[:, fc * TT:(fc + 1) * TT],
                              start=(fc == 0), stop=(fc == 31)),
                              waits=w_, inc=(sp if (fc == 31 and osub == 3) else None))
                  SP_O = sp.n
                  for osub in range(4):
                      ot = oh * 4 + osub
                      w_ = [(gout, gs_out_prev)] if osub % 2 == 0 else [(sv, sv.n)]
                      if osub == 0:
                          w_.append((sp, SP_O))
                      P.op("vector", lambda e, osub=osub, ot=ot, mid2=mid2: e.scalar_tensor_tensor(
                          out=o_sb[:, (osub % 2) * TT:((osub % 2) + 1) * TT],
                          in0=PO[osub][:, :],
                          scalar=bco[:, ot:ot + 1], in1=mid2[:, ot * TT:(ot + 1) * TT],
                          op0=ALU.add, op1=ALU.add),
                          waits=w_, inc=sv)
                      if osub % 2 == 1:
                          SV_O = sv.n
                          for oo in (osub - 1, osub):
                              ot2 = oh * 4 + oo
                              P.op("scalar", lambda e, oo=oo, ot2=ot2, seq=seq, t0=t0: e.dma_start(
                                  out=out_fm[seq, ot2 * 128:(ot2 + 1) * 128, t0:t0 + TT],
                                  in_=o_sb[:, (oo % 2) * TT:((oo % 2) + 1) * TT]),
                                  waits=[(sv, SV_O)] if oo == osub - 1 else [], inc=gout)
                          gs_out_prev = gout.n
              sv_o_prev2 = sv_o_prev
              sv_o_prev = sv.n

          P.wait("scalar", (gout, gout.n))

    P.wait("sync", (qc, qc.n), (qx0, qx0.n), (qx1, qx1.n), (qp, qp.n),
           (qp2, qp2.n), (qf, qf.n), (qf2, qf2.n))
    P.wait("scalar", (gs, gs.n), (gso, gso.n), (gstat, gstat.n), (gmid, gmid.n),
           (gout, gout.n))

    with nc.allow_low_precision(
            reason="intentional bf16 operand rounding (matmul inputs / 1/Z "
                   "scale); all accumulation stays fp32 in PSUM"), \
         nc.Block() as block:
        @block.sync
        def _(e):
            for fn in P.q["sync"]:
                fn(e)

        @block.tensor
        def _(e):
            for fn in P.q["tensor"]:
                fn(e)

        @block.vector
        def _(e):
            for fn in P.q["vector"]:
                fn(e)

        @block.scalar
        def _(e):
            for fn in P.q["scalar"]:
                fn(e)

        @block.gpsimd
        def _(e):
            for fn in P.q["gpsimd"]:
                fn(e)

    es.close()
    return nc



def make_consts():
    cf = np.zeros((128, 132), np.float32)
    cf[:, 0:128] = np.eye(128, dtype=np.float32)
    cf[:, 128] = EPS
    cbm = np.zeros((128, 132), np.float32)
    cbm[:, 0:128] = np.eye(128)
    cbm[:, 128] = 1.0
    rf = np.zeros((4, 128), np.float32)
    rf[0, :] = 1.0
    rf[1, 0:64] = 1.0
    rf[2, 64:128] = 1.0
    rb2 = np.zeros((2, 128), np.float32)
    rb2[0, 0:64] = 1.0
    rb2[1, 64:128] = 1.0
    ef = np.zeros((128, 128), np.float32)
    ef[0, 0:64] = 1.0
    ef[64, 64:128] = 1.0
    cb2 = np.zeros((128, 4), np.float32)
    cb2[:, 0] = 1.0
    return (cf, cbm.astype(ml_dtypes.bfloat16), rf,
            rb2.astype(ml_dtypes.bfloat16), ef, cb2.astype(ml_dtypes.bfloat16))


def make_inputs(L0_attn_w_attn, L0_ffn_W_exp, L0_ffn_b_exp, L0_ffn_up_weights,
                L0_ffn_up_bias, L0_ffn_down_weights, L0_ffn_W_con, L0_ffn_b_con,
                L0_ffn_down_bias, L0_attn_w_proj):
    SA_u, SQ_u = build_stage_mats(np.asarray(L0_ffn_up_weights, np.float32))
    SA_d, SQ_d = build_stage_mats(np.asarray(L0_ffn_down_weights, np.float32))
    cfc, cbb, rf, rb2, ef, cb2 = make_consts()
    fmap = f_of_hprime()
    return {
        "w_attnT": np.ascontiguousarray(np.asarray(L0_attn_w_attn, np.float32).T),
        "w_projT": np.ascontiguousarray(np.asarray(L0_attn_w_proj, np.float32).T),
        "w_expT": np.ascontiguousarray(np.asarray(L0_ffn_W_exp, np.float32).T),
        "w_con": np.ascontiguousarray(np.asarray(L0_ffn_W_con, np.float32)),
        "b_exp": np.asarray(L0_ffn_b_exp, np.float32),
        "ub_perm": np.ascontiguousarray(np.asarray(L0_ffn_up_bias, np.float32)[fmap]),
        "b_con": np.asarray(L0_ffn_b_con, np.float32),
        "d_bias": np.asarray(L0_ffn_down_bias, np.float32),
        "SAu": np.ascontiguousarray(SA_u), "SQu": np.ascontiguousarray(SQ_u),
        "SAdT": np.ascontiguousarray(SA_d.transpose(0, 1, 3, 2)),
        "SQdT": np.ascontiguousarray(SQ_d.transpose(0, 1, 3, 2)),
        "cf_in": cfc, "cb_in": cbb, "rf_in": rf, "rb_in": rb2, "ef_in": ef,
        "cb2_in": cb2,
    }


def kernel(L0_attn_x, L0_attn_w_attn, L0_attn_w_proj, L0_ffn_W_exp, L0_ffn_b_exp,
           L0_ffn_up_weights, L0_ffn_up_bias, L0_ffn_down_weights, L0_ffn_W_con,
           L0_ffn_b_con, L0_ffn_down_bias):
    x = np.asarray(L0_attn_x, dtype=np.float32)
    B = x.shape[0]
    n_seq = B // N_CORES
    shared = make_inputs(L0_attn_w_attn, L0_ffn_W_exp, L0_ffn_b_exp,
                         L0_ffn_up_weights, L0_ffn_up_bias, L0_ffn_down_weights,
                         L0_ffn_W_con, L0_ffn_b_con, L0_ffn_down_bias,
                         L0_attn_w_proj)
    nc = build_program(n_seq=n_seq)
    in_maps = []
    for core in range(N_CORES):
        m = dict(shared)
        m["x_fm"] = np.ascontiguousarray(
            x[core * n_seq:(core + 1) * n_seq].transpose(0, 2, 1))
        in_maps.append(m)
    res = run_bass_kernel_spmd(nc, in_maps, core_ids=list(range(N_CORES)))
    outs = [np.ascontiguousarray(res.results[c]["out_fm"].transpose(0, 2, 1))
            for c in range(N_CORES)]
    return np.concatenate(outs, axis=0)


# version 15
# speedup vs baseline: 1.0704x; 1.0415x over previous
import contextlib
import numpy as np
import ml_dtypes
import concourse.bass as bass
import concourse.mybir as mybir
from concourse.bass_utils import run_bass_kernel_spmd

F32 = mybir.dt.float32
BF16 = mybir.dt.bfloat16
AF = mybir.ActivationFunctionType
ALU = mybir.AluOpType
AXC = mybir.AxisListType.C

N_CORES = 8
D = 1024
HID = 4096
NH = 16
WIN = 256
TSEQ = 2048
TT = 512
EPS = 1e-5



def build_stage_mats(W):
    SA = np.zeros((7, 32, 128, 128), np.float32)
    l = np.arange(128)
    for st in range(7):
        s = 1 << st
        q = l >> (st + 1)
        half = (l >> st) & 1
        r = l & (s - 1)
        kloc = q * s + r
        part = l + np.where(half == 0, s, -s)
        for B in range(32):
            k = B * 64 + kloc
            SA[st, B, l, l] = W[st, k, half, half]
            SA[st, B, l, part] = W[st, k, half, 1 - half]
    SQ = np.zeros((5, 32, 128, 128), np.float32)
    p = np.arange(128)
    Bp = p >> 2
    jj = p & 3
    for st in range(7, 12):
        s = 1 << st
        sb = 1 << (st - 7)
        for g in range(32):
            f = Bp * 128 + jj * 32 + g
            half = (f >> st) & 1
            k = (f >> (st + 1)) * s + (f & (s - 1))
            partner = p + np.where(half == 0, sb * 4, -sb * 4)
            SQ[st - 7, g, p, p] = W[st, k, half, half]
            SQ[st - 7, g, p, partner] = W[st, k, half, 1 - half]
    return SA, SQ


def f_of_hprime():
    hp = np.arange(HID)
    g = hp >> 7
    p = hp & 127
    return (p >> 2) * 128 + (p & 3) * 32 + g



ENGS = ("sync", "tensor", "vector", "scalar", "gpsimd")


class SemCtr:
    __slots__ = ("sem", "step", "n")

    def __init__(self, sem, step):
        self.sem = sem
        self.step = step
        self.n = 0


class Plan:
    def __init__(self):
        self.q = {e: [] for e in ENGS}

    def op(self, eng, build, waits=(), inc=None, inck=1):
        wl = tuple((sc.sem, v) for sc, v in waits if v > 0)
        amt = None
        sem = None
        if inc is not None:
            amt = inck * inc.step
            inc.n += amt
            sem = inc.sem

        def fn(e, wl=wl, build=build, amt=amt, sem=sem):
            for s, v in wl:
                e.wait_ge(s, v)
            inst = build(e)
            if amt is not None:
                inst.then_inc(sem, amt)
        self.q[eng].append(fn)

    def wait(self, eng, *waits):
        wl = tuple((sc.sem, v) for sc, v in waits if v > 0)

        def fn(e, wl=wl):
            for s, v in wl:
                e.wait_ge(s, v)
        self.q[eng].append(fn)



def build_program(n_seq=4, phases="paf", attn_tiles=None, attn_stage=9, win_stage=9, skip_prep=False, win_heads=4, win_wins=None, prep_stage=9):
    nc = bass.Bass("TRN2", target_bir_lowering=False)
    NTOK = n_seq * TSEQ
    NTIL = NTOK // TT
    TPS = TSEQ // TT
    WPT = TT // WIN

    dt = nc.dram_tensor
    x_fm = dt("x_fm", [n_seq, D, TSEQ], F32, kind="ExternalInput").ap()
    w_attnT = dt("w_attnT", [D, 3 * D], F32, kind="ExternalInput").ap()
    w_projT = dt("w_projT", [D, D], F32, kind="ExternalInput").ap()
    w_expT = dt("w_expT", [HID, D], F32, kind="ExternalInput").ap()
    w_con = dt("w_con", [HID, D], F32, kind="ExternalInput").ap()
    b_exp = dt("b_exp", [HID], F32, kind="ExternalInput").ap()
    ub_perm = dt("ub_perm", [HID], F32, kind="ExternalInput").ap()
    b_con = dt("b_con", [D], F32, kind="ExternalInput").ap()
    d_bias = dt("d_bias", [D], F32, kind="ExternalInput").ap()
    SAu = dt("SAu", [7, 32, 128, 128], F32, kind="ExternalInput").ap()
    SQu = dt("SQu", [5, 32, 128, 128], F32, kind="ExternalInput").ap()
    SAdT = dt("SAdT", [7, 32, 128, 128], F32, kind="ExternalInput").ap()
    SQdT = dt("SQdT", [5, 32, 128, 128], F32, kind="ExternalInput").ap()
    cf_in = dt("cf_in", [128, 132], F32, kind="ExternalInput").ap()
    cb_in = dt("cb_in", [128, 132], BF16, kind="ExternalInput").ap()
    cb2_in = dt("cb2_in", [128, 4], BF16, kind="ExternalInput").ap()
    rf_in = dt("rf_in", [4, 128], F32, kind="ExternalInput").ap()
    ef_in = dt("ef_in", [128, 128], F32, kind="ExternalInput").ap()
    rb_in = dt("rb_in", [2, 128], BF16, kind="ExternalInput").ap()

    out_fm = dt("out_fm", [n_seq, D, TSEQ], F32, kind="ExternalOutput").ap()

    mid_scr = dt("mid_scr", [n_seq, D, TSEQ], F32, kind="Internal").ap()
    p_scr = dt("p_scr", [HID, 1028], F32, kind="Internal").ap()
    wexp_scr = dt("wexp_scr", [D, HID], BF16, kind="Internal").ap()
    wcon_scr = dt("wcon_scr", [HID, D], BF16, kind="Internal").ap()
    stats_scr = dt("stats_scr", [2, NTOK], F32, kind="Internal").ap()

    es = contextlib.ExitStack()
    sb = lambda name, shape, dtp: es.enter_context(nc.sbuf_tensor(name, shape, dtp))
    sem = lambda name, step: SemCtr(es.enter_context(nc.semaphore(name)), step)

    cf = sb("cf", [128, 132], F32)
    cbc = sb("cbc", [128, 4], BF16)
    orow = sb("orow", [33, 128], F32)
    e2f = sb("e2f", [128, 128], F32)
    biasg = sb("biasg", [128, 32], F32)
    bex = sb("bex", [128, 32], F32)
    bco = sb("bco", [128, 8], F32)

    qc = sem("qc", 16)
    qx0 = sem("qx0", 16)
    qx1 = sem("qx1", 16)
    qp = sem("qp", 16)
    qp2 = sem("qp2", 16)
    gso = sem("gso", 16)
    qf = sem("qf", 16)
    qf2 = sem("qf2", 16)
    gs = sem("gs", 16)
    gstat = sem("gstat", 16)
    gmid = sem("gmid", 16)
    gout = sem("gout", 16)
    sp = sem("sp", 1)
    sv = sem("sv", 1)
    sa = sem("sa", 1)
    sg = sem("sg", 1)

    P = Plan()

    P.op("sync", lambda e: e.dma_start(out=cf[:], in_=cf_in[:, :]), inc=qc)
    P.op("sync", lambda e: e.dma_start(out=cbc[:], in_=cb2_in[:, :]), inc=qc)
    P.op("sync", lambda e: e.dma_start(out=orow[0:1, :], in_=rf_in[0:1, :]), inc=qc)
    P.op("sync", lambda e: e.dma_start(out=orow[32:33, :], in_=rf_in[0:1, :]), inc=qc)
    P.op("sync", lambda e: e.dma_start(out=e2f[:], in_=ef_in[:, :]), inc=qc)
    QC0 = qc.n
    ID = cf[:, 0:128]
    EPSC = cf[:, 128:129]
    ONEB = cbc[:, 0:1]

    with contextlib.ExitStack() as pes:
      if not skip_prep:
        psb = lambda n, s, d: pes.enter_context(nc.sbuf_tensor(n, s, d))
        pps = lambda n, s, d: pes.enter_context(nc.psum_tensor(n, s, d))
        sall = [psb(f"sall{i}", [128, 32 * 128], F32) for i in range(2)]
        gall = [psb(f"gall{i}", [128, 32 * 128], F32) for i in range(2)]
        svin2 = [psb(f"svin{i}", [128, 1028], F32) for i in range(2)]
        svout2 = [psb(f"svout{i}", [128, 1028], F32) for i in range(2)]
        sv2b2 = [psb(f"sv2b{i}", [128, 1024], BF16) for i in range(2)]
        swx2 = [psb(f"swx{i}", [128, 1024], BF16) for i in range(2)]
        bx = psb("bx", [128, 96], F32)
        idbp = psb("idbp", [128, 128], BF16)
        P.op("sync", lambda e: e.dma_start(out=idbp[:], in_=cb_in[:, 0:128]), inc=qc)
        QC0b = qc.n
        QC0 = QC0b
        PP = [pps(f"PP{i}", [128, 512], F32) for i in range(3)]
        PQ2 = [pps(f"PQ2{i}", [128, 512], F32) for i in range(3)]
        PPT = [pps(f"PPT{i}", [128, 512], BF16) for i in range(2)]

        state = {"sall_mark": [0, 0], "gall_guard": 0, "ppmark": {},
                 "chain_end": 0}

        def batched_loop(stages_ap, n_st, reverse, emit_load, ncols, emit_out,
                         gslot_final_holder):
            order = list(range(n_st))
            if reverse:
                order = order[::-1]
            for i, j in enumerate(order):
                sslot = i % 2
                gsl = i % 2
                sv_ = stages_ap[j].rearrange("b p c -> p b c")
                qsl = (qp, qp2)[sslot]
                for part in range(8):
                    P.op("sync", lambda e, sv_=sv_, sslot=sslot, part=part: e.dma_start(
                        out=sall[sslot][:, :].rearrange("p (b c) -> p b c", b=32)[
                            :, part * 4:(part + 1) * 4, :],
                        in_=sv_[:, part * 4:(part + 1) * 4, :]),
                        waits=([(sp, state["sall_mark"][sslot]), (qc, QC0)]
                               if part == 0 else []), inc=qsl)
                ld = (qsl, qsl.n)
                for b in range(32):
                    tile = PP[b % 3][:, 0:128]
                    w_ = []
                    if b == 0:
                        w_ = [ld]
                        if i == 0:
                            w_.append((sv, sv.n))
                            w_.append((sa, sa.n))
                            w_.append((sp, state["gall_guard"]))
                            w_.append((qc, QC0))
                    pm = state["ppmark"].get(b % 3)
                    if pm is not None:
                        w_ = w_ + [pm]
                    if i == 0:
                        P.op("tensor", lambda e, b=b, tile=tile, sslot=sslot: e.matmul(
                            tile, sall[sslot][:, b * 128:(b + 1) * 128], ID,
                            start=True, stop=True), waits=w_, inc=sp)
                    else:
                        P.op("tensor", lambda e, b=b, tile=tile, sslot=sslot, gsl=gsl: e.matmul(
                            tile, sall[sslot][:, b * 128:(b + 1) * 128],
                            gall[1 - gsl][:, b * 128:(b + 1) * 128],
                            start=True, stop=True), waits=w_, inc=sp)
                    if b % 2 == 0:
                        P.op("vector", lambda e, b=b, tile=tile, gsl=gsl: e.tensor_copy(
                            gall[gsl][:, b * 128:(b + 1) * 128], tile),
                            waits=[(sp, sp.n)], inc=sv)
                        state["ppmark"][b % 3] = (sv, sv.n)
                    else:
                        P.op("scalar", lambda e, b=b, tile=tile, gsl=gsl: e.copy(
                            out=gall[gsl][:, b * 128:(b + 1) * 128], in_=tile),
                            waits=[(sp, sp.n)], inc=sa)
                        state["ppmark"][b % 3] = (sa, sa.n)
                state["sall_mark"][sslot] = sp.n
            gfin = (n_st - 1) % 2
            SV_CH, SA_CH = sv.n, sa.n
            state["chain_end"] = sp.n
            state.pop("apply_mm0", None)
            state.pop("apply_mm1", None)
            for b in range(32 if prep_stage >= 2 else 0):
                slot = b % 2
                qsl = (qp, qp2)[slot]
                emit_load(b, slot, qsl,
                          [(sp, state.get(f"apply_mm{slot}", state["chain_end"]))])
                ld = (qsl, qsl.n)
                splits = []
                c0 = 0
                for pi, pq in enumerate(PQ2):
                    if c0 >= ncols:
                        break
                    n = min(512, ncols - c0)
                    splits.append((pq, c0, n, slot))
                    c0 += n
                for k, (pq, c0, n, sl) in enumerate(splits):
                    w_ = []
                    if k == 0:
                        w_ = [ld, (sv, SV_CH), (sa, SA_CH)]
                        if state.get("pqmark"):
                            w_.append(state["pqmark"])
                    P.op("tensor", lambda e, pq=pq, c0=c0, n=n, b=b, sl=sl, gfin=gfin: e.matmul(
                        pq[:, (sl) * 0: n] if False else pq[:, 0:n],
                        gall[gfin][:, b * 128:(b + 1) * 128],
                        svin2[sl][:, c0:c0 + n], start=True, stop=True),
                        waits=w_, inc=(sp if k == len(splits) - 1 else None))
                state[f"apply_mm{slot}"] = sp.n
                for ki, (pq, c0, n, sl) in enumerate(splits):
                    w_ = [(sp, sp.n)]
                    if ki == 0 and state.get(f"outst{slot}"):
                        w_.append(state[f"outst{slot}"])
                    P.op("vector", lambda e, pq=pq, c0=c0, n=n, sl=sl: e.tensor_copy(
                        svout2[sl][:, c0:c0 + n], pq[:, 0:n]),
                        waits=w_, inc=sv)
                state["pqmark"] = (sv, sv.n)
                state[f"apply_ev{slot}"] = sv.n
                if prep_stage >= 3:
                    emit_out(b, slot)
            state["gall_guard"] = sp.n
            state["sall_mark"] = [sp.n, sp.n]

        def ld_upA(b, slot, qsl, w_):
            P.op("sync", lambda e, b=b, slot=slot: e.dma_start(
                out=svin2[slot][:, 0:1024], in_=w_expT[b * 128:(b + 1) * 128, :]),
                waits=w_, inc=qsl)
            P.op("sync", lambda e, b=b, slot=slot: e.dma_start(
                out=svin2[slot][:, 1024:1025],
                in_=b_exp[b * 128:(b + 1) * 128].unsqueeze(1)), inc=qsl)

        def out_upA(b, slot):
            gq = (gs, gso)[slot]
            P.op("scalar", lambda e, b=b, slot=slot: e.dma_start(
                out=p_scr[b * 128:(b + 1) * 128, 0:1025], in_=svout2[slot][:, 0:1025]),
                waits=[(sv, sv.n)], inc=gq)
            state[f"outst{slot}"] = (gq, gq.n)

        batched_loop(SAu, 7, True, ld_upA, 1025, out_upA, None)
        P.wait("sync", (gs, gs.n), (gso, gso.n))

        p_view = p_scr.rearrange("(B jj s) c -> (B jj) s c", B=32, jj=4, s=32)

        def ld_upQ(g, slot, qsl, w_):
            P.op("sync", lambda e, g=g, slot=slot: e.dma_start(
                out=svin2[slot][:, 0:1025], in_=p_view[:, g, 0:1025]),
                waits=w_, inc=qsl)

        def out_upQ(g, slot):
            P.op("vector", lambda e, g=g, slot=slot: e.tensor_copy(
                bex[:, g:g + 1], svout2[slot][:, 1024:1025]),
                waits=[(sv, sv.n)], inc=sv)
            P.op("vector", lambda e, slot=slot: e.tensor_copy(
                sv2b2[slot][:], svout2[slot][:, 0:1024]),
                waits=[(sv, sv.n), (sp, state.get(f"tp{slot}", 0))], inc=sv)
            SV_CAST = sv.n
            for ci in range(8):
                pt = PPT[ci % 2][:, 0:128]
                w_ = [(sv, SV_CAST)] if ci == 0 else []
                ptm = state.setdefault("ptmark", {}).get(ci % 2)
                if ptm is not None:
                    w_ = w_ + [ptm]
                else:
                    w_ = w_ + [(qc, QC0b)]
                P.op("tensor", lambda e, ci=ci, slot=slot, pt=pt: e.transpose(
                    pt, sv2b2[slot][:, ci * 128:(ci + 1) * 128], idbp[:]),
                    waits=w_, inc=sp)
                sw = state.get(f"swxst{slot}")
                P.op("scalar", lambda e, ci=ci, slot=slot, pt=pt: e.copy(
                    out=swx2[slot][:, ci * 128:(ci + 1) * 128], in_=pt),
                    waits=[(sp, sp.n)] + ([sw] if ci == 0 and sw else []), inc=sa)
                state["ptmark"][ci % 2] = (sa, sa.n)
            state[f"tp{slot}"] = sp.n
            SA_SWX = sa.n
            P.op("scalar", lambda e, g=g, slot=slot: e.dma_start(
                out=wexp_scr.rearrange("(cc p) f -> p (cc f)", p=128)[
                    :, 0:0] if False else wexp_scr[0:128, 0:128],
                in_=swx2[slot][:, 0:128]), waits=[], inc=gs) if False else None
            gq = (gs, gso)[slot]
            for ci in range(8):
                P.op("scalar", lambda e, ci=ci, g=g, slot=slot: e.dma_start(
                    out=wexp_scr[ci * 128:(ci + 1) * 128, g * 128:(g + 1) * 128],
                    in_=swx2[slot][:, ci * 128:(ci + 1) * 128]),
                    waits=[(sa, SA_SWX)] if ci == 0 else [], inc=gq)
            state[f"swxst{slot}"] = (gq, gq.n)
            state[f"outst{slot}"] = (sv, SV_CAST)

        batched_loop(SQu, 5, True, ld_upQ, 1025, out_upQ, None)
        P.wait("sync", (gs, gs.n), (gso, gso.n))

        w_view = w_con.rearrange("(B jj s) c -> (B jj) s c", B=32, jj=4, s=32)
        pv2 = p_scr.rearrange("(B jj s) c -> (B jj) s c", B=32, jj=4, s=32)

        def ld_dnQ(g, slot, qsl, w_):
            P.op("sync", lambda e, g=g, slot=slot: e.dma_start(
                out=svin2[slot][:, 0:1024], in_=w_view[:, g, :]),
                waits=w_, inc=qsl)

        def out_dnQ(g, slot):
            gq = (gs, gso)[slot]
            P.op("scalar", lambda e, g=g, slot=slot: e.dma_start(
                out=pv2[:, g, 0:1024], in_=svout2[slot][:, 0:1024]),
                waits=[(sv, sv.n)], inc=gq)
            state[f"outst{slot}"] = (gq, gq.n)

        batched_loop(SQdT, 5, False, ld_dnQ, 1024, out_dnQ, None)
        P.wait("sync", (gs, gs.n), (gso, gso.n))

        def ld_dnA(b, slot, qsl, w_):
            P.op("sync", lambda e, b=b, slot=slot: e.dma_start(
                out=svin2[slot][:, 0:1024], in_=p_scr[b * 128:(b + 1) * 128, 0:1024]),
                waits=w_, inc=qsl)

        def out_dnA(b, slot):
            wc = state.get(f"wcst{slot}")
            P.op("vector", lambda e, slot=slot: e.tensor_copy(
                sv2b2[slot][:], svout2[slot][:, 0:1024]),
                waits=[(sv, sv.n)] + ([wc] if wc else []), inc=sv)
            gq = (gs, gso)[slot]
            P.op("scalar", lambda e, b=b, slot=slot: e.dma_start(
                out=wcon_scr[b * 128:(b + 1) * 128, :], in_=sv2b2[slot][:]),
                waits=[(sv, sv.n)], inc=gq)
            state[f"wcst{slot}"] = (gq, gq.n)
            state[f"outst{slot}"] = (sv, sv.n)

        batched_loop(SAdT, 7, False, ld_dnA, 1024, out_dnA, None)
        P.wait("sync", (gs, gs.n), (gso, gso.n))

        uview = ub_perm.rearrange("(c p) -> c p", p=128)
        for c in range(32):
            P.op("sync", lambda e, c=c: e.dma_start(
                out=bx[:, 32 + c:33 + c], in_=uview[c, :].unsqueeze(1)), inc=qp)
        bcv = b_con.rearrange("(c p) -> c p", p=128)
        dbv = d_bias.rearrange("(c p) -> c p", p=128)
        for c in range(8):
            P.op("sync", lambda e, c=c: e.dma_start(
                out=bx[:, 64 + c:65 + c], in_=bcv[c, :].unsqueeze(1)), inc=qp)
            P.op("sync", lambda e, c=c: e.dma_start(
                out=bx[:, 72 + c:73 + c], in_=dbv[c, :].unsqueeze(1)), inc=qp)
        P.op("vector", lambda e: e.tensor_tensor(
            out=biasg[:], in0=bex[:], in1=bx[:, 32:64], op=ALU.add),
            waits=[(qp, qp.n), (sv, sv.n)], inc=sv)
        P.op("vector", lambda e: e.tensor_tensor(
            out=bco[:], in0=bx[:, 64:72], in1=bx[:, 72:80], op=ALU.add), inc=sv)

    SP_PREP, SV_PREP, SA_PREP, GS_PREP = sp.n, sv.n, sa.n, gs.n

    if "p" == phases:
        with contextlib.ExitStack() as des:
            db = des.enter_context(nc.sbuf_tensor("db", [128, 1024], BF16))
            df = des.enter_context(nc.sbuf_tensor("df", [128, 1024], F32))
            qd = SemCtr(des.enter_context(nc.semaphore("qd")), 16)
            for i in range(16):
                P.op("sync", lambda e, i=i: e.dma_start(
                    out=db[:], in_=wcon_scr[i * 128:(i + 1) * 128, :]),
                    waits=[(gs, gs.n), (sv, sv.n)], inc=qd)
                P.op("vector", lambda e, i=i: e.tensor_copy(df[:], db[:]),
                     waits=[(qd, qd.n), (gout, gout.n)], inc=sv)
                P.op("scalar", lambda e, i=i: e.dma_start(
                    out=out_fm[0, (i % 8) * 128:(i % 8 + 1) * 128,
                               (i // 8) * 1024:(i // 8 + 1) * 1024],
                    in_=df[:]), waits=[(sv, sv.n)], inc=gout)
                P.wait("sync", (gout, gout.n))
            P.wait("scalar", (gout, gout.n))
    attn_on = "a" in phases
    with contextlib.ExitStack() as aes:
      if attn_on:
          asb = lambda n, s, d: aes.enter_context(nc.sbuf_tensor(n, s, d))
          aps = lambda n, s, d: aes.enter_context(nc.psum_tensor(n, s, d))
          waT = asb("waT", [128, 8 * 3072], BF16)
          wpT = asb("wpT", [128, 8 * 1024], BF16)
          with nc.sbuf_tensor("wstg0", [128, 3072], F32) as wstg0, \
               nc.sbuf_tensor("wstg1", [128, 3072], F32) as wstg1:
              wst = (wstg0, wstg1)
              qw = (qx0, qx1)
              cast_mark = [0, 0]
              for c in range(8):
                  sl = c % 2
                  P.op("sync", lambda e, c=c, sl=sl: e.dma_start(
                      out=wst[sl][:, :], in_=w_attnT[c * 128:(c + 1) * 128, :]),
                      waits=[(sv, cast_mark[sl])], inc=qw[sl])
                  P.op("vector", lambda e, c=c, sl=sl: e.tensor_copy(
                      waT[:, c * 3072:(c + 1) * 3072], wst[sl][:, :]),
                      waits=[(qw[sl], qw[sl].n)], inc=sv)
                  cast_mark[sl] = sv.n
              for c in range(8):
                  sl = c % 2
                  P.op("sync", lambda e, c=c, sl=sl: e.dma_start(
                      out=wst[sl][:, 0:1024], in_=w_projT[c * 128:(c + 1) * 128, :]),
                      waits=[(sv, cast_mark[sl])], inc=qw[sl])
                  P.op("vector", lambda e, c=c, sl=sl: e.tensor_copy(
                      wpT[:, c * 1024:(c + 1) * 1024], wst[sl][:, 0:1024]),
                      waits=[(qw[sl], qw[sl].n)], inc=sv)
                  cast_mark[sl] = sv.n
              P.wait("sync", (sv, sv.n))
          SV_W = sv.n

          x_sb = [asb(f"x{i}", [128, 8 * TT], F32) for i in range(2)]
          xbt = asb("xbt", [128, 8 * TT], BF16)
          x2t = asb("x2t", [128, 8 * TT], BF16)
          xnt = asb("xnt", [128, 8 * TT], BF16)
          qkt = asb("qkt", [128, 16 * TT], BF16)
          v_sb = asb("v_sb", [128, 8 * TT], BF16)
          y_sb = asb("y_sb", [128, 8 * TT], BF16)
          ett = asb("ett", [128, 2 * 1024], BF16)
          midt = asb("midt", [128, 8 * TT], F32)
          bgt = asb("bgt", [128, 512], F32)
          mrow = asb("mrow", [1, 4 * TT], F32)
          zz = asb("zz", [128, 1024], F32)
          P.op("vector", lambda e: e.memset(zz[:], 0.0), inc=sv)
          mbt = asb("mbt", [128, TT], BF16)
          rbt = asb("rbt", [128, TT], BF16)
          PA = aps("PA", [128, 1024], F32)
          PB = aps("PB", [128, 1024], F32)
          PC = aps("PC", [128, 512], F32)
          PD = aps("PD", [128, 512], F32)
          PE_ = aps("PE_", [128, 512], F32)
          PF = aps("PF", [128, 512], F32)

          prev = dict(sp_st2=SP_PREP, sa_c2=SA_PREP, gs_mid=0,
                      gs_stat=0, sv_v2=SV_PREP, sp_vend=SP_PREP,
                      sv_mid=SV_W, sv_mid2=SV_W, sp_bc_done=SP_PREP)

          for t in range(NTIL):
              seq = t // TPS
              t0 = (t % TPS) * TT
              qx = (qx0, qx1)[t % 2]
              xs = x_sb[t % 2]

              for c in range(8):
                  P.op("sync", lambda e, c=c, seq=seq, t0=t0, xs=xs: e.dma_start(
                      out=xs[:, c * TT:(c + 1) * TT],
                      in_=x_fm[seq, c * 128:(c + 1) * 128, t0:t0 + TT]),
                      waits=[(sv, prev["sv_mid2"])] if c == 0 else [], inc=qx)
              LD_X = qx.n

              for c in range(8):
                  P.op("scalar", lambda e, c=c, xs=xs: e.copy(
                      out=xbt[:, c * TT:(c + 1) * TT], in_=xs[:, c * TT:(c + 1) * TT]),
                      waits=([(qx, LD_X), (sp, prev["sp_st2"])] if c == 0 else []),
                      inc=sa)
              for c in range(8):
                  P.op("scalar", lambda e, c=c, xs=xs: e.activation(
                      out=x2t[:, c * TT:(c + 1) * TT], in_=xs[:, c * TT:(c + 1) * TT],
                      func=AF.Square), inc=sa)
              SA_CAST = sa.n

              if attn_stage < 2:
                prev = dict(prev, sv_mid=sv.n, sa_c2=sa.n, sp_st2=sp.n,
                            sv_v2=sv.n, sp_vend=sp.n)
                continue
              for c in range(8):
                  P.op("tensor", lambda e, c=c: e.matmul(
                      PC[0:1, 0:TT], ONEB, xbt[:, c * TT:(c + 1) * TT],
                      start=(c == 0), stop=(c == 7)),
                      waits=([(sa, SA_CAST), (sv, prev["sv_v2"])] if c == 0 else []),
                      inc=(sp if c == 7 else None))
              for c in range(8):
                  P.op("tensor", lambda e, c=c: e.matmul(
                      PD[0:1, 0:TT], ONEB, x2t[:, c * TT:(c + 1) * TT],
                      start=(c == 0), stop=(c == 7)), inc=(sp if c == 7 else None))
              SP_STAT = sp.n

              P.op("vector", lambda e: e.tensor_scalar(
                  out=mrow[:, 0:TT], in0=PC[0:1, 0:TT], scalar1=1.0 / D,
                  scalar2=None, op0=ALU.mult),
                  waits=[(sp, SP_STAT), (gstat, prev["gs_stat"])], inc=sv)
              P.op("vector", lambda e: e.tensor_scalar(
                  out=mrow[:, 2 * TT:3 * TT], in0=PD[0:1, 0:TT], scalar1=1.0 / D,
                  scalar2=None, op0=ALU.mult), inc=sv)
              P.op("vector", lambda e: e.tensor_tensor(
                  out=mrow[:, 3 * TT:4 * TT], in0=mrow[:, 0:TT],
                  in1=mrow[:, 0:TT], op=ALU.mult), waits=[(sv, sv.n)], inc=sv)
              P.op("vector", lambda e: e.tensor_tensor(
                  out=mrow[:, 2 * TT:3 * TT], in0=mrow[:, 2 * TT:3 * TT],
                  in1=mrow[:, 3 * TT:4 * TT], op=ALU.subtract),
                  waits=[(sv, sv.n)], inc=sv)
              SV_VAR = sv.n
              P.op("scalar", lambda e: e.activation(
                  out=mrow[:, 3 * TT:4 * TT], in_=mrow[:, 2 * TT:3 * TT],
                  func=AF.Ln, bias=EPSC[0:1, :]), waits=[(sv, SV_VAR)], inc=sa)
              P.op("scalar", lambda e: e.activation(
                  out=mrow[:, TT:2 * TT], in_=mrow[:, 3 * TT:4 * TT],
                  func=AF.Exp, scale=-0.5), waits=[(sa, sa.n)], inc=sa)
              SA_LN = sa.n

              if attn_stage < 3:
                prev = dict(prev, sv_mid=sv.n, sa_c2=sa.n, sp_st2=sp.n,
                            sv_v2=sv.n, sp_vend=sp.n)
                continue
              P.op("tensor", lambda e: e.matmul(
                  PE_[:, 0:TT], orow[0:1, :], mrow[0:1, 0:TT], start=True, stop=True),
                  waits=[(sa, SA_LN), (sv, prev["sv_mid"])], inc=sp)
              P.op("tensor", lambda e: e.matmul(
                  PF[:, 0:TT], orow[0:1, :], mrow[0:1, TT:2 * TT], start=True, stop=True),
                  inc=sp)
              SP_BC = sp.n
              P.op("vector", lambda e: e.tensor_copy(mbt[:], PE_[:, 0:TT]),
                   waits=[(sp, SP_BC)], inc=sv)
              P.op("vector", lambda e: e.tensor_copy(rbt[:], PF[:, 0:TT]), inc=sv)
              SV_BC = sv.n

              for c in range(8):
                  P.op("vector", lambda e, c=c: e.tensor_tensor(
                      out=x2t[:, c * TT:(c + 1) * TT], in0=xbt[:, c * TT:(c + 1) * TT],
                      in1=mbt[:], op=ALU.subtract),
                      waits=([(sv, SV_BC)] if c == 0 else [(sv, sv.n)]), inc=sv)
                  P.op("vector", lambda e, c=c: e.tensor_tensor(
                      out=xnt[:, c * TT:(c + 1) * TT], in0=x2t[:, c * TT:(c + 1) * TT],
                      in1=rbt[:], op=ALU.mult),
                      waits=[(sv, sv.n)] + ([(sp, prev["sp_vend"]),
                                             (sp, prev["sp_st2"])] if c == 0 else []),
                      inc=sv)
              SV_XN = sv.n

              if attn_stage < 4:
                prev = dict(prev, sv_mid=sv.n, sa_c2=sa.n, sp_st2=sp.n,
                            sv_v2=sv.n, sp_vend=sp.n)
                continue
              pq = (PC, PD)
              for ei in range(16):
                  pp = pq[ei % 2]
                  for c in range(8):
                      w_ = []
                      if ei == 0 and c == 0:
                          w_ = [(sv, SV_XN)]
                      elif c == 0 and ei >= 2:
                          w_ = [(sv, sv.n)]
                      P.op("tensor", lambda e, ei=ei, c=c, pp=pp: e.matmul(
                          pp[:, 0:TT],
                          waT[:, c * 3072 + ei * 128: c * 3072 + (ei + 1) * 128],
                          xnt[:, c * TT:(c + 1) * TT],
                          start=(c == 0), stop=(c == 7)),
                          waits=w_, inc=(sp if c == 7 else None))
                  P.op("vector", lambda e, ei=ei, pp=pp: e.tensor_copy(
                      qkt[:, ei * TT:(ei + 1) * TT], pp[:, 0:TT]),
                      waits=[(sp, sp.n)], inc=sv)
              for ts4 in range(TT // 128):
                  for eh in range(2):
                      pp = pq[(ts4 * 2 + eh) % 2]
                      for c in range(8):
                          P.op("tensor", lambda e, ts4=ts4, eh=eh, c=c, pp=pp: e.matmul(
                              pp[:, :],
                              xnt[:, c * TT + ts4 * 128: c * TT + (ts4 + 1) * 128],
                              waT[:, c * 3072 + 2048 + eh * 512: c * 3072 + 2048 + (eh + 1) * 512],
                              start=(c == 0), stop=(c == 7)),
                              waits=([(sv, sv.n)] if c == 0 else []),
                              inc=(sp if c == 7 else None))
                      P.op("vector", lambda e, ts4=ts4, eh=eh, pp=pp: e.tensor_copy(
                          v_sb[:, ts4 * 1024 + eh * 512: ts4 * 1024 + (eh + 1) * 512],
                          pp[:, :]), waits=[(sp, sp.n)], inc=sv)
              SV_QKV = sv.n
              SP_VEND = sp.n

              if attn_stage < 5:
                  prev = dict(prev, sv_mid=sv.n, sa_c2=sa.n, sp_st2=sp.n,
                              sv_v2=sv.n, sp_vend=SP_VEND)
                  continue
              CS2 = {(0, 0): 0, (0, 1): 256, (1, 0): 512, (1, 1): 768}
              NIT = WPT * 8
              mk = {"st": {}, "et": {-2: prev["sa_c2"], -1: prev["sa_c2"]},
                    "z": {}, "rc": {-2: prev["sa_c2"], -1: prev["sa_c2"]},
                    "av": {}, "bg": {}, "bge": {-2: prev["sa_c2"], -1: prev["sa_c2"]},
                    "y": {-2: SV_BC, -1: SV_BC, -3: SV_BC, -4: SV_BC}}

              def emit_st(k):
                  w, gi = divmod(k, 8)
                  pp = (PA, PB)[k % 2]
                  first = (k == 0)
                  for idx, (hp, jh) in enumerate(((0, 0), (0, 1), (1, 0), (1, 1))):
                      h = 2 * gi + hp
                      qc_ = h // 2
                      qb = (h % 2) * 64
                      w_ = []
                      if idx == 0:
                          w_ = [(sa, mk["et"][k - 2]), (sa, mk["rc"][k - 2])]
                          if first:
                              w_.append((sv, SV_QKV))
                      P.op("tensor", lambda e, pp=pp, qc_=qc_, qb=qb, jh=jh, w=w, hp=hp: e.matmul(
                          pp[:, CS2[(hp, jh)]:CS2[(hp, jh)] + 256],
                          qkt[qb:qb + 64, (8 + qc_) * TT + w * 256 + jh * 128:
                              (8 + qc_) * TT + w * 256 + (jh + 1) * 128],
                          qkt[qb:qb + 64, qc_ * TT + w * 256: qc_ * TT + (w + 1) * 256],
                          start=True, stop=True),
                          waits=w_, inc=(sp if idx == 3 else None))
                  mk["st"][k] = sp.n

              def emit_exp(k):
                  pp = (PA, PB)[k % 2]
                  eb = (k % 2) * 1024
                  P.op("scalar", lambda e, pp=pp, eb=eb: e.activation(
                      out=ett[:, eb:eb + 1024], in_=pp[:, :], func=AF.Exp, scale=0.125),
                      waits=[(sp, mk["st"][k]), (sp, mk["av"].get(k - 2, 0))], inc=sa)
                  mk["et"][k] = sa.n

              def emit_z(k):
                  zp = (PA, PB)[k % 2]
                  eb = (k % 2) * 1024
                  for idx, (hp, col) in enumerate(((0, 0), (1, 256))):
                      for jh in range(2):
                          w_ = []
                          if idx == 0 and jh == 0:
                              w_ = [(sa, mk["et"][k]), (sa, mk["rc"][k - 2])]
                          P.op("tensor", lambda e, zp=zp, col=col, eb=eb, hp=hp, jh=jh: e.matmul(
                              zp[0:1, col:col + 256], ONEB,
                              ett[:, eb + CS2[(hp, jh)]: eb + CS2[(hp, jh)] + 256],
                              start=(jh == 0), stop=(jh == 1)),
                              waits=w_, inc=(sp if (idx == 1 and jh == 1) else None))
                  mk["z"][k] = sp.n

              def emit_recip(k):
                  zp = (PA, PB)[k % 2]
                  sc = (k % 2) * 512
                  P.op("scalar", lambda e, zp=zp, sc=sc: e.activation(
                      out=zz[0:1, sc:sc + 256], in_=zp[0:1, 0:256], func=AF.Ln),
                      waits=[(sp, mk["z"][k]), (sp, mk["bg"].get(k - 2, 0))], inc=sa)
                  P.op("scalar", lambda e, zp=zp, sc=sc: e.activation(
                      out=zz[64:65, sc:sc + 256], in_=zp[0:1, 256:512], func=AF.Ln),
                      inc=sa)
                  mk["rc"][k] = sa.n

              def emit_av(k):
                  w, gi = divmod(k, 8)
                  eb = (k % 2) * 1024
                  yp = (PD, PE_)[k % 2]
                  for idx, hp in enumerate((0, 1)):
                      h = 2 * gi + hp
                      ob = hp * 64
                      for jh in range(2):
                          vt = w * 2 + jh
                          w_ = []
                          if idx == 0 and jh == 0:
                              w_ = [(sa, mk["et"][k]), (sv, mk["y"][k - 2])]
                          P.op("tensor", lambda e, ob=ob, h=h, jh=jh, vt=vt, eb=eb, yp=yp, hp=hp: e.matmul(
                              yp[ob:ob + 64, 0:256],
                              v_sb[:, vt * 1024 + h * 64: vt * 1024 + (h + 1) * 64],
                              ett[:, eb + CS2[(hp, jh)]: eb + CS2[(hp, jh)] + 256],
                              start=(jh == 0), stop=(jh == 1)),
                              waits=w_, inc=(sp if (idx == 1 and jh == 1) else None))
                  mk["av"][k] = sp.n

              def emit_bg(k):
                  sc = (k % 2) * 512
                  bp = (PC, PF)[k % 2]
                  P.op("tensor", lambda e, sc=sc, bp=bp: e.matmul(
                      bp[:, 0:256], e2f[0:65, :], zz[0:65, sc:sc + 256],
                      start=True, stop=True),
                      waits=[(sa, mk["rc"][k]), (sa, mk["bge"][k - 2])], inc=sp)
                  mk["bg"][k] = sp.n

              def emit_bge(k):
                  bc = (k % 2) * 256
                  bp = (PC, PF)[k % 2]
                  P.op("scalar", lambda e, bc=bc, bp=bp: e.activation(
                      out=bgt[:, bc:bc + 256], in_=bp[:, 0:256], func=AF.Exp,
                      scale=-1.0),
                      waits=[(sp, mk["bg"][k]), (sv, mk["y"][k - 2])], inc=sa)
                  mk["bge"][k] = sa.n

              def emit_y(k):
                  w, gi = divmod(k, 8)
                  yc = (k % 2) * 256
                  yp = (PD, PE_)[k % 2]
                  P.op("vector", lambda e, yc=yc, gi=gi, w=w, yp=yp: e.tensor_tensor(
                      out=y_sb[:, gi * TT + w * 256: gi * TT + (w + 1) * 256],
                      in0=yp[:, 0:256], in1=bgt[:, yc:yc + 256], op=ALU.mult),
                      waits=[(sp, mk["av"][k]), (sa, mk["bge"][k])], inc=sv)
                  mk["y"][k] = sv.n

              for k in range(NIT + 2):
                  if k < NIT:
                      emit_st(k)
                  if 1 <= k <= NIT:
                      emit_z(k - 1)
                  if k >= 2:
                      emit_av(k - 2)
                      emit_bg(k - 2)
                  if k < NIT:
                      emit_exp(k)
                  if 1 <= k <= NIT:
                      emit_recip(k - 1)
                  if k >= 2:
                      emit_bge(k - 2)
                      emit_y(k - 2)
              SV_Y = sv.n


              if attn_stage < 6:
                prev = dict(prev, sv_mid=sv.n, sa_c2=sa.n, sp_st2=sp.n,
                            sv_v2=sv.n, sp_vend=SP_VEND)
                continue
              for ot in range(8):
                  pp = (PE_, PF)[ot % 2]
                  for c in range(8):
                      w_ = []
                      if ot == 0 and c == 0:
                          w_ = [(sv, SV_Y)]
                      elif c == 0 and ot >= 2:
                          w_ = [(sv, sv.n)]
                      P.op("tensor", lambda e, ot=ot, c=c, pp=pp: e.matmul(
                          pp[:, :],
                          wpT[:, c * 1024 + ot * 128: c * 1024 + (ot + 1) * 128],
                          y_sb[:, c * TT:(c + 1) * TT],
                          start=(c == 0), stop=(c == 7)),
                          waits=w_, inc=(sp if c == 7 else None))
                  P.op("vector", lambda e, ot=ot, pp=pp, xs=xs: e.tensor_tensor(
                      out=midt[:, ot * TT:(ot + 1) * TT], in0=pp[:, :],
                      in1=xs[:, ot * TT:(ot + 1) * TT], op=ALU.add),
                      waits=[(sp, sp.n)] + ([(gmid, prev["gs_mid"]),
                                             (sa, prev["sa_c2"])] if ot == 0 else []),
                      inc=sv)
              SV_MID = sv.n

              for c in range(8):
                  P.op("scalar", lambda e, c=c: e.copy(
                      out=xbt[:, c * TT:(c + 1) * TT], in_=midt[:, c * TT:(c + 1) * TT]),
                      waits=([(sv, SV_MID)] if c == 0 else []), inc=sa)
              for c in range(8):
                  P.op("scalar", lambda e, c=c: e.activation(
                      out=x2t[:, c * TT:(c + 1) * TT], in_=midt[:, c * TT:(c + 1) * TT],
                      func=AF.Square), inc=sa)
              SA_C2 = sa.n
              for c in range(8):
                  P.op("tensor", lambda e, c=c: e.matmul(
                      PC[0:1, 0:TT], ONEB, xbt[:, c * TT:(c + 1) * TT],
                      start=(c == 0), stop=(c == 7)),
                      waits=([(sa, SA_C2), (sv, sv.n)] if c == 0 else []),
                      inc=(sp if c == 7 else None))
              for c in range(8):
                  P.op("tensor", lambda e, c=c: e.matmul(
                      PD[0:1, 0:TT], ONEB, x2t[:, c * TT:(c + 1) * TT],
                      start=(c == 0), stop=(c == 7)), inc=(sp if c == 7 else None))
              SP_ST2 = sp.n
              P.op("vector", lambda e: e.tensor_scalar(
                  out=mrow[:, 0:TT], in0=PC[0:1, 0:TT], scalar1=1.0 / D,
                  scalar2=None, op0=ALU.mult), waits=[(sp, SP_ST2)], inc=sv)
              P.op("vector", lambda e: e.tensor_scalar(
                  out=mrow[:, 2 * TT:3 * TT], in0=PD[0:1, 0:TT], scalar1=1.0 / D,
                  scalar2=None, op0=ALU.mult), inc=sv)
              P.op("vector", lambda e: e.tensor_tensor(
                  out=mrow[:, 3 * TT:4 * TT], in0=mrow[:, 0:TT],
                  in1=mrow[:, 0:TT], op=ALU.mult), waits=[(sv, sv.n)], inc=sv)
              P.op("vector", lambda e: e.tensor_tensor(
                  out=mrow[:, 2 * TT:3 * TT], in0=mrow[:, 2 * TT:3 * TT],
                  in1=mrow[:, 3 * TT:4 * TT], op=ALU.subtract),
                  waits=[(sv, sv.n)], inc=sv)
              SV_V2 = sv.n
              P.op("scalar", lambda e: e.activation(
                  out=mrow[:, 3 * TT:4 * TT], in_=mrow[:, 2 * TT:3 * TT],
                  func=AF.Ln, bias=EPSC[0:1, :]), waits=[(sv, SV_V2)], inc=sa)
              P.op("scalar", lambda e: e.activation(
                  out=mrow[:, TT:2 * TT], in_=mrow[:, 3 * TT:4 * TT],
                  func=AF.Exp, scale=-0.5), waits=[(sa, sa.n)], inc=sa)
              SA_LN2 = sa.n

              tok0 = t * TT
              P.op("scalar", lambda e, tok0=tok0: e.dma_start(
                  out=stats_scr[0, tok0:tok0 + TT].unsqueeze(0), in_=mrow[:, 0:TT]),
                  waits=[(sa, SA_LN2)], inc=gstat)
              P.op("scalar", lambda e, tok0=tok0: e.dma_start(
                  out=stats_scr[1, tok0:tok0 + TT].unsqueeze(0), in_=mrow[:, TT:2 * TT]),
                  inc=gstat)
              GS_STAT = gstat.n
              for c in range(8):
                  P.op("scalar", lambda e, c=c, seq=seq, t0=t0: e.dma_start(
                      out=mid_scr[seq, c * 128:(c + 1) * 128, t0:t0 + TT],
                      in_=midt[:, c * TT:(c + 1) * TT]),
                      waits=[(sv, SV_MID)] if c == 0 else [], inc=gmid)
              GS_MID = gmid.n

              prev = dict(sp_st2=SP_ST2, sa_c2=SA_C2, gs_mid=GS_MID,
                          gs_stat=GS_STAT, sv_v2=SV_V2, sp_vend=SP_VEND,
                          sv_mid=SV_MID, sv_mid2=prev["sv_mid"],
                          sp_bc_done=SP_BC)

    SP_ATTN, SV_ATTN, SA_ATTN = sp.n, sv.n, sa.n
    GMID_ATTN, GSTAT_ATTN, GS_ATTN = gmid.n, gstat.n, gs.n


    if phases == "pa":
        with contextlib.ExitStack() as des:
            df = des.enter_context(nc.sbuf_tensor("dfa", [128, 2048], F32))
            qd = SemCtr(des.enter_context(nc.semaphore("qda")), 16)
            for s_ in range(n_seq):
                for c in range(8):
                    P.op("sync", lambda e, s_=s_, c=c: e.dma_start(
                        out=df[:], in_=mid_scr[s_, c * 128:(c + 1) * 128, :]),
                        waits=[(gmid, gmid.n), (gout, gout.n)], inc=qd)
                    P.op("scalar", lambda e, s_=s_, c=c: e.dma_start(
                        out=out_fm[s_, c * 128:(c + 1) * 128, :], in_=df[:]),
                        waits=[(qd, qd.n)], inc=gout)
                    P.wait("sync", (gout, gout.n))
            P.wait("scalar", (gout, gout.n))

    with contextlib.ExitStack() as fes:
      if phases == "paf":
          fsb = lambda n, s, d: fes.enter_context(nc.sbuf_tensor(n, s, d))
          fps = lambda n, s, d: fes.enter_context(nc.psum_tensor(n, s, d))
          wex = fsb("wex", [128, 8 * HID], BF16)
          wcn = fsb("wcn", [128, 32 * 1024], BF16)
          mid2_ = [fsb(f"mid2{i}", [128, 8 * TT], F32) for i in range(2)]
          xn2 = fsb("xn2", [128, 8 * TT], BF16)
          g_sb = fsb("g_sb", [128, 32 * TT], BF16)
          o_sb = fsb("o_sb", [128, 2 * TT], F32)
          mb2 = fsb("mb2", [128, TT], BF16)
          rb2 = fsb("rb2", [128, TT], BF16)
          HP = [fps(f"HP{i}", [128, 512], F32) for i in range(2)]
          PO = [fps(f"PO{i}", [128, 512], F32) for i in range(4)]
          PBC = [fps(f"PBC{i}", [128, 512], F32) for i in range(2)]

          P.wait("sync", (gs, GS_ATTN), (gmid, GMID_ATTN), (gstat, GSTAT_ATTN))
          for c in range(8):
              P.op("sync", lambda e, c=c: e.dma_start(
                  out=wex[:, c * HID:(c + 1) * HID],
                  in_=wexp_scr[c * 128:(c + 1) * 128, :]), inc=qf)
          wcv = wcon_scr.rearrange("(B jj s) c -> (B jj) s c", B=32, jj=4, s=32)
          for g in range(32):
              P.op("sync", lambda e, g=g: e.dma_start(
                  out=wcn[:, g * 1024:(g + 1) * 1024], in_=wcv[:, g, :]), inc=qf)
          QF_W = qf.n

          S = {"sv_evac": SV_ATTN}
          qf_t = {}
          sp_bc2 = {-1: SP_ATTN}
          sv_xn2 = {}
          gs_out_prev = 0

          def emit_loads(t, gs_out):
              seq = t // TPS
              t0 = (t % TPS) * TT
              tok0 = t * TT
              qft = (qf, qf2)[t % 2]
              mid2 = mid2_[t % 2]
              for c in range(8):
                  P.op("sync", lambda e, c=c, seq=seq, t0=t0, mid2=mid2: e.dma_start(
                      out=mid2[:, c * TT:(c + 1) * TT],
                      in_=mid_scr[seq, c * 128:(c + 1) * 128, t0:t0 + TT]),
                      waits=[(sv, S["sv_evac"])] if c == 0 else [], inc=qft)
              P.op("sync", lambda e, tok0=tok0: e.dma_start(
                  out=o_sb[0:1, TT:2 * TT],
                  in_=stats_scr[0, tok0:tok0 + TT].unsqueeze(0)),
                  waits=[(sp, sp_bc2[t - 1]), (gout, gs_out)], inc=qft)
              P.op("sync", lambda e, tok0=tok0: e.dma_start(
                  out=o_sb[32:33, TT:2 * TT],
                  in_=stats_scr[1, tok0:tok0 + TT].unsqueeze(0)),
                  inc=qft)
              qf_t[t] = qft.n

          def emit_head(t, sp_h):
              qft = (qf, qf2)[t % 2]
              mid2 = mid2_[t % 2]
              w_ = [(qft, qf_t[t])]
              if t == 0:
                  w_ += [(sa, SA_ATTN), (sv, SV_ATTN)]
              P.op("tensor", lambda e: e.matmul(
                  PBC[0][:, :], orow[0:1, :], o_sb[0:1, TT:2 * TT],
                  start=True, stop=True), waits=w_, inc=sp)
              P.op("tensor", lambda e: e.matmul(
                  PBC[1][:, :], orow[32:33, :], o_sb[32:33, TT:2 * TT],
                  start=True, stop=True), inc=sp)
              sp_bc2[t] = sp.n
              P.op("vector", lambda e: e.tensor_copy(mb2[:], PBC[0][:, :]),
                   waits=[(sp, sp_bc2[t])], inc=sv)
              P.op("vector", lambda e: e.tensor_copy(rb2[:], PBC[1][:, :]), inc=sv)
              SV_BC2 = sv.n
              for c in range(8):
                  if c == 0:
                      w_ = [(sv, SV_BC2)] + ([(sp, sp_h)] if sp_h else [])
                  else:
                      w_ = [(sv, sv.n)]
                  P.op("vector", lambda e, c=c, mid2=mid2: e.tensor_tensor(
                      out=PBC[1][:, :], in0=mid2[:, c * TT:(c + 1) * TT],
                      in1=mb2[:], op=ALU.subtract),
                      waits=w_, inc=sv)
                  P.op("vector", lambda e, c=c: e.tensor_tensor(
                      out=xn2[:, c * TT:(c + 1) * TT], in0=PBC[1][:, :],
                      in1=rb2[:], op=ALU.mult), waits=[(sv, sv.n)], inc=sv)
              sv_xn2[t] = sv.n

          emit_loads(0, 0)
          emit_head(0, None)
          for t in range(NTIL):
              seq = t // TPS
              t0 = (t % TPS) * TT
              qft = (qf, qf2)[t % 2]
              mid2 = mid2_[t % 2]

              for fc in range(32):
                  pp = HP[fc % 2]
                  for c in range(8):
                      w_ = []
                      if fc == 0 and c == 0:
                          w_ = [(qft, qf_t[t]), (sv, sv_xn2[t])]
                      elif c == 0 and fc >= 2:
                          w_ = [(sa, sa.n)]
                      P.op("tensor", lambda e, fc=fc, c=c, pp=pp: e.matmul(
                          pp[:, :],
                          wex[:, c * HID + fc * 128: c * HID + (fc + 1) * 128],
                          xn2[:, c * TT:(c + 1) * TT],
                          start=(c == 0), stop=(c == 7)),
                          waits=w_, inc=(sp if c == 7 else None))
                  P.op("scalar", lambda e, fc=fc, pp=pp: e.activation(
                      out=g_sb[:, fc * TT:(fc + 1) * TT], in_=pp[:, :], func=AF.Gelu,
                      bias=biasg[:, fc:fc + 1]), waits=[(sp, sp.n)], inc=sa)
              SA_G = sa.n
              SP_H = sp.n

              if t + 1 < NTIL:
                  emit_loads(t + 1, gs_out_prev)
                  emit_head(t + 1, SP_H)

              for oh in range(2):
                  for fc in range(32):
                      for osub in range(4):
                          ot = oh * 4 + osub
                          w_ = []
                          if fc == 0 and osub == 0:
                              w_ = [(sa, SA_G)] if oh == 0 else [(sv, sv.n)]
                          P.op("tensor", lambda e, fc=fc, ot=ot, osub=osub: e.matmul(
                              PO[osub][:, :],
                              wcn[:, fc * 1024 + ot * 128: fc * 1024 + (ot + 1) * 128],
                              g_sb[:, fc * TT:(fc + 1) * TT],
                              start=(fc == 0), stop=(fc == 31)),
                              waits=w_, inc=(sp if (fc == 31 and osub == 3) else None))
                  SP_O = sp.n
                  for osub in range(4):
                      ot = oh * 4 + osub
                      w_ = [(gout, gs_out_prev)] if osub % 2 == 0 else [(sv, sv.n)]
                      if osub == 0:
                          w_.append((sp, SP_O))
                      P.op("vector", lambda e, osub=osub, ot=ot, mid2=mid2: e.scalar_tensor_tensor(
                          out=o_sb[:, (osub % 2) * TT:((osub % 2) + 1) * TT],
                          in0=PO[osub][:, :],
                          scalar=bco[:, ot:ot + 1], in1=mid2[:, ot * TT:(ot + 1) * TT],
                          op0=ALU.add, op1=ALU.add),
                          waits=w_, inc=sv)
                      if osub % 2 == 1:
                          SV_O = sv.n
                          for oo in (osub - 1, osub):
                              ot2 = oh * 4 + oo
                              P.op("scalar", lambda e, oo=oo, ot2=ot2, seq=seq, t0=t0: e.dma_start(
                                  out=out_fm[seq, ot2 * 128:(ot2 + 1) * 128, t0:t0 + TT],
                                  in_=o_sb[:, (oo % 2) * TT:((oo % 2) + 1) * TT]),
                                  waits=[(sv, SV_O)] if oo == osub - 1 else [], inc=gout)
                          gs_out_prev = gout.n
              S["sv_evac"] = sv.n

          P.wait("scalar", (gout, gout.n))

    P.wait("sync", (qc, qc.n), (qx0, qx0.n), (qx1, qx1.n), (qp, qp.n),
           (qp2, qp2.n), (qf, qf.n), (qf2, qf2.n))
    P.wait("scalar", (gs, gs.n), (gso, gso.n), (gstat, gstat.n), (gmid, gmid.n),
           (gout, gout.n))

    with nc.allow_low_precision(
            reason="intentional bf16 operand rounding (matmul inputs / 1/Z "
                   "scale); all accumulation stays fp32 in PSUM"), \
         nc.Block() as block:
        @block.sync
        def _(e):
            for fn in P.q["sync"]:
                fn(e)

        @block.tensor
        def _(e):
            for fn in P.q["tensor"]:
                fn(e)

        @block.vector
        def _(e):
            for fn in P.q["vector"]:
                fn(e)

        @block.scalar
        def _(e):
            for fn in P.q["scalar"]:
                fn(e)

        @block.gpsimd
        def _(e):
            for fn in P.q["gpsimd"]:
                fn(e)

    es.close()
    return nc



def make_consts():
    cf = np.zeros((128, 132), np.float32)
    cf[:, 0:128] = np.eye(128, dtype=np.float32)
    cf[:, 128] = EPS
    cbm = np.zeros((128, 132), np.float32)
    cbm[:, 0:128] = np.eye(128)
    cbm[:, 128] = 1.0
    rf = np.zeros((4, 128), np.float32)
    rf[0, :] = 1.0
    rf[1, 0:64] = 1.0
    rf[2, 64:128] = 1.0
    rb2 = np.zeros((2, 128), np.float32)
    rb2[0, 0:64] = 1.0
    rb2[1, 64:128] = 1.0
    ef = np.zeros((128, 128), np.float32)
    ef[0, 0:64] = 1.0
    ef[64, 64:128] = 1.0
    cb2 = np.zeros((128, 4), np.float32)
    cb2[:, 0] = 1.0
    return (cf, cbm.astype(ml_dtypes.bfloat16), rf,
            rb2.astype(ml_dtypes.bfloat16), ef, cb2.astype(ml_dtypes.bfloat16))


def make_inputs(L0_attn_w_attn, L0_ffn_W_exp, L0_ffn_b_exp, L0_ffn_up_weights,
                L0_ffn_up_bias, L0_ffn_down_weights, L0_ffn_W_con, L0_ffn_b_con,
                L0_ffn_down_bias, L0_attn_w_proj):
    SA_u, SQ_u = build_stage_mats(np.asarray(L0_ffn_up_weights, np.float32))
    SA_d, SQ_d = build_stage_mats(np.asarray(L0_ffn_down_weights, np.float32))
    cfc, cbb, rf, rb2, ef, cb2 = make_consts()
    fmap = f_of_hprime()
    return {
        "w_attnT": np.ascontiguousarray(np.asarray(L0_attn_w_attn, np.float32).T),
        "w_projT": np.ascontiguousarray(np.asarray(L0_attn_w_proj, np.float32).T),
        "w_expT": np.ascontiguousarray(np.asarray(L0_ffn_W_exp, np.float32).T),
        "w_con": np.ascontiguousarray(np.asarray(L0_ffn_W_con, np.float32)),
        "b_exp": np.asarray(L0_ffn_b_exp, np.float32),
        "ub_perm": np.ascontiguousarray(np.asarray(L0_ffn_up_bias, np.float32)[fmap]),
        "b_con": np.asarray(L0_ffn_b_con, np.float32),
        "d_bias": np.asarray(L0_ffn_down_bias, np.float32),
        "SAu": np.ascontiguousarray(SA_u), "SQu": np.ascontiguousarray(SQ_u),
        "SAdT": np.ascontiguousarray(SA_d.transpose(0, 1, 3, 2)),
        "SQdT": np.ascontiguousarray(SQ_d.transpose(0, 1, 3, 2)),
        "cf_in": cfc, "cb_in": cbb, "rf_in": rf, "rb_in": rb2, "ef_in": ef,
        "cb2_in": cb2,
    }


def kernel(L0_attn_x, L0_attn_w_attn, L0_attn_w_proj, L0_ffn_W_exp, L0_ffn_b_exp,
           L0_ffn_up_weights, L0_ffn_up_bias, L0_ffn_down_weights, L0_ffn_W_con,
           L0_ffn_b_con, L0_ffn_down_bias):
    x = np.asarray(L0_attn_x, dtype=np.float32)
    B = x.shape[0]
    n_seq = B // N_CORES
    shared = make_inputs(L0_attn_w_attn, L0_ffn_W_exp, L0_ffn_b_exp,
                         L0_ffn_up_weights, L0_ffn_up_bias, L0_ffn_down_weights,
                         L0_ffn_W_con, L0_ffn_b_con, L0_ffn_down_bias,
                         L0_attn_w_proj)
    nc = build_program(n_seq=n_seq)
    in_maps = []
    for core in range(N_CORES):
        m = dict(shared)
        m["x_fm"] = np.ascontiguousarray(
            x[core * n_seq:(core + 1) * n_seq].transpose(0, 2, 1))
        in_maps.append(m)
    res = run_bass_kernel_spmd(nc, in_maps, core_ids=list(range(N_CORES)))
    outs = [np.ascontiguousarray(res.results[c]["out_fm"].transpose(0, 2, 1))
            for c in range(N_CORES)]
    return np.concatenate(outs, axis=0)


# version 16
# speedup vs baseline: 1.2541x; 1.1716x over previous
import contextlib
import numpy as np
import ml_dtypes
import concourse.bass as bass
import concourse.mybir as mybir
from concourse.bass_utils import run_bass_kernel_spmd

F32 = mybir.dt.float32
BF16 = mybir.dt.bfloat16
AF = mybir.ActivationFunctionType
ALU = mybir.AluOpType
AXC = mybir.AxisListType.C

N_CORES = 8
D = 1024
HID = 4096
NH = 16
WIN = 256
TSEQ = 2048
TT = 512
EPS = 1e-5



def build_stage_mats(W):
    SA = np.zeros((7, 32, 128, 128), np.float32)
    l = np.arange(128)
    for st in range(7):
        s = 1 << st
        q = l >> (st + 1)
        half = (l >> st) & 1
        r = l & (s - 1)
        kloc = q * s + r
        part = l + np.where(half == 0, s, -s)
        for B in range(32):
            k = B * 64 + kloc
            SA[st, B, l, l] = W[st, k, half, half]
            SA[st, B, l, part] = W[st, k, half, 1 - half]
    SQ = np.zeros((5, 32, 128, 128), np.float32)
    p = np.arange(128)
    Bp = p >> 2
    jj = p & 3
    for st in range(7, 12):
        s = 1 << st
        sb = 1 << (st - 7)
        for g in range(32):
            f = Bp * 128 + jj * 32 + g
            half = (f >> st) & 1
            k = (f >> (st + 1)) * s + (f & (s - 1))
            partner = p + np.where(half == 0, sb * 4, -sb * 4)
            SQ[st - 7, g, p, p] = W[st, k, half, half]
            SQ[st - 7, g, p, partner] = W[st, k, half, 1 - half]
    return SA, SQ


def f_of_hprime():
    hp = np.arange(HID)
    g = hp >> 7
    p = hp & 127
    return (p >> 2) * 128 + (p & 3) * 32 + g



ENGS = ("sync", "tensor", "vector", "scalar", "gpsimd")


class SemCtr:
    __slots__ = ("sem", "step", "n")

    def __init__(self, sem, step):
        self.sem = sem
        self.step = step
        self.n = 0


class Plan:
    def __init__(self):
        self.q = {e: [] for e in ENGS}

    def op(self, eng, build, waits=(), inc=None, inck=1):
        wl = tuple((sc.sem, v) for sc, v in waits if v > 0)
        amt = None
        sem = None
        if inc is not None:
            amt = inck * inc.step
            inc.n += amt
            sem = inc.sem

        def fn(e, wl=wl, build=build, amt=amt, sem=sem):
            for s, v in wl:
                e.wait_ge(s, v)
            inst = build(e)
            if amt is not None:
                inst.then_inc(sem, amt)
        self.q[eng].append(fn)

    def wait(self, eng, *waits):
        wl = tuple((sc.sem, v) for sc, v in waits if v > 0)

        def fn(e, wl=wl):
            for s, v in wl:
                e.wait_ge(s, v)
        self.q[eng].append(fn)



def build_program(n_seq=4, phases="paf", attn_tiles=None, attn_stage=9, win_stage=9, skip_prep=False, win_heads=4, win_wins=None, prep_stage=9):
    nc = bass.Bass("TRN2", target_bir_lowering=False)
    NTOK = n_seq * TSEQ
    NTIL = NTOK // TT
    TPS = TSEQ // TT
    WPT = TT // WIN

    dt = nc.dram_tensor
    x_fm = dt("x_fm", [n_seq, D, TSEQ], F32, kind="ExternalInput").ap()
    w_attnT = dt("w_attnT", [D, 3 * D], F32, kind="ExternalInput").ap()
    w_projT = dt("w_projT", [D, D], F32, kind="ExternalInput").ap()
    w_expT = dt("w_expT", [HID, D], F32, kind="ExternalInput").ap()
    w_con = dt("w_con", [HID, D], F32, kind="ExternalInput").ap()
    b_exp = dt("b_exp", [HID], F32, kind="ExternalInput").ap()
    ub_perm = dt("ub_perm", [HID], F32, kind="ExternalInput").ap()
    b_con = dt("b_con", [D], F32, kind="ExternalInput").ap()
    d_bias = dt("d_bias", [D], F32, kind="ExternalInput").ap()
    SAu = dt("SAu", [7, 32, 128, 128], F32, kind="ExternalInput").ap()
    SQu = dt("SQu", [5, 32, 128, 128], F32, kind="ExternalInput").ap()
    SAdT = dt("SAdT", [7, 32, 128, 128], F32, kind="ExternalInput").ap()
    SQdT = dt("SQdT", [5, 32, 128, 128], F32, kind="ExternalInput").ap()
    cf_in = dt("cf_in", [128, 132], F32, kind="ExternalInput").ap()
    cb_in = dt("cb_in", [128, 132], BF16, kind="ExternalInput").ap()
    cb2_in = dt("cb2_in", [128, 4], BF16, kind="ExternalInput").ap()
    rf_in = dt("rf_in", [4, 128], F32, kind="ExternalInput").ap()
    ef_in = dt("ef_in", [128, 128], F32, kind="ExternalInput").ap()
    rb_in = dt("rb_in", [2, 128], BF16, kind="ExternalInput").ap()

    out_fm = dt("out_fm", [n_seq, D, TSEQ], F32, kind="ExternalOutput").ap()

    mid_scr = dt("mid_scr", [n_seq, D, TSEQ], F32, kind="Internal").ap()
    p_scr = dt("p_scr", [HID, 1028], F32, kind="Internal").ap()
    wexp_scr = dt("wexp_scr", [D, HID], BF16, kind="Internal").ap()
    wcon_scr = dt("wcon_scr", [HID, D], BF16, kind="Internal").ap()
    stats_scr = dt("stats_scr", [2, NTOK], F32, kind="Internal").ap()

    es = contextlib.ExitStack()
    sb = lambda name, shape, dtp: es.enter_context(nc.sbuf_tensor(name, shape, dtp))
    sem = lambda name, step: SemCtr(es.enter_context(nc.semaphore(name)), step)

    cf = sb("cf", [128, 132], F32)
    cbc = sb("cbc", [128, 4], BF16)
    orow = sb("orow", [33, 128], F32)
    e2f = sb("e2f", [128, 128], F32)
    biasg = sb("biasg", [128, 32], F32)
    bex = sb("bex", [128, 32], F32)
    bco = sb("bco", [128, 8], F32)

    qc = sem("qc", 16)
    qx0 = sem("qx0", 16)
    qx1 = sem("qx1", 16)
    qp = sem("qp", 16)
    qp2 = sem("qp2", 16)
    gso = sem("gso", 16)
    qf = sem("qf", 16)
    qf2 = sem("qf2", 16)
    gs = sem("gs", 16)
    gstat = sem("gstat", 16)
    gmid = sem("gmid", 16)
    gout = sem("gout", 16)
    sp = sem("sp", 1)
    sv = sem("sv", 1)
    sa = sem("sa", 1)
    sg = sem("sg", 1)

    P = Plan()

    P.op("sync", lambda e: e.dma_start(out=cf[:], in_=cf_in[:, :]), inc=qc)
    P.op("sync", lambda e: e.dma_start(out=cbc[:], in_=cb2_in[:, :]), inc=qc)
    P.op("sync", lambda e: e.dma_start(out=orow[0:1, :], in_=rf_in[0:1, :]), inc=qc)
    P.op("sync", lambda e: e.dma_start(out=orow[32:33, :], in_=rf_in[0:1, :]), inc=qc)
    P.op("sync", lambda e: e.dma_start(out=e2f[:], in_=ef_in[:, :]), inc=qc)
    QC0 = qc.n
    ID = cf[:, 0:128]
    EPSC = cf[:, 128:129]
    ONEB = cbc[:, 0:1]

    wes = contextlib.ExitStack()
    waT = wes.enter_context(nc.sbuf_tensor("waT", [128, 8 * 3072], BF16))
    wpT = wes.enter_context(nc.sbuf_tensor("wpT", [128, 8 * 1024], BF16))
    x_sb = [wes.enter_context(nc.sbuf_tensor(f"x{i}", [128, 8 * TT], F32))
            for i in range(2)]
    qw = (qx0, qx1)
    cast_mark = [0, 0]
    for c in range(8):
        sl = c % 2
        P.op("sync", lambda e, c=c, sl=sl: e.dma_start(
            out=x_sb[sl][:, 0:3072], in_=w_attnT[c * 128:(c + 1) * 128, :]),
            waits=[(sv, cast_mark[sl])], inc=qw[sl])
        P.op("vector", lambda e, c=c, sl=sl: e.tensor_copy(
            waT[:, c * 3072:(c + 1) * 3072], x_sb[sl][:, 0:3072]),
            waits=[(qw[sl], qw[sl].n)], inc=sv)
        cast_mark[sl] = sv.n
    for c in range(8):
        sl = c % 2
        P.op("sync", lambda e, c=c, sl=sl: e.dma_start(
            out=x_sb[sl][:, 0:1024], in_=w_projT[c * 128:(c + 1) * 128, :]),
            waits=[(sv, cast_mark[sl])], inc=qw[sl])
        P.op("vector", lambda e, c=c, sl=sl: e.tensor_copy(
            wpT[:, c * 1024:(c + 1) * 1024], x_sb[sl][:, 0:1024]),
            waits=[(qw[sl], qw[sl].n)], inc=sv)
        cast_mark[sl] = sv.n
    SV_W = sv.n
    X_LD = [0, 0]
    for tpre in range(2):
        for c in range(8):
            P.op("sync", lambda e, tpre=tpre, c=c: e.dma_start(
                out=x_sb[tpre][:, c * TT:(c + 1) * TT],
                in_=x_fm[tpre // TPS, c * 128:(c + 1) * 128,
                         (tpre % TPS) * TT:(tpre % TPS) * TT + TT]),
                waits=[(sv, cast_mark[tpre])] if c == 0 else [], inc=qw[tpre])
        X_LD[tpre] = qw[tpre].n

    with contextlib.ExitStack() as pes:
      if not skip_prep:
        psb = lambda n, s, d: pes.enter_context(nc.sbuf_tensor(n, s, d))
        pps = lambda n, s, d: pes.enter_context(nc.psum_tensor(n, s, d))
        sall = [psb(f"sall{i}", [128, 32 * 128], F32) for i in range(2)]
        gall = [psb(f"gall{i}", [128, 32 * 128], F32) for i in range(2)]
        svin2 = [psb(f"svin{i}", [128, 1028], F32) for i in range(2)]
        svout2 = [psb(f"svout{i}", [128, 1028], F32) for i in range(2)]
        sv2b2 = [psb(f"sv2b{i}", [128, 1024], BF16) for i in range(2)]
        swx2 = [psb(f"swx{i}", [128, 1024], BF16) for i in range(2)]
        bx = psb("bx", [128, 96], F32)
        idbp = psb("idbp", [128, 128], BF16)
        P.op("sync", lambda e: e.dma_start(out=idbp[:], in_=cb_in[:, 0:128]), inc=qc)
        QC0b = qc.n
        QC0 = QC0b
        PP = [pps(f"PP{i}", [128, 512], F32) for i in range(3)]
        PQ2 = [pps(f"PQ2{i}", [128, 512], F32) for i in range(3)]
        PPT = [pps(f"PPT{i}", [128, 512], BF16) for i in range(2)]

        state = {"sall_mark": [0, 0], "gall_guard": 0, "ppmark": {},
                 "chain_end": 0}

        def batched_loop(stages_ap, n_st, reverse, emit_load, ncols, emit_out,
                         gslot_final_holder):
            order = list(range(n_st))
            if reverse:
                order = order[::-1]
            for i, j in enumerate(order):
                sslot = i % 2
                gsl = i % 2
                sv_ = stages_ap[j].rearrange("b p c -> p b c")
                qsl = (qp, qp2)[sslot]
                for part in range(8):
                    P.op("sync", lambda e, sv_=sv_, sslot=sslot, part=part: e.dma_start(
                        out=sall[sslot][:, :].rearrange("p (b c) -> p b c", b=32)[
                            :, part * 4:(part + 1) * 4, :],
                        in_=sv_[:, part * 4:(part + 1) * 4, :]),
                        waits=([(sp, state["sall_mark"][sslot]), (qc, QC0)]
                               if part == 0 else []), inc=qsl)
                ld = (qsl, qsl.n)
                for b in range(32):
                    tile = PP[b % 3][:, 0:128]
                    w_ = []
                    if b == 0:
                        w_ = [ld]
                        if i == 0:
                            w_.append((sv, sv.n))
                            w_.append((sa, sa.n))
                            w_.append((sp, state["gall_guard"]))
                            w_.append((qc, QC0))
                    pm = state["ppmark"].get(b % 3)
                    if pm is not None:
                        w_ = w_ + [pm]
                    if i == 0:
                        P.op("tensor", lambda e, b=b, tile=tile, sslot=sslot: e.matmul(
                            tile, sall[sslot][:, b * 128:(b + 1) * 128], ID,
                            start=True, stop=True), waits=w_, inc=sp)
                    else:
                        P.op("tensor", lambda e, b=b, tile=tile, sslot=sslot, gsl=gsl: e.matmul(
                            tile, sall[sslot][:, b * 128:(b + 1) * 128],
                            gall[1 - gsl][:, b * 128:(b + 1) * 128],
                            start=True, stop=True), waits=w_, inc=sp)
                    if b % 2 == 0:
                        P.op("vector", lambda e, b=b, tile=tile, gsl=gsl: e.tensor_copy(
                            gall[gsl][:, b * 128:(b + 1) * 128], tile),
                            waits=[(sp, sp.n)], inc=sv)
                        state["ppmark"][b % 3] = (sv, sv.n)
                    else:
                        P.op("scalar", lambda e, b=b, tile=tile, gsl=gsl: e.copy(
                            out=gall[gsl][:, b * 128:(b + 1) * 128], in_=tile),
                            waits=[(sp, sp.n)], inc=sa)
                        state["ppmark"][b % 3] = (sa, sa.n)
                state["sall_mark"][sslot] = sp.n
            gfin = (n_st - 1) % 2
            SV_CH, SA_CH = sv.n, sa.n
            state["chain_end"] = sp.n
            state.pop("apply_mm0", None)
            state.pop("apply_mm1", None)
            for b in range(32 if prep_stage >= 2 else 0):
                slot = b % 2
                qsl = (qp, qp2)[slot]
                emit_load(b, slot, qsl,
                          [(sp, state.get(f"apply_mm{slot}", state["chain_end"]))])
                ld = (qsl, qsl.n)
                splits = []
                c0 = 0
                for pi, pq in enumerate(PQ2):
                    if c0 >= ncols:
                        break
                    n = min(512, ncols - c0)
                    splits.append((pq, c0, n, slot))
                    c0 += n
                for k, (pq, c0, n, sl) in enumerate(splits):
                    w_ = []
                    if k == 0:
                        w_ = [ld, (sv, SV_CH), (sa, SA_CH)]
                        if state.get("pqmark"):
                            w_.append(state["pqmark"])
                    P.op("tensor", lambda e, pq=pq, c0=c0, n=n, b=b, sl=sl, gfin=gfin: e.matmul(
                        pq[:, (sl) * 0: n] if False else pq[:, 0:n],
                        gall[gfin][:, b * 128:(b + 1) * 128],
                        svin2[sl][:, c0:c0 + n], start=True, stop=True),
                        waits=w_, inc=(sp if k == len(splits) - 1 else None))
                state[f"apply_mm{slot}"] = sp.n
                for ki, (pq, c0, n, sl) in enumerate(splits):
                    w_ = [(sp, sp.n)]
                    if ki == 0 and state.get(f"outst{slot}"):
                        w_.append(state[f"outst{slot}"])
                    P.op("vector", lambda e, pq=pq, c0=c0, n=n, sl=sl: e.tensor_copy(
                        svout2[sl][:, c0:c0 + n], pq[:, 0:n]),
                        waits=w_, inc=sv)
                state["pqmark"] = (sv, sv.n)
                state[f"apply_ev{slot}"] = sv.n
                if prep_stage >= 3:
                    emit_out(b, slot)
            state["gall_guard"] = sp.n
            state["sall_mark"] = [sp.n, sp.n]

        def ld_upA(b, slot, qsl, w_):
            P.op("sync", lambda e, b=b, slot=slot: e.dma_start(
                out=svin2[slot][:, 0:1024], in_=w_expT[b * 128:(b + 1) * 128, :]),
                waits=w_, inc=qsl)
            P.op("sync", lambda e, b=b, slot=slot: e.dma_start(
                out=svin2[slot][:, 1024:1025],
                in_=b_exp[b * 128:(b + 1) * 128].unsqueeze(1)), inc=qsl)

        def out_upA(b, slot):
            gq = (gs, gso)[slot]
            P.op("scalar", lambda e, b=b, slot=slot: e.dma_start(
                out=p_scr[b * 128:(b + 1) * 128, 0:1025], in_=svout2[slot][:, 0:1025]),
                waits=[(sv, sv.n)], inc=gq)
            state[f"outst{slot}"] = (gq, gq.n)

        batched_loop(SAu, 7, True, ld_upA, 1025, out_upA, None)
        P.wait("sync", (gs, gs.n), (gso, gso.n))

        p_view = p_scr.rearrange("(B jj s) c -> (B jj) s c", B=32, jj=4, s=32)

        def ld_upQ(g, slot, qsl, w_):
            P.op("sync", lambda e, g=g, slot=slot: e.dma_start(
                out=svin2[slot][:, 0:1025], in_=p_view[:, g, 0:1025]),
                waits=w_, inc=qsl)

        def out_upQ(g, slot):
            P.op("vector", lambda e, g=g, slot=slot: e.tensor_copy(
                bex[:, g:g + 1], svout2[slot][:, 1024:1025]),
                waits=[(sv, sv.n)], inc=sv)
            P.op("vector", lambda e, slot=slot: e.tensor_copy(
                sv2b2[slot][:], svout2[slot][:, 0:1024]),
                waits=[(sv, sv.n), (sp, state.get(f"tp{slot}", 0))], inc=sv)
            SV_CAST = sv.n
            for ci in range(8):
                pt = PPT[ci % 2][:, 0:128]
                w_ = [(sv, SV_CAST)] if ci == 0 else []
                ptm = state.setdefault("ptmark", {}).get(ci % 2)
                if ptm is not None:
                    w_ = w_ + [ptm]
                else:
                    w_ = w_ + [(qc, QC0b)]
                P.op("tensor", lambda e, ci=ci, slot=slot, pt=pt: e.transpose(
                    pt, sv2b2[slot][:, ci * 128:(ci + 1) * 128], idbp[:]),
                    waits=w_, inc=sp)
                sw = state.get(f"swxst{slot}")
                P.op("scalar", lambda e, ci=ci, slot=slot, pt=pt: e.copy(
                    out=swx2[slot][:, ci * 128:(ci + 1) * 128], in_=pt),
                    waits=[(sp, sp.n)] + ([sw] if ci == 0 and sw else []), inc=sa)
                state["ptmark"][ci % 2] = (sa, sa.n)
            state[f"tp{slot}"] = sp.n
            SA_SWX = sa.n
            P.op("scalar", lambda e, g=g, slot=slot: e.dma_start(
                out=wexp_scr.rearrange("(cc p) f -> p (cc f)", p=128)[
                    :, 0:0] if False else wexp_scr[0:128, 0:128],
                in_=swx2[slot][:, 0:128]), waits=[], inc=gs) if False else None
            gq = (gs, gso)[slot]
            for ci in range(8):
                P.op("scalar", lambda e, ci=ci, g=g, slot=slot: e.dma_start(
                    out=wexp_scr[ci * 128:(ci + 1) * 128, g * 128:(g + 1) * 128],
                    in_=swx2[slot][:, ci * 128:(ci + 1) * 128]),
                    waits=[(sa, SA_SWX)] if ci == 0 else [], inc=gq)
            state[f"swxst{slot}"] = (gq, gq.n)
            state[f"outst{slot}"] = (sv, SV_CAST)

        batched_loop(SQu, 5, True, ld_upQ, 1025, out_upQ, None)
        P.wait("sync", (gs, gs.n), (gso, gso.n))

        w_view = w_con.rearrange("(B jj s) c -> (B jj) s c", B=32, jj=4, s=32)
        pv2 = p_scr.rearrange("(B jj s) c -> (B jj) s c", B=32, jj=4, s=32)

        def ld_dnQ(g, slot, qsl, w_):
            P.op("sync", lambda e, g=g, slot=slot: e.dma_start(
                out=svin2[slot][:, 0:1024], in_=w_view[:, g, :]),
                waits=w_, inc=qsl)

        def out_dnQ(g, slot):
            gq = (gs, gso)[slot]
            P.op("scalar", lambda e, g=g, slot=slot: e.dma_start(
                out=pv2[:, g, 0:1024], in_=svout2[slot][:, 0:1024]),
                waits=[(sv, sv.n)], inc=gq)
            state[f"outst{slot}"] = (gq, gq.n)

        batched_loop(SQdT, 5, False, ld_dnQ, 1024, out_dnQ, None)
        P.wait("sync", (gs, gs.n), (gso, gso.n))

        def ld_dnA(b, slot, qsl, w_):
            P.op("sync", lambda e, b=b, slot=slot: e.dma_start(
                out=svin2[slot][:, 0:1024], in_=p_scr[b * 128:(b + 1) * 128, 0:1024]),
                waits=w_, inc=qsl)

        def out_dnA(b, slot):
            wc = state.get(f"wcst{slot}")
            P.op("vector", lambda e, slot=slot: e.tensor_copy(
                sv2b2[slot][:], svout2[slot][:, 0:1024]),
                waits=[(sv, sv.n)] + ([wc] if wc else []), inc=sv)
            gq = (gs, gso)[slot]
            P.op("scalar", lambda e, b=b, slot=slot: e.dma_start(
                out=wcon_scr[b * 128:(b + 1) * 128, :], in_=sv2b2[slot][:]),
                waits=[(sv, sv.n)], inc=gq)
            state[f"wcst{slot}"] = (gq, gq.n)
            state[f"outst{slot}"] = (sv, sv.n)

        batched_loop(SAdT, 7, False, ld_dnA, 1024, out_dnA, None)
        P.wait("sync", (gs, gs.n), (gso, gso.n))

        uview = ub_perm.rearrange("(c p) -> c p", p=128)
        for c in range(32):
            P.op("sync", lambda e, c=c: e.dma_start(
                out=bx[:, 32 + c:33 + c], in_=uview[c, :].unsqueeze(1)), inc=qp)
        bcv = b_con.rearrange("(c p) -> c p", p=128)
        dbv = d_bias.rearrange("(c p) -> c p", p=128)
        for c in range(8):
            P.op("sync", lambda e, c=c: e.dma_start(
                out=bx[:, 64 + c:65 + c], in_=bcv[c, :].unsqueeze(1)), inc=qp)
            P.op("sync", lambda e, c=c: e.dma_start(
                out=bx[:, 72 + c:73 + c], in_=dbv[c, :].unsqueeze(1)), inc=qp)
        P.op("vector", lambda e: e.tensor_tensor(
            out=biasg[:], in0=bex[:], in1=bx[:, 32:64], op=ALU.add),
            waits=[(qp, qp.n), (sv, sv.n)], inc=sv)
        P.op("vector", lambda e: e.tensor_tensor(
            out=bco[:], in0=bx[:, 64:72], in1=bx[:, 72:80], op=ALU.add), inc=sv)

    SP_PREP, SV_PREP, SA_PREP, GS_PREP = sp.n, sv.n, sa.n, gs.n

    if "p" == phases:
        with contextlib.ExitStack() as des:
            db = des.enter_context(nc.sbuf_tensor("db", [128, 1024], BF16))
            df = des.enter_context(nc.sbuf_tensor("df", [128, 1024], F32))
            qd = SemCtr(des.enter_context(nc.semaphore("qd")), 16)
            for i in range(16):
                P.op("sync", lambda e, i=i: e.dma_start(
                    out=db[:], in_=wcon_scr[i * 128:(i + 1) * 128, :]),
                    waits=[(gs, gs.n), (sv, sv.n)], inc=qd)
                P.op("vector", lambda e, i=i: e.tensor_copy(df[:], db[:]),
                     waits=[(qd, qd.n), (gout, gout.n)], inc=sv)
                P.op("scalar", lambda e, i=i: e.dma_start(
                    out=out_fm[0, (i % 8) * 128:(i % 8 + 1) * 128,
                               (i // 8) * 1024:(i // 8 + 1) * 1024],
                    in_=df[:]), waits=[(sv, sv.n)], inc=gout)
                P.wait("sync", (gout, gout.n))
            P.wait("scalar", (gout, gout.n))
    attn_on = "a" in phases
    with contextlib.ExitStack() as aes:
      if attn_on:
          asb = lambda n, s, d: aes.enter_context(nc.sbuf_tensor(n, s, d))
          aps = lambda n, s, d: aes.enter_context(nc.psum_tensor(n, s, d))
          xbt = asb("xbt", [128, 8 * TT], BF16)
          x2t = asb("x2t", [128, 8 * TT], BF16)
          xnt = asb("xnt", [128, 8 * TT], BF16)
          qkt = asb("qkt", [128, 16 * TT], BF16)
          v_sb = asb("v_sb", [128, 8 * TT], BF16)
          y_sb = asb("y_sb", [128, 8 * TT], BF16)
          ett = asb("ett", [128, 2 * 1024], BF16)
          midt = asb("midt", [128, 8 * TT], F32)
          bgt = asb("bgt", [128, 512], F32)
          mrow = asb("mrow", [1, 4 * TT], F32)
          zz = asb("zz", [128, 1024], F32)
          P.op("vector", lambda e: e.memset(zz[:], 0.0),
               waits=[(sv, sv.n)], inc=sv)
          mbt = asb("mbt", [128, TT], BF16)
          rbt = asb("rbt", [128, TT], BF16)
          PA = aps("PA", [128, 1024], F32)
          PB = aps("PB", [128, 1024], F32)
          PC = aps("PC", [128, 512], F32)
          PD = aps("PD", [128, 512], F32)
          PE_ = aps("PE_", [128, 512], F32)
          PF = aps("PF", [128, 512], F32)

          prev = dict(sp_st2=SP_PREP, sa_c2=SA_PREP, gs_mid=0,
                      gs_stat=0, sv_v2=SV_PREP, sp_vend=SP_PREP,
                      sv_mid=SV_W, sv_mid2=SV_W, sp_bc_done=SP_PREP)

          for t in range(NTIL):
              seq = t // TPS
              t0 = (t % TPS) * TT
              qx = (qx0, qx1)[t % 2]
              xs = x_sb[t % 2]

              if t < 2:
                  LD_X = X_LD[t]
              else:
                  for c in range(8):
                      P.op("sync", lambda e, c=c, seq=seq, t0=t0, xs=xs: e.dma_start(
                          out=xs[:, c * TT:(c + 1) * TT],
                          in_=x_fm[seq, c * 128:(c + 1) * 128, t0:t0 + TT]),
                          waits=[(sv, prev["sv_mid2"])] if c == 0 else [], inc=qx)
                  LD_X = qx.n

              for c in range(8):
                  P.op("scalar", lambda e, c=c, xs=xs: e.copy(
                      out=xbt[:, c * TT:(c + 1) * TT], in_=xs[:, c * TT:(c + 1) * TT]),
                      waits=([(qx, LD_X), (sp, prev["sp_st2"])] if c == 0 else []),
                      inc=sa)
              for c in range(8):
                  P.op("scalar", lambda e, c=c, xs=xs: e.activation(
                      out=x2t[:, c * TT:(c + 1) * TT], in_=xs[:, c * TT:(c + 1) * TT],
                      func=AF.Square), inc=sa)
              SA_CAST = sa.n

              if attn_stage < 2:
                prev = dict(prev, sv_mid=sv.n, sa_c2=sa.n, sp_st2=sp.n,
                            sv_v2=sv.n, sp_vend=sp.n)
                continue
              for c in range(8):
                  P.op("tensor", lambda e, c=c: e.matmul(
                      PC[0:1, 0:TT], ONEB, xbt[:, c * TT:(c + 1) * TT],
                      start=(c == 0), stop=(c == 7)),
                      waits=([(sa, SA_CAST), (sv, prev["sv_v2"])] if c == 0 else []),
                      inc=(sp if c == 7 else None))
              for c in range(8):
                  P.op("tensor", lambda e, c=c: e.matmul(
                      PD[0:1, 0:TT], ONEB, x2t[:, c * TT:(c + 1) * TT],
                      start=(c == 0), stop=(c == 7)), inc=(sp if c == 7 else None))
              SP_STAT = sp.n

              P.op("vector", lambda e: e.tensor_scalar(
                  out=mrow[:, 0:TT], in0=PC[0:1, 0:TT], scalar1=1.0 / D,
                  scalar2=None, op0=ALU.mult),
                  waits=[(sp, SP_STAT), (gstat, prev["gs_stat"])], inc=sv)
              P.op("vector", lambda e: e.tensor_scalar(
                  out=mrow[:, 2 * TT:3 * TT], in0=PD[0:1, 0:TT], scalar1=1.0 / D,
                  scalar2=None, op0=ALU.mult), inc=sv)
              P.op("vector", lambda e: e.tensor_tensor(
                  out=mrow[:, 3 * TT:4 * TT], in0=mrow[:, 0:TT],
                  in1=mrow[:, 0:TT], op=ALU.mult), waits=[(sv, sv.n)], inc=sv)
              P.op("vector", lambda e: e.tensor_tensor(
                  out=mrow[:, 2 * TT:3 * TT], in0=mrow[:, 2 * TT:3 * TT],
                  in1=mrow[:, 3 * TT:4 * TT], op=ALU.subtract),
                  waits=[(sv, sv.n)], inc=sv)
              SV_VAR = sv.n
              P.op("scalar", lambda e: e.activation(
                  out=mrow[:, 3 * TT:4 * TT], in_=mrow[:, 2 * TT:3 * TT],
                  func=AF.Ln, bias=EPSC[0:1, :]), waits=[(sv, SV_VAR)], inc=sa)
              P.op("scalar", lambda e: e.activation(
                  out=mrow[:, TT:2 * TT], in_=mrow[:, 3 * TT:4 * TT],
                  func=AF.Exp, scale=-0.5), waits=[(sa, sa.n)], inc=sa)
              SA_LN = sa.n

              if attn_stage < 3:
                prev = dict(prev, sv_mid=sv.n, sa_c2=sa.n, sp_st2=sp.n,
                            sv_v2=sv.n, sp_vend=sp.n)
                continue
              P.op("tensor", lambda e: e.matmul(
                  PE_[:, 0:TT], orow[0:1, :], mrow[0:1, 0:TT], start=True, stop=True),
                  waits=[(sa, SA_LN), (sv, prev["sv_mid"])], inc=sp)
              P.op("tensor", lambda e: e.matmul(
                  PF[:, 0:TT], orow[0:1, :], mrow[0:1, TT:2 * TT], start=True, stop=True),
                  inc=sp)
              SP_BC = sp.n
              P.op("vector", lambda e: e.tensor_copy(mbt[:], PE_[:, 0:TT]),
                   waits=[(sp, SP_BC)], inc=sv)
              P.op("vector", lambda e: e.tensor_copy(rbt[:], PF[:, 0:TT]), inc=sv)
              SV_BC = sv.n

              for c in range(8):
                  P.op("vector", lambda e, c=c: e.tensor_tensor(
                      out=x2t[:, c * TT:(c + 1) * TT], in0=xbt[:, c * TT:(c + 1) * TT],
                      in1=mbt[:], op=ALU.subtract),
                      waits=([(sv, SV_BC)] if c == 0 else [(sv, sv.n)]), inc=sv)
                  P.op("vector", lambda e, c=c: e.tensor_tensor(
                      out=xnt[:, c * TT:(c + 1) * TT], in0=x2t[:, c * TT:(c + 1) * TT],
                      in1=rbt[:], op=ALU.mult),
                      waits=[(sv, sv.n)] + ([(sp, prev["sp_vend"]),
                                             (sp, prev["sp_st2"])] if c == 0 else []),
                      inc=sv)
              SV_XN = sv.n

              if attn_stage < 4:
                prev = dict(prev, sv_mid=sv.n, sa_c2=sa.n, sp_st2=sp.n,
                            sv_v2=sv.n, sp_vend=sp.n)
                continue
              pq = (PC, PD)
              pdm = []
              for ei in range(16):
                  pp = pq[ei % 2]
                  for c in range(8):
                      w_ = []
                      if ei == 0 and c == 0:
                          w_ = [(sv, SV_XN)]
                      elif c == 0 and ei >= 2:
                          w_ = [(sv, pdm[ei - 2])]
                      P.op("tensor", lambda e, ei=ei, c=c, pp=pp: e.matmul(
                          pp[:, 0:TT],
                          waT[:, c * 3072 + ei * 128: c * 3072 + (ei + 1) * 128],
                          xnt[:, c * TT:(c + 1) * TT],
                          start=(c == 0), stop=(c == 7)),
                          waits=w_, inc=(sp if c == 7 else None))
                  P.op("vector", lambda e, ei=ei, pp=pp: e.tensor_copy(
                      qkt[:, ei * TT:(ei + 1) * TT], pp[:, 0:TT]),
                      waits=[(sp, sp.n)], inc=sv)
                  pdm.append(sv.n)
              for ts4 in range(TT // 128):
                  for eh in range(2):
                      gi_ = 16 + ts4 * 2 + eh
                      pp = pq[(ts4 * 2 + eh) % 2]
                      for c in range(8):
                          P.op("tensor", lambda e, ts4=ts4, eh=eh, c=c, pp=pp: e.matmul(
                              pp[:, :],
                              xnt[:, c * TT + ts4 * 128: c * TT + (ts4 + 1) * 128],
                              waT[:, c * 3072 + 2048 + eh * 512: c * 3072 + 2048 + (eh + 1) * 512],
                              start=(c == 0), stop=(c == 7)),
                              waits=([(sv, pdm[gi_ - 2])] if c == 0 else []),
                              inc=(sp if c == 7 else None))
                      P.op("vector", lambda e, ts4=ts4, eh=eh, pp=pp: e.tensor_copy(
                          v_sb[:, ts4 * 1024 + eh * 512: ts4 * 1024 + (eh + 1) * 512],
                          pp[:, :]), waits=[(sp, sp.n)], inc=sv)
                      pdm.append(sv.n)
              SV_QKV = sv.n
              SP_VEND = sp.n

              if attn_stage < 5:
                  prev = dict(prev, sv_mid=sv.n, sa_c2=sa.n, sp_st2=sp.n,
                              sv_v2=sv.n, sp_vend=SP_VEND)
                  continue
              CS2 = {(0, 0): 0, (0, 1): 256, (1, 0): 512, (1, 1): 768}
              NIT = WPT * 8
              mk = {"st": {}, "et": {-2: prev["sa_c2"], -1: prev["sa_c2"]},
                    "z": {}, "rc": {-2: prev["sa_c2"], -1: prev["sa_c2"]},
                    "av": {}, "bg": {}, "bge": {-2: prev["sa_c2"], -1: prev["sa_c2"]},
                    "y": {-2: SV_BC, -1: SV_BC, -3: SV_BC, -4: SV_BC}}

              def emit_st(k):
                  w, gi = divmod(k, 8)
                  pp = (PA, PB)[k % 2]
                  first = (k == 0)
                  for idx, (hp, jh) in enumerate(((0, 0), (0, 1), (1, 0), (1, 1))):
                      h = 2 * gi + hp
                      qc_ = h // 2
                      qb = (h % 2) * 64
                      w_ = []
                      if idx == 0:
                          w_ = [(sa, mk["et"][k - 2]), (sa, mk["rc"][k - 2])]
                          if first:
                              w_.append((sv, SV_QKV))
                      P.op("tensor", lambda e, pp=pp, qc_=qc_, qb=qb, jh=jh, w=w, hp=hp: e.matmul(
                          pp[:, CS2[(hp, jh)]:CS2[(hp, jh)] + 256],
                          qkt[qb:qb + 64, (8 + qc_) * TT + w * 256 + jh * 128:
                              (8 + qc_) * TT + w * 256 + (jh + 1) * 128],
                          qkt[qb:qb + 64, qc_ * TT + w * 256: qc_ * TT + (w + 1) * 256],
                          start=True, stop=True),
                          waits=w_, inc=(sp if idx == 3 else None))
                  mk["st"][k] = sp.n

              def emit_exp(k):
                  pp = (PA, PB)[k % 2]
                  eb = (k % 2) * 1024
                  P.op("scalar", lambda e, pp=pp, eb=eb: e.activation(
                      out=ett[:, eb:eb + 1024], in_=pp[:, :], func=AF.Exp, scale=0.125),
                      waits=[(sp, mk["st"][k]), (sp, mk["av"].get(k - 2, 0))], inc=sa)
                  mk["et"][k] = sa.n

              def emit_z(k):
                  zp = (PA, PB)[k % 2]
                  eb = (k % 2) * 1024
                  for idx, (hp, col) in enumerate(((0, 0), (1, 256))):
                      for jh in range(2):
                          w_ = []
                          if idx == 0 and jh == 0:
                              w_ = [(sa, mk["et"][k]), (sa, mk["rc"][k - 2])]
                          P.op("tensor", lambda e, zp=zp, col=col, eb=eb, hp=hp, jh=jh: e.matmul(
                              zp[0:1, col:col + 256], ONEB,
                              ett[:, eb + CS2[(hp, jh)]: eb + CS2[(hp, jh)] + 256],
                              start=(jh == 0), stop=(jh == 1)),
                              waits=w_, inc=(sp if (idx == 1 and jh == 1) else None))
                  mk["z"][k] = sp.n

              def emit_recip(k):
                  zp = (PA, PB)[k % 2]
                  sc = (k % 2) * 512
                  P.op("scalar", lambda e, zp=zp, sc=sc: e.activation(
                      out=zz[0:1, sc:sc + 256], in_=zp[0:1, 0:256], func=AF.Ln),
                      waits=[(sp, mk["z"][k]), (sp, mk["bg"].get(k - 2, 0))], inc=sa)
                  P.op("scalar", lambda e, zp=zp, sc=sc: e.activation(
                      out=zz[64:65, sc:sc + 256], in_=zp[0:1, 256:512], func=AF.Ln),
                      inc=sa)
                  mk["rc"][k] = sa.n

              def emit_av(k):
                  w, gi = divmod(k, 8)
                  eb = (k % 2) * 1024
                  yp = (PD, PE_)[k % 2]
                  for idx, hp in enumerate((0, 1)):
                      h = 2 * gi + hp
                      ob = hp * 64
                      for jh in range(2):
                          vt = w * 2 + jh
                          w_ = []
                          if idx == 0 and jh == 0:
                              w_ = [(sa, mk["et"][k]), (sv, mk["y"][k - 2])]
                          P.op("tensor", lambda e, ob=ob, h=h, jh=jh, vt=vt, eb=eb, yp=yp, hp=hp: e.matmul(
                              yp[ob:ob + 64, 0:256],
                              v_sb[:, vt * 1024 + h * 64: vt * 1024 + (h + 1) * 64],
                              ett[:, eb + CS2[(hp, jh)]: eb + CS2[(hp, jh)] + 256],
                              start=(jh == 0), stop=(jh == 1)),
                              waits=w_, inc=(sp if (idx == 1 and jh == 1) else None))
                  mk["av"][k] = sp.n

              def emit_bg(k):
                  sc = (k % 2) * 512
                  bp = (PC, PF)[k % 2]
                  P.op("tensor", lambda e, sc=sc, bp=bp: e.matmul(
                      bp[:, 0:256], e2f[0:65, :], zz[0:65, sc:sc + 256],
                      start=True, stop=True),
                      waits=[(sa, mk["rc"][k]), (sa, mk["bge"][k - 2])], inc=sp)
                  mk["bg"][k] = sp.n

              def emit_bge(k):
                  bc = (k % 2) * 256
                  bp = (PC, PF)[k % 2]
                  P.op("scalar", lambda e, bc=bc, bp=bp: e.activation(
                      out=bgt[:, bc:bc + 256], in_=bp[:, 0:256], func=AF.Exp,
                      scale=-1.0),
                      waits=[(sp, mk["bg"][k]), (sv, mk["y"][k - 2])], inc=sa)
                  mk["bge"][k] = sa.n

              def emit_y(k):
                  w, gi = divmod(k, 8)
                  yc = (k % 2) * 256
                  yp = (PD, PE_)[k % 2]
                  P.op("vector", lambda e, yc=yc, gi=gi, w=w, yp=yp: e.tensor_tensor(
                      out=y_sb[:, gi * TT + w * 256: gi * TT + (w + 1) * 256],
                      in0=yp[:, 0:256], in1=bgt[:, yc:yc + 256], op=ALU.mult),
                      waits=[(sp, mk["av"][k]), (sa, mk["bge"][k])], inc=sv)
                  mk["y"][k] = sv.n

              for k in range(NIT + 2):
                  if k < NIT:
                      emit_st(k)
                  if 1 <= k <= NIT:
                      emit_z(k - 1)
                  if k >= 2:
                      emit_av(k - 2)
                      emit_bg(k - 2)
                  if k < NIT:
                      emit_exp(k)
                  if 1 <= k <= NIT:
                      emit_recip(k - 1)
                  if k >= 2:
                      emit_bge(k - 2)
                      emit_y(k - 2)
              SV_Y = sv.n


              if attn_stage < 6:
                prev = dict(prev, sv_mid=sv.n, sa_c2=sa.n, sp_st2=sp.n,
                            sv_v2=sv.n, sp_vend=SP_VEND)
                continue
              prm = []
              for ot in range(8):
                  pp = (PE_, PF)[ot % 2]
                  for c in range(8):
                      w_ = []
                      if ot == 0 and c == 0:
                          w_ = [(sv, SV_Y)]
                      elif c == 0 and ot >= 2:
                          w_ = [(sv, prm[ot - 2])]
                      P.op("tensor", lambda e, ot=ot, c=c, pp=pp: e.matmul(
                          pp[:, :],
                          wpT[:, c * 1024 + ot * 128: c * 1024 + (ot + 1) * 128],
                          y_sb[:, c * TT:(c + 1) * TT],
                          start=(c == 0), stop=(c == 7)),
                          waits=w_, inc=(sp if c == 7 else None))
                  P.op("vector", lambda e, ot=ot, pp=pp, xs=xs: e.tensor_tensor(
                      out=midt[:, ot * TT:(ot + 1) * TT], in0=pp[:, :],
                      in1=xs[:, ot * TT:(ot + 1) * TT], op=ALU.add),
                      waits=[(sp, sp.n)] + ([(gmid, prev["gs_mid"]),
                                             (sa, prev["sa_c2"])] if ot == 0 else []),
                      inc=sv)
                  prm.append(sv.n)
              SV_MID = sv.n

              for c in range(8):
                  P.op("scalar", lambda e, c=c: e.copy(
                      out=xbt[:, c * TT:(c + 1) * TT], in_=midt[:, c * TT:(c + 1) * TT]),
                      waits=([(sv, SV_MID)] if c == 0 else []), inc=sa)
              for c in range(8):
                  P.op("scalar", lambda e, c=c: e.activation(
                      out=x2t[:, c * TT:(c + 1) * TT], in_=midt[:, c * TT:(c + 1) * TT],
                      func=AF.Square), inc=sa)
              SA_C2 = sa.n
              for c in range(8):
                  P.op("tensor", lambda e, c=c: e.matmul(
                      PC[0:1, 0:TT], ONEB, xbt[:, c * TT:(c + 1) * TT],
                      start=(c == 0), stop=(c == 7)),
                      waits=([(sa, SA_C2), (sv, sv.n)] if c == 0 else []),
                      inc=(sp if c == 7 else None))
              for c in range(8):
                  P.op("tensor", lambda e, c=c: e.matmul(
                      PD[0:1, 0:TT], ONEB, x2t[:, c * TT:(c + 1) * TT],
                      start=(c == 0), stop=(c == 7)), inc=(sp if c == 7 else None))
              SP_ST2 = sp.n
              P.op("vector", lambda e: e.tensor_scalar(
                  out=mrow[:, 0:TT], in0=PC[0:1, 0:TT], scalar1=1.0 / D,
                  scalar2=None, op0=ALU.mult), waits=[(sp, SP_ST2)], inc=sv)
              P.op("vector", lambda e: e.tensor_scalar(
                  out=mrow[:, 2 * TT:3 * TT], in0=PD[0:1, 0:TT], scalar1=1.0 / D,
                  scalar2=None, op0=ALU.mult), inc=sv)
              P.op("vector", lambda e: e.tensor_tensor(
                  out=mrow[:, 3 * TT:4 * TT], in0=mrow[:, 0:TT],
                  in1=mrow[:, 0:TT], op=ALU.mult), waits=[(sv, sv.n)], inc=sv)
              P.op("vector", lambda e: e.tensor_tensor(
                  out=mrow[:, 2 * TT:3 * TT], in0=mrow[:, 2 * TT:3 * TT],
                  in1=mrow[:, 3 * TT:4 * TT], op=ALU.subtract),
                  waits=[(sv, sv.n)], inc=sv)
              SV_V2 = sv.n
              P.op("scalar", lambda e: e.activation(
                  out=mrow[:, 3 * TT:4 * TT], in_=mrow[:, 2 * TT:3 * TT],
                  func=AF.Ln, bias=EPSC[0:1, :]), waits=[(sv, SV_V2)], inc=sa)
              P.op("scalar", lambda e: e.activation(
                  out=mrow[:, TT:2 * TT], in_=mrow[:, 3 * TT:4 * TT],
                  func=AF.Exp, scale=-0.5), waits=[(sa, sa.n)], inc=sa)
              SA_LN2 = sa.n

              tok0 = t * TT
              P.op("scalar", lambda e, tok0=tok0: e.dma_start(
                  out=stats_scr[0, tok0:tok0 + TT].unsqueeze(0), in_=mrow[:, 0:TT]),
                  waits=[(sa, SA_LN2)], inc=gstat)
              P.op("scalar", lambda e, tok0=tok0: e.dma_start(
                  out=stats_scr[1, tok0:tok0 + TT].unsqueeze(0), in_=mrow[:, TT:2 * TT]),
                  inc=gstat)
              GS_STAT = gstat.n
              for c in range(8):
                  P.op("scalar", lambda e, c=c, seq=seq, t0=t0: e.dma_start(
                      out=mid_scr[seq, c * 128:(c + 1) * 128, t0:t0 + TT],
                      in_=midt[:, c * TT:(c + 1) * TT]),
                      waits=[(sv, SV_MID)] if c == 0 else [], inc=gmid)
              GS_MID = gmid.n

              prev = dict(sp_st2=SP_ST2, sa_c2=SA_C2, gs_mid=GS_MID,
                          gs_stat=GS_STAT, sv_v2=SV_V2, sp_vend=SP_VEND,
                          sv_mid=SV_MID, sv_mid2=prev["sv_mid"],
                          sp_bc_done=SP_BC)

    SP_ATTN, SV_ATTN, SA_ATTN = sp.n, sv.n, sa.n
    GMID_ATTN, GSTAT_ATTN, GS_ATTN = gmid.n, gstat.n, gs.n
    wes.close()


    if phases == "pa":
        with contextlib.ExitStack() as des:
            df = des.enter_context(nc.sbuf_tensor("dfa", [128, 2048], F32))
            qd = SemCtr(des.enter_context(nc.semaphore("qda")), 16)
            for s_ in range(n_seq):
                for c in range(8):
                    P.op("sync", lambda e, s_=s_, c=c: e.dma_start(
                        out=df[:], in_=mid_scr[s_, c * 128:(c + 1) * 128, :]),
                        waits=[(gmid, gmid.n), (gout, gout.n)], inc=qd)
                    P.op("scalar", lambda e, s_=s_, c=c: e.dma_start(
                        out=out_fm[s_, c * 128:(c + 1) * 128, :], in_=df[:]),
                        waits=[(qd, qd.n)], inc=gout)
                    P.wait("sync", (gout, gout.n))
            P.wait("scalar", (gout, gout.n))

    with contextlib.ExitStack() as fes:
      if phases == "paf":
          fsb = lambda n, s, d: fes.enter_context(nc.sbuf_tensor(n, s, d))
          fps = lambda n, s, d: fes.enter_context(nc.psum_tensor(n, s, d))
          wex = fsb("wex", [128, 8 * HID], BF16)
          wcn = fsb("wcn", [128, 32 * 1024], BF16)
          mid2_ = [fsb(f"mid2{i}", [128, 8 * TT], F32) for i in range(2)]
          xn2 = fsb("xn2", [128, 8 * TT], BF16)
          g_sb = fsb("g_sb", [128, 32 * TT], BF16)
          o_sb = fsb("o_sb", [128, 2 * TT], F32)
          mb2 = fsb("mb2", [128, TT], BF16)
          rb2 = fsb("rb2", [128, TT], BF16)
          HP = [fps(f"HP{i}", [128, 512], F32) for i in range(2)]
          PO = [fps(f"PO{i}", [128, 512], F32) for i in range(4)]
          PBC = [fps(f"PBC{i}", [128, 512], F32) for i in range(2)]

          P.wait("sync", (gs, GS_ATTN), (gmid, GMID_ATTN), (gstat, GSTAT_ATTN))

          S = {"sv_evac": SV_ATTN}
          qf_t = {}
          sp_bc2 = {-1: SP_ATTN}
          sv_xn2 = {}
          gs_out_prev = 0

          def emit_loads(t, gs_out, qprev=None):
              seq = t // TPS
              t0 = (t % TPS) * TT
              tok0 = t * TT
              qft = (qf, qf2)[t % 2]
              mid2 = mid2_[t % 2]
              for c in range(8):
                  w_ = []
                  if c == 0:
                      w_ = [(sv, S["sv_evac"])]
                      if qprev is not None:
                          w_.append((qft, qprev))
                  P.op("sync", lambda e, c=c, seq=seq, t0=t0, mid2=mid2: e.dma_start(
                      out=mid2[:, c * TT:(c + 1) * TT],
                      in_=mid_scr[seq, c * 128:(c + 1) * 128, t0:t0 + TT]),
                      waits=w_, inc=qft)
              P.op("sync", lambda e, tok0=tok0: e.dma_start(
                  out=o_sb[0:1, TT:2 * TT],
                  in_=stats_scr[0, tok0:tok0 + TT].unsqueeze(0)),
                  waits=[(sp, sp_bc2[t - 1]), (gout, gs_out)], inc=qft)
              P.op("sync", lambda e, tok0=tok0: e.dma_start(
                  out=o_sb[32:33, TT:2 * TT],
                  in_=stats_scr[1, tok0:tok0 + TT].unsqueeze(0)),
                  inc=qft)
              qf_t[t] = qft.n

          def emit_head(t, sp_h):
              qft = (qf, qf2)[t % 2]
              mid2 = mid2_[t % 2]
              w_ = [(qft, qf_t[t])]
              if t == 0:
                  w_ += [(sa, SA_ATTN), (sv, SV_ATTN)]
              P.op("tensor", lambda e: e.matmul(
                  PBC[0][:, :], orow[0:1, :], o_sb[0:1, TT:2 * TT],
                  start=True, stop=True), waits=w_, inc=sp)
              P.op("tensor", lambda e: e.matmul(
                  PBC[1][:, :], orow[32:33, :], o_sb[32:33, TT:2 * TT],
                  start=True, stop=True), inc=sp)
              sp_bc2[t] = sp.n
              P.op("vector", lambda e: e.tensor_copy(mb2[:], PBC[0][:, :]),
                   waits=[(sp, sp_bc2[t])], inc=sv)
              P.op("vector", lambda e: e.tensor_copy(rb2[:], PBC[1][:, :]), inc=sv)
              SV_BC2 = sv.n
              for c in range(8):
                  if c == 0:
                      w_ = [(sv, SV_BC2)] + ([(sp, sp_h)] if sp_h else [])
                  else:
                      w_ = [(sv, sv.n)]
                  P.op("vector", lambda e, c=c, mid2=mid2: e.tensor_tensor(
                      out=PBC[1][:, :], in0=mid2[:, c * TT:(c + 1) * TT],
                      in1=mb2[:], op=ALU.subtract),
                      waits=w_, inc=sv)
                  P.op("vector", lambda e, c=c: e.tensor_tensor(
                      out=xn2[:, c * TT:(c + 1) * TT], in0=PBC[1][:, :],
                      in1=rb2[:], op=ALU.mult), waits=[(sv, sv.n)], inc=sv)
              sv_xn2[t] = sv.n

          emit_loads(0, 0)
          for c in range(8):
              P.op("sync", lambda e, c=c: e.dma_start(
                  out=wex[:, c * HID:(c + 1) * HID],
                  in_=wexp_scr[c * 128:(c + 1) * 128, :]),
                  waits=[(qf, qf_t[0])] if c == 0 else [], inc=qf)
          QF_WEX = qf.n
          wcv = wcon_scr.rearrange("(B jj s) c -> (B jj) s c", B=32, jj=4, s=32)
          for g in range(32):
              P.op("sync", lambda e, g=g: e.dma_start(
                  out=wcn[:, g * 1024:(g + 1) * 1024], in_=wcv[:, g, :]), inc=qf2)
          QF_WCN = qf2.n
          emit_head(0, None)
          for t in range(NTIL):
              seq = t // TPS
              t0 = (t % TPS) * TT
              qft = (qf, qf2)[t % 2]
              mid2 = mid2_[t % 2]

              gmk = []
              for fc in range(32):
                  pp = HP[fc % 2]
                  for c in range(8):
                      w_ = []
                      if fc == 0 and c == 0:
                          w_ = [(qft, qf_t[t]), (sv, sv_xn2[t])]
                          if t == 0:
                              w_.append((qf, QF_WEX))
                      elif c == 0 and fc >= 2:
                          w_ = [(sa, gmk[fc - 2])]
                      P.op("tensor", lambda e, fc=fc, c=c, pp=pp: e.matmul(
                          pp[:, :],
                          wex[:, c * HID + fc * 128: c * HID + (fc + 1) * 128],
                          xn2[:, c * TT:(c + 1) * TT],
                          start=(c == 0), stop=(c == 7)),
                          waits=w_, inc=(sp if c == 7 else None))
                  P.op("scalar", lambda e, fc=fc, pp=pp: e.activation(
                      out=g_sb[:, fc * TT:(fc + 1) * TT], in_=pp[:, :], func=AF.Gelu,
                      bias=biasg[:, fc:fc + 1]), waits=[(sp, sp.n)], inc=sa)
                  gmk.append(sa.n)
              SA_G = sa.n
              SP_H = sp.n

              if t + 1 < NTIL:
                  qprev = (QF_WCN if t + 1 == 1 else
                           QF_WEX if t + 1 == 2 else qf_t[t - 1])
                  emit_loads(t + 1, gs_out_prev, qprev)
                  emit_head(t + 1, SP_H)

              for oh in range(2):
                  for fc in range(32):
                      for osub in range(4):
                          ot = oh * 4 + osub
                          w_ = []
                          if fc == 0 and osub == 0:
                              w_ = [(sa, SA_G)] if oh == 0 else [(sv, sv.n)]
                              if t == 0 and oh == 0:
                                  w_.append((qf2, QF_WCN))
                          P.op("tensor", lambda e, fc=fc, ot=ot, osub=osub: e.matmul(
                              PO[osub][:, :],
                              wcn[:, fc * 1024 + ot * 128: fc * 1024 + (ot + 1) * 128],
                              g_sb[:, fc * TT:(fc + 1) * TT],
                              start=(fc == 0), stop=(fc == 31)),
                              waits=w_, inc=(sp if (fc == 31 and osub == 3) else None))
                  SP_O = sp.n
                  for osub in range(4):
                      ot = oh * 4 + osub
                      w_ = [(gout, gs_out_prev)] if osub % 2 == 0 else [(sv, sv.n)]
                      if osub == 0:
                          w_.append((sp, SP_O))
                      P.op("vector", lambda e, osub=osub, ot=ot, mid2=mid2: e.scalar_tensor_tensor(
                          out=o_sb[:, (osub % 2) * TT:((osub % 2) + 1) * TT],
                          in0=PO[osub][:, :],
                          scalar=bco[:, ot:ot + 1], in1=mid2[:, ot * TT:(ot + 1) * TT],
                          op0=ALU.add, op1=ALU.add),
                          waits=w_, inc=sv)
                      if osub % 2 == 1:
                          SV_O = sv.n
                          for oo in (osub - 1, osub):
                              ot2 = oh * 4 + oo
                              P.op("scalar", lambda e, oo=oo, ot2=ot2, seq=seq, t0=t0: e.dma_start(
                                  out=out_fm[seq, ot2 * 128:(ot2 + 1) * 128, t0:t0 + TT],
                                  in_=o_sb[:, (oo % 2) * TT:((oo % 2) + 1) * TT]),
                                  waits=[(sv, SV_O)] if oo == osub - 1 else [], inc=gout)
                          gs_out_prev = gout.n
              S["sv_evac"] = sv.n

          P.wait("scalar", (gout, gout.n))

    P.wait("sync", (qc, qc.n), (qx0, qx0.n), (qx1, qx1.n), (qp, qp.n),
           (qp2, qp2.n), (qf, qf.n), (qf2, qf2.n))
    P.wait("scalar", (gs, gs.n), (gso, gso.n), (gstat, gstat.n), (gmid, gmid.n),
           (gout, gout.n))

    with nc.allow_low_precision(
            reason="intentional bf16 operand rounding (matmul inputs / 1/Z "
                   "scale); all accumulation stays fp32 in PSUM"), \
         nc.Block() as block:
        @block.sync
        def _(e):
            for fn in P.q["sync"]:
                fn(e)

        @block.tensor
        def _(e):
            for fn in P.q["tensor"]:
                fn(e)

        @block.vector
        def _(e):
            for fn in P.q["vector"]:
                fn(e)

        @block.scalar
        def _(e):
            for fn in P.q["scalar"]:
                fn(e)

        @block.gpsimd
        def _(e):
            for fn in P.q["gpsimd"]:
                fn(e)

    es.close()
    return nc



def make_consts():
    cf = np.zeros((128, 132), np.float32)
    cf[:, 0:128] = np.eye(128, dtype=np.float32)
    cf[:, 128] = EPS
    cbm = np.zeros((128, 132), np.float32)
    cbm[:, 0:128] = np.eye(128)
    cbm[:, 128] = 1.0
    rf = np.zeros((4, 128), np.float32)
    rf[0, :] = 1.0
    rf[1, 0:64] = 1.0
    rf[2, 64:128] = 1.0
    rb2 = np.zeros((2, 128), np.float32)
    rb2[0, 0:64] = 1.0
    rb2[1, 64:128] = 1.0
    ef = np.zeros((128, 128), np.float32)
    ef[0, 0:64] = 1.0
    ef[64, 64:128] = 1.0
    cb2 = np.zeros((128, 4), np.float32)
    cb2[:, 0] = 1.0
    return (cf, cbm.astype(ml_dtypes.bfloat16), rf,
            rb2.astype(ml_dtypes.bfloat16), ef, cb2.astype(ml_dtypes.bfloat16))


def make_inputs(L0_attn_w_attn, L0_ffn_W_exp, L0_ffn_b_exp, L0_ffn_up_weights,
                L0_ffn_up_bias, L0_ffn_down_weights, L0_ffn_W_con, L0_ffn_b_con,
                L0_ffn_down_bias, L0_attn_w_proj):
    SA_u, SQ_u = build_stage_mats(np.asarray(L0_ffn_up_weights, np.float32))
    SA_d, SQ_d = build_stage_mats(np.asarray(L0_ffn_down_weights, np.float32))
    cfc, cbb, rf, rb2, ef, cb2 = make_consts()
    fmap = f_of_hprime()
    return {
        "w_attnT": np.ascontiguousarray(np.asarray(L0_attn_w_attn, np.float32).T),
        "w_projT": np.ascontiguousarray(np.asarray(L0_attn_w_proj, np.float32).T),
        "w_expT": np.ascontiguousarray(np.asarray(L0_ffn_W_exp, np.float32).T),
        "w_con": np.ascontiguousarray(np.asarray(L0_ffn_W_con, np.float32)),
        "b_exp": np.asarray(L0_ffn_b_exp, np.float32),
        "ub_perm": np.ascontiguousarray(np.asarray(L0_ffn_up_bias, np.float32)[fmap]),
        "b_con": np.asarray(L0_ffn_b_con, np.float32),
        "d_bias": np.asarray(L0_ffn_down_bias, np.float32),
        "SAu": np.ascontiguousarray(SA_u), "SQu": np.ascontiguousarray(SQ_u),
        "SAdT": np.ascontiguousarray(SA_d.transpose(0, 1, 3, 2)),
        "SQdT": np.ascontiguousarray(SQ_d.transpose(0, 1, 3, 2)),
        "cf_in": cfc, "cb_in": cbb, "rf_in": rf, "rb_in": rb2, "ef_in": ef,
        "cb2_in": cb2,
    }


def kernel(L0_attn_x, L0_attn_w_attn, L0_attn_w_proj, L0_ffn_W_exp, L0_ffn_b_exp,
           L0_ffn_up_weights, L0_ffn_up_bias, L0_ffn_down_weights, L0_ffn_W_con,
           L0_ffn_b_con, L0_ffn_down_bias):
    x = np.asarray(L0_attn_x, dtype=np.float32)
    B = x.shape[0]
    n_seq = B // N_CORES
    shared = make_inputs(L0_attn_w_attn, L0_ffn_W_exp, L0_ffn_b_exp,
                         L0_ffn_up_weights, L0_ffn_up_bias, L0_ffn_down_weights,
                         L0_ffn_W_con, L0_ffn_b_con, L0_ffn_down_bias,
                         L0_attn_w_proj)
    nc = build_program(n_seq=n_seq)
    in_maps = []
    for core in range(N_CORES):
        m = dict(shared)
        m["x_fm"] = np.ascontiguousarray(
            x[core * n_seq:(core + 1) * n_seq].transpose(0, 2, 1))
        in_maps.append(m)
    res = run_bass_kernel_spmd(nc, in_maps, core_ids=list(range(N_CORES)))
    outs = [np.ascontiguousarray(res.results[c]["out_fm"].transpose(0, 2, 1))
            for c in range(N_CORES)]
    return np.concatenate(outs, axis=0)


# version 17 (final)
# speedup vs baseline: 1.2800x; 1.0207x over previous
import contextlib
import numpy as np
import ml_dtypes
import concourse.bass as bass
import concourse.mybir as mybir
from concourse.bass_utils import run_bass_kernel_spmd

F32 = mybir.dt.float32
BF16 = mybir.dt.bfloat16
AF = mybir.ActivationFunctionType
ALU = mybir.AluOpType
AXC = mybir.AxisListType.C

N_CORES = 8
D = 1024
HID = 4096
NH = 16
WIN = 256
TSEQ = 2048
TT = 512
EPS = 1e-5



def build_stage_mats(W):
    SA = np.zeros((7, 32, 128, 128), np.float32)
    l = np.arange(128)
    for st in range(7):
        s = 1 << st
        q = l >> (st + 1)
        half = (l >> st) & 1
        r = l & (s - 1)
        kloc = q * s + r
        part = l + np.where(half == 0, s, -s)
        for B in range(32):
            k = B * 64 + kloc
            SA[st, B, l, l] = W[st, k, half, half]
            SA[st, B, l, part] = W[st, k, half, 1 - half]
    SQ = np.zeros((5, 32, 128, 128), np.float32)
    p = np.arange(128)
    Bp = p >> 2
    jj = p & 3
    for st in range(7, 12):
        s = 1 << st
        sb = 1 << (st - 7)
        for g in range(32):
            f = Bp * 128 + jj * 32 + g
            half = (f >> st) & 1
            k = (f >> (st + 1)) * s + (f & (s - 1))
            partner = p + np.where(half == 0, sb * 4, -sb * 4)
            SQ[st - 7, g, p, p] = W[st, k, half, half]
            SQ[st - 7, g, p, partner] = W[st, k, half, 1 - half]
    return SA, SQ


def f_of_hprime():
    hp = np.arange(HID)
    g = hp >> 7
    p = hp & 127
    return (p >> 2) * 128 + (p & 3) * 32 + g



ENGS = ("sync", "tensor", "vector", "scalar", "gpsimd")


class SemCtr:
    __slots__ = ("sem", "step", "n")

    def __init__(self, sem, step):
        self.sem = sem
        self.step = step
        self.n = 0


class Plan:
    def __init__(self):
        self.q = {e: [] for e in ENGS}

    def op(self, eng, build, waits=(), inc=None, inck=1):
        wl = tuple((sc.sem, v) for sc, v in waits if v > 0)
        amt = None
        sem = None
        if inc is not None:
            amt = inck * inc.step
            inc.n += amt
            sem = inc.sem

        def fn(e, wl=wl, build=build, amt=amt, sem=sem):
            for s, v in wl:
                e.wait_ge(s, v)
            inst = build(e)
            if amt is not None:
                inst.then_inc(sem, amt)
        self.q[eng].append(fn)

    def wait(self, eng, *waits):
        wl = tuple((sc.sem, v) for sc, v in waits if v > 0)

        def fn(e, wl=wl):
            for s, v in wl:
                e.wait_ge(s, v)
        self.q[eng].append(fn)



def build_program(n_seq=4, phases="paf", attn_tiles=None, attn_stage=9, win_stage=9, skip_prep=False, win_heads=4, win_wins=None, prep_stage=9):
    nc = bass.Bass("TRN2", target_bir_lowering=False)
    NTOK = n_seq * TSEQ
    NTIL = NTOK // TT
    TPS = TSEQ // TT
    WPT = TT // WIN

    dt = nc.dram_tensor
    x_fm = dt("x_fm", [n_seq, D, TSEQ], F32, kind="ExternalInput").ap()
    w_attnT = dt("w_attnT", [D, 3 * D], F32, kind="ExternalInput").ap()
    w_projT = dt("w_projT", [D, D], F32, kind="ExternalInput").ap()
    w_expT = dt("w_expT", [HID, D], F32, kind="ExternalInput").ap()
    w_con = dt("w_con", [HID, D], F32, kind="ExternalInput").ap()
    b_exp = dt("b_exp", [HID], F32, kind="ExternalInput").ap()
    ub_perm = dt("ub_perm", [HID], F32, kind="ExternalInput").ap()
    b_con = dt("b_con", [D], F32, kind="ExternalInput").ap()
    d_bias = dt("d_bias", [D], F32, kind="ExternalInput").ap()
    SAu = dt("SAu", [7, 32, 128, 128], F32, kind="ExternalInput").ap()
    SQu = dt("SQu", [5, 32, 128, 128], F32, kind="ExternalInput").ap()
    SAdT = dt("SAdT", [7, 32, 128, 128], F32, kind="ExternalInput").ap()
    SQdT = dt("SQdT", [5, 32, 128, 128], F32, kind="ExternalInput").ap()
    cf_in = dt("cf_in", [128, 132], F32, kind="ExternalInput").ap()
    cb_in = dt("cb_in", [128, 132], BF16, kind="ExternalInput").ap()
    cb2_in = dt("cb2_in", [128, 4], BF16, kind="ExternalInput").ap()
    rf_in = dt("rf_in", [4, 128], F32, kind="ExternalInput").ap()
    ef_in = dt("ef_in", [128, 128], F32, kind="ExternalInput").ap()
    rb_in = dt("rb_in", [2, 128], BF16, kind="ExternalInput").ap()

    out_fm = dt("out_fm", [n_seq, D, TSEQ], F32, kind="ExternalOutput").ap()

    mid_scr = dt("mid_scr", [n_seq, D, TSEQ], F32, kind="Internal").ap()
    p_scr = dt("p_scr", [HID, 1028], F32, kind="Internal").ap()
    wexp_scr = dt("wexp_scr", [D, HID], BF16, kind="Internal").ap()
    wcon_scr = dt("wcon_scr", [HID, D], BF16, kind="Internal").ap()
    stats_scr = dt("stats_scr", [2, NTOK], F32, kind="Internal").ap()

    es = contextlib.ExitStack()
    sb = lambda name, shape, dtp: es.enter_context(nc.sbuf_tensor(name, shape, dtp))
    sem = lambda name, step: SemCtr(es.enter_context(nc.semaphore(name)), step)

    cf = sb("cf", [128, 132], F32)
    cbc = sb("cbc", [128, 4], BF16)
    orow = sb("orow", [33, 128], F32)
    e2f = sb("e2f", [128, 128], F32)
    biasg = sb("biasg", [128, 32], F32)
    bex = sb("bex", [128, 32], F32)
    bco = sb("bco", [128, 8], F32)

    qc = sem("qc", 16)
    qx0 = sem("qx0", 16)
    qx1 = sem("qx1", 16)
    qp = sem("qp", 16)
    qp2 = sem("qp2", 16)
    gso = sem("gso", 16)
    qf = sem("qf", 16)
    qf2 = sem("qf2", 16)
    gs = sem("gs", 16)
    gstat = sem("gstat", 16)
    gmid = sem("gmid", 16)
    gout = sem("gout", 16)
    sp = sem("sp", 1)
    sv = sem("sv", 1)
    sa = sem("sa", 1)
    sg = sem("sg", 1)

    P = Plan()

    P.op("sync", lambda e: e.dma_start(out=cf[:], in_=cf_in[:, :]), inc=qc)
    P.op("sync", lambda e: e.dma_start(out=cbc[:], in_=cb2_in[:, :]), inc=qc)
    P.op("sync", lambda e: e.dma_start(out=orow[0:1, :], in_=rf_in[0:1, :]), inc=qc)
    P.op("sync", lambda e: e.dma_start(out=orow[32:33, :], in_=rf_in[0:1, :]), inc=qc)
    P.op("sync", lambda e: e.dma_start(out=e2f[:], in_=ef_in[:, :]), inc=qc)
    QC0 = qc.n
    ID = cf[:, 0:128]
    EPSC = cf[:, 128:129]
    ONEB = cbc[:, 0:1]

    wes = contextlib.ExitStack()
    waT = wes.enter_context(nc.sbuf_tensor("waT", [128, 8 * 3072], BF16))
    wpT = wes.enter_context(nc.sbuf_tensor("wpT", [128, 8 * 1024], BF16))
    x_sb = [wes.enter_context(nc.sbuf_tensor(f"x{i}", [128, 8 * TT], F32))
            for i in range(2)]
    qw = (qx0, qx1)
    cast_mark = [0, 0]
    for c in range(8):
        sl = c % 2
        P.op("sync", lambda e, c=c, sl=sl: e.dma_start(
            out=x_sb[sl][:, 0:3072], in_=w_attnT[c * 128:(c + 1) * 128, :]),
            waits=[(sv, cast_mark[sl])], inc=qw[sl])
        P.op("vector", lambda e, c=c, sl=sl: e.tensor_copy(
            waT[:, c * 3072:(c + 1) * 3072], x_sb[sl][:, 0:3072]),
            waits=[(qw[sl], qw[sl].n)], inc=sv)
        cast_mark[sl] = sv.n
    for c in range(8):
        sl = c % 2
        P.op("sync", lambda e, c=c, sl=sl: e.dma_start(
            out=x_sb[sl][:, 0:1024], in_=w_projT[c * 128:(c + 1) * 128, :]),
            waits=[(sv, cast_mark[sl])], inc=qw[sl])
        P.op("vector", lambda e, c=c, sl=sl: e.tensor_copy(
            wpT[:, c * 1024:(c + 1) * 1024], x_sb[sl][:, 0:1024]),
            waits=[(qw[sl], qw[sl].n)], inc=sv)
        cast_mark[sl] = sv.n
    SV_W = sv.n
    X_LD = [0, 0]
    for tpre in range(2):
        for c in range(8):
            P.op("sync", lambda e, tpre=tpre, c=c: e.dma_start(
                out=x_sb[tpre][:, c * TT:(c + 1) * TT],
                in_=x_fm[tpre // TPS, c * 128:(c + 1) * 128,
                         (tpre % TPS) * TT:(tpre % TPS) * TT + TT]),
                waits=[(sv, cast_mark[tpre])] if c == 0 else [], inc=qw[tpre])
        X_LD[tpre] = qw[tpre].n

    with contextlib.ExitStack() as pes:
      if not skip_prep:
        psb = lambda n, s, d: pes.enter_context(nc.sbuf_tensor(n, s, d))
        pps = lambda n, s, d: pes.enter_context(nc.psum_tensor(n, s, d))
        sall = [psb(f"sall{i}", [128, 32 * 128], F32) for i in range(2)]
        gall = [psb(f"gall{i}", [128, 32 * 128], F32) for i in range(2)]
        svin2 = [psb(f"svin{i}", [128, 1028], F32) for i in range(2)]
        svout2 = [psb(f"svout{i}", [128, 1028], F32) for i in range(2)]
        sv2b2 = [psb(f"sv2b{i}", [128, 1024], BF16) for i in range(2)]
        swx2 = [psb(f"swx{i}", [128, 1024], BF16) for i in range(2)]
        bx = psb("bx", [128, 96], F32)
        idbp = psb("idbp", [128, 128], BF16)
        P.op("sync", lambda e: e.dma_start(out=idbp[:], in_=cb_in[:, 0:128]), inc=qc)
        QC0b = qc.n
        QC0 = QC0b
        PP = [pps(f"PP{i}", [128, 512], F32) for i in range(3)]
        PQ2 = [pps(f"PQ2{i}", [128, 512], F32) for i in range(3)]
        PPT = [pps(f"PPT{i}", [128, 512], BF16) for i in range(2)]

        state = {"sall_mark": [0, 0], "gall_guard": 0, "ppmark": {},
                 "chain_end": 0}

        def batched_loop(stages_ap, n_st, reverse, emit_load, ncols, emit_out,
                         gslot_final_holder):
            order = list(range(n_st))
            if reverse:
                order = order[::-1]
            for i, j in enumerate(order):
                sslot = i % 2
                gsl = i % 2
                sv_ = stages_ap[j].rearrange("b p c -> p b c")
                qsl = (qp, qp2)[sslot]
                for part in range(8):
                    P.op("sync", lambda e, sv_=sv_, sslot=sslot, part=part: e.dma_start(
                        out=sall[sslot][:, :].rearrange("p (b c) -> p b c", b=32)[
                            :, part * 4:(part + 1) * 4, :],
                        in_=sv_[:, part * 4:(part + 1) * 4, :]),
                        waits=([(sp, state["sall_mark"][sslot]), (qc, QC0)]
                               if part == 0 else []), inc=qsl)
                ld = (qsl, qsl.n)
                for b in range(32):
                    tile = PP[b % 3][:, 0:128]
                    w_ = []
                    if b == 0:
                        w_ = [ld]
                        if i == 0:
                            w_.append((sv, sv.n))
                            w_.append((sa, sa.n))
                            w_.append((sp, state["gall_guard"]))
                            w_.append((qc, QC0))
                    pm = state["ppmark"].get(b % 3)
                    if pm is not None:
                        w_ = w_ + [pm]
                    if i == 0:
                        P.op("tensor", lambda e, b=b, tile=tile, sslot=sslot: e.matmul(
                            tile, sall[sslot][:, b * 128:(b + 1) * 128], ID,
                            start=True, stop=True), waits=w_, inc=sp)
                    else:
                        P.op("tensor", lambda e, b=b, tile=tile, sslot=sslot, gsl=gsl: e.matmul(
                            tile, sall[sslot][:, b * 128:(b + 1) * 128],
                            gall[1 - gsl][:, b * 128:(b + 1) * 128],
                            start=True, stop=True), waits=w_, inc=sp)
                    if b % 2 == 0:
                        P.op("vector", lambda e, b=b, tile=tile, gsl=gsl: e.tensor_copy(
                            gall[gsl][:, b * 128:(b + 1) * 128], tile),
                            waits=[(sp, sp.n)], inc=sv)
                        state["ppmark"][b % 3] = (sv, sv.n)
                    else:
                        P.op("scalar", lambda e, b=b, tile=tile, gsl=gsl: e.copy(
                            out=gall[gsl][:, b * 128:(b + 1) * 128], in_=tile),
                            waits=[(sp, sp.n)], inc=sa)
                        state["ppmark"][b % 3] = (sa, sa.n)
                state["sall_mark"][sslot] = sp.n
            gfin = (n_st - 1) % 2
            SV_CH, SA_CH = sv.n, sa.n
            state["chain_end"] = sp.n
            state.pop("apply_mm0", None)
            state.pop("apply_mm1", None)
            for b in range(32 if prep_stage >= 2 else 0):
                slot = b % 2
                qsl = (qp, qp2)[slot]
                emit_load(b, slot, qsl,
                          [(sp, state.get(f"apply_mm{slot}", state["chain_end"]))])
                ld = (qsl, qsl.n)
                splits = []
                c0 = 0
                for pi, pq in enumerate(PQ2):
                    if c0 >= ncols:
                        break
                    n = min(512, ncols - c0)
                    splits.append((pq, c0, n, slot))
                    c0 += n
                for k, (pq, c0, n, sl) in enumerate(splits):
                    w_ = []
                    if k == 0:
                        w_ = [ld, (sv, SV_CH), (sa, SA_CH)]
                        if state.get("pqmark"):
                            w_.append(state["pqmark"])
                    P.op("tensor", lambda e, pq=pq, c0=c0, n=n, b=b, sl=sl, gfin=gfin: e.matmul(
                        pq[:, (sl) * 0: n] if False else pq[:, 0:n],
                        gall[gfin][:, b * 128:(b + 1) * 128],
                        svin2[sl][:, c0:c0 + n], start=True, stop=True),
                        waits=w_, inc=(sp if k == len(splits) - 1 else None))
                state[f"apply_mm{slot}"] = sp.n
                for ki, (pq, c0, n, sl) in enumerate(splits):
                    w_ = [(sp, sp.n)]
                    if ki == 0 and state.get(f"outst{slot}"):
                        w_.append(state[f"outst{slot}"])
                    P.op("vector", lambda e, pq=pq, c0=c0, n=n, sl=sl: e.tensor_copy(
                        svout2[sl][:, c0:c0 + n], pq[:, 0:n]),
                        waits=w_, inc=sv)
                state["pqmark"] = (sv, sv.n)
                state[f"apply_ev{slot}"] = sv.n
                if prep_stage >= 3:
                    emit_out(b, slot)
            state["gall_guard"] = sp.n
            state["sall_mark"] = [sp.n, sp.n]

        def ld_upA(b, slot, qsl, w_):
            P.op("sync", lambda e, b=b, slot=slot: e.dma_start(
                out=svin2[slot][:, 0:1024], in_=w_expT[b * 128:(b + 1) * 128, :]),
                waits=w_, inc=qsl)
            P.op("sync", lambda e, b=b, slot=slot: e.dma_start(
                out=svin2[slot][:, 1024:1025],
                in_=b_exp[b * 128:(b + 1) * 128].unsqueeze(1)), inc=qsl)

        def out_upA(b, slot):
            gq = (gs, gso)[slot]
            P.op("scalar", lambda e, b=b, slot=slot: e.dma_start(
                out=p_scr[b * 128:(b + 1) * 128, 0:1025], in_=svout2[slot][:, 0:1025]),
                waits=[(sv, sv.n)], inc=gq)
            state[f"outst{slot}"] = (gq, gq.n)

        batched_loop(SAu, 7, True, ld_upA, 1025, out_upA, None)
        P.wait("sync", (gs, gs.n), (gso, gso.n))

        p_view = p_scr.rearrange("(B jj s) c -> (B jj) s c", B=32, jj=4, s=32)

        def ld_upQ(g, slot, qsl, w_):
            P.op("sync", lambda e, g=g, slot=slot: e.dma_start(
                out=svin2[slot][:, 0:1025], in_=p_view[:, g, 0:1025]),
                waits=w_, inc=qsl)

        def out_upQ(g, slot):
            P.op("vector", lambda e, g=g, slot=slot: e.tensor_copy(
                bex[:, g:g + 1], svout2[slot][:, 1024:1025]),
                waits=[(sv, sv.n)], inc=sv)
            P.op("vector", lambda e, slot=slot: e.tensor_copy(
                sv2b2[slot][:], svout2[slot][:, 0:1024]),
                waits=[(sv, sv.n), (sp, state.get(f"tp{slot}", 0))], inc=sv)
            SV_CAST = sv.n
            for ci in range(8):
                pt = PPT[ci % 2][:, 0:128]
                w_ = [(sv, SV_CAST)] if ci == 0 else []
                ptm = state.setdefault("ptmark", {}).get(ci % 2)
                if ptm is not None:
                    w_ = w_ + [ptm]
                else:
                    w_ = w_ + [(qc, QC0b)]
                P.op("tensor", lambda e, ci=ci, slot=slot, pt=pt: e.transpose(
                    pt, sv2b2[slot][:, ci * 128:(ci + 1) * 128], idbp[:]),
                    waits=w_, inc=sp)
                sw = state.get(f"swxst{slot}")
                P.op("scalar", lambda e, ci=ci, slot=slot, pt=pt: e.copy(
                    out=swx2[slot][:, ci * 128:(ci + 1) * 128], in_=pt),
                    waits=[(sp, sp.n)] + ([sw] if ci == 0 and sw else []), inc=sa)
                state["ptmark"][ci % 2] = (sa, sa.n)
            state[f"tp{slot}"] = sp.n
            SA_SWX = sa.n
            P.op("scalar", lambda e, g=g, slot=slot: e.dma_start(
                out=wexp_scr.rearrange("(cc p) f -> p (cc f)", p=128)[
                    :, 0:0] if False else wexp_scr[0:128, 0:128],
                in_=swx2[slot][:, 0:128]), waits=[], inc=gs) if False else None
            gq = (gs, gso)[slot]
            for ci in range(8):
                P.op("scalar", lambda e, ci=ci, g=g, slot=slot: e.dma_start(
                    out=wexp_scr[ci * 128:(ci + 1) * 128, g * 128:(g + 1) * 128],
                    in_=swx2[slot][:, ci * 128:(ci + 1) * 128]),
                    waits=[(sa, SA_SWX)] if ci == 0 else [], inc=gq)
            state[f"swxst{slot}"] = (gq, gq.n)
            state[f"outst{slot}"] = (sv, SV_CAST)

        batched_loop(SQu, 5, True, ld_upQ, 1025, out_upQ, None)
        P.wait("sync", (gs, gs.n), (gso, gso.n))

        w_view = w_con.rearrange("(B jj s) c -> (B jj) s c", B=32, jj=4, s=32)
        pv2 = p_scr.rearrange("(B jj s) c -> (B jj) s c", B=32, jj=4, s=32)

        def ld_dnQ(g, slot, qsl, w_):
            P.op("sync", lambda e, g=g, slot=slot: e.dma_start(
                out=svin2[slot][:, 0:1024], in_=w_view[:, g, :]),
                waits=w_, inc=qsl)

        def out_dnQ(g, slot):
            gq = (gs, gso)[slot]
            P.op("scalar", lambda e, g=g, slot=slot: e.dma_start(
                out=pv2[:, g, 0:1024], in_=svout2[slot][:, 0:1024]),
                waits=[(sv, sv.n)], inc=gq)
            state[f"outst{slot}"] = (gq, gq.n)

        batched_loop(SQdT, 5, False, ld_dnQ, 1024, out_dnQ, None)
        P.wait("sync", (gs, gs.n), (gso, gso.n))

        def ld_dnA(b, slot, qsl, w_):
            P.op("sync", lambda e, b=b, slot=slot: e.dma_start(
                out=svin2[slot][:, 0:1024], in_=p_scr[b * 128:(b + 1) * 128, 0:1024]),
                waits=w_, inc=qsl)

        def out_dnA(b, slot):
            wc = state.get(f"wcst{slot}")
            P.op("vector", lambda e, slot=slot: e.tensor_copy(
                sv2b2[slot][:], svout2[slot][:, 0:1024]),
                waits=[(sv, sv.n)] + ([wc] if wc else []), inc=sv)
            gq = (gs, gso)[slot]
            P.op("scalar", lambda e, b=b, slot=slot: e.dma_start(
                out=wcon_scr[b * 128:(b + 1) * 128, :], in_=sv2b2[slot][:]),
                waits=[(sv, sv.n)], inc=gq)
            state[f"wcst{slot}"] = (gq, gq.n)
            state[f"outst{slot}"] = (sv, sv.n)

        batched_loop(SAdT, 7, False, ld_dnA, 1024, out_dnA, None)
        P.wait("sync", (gs, gs.n), (gso, gso.n))

        uview = ub_perm.rearrange("(c p) -> c p", p=128)
        for c in range(32):
            P.op("sync", lambda e, c=c: e.dma_start(
                out=bx[:, 32 + c:33 + c], in_=uview[c, :].unsqueeze(1)), inc=qp)
        bcv = b_con.rearrange("(c p) -> c p", p=128)
        dbv = d_bias.rearrange("(c p) -> c p", p=128)
        for c in range(8):
            P.op("sync", lambda e, c=c: e.dma_start(
                out=bx[:, 64 + c:65 + c], in_=bcv[c, :].unsqueeze(1)), inc=qp)
            P.op("sync", lambda e, c=c: e.dma_start(
                out=bx[:, 72 + c:73 + c], in_=dbv[c, :].unsqueeze(1)), inc=qp)
        P.op("vector", lambda e: e.tensor_tensor(
            out=biasg[:], in0=bex[:], in1=bx[:, 32:64], op=ALU.add),
            waits=[(qp, qp.n), (sv, sv.n)], inc=sv)
        P.op("vector", lambda e: e.tensor_tensor(
            out=bco[:], in0=bx[:, 64:72], in1=bx[:, 72:80], op=ALU.add), inc=sv)

    SP_PREP, SV_PREP, SA_PREP, GS_PREP = sp.n, sv.n, sa.n, gs.n

    if "p" == phases:
        with contextlib.ExitStack() as des:
            db = des.enter_context(nc.sbuf_tensor("db", [128, 1024], BF16))
            df = des.enter_context(nc.sbuf_tensor("df", [128, 1024], F32))
            qd = SemCtr(des.enter_context(nc.semaphore("qd")), 16)
            for i in range(16):
                P.op("sync", lambda e, i=i: e.dma_start(
                    out=db[:], in_=wcon_scr[i * 128:(i + 1) * 128, :]),
                    waits=[(gs, gs.n), (sv, sv.n)], inc=qd)
                P.op("vector", lambda e, i=i: e.tensor_copy(df[:], db[:]),
                     waits=[(qd, qd.n), (gout, gout.n)], inc=sv)
                P.op("scalar", lambda e, i=i: e.dma_start(
                    out=out_fm[0, (i % 8) * 128:(i % 8 + 1) * 128,
                               (i // 8) * 1024:(i // 8 + 1) * 1024],
                    in_=df[:]), waits=[(sv, sv.n)], inc=gout)
                P.wait("sync", (gout, gout.n))
            P.wait("scalar", (gout, gout.n))
    attn_on = "a" in phases
    with contextlib.ExitStack() as aes:
      if attn_on:
          asb = lambda n, s, d: aes.enter_context(nc.sbuf_tensor(n, s, d))
          aps = lambda n, s, d: aes.enter_context(nc.psum_tensor(n, s, d))
          xbt = asb("xbt", [128, 8 * TT], BF16)
          x2t = asb("x2t", [128, 8 * TT], BF16)
          xnt = asb("xnt", [128, 8 * TT], BF16)
          qkt = asb("qkt", [128, 16 * TT], BF16)
          v_sb = asb("v_sb", [128, 8 * TT], BF16)
          y_sb = asb("y_sb", [128, 8 * TT], BF16)
          ett = asb("ett", [128, 2 * 1024], BF16)
          midt = asb("midt", [128, 8 * TT], F32)
          bgt = asb("bgt", [128, 512], F32)
          mrow = asb("mrow", [1, 4 * TT], F32)
          mrow2 = asb("mrow2", [1, 4 * TT], F32)
          zz = asb("zz", [128, 1024], F32)
          P.op("vector", lambda e: e.memset(zz[:], 0.0),
               waits=[(sv, sv.n)], inc=sv)
          mbt = asb("mbt", [128, TT], BF16)
          rbt = asb("rbt", [128, TT], BF16)
          PA = aps("PA", [128, 1024], F32)
          PB = aps("PB", [128, 1024], F32)
          PC = aps("PC", [128, 512], F32)
          PD = aps("PD", [128, 512], F32)
          PE_ = aps("PE_", [128, 512], F32)
          PF = aps("PF", [128, 512], F32)

          prev = dict(sp_st2=SP_PREP, sa_c2=SA_PREP, gs_mid=0,
                      gs_stat=0, sv_v2=SV_PREP, sp_vend=SP_PREP,
                      sv_mid=SV_W)

          def emit_ln1_head(u, ldx, sp_vend_cur, sp_guard):
              qxu = (qx0, qx1)[u % 2]
              xsu = x_sb[u % 2]
              for c in range(8):
                  w_ = ([(qxu, ldx), (sp, sp_guard), (sv, sv.n)]
                        if c == 0 else [])
                  P.op("vector", lambda e, c=c, xsu=xsu: e.tensor_copy(
                      xbt[:, c * TT:(c + 1) * TT], xsu[:, c * TT:(c + 1) * TT]),
                      waits=w_, inc=sv)
                  P.op("vector", lambda e, c=c, xsu=xsu: e.tensor_tensor(
                      out=x2t[:, c * TT:(c + 1) * TT],
                      in0=xsu[:, c * TT:(c + 1) * TT],
                      in1=xsu[:, c * TT:(c + 1) * TT], op=ALU.mult), inc=sv)
              SV_CASTU = sv.n
              for c in range(8):
                  P.op("tensor", lambda e, c=c: e.matmul(
                      PC[0:1, 0:TT], ONEB, xbt[:, c * TT:(c + 1) * TT],
                      start=(c == 0), stop=(c == 7)),
                      waits=([(sv, SV_CASTU)] if c == 0 else []),
                      inc=(sp if c == 7 else None))
              for c in range(8):
                  P.op("tensor", lambda e, c=c: e.matmul(
                      PD[0:1, 0:TT], ONEB, x2t[:, c * TT:(c + 1) * TT],
                      start=(c == 0), stop=(c == 7)), inc=(sp if c == 7 else None))
              SP_STATU = sp.n
              P.op("vector", lambda e: e.tensor_scalar(
                  out=mrow[:, 0:TT], in0=PC[0:1, 0:TT], scalar1=1.0 / D,
                  scalar2=None, op0=ALU.mult),
                  waits=[(sp, SP_STATU), (gstat, prev["gs_stat"])], inc=sv)
              P.op("vector", lambda e: e.tensor_scalar(
                  out=mrow[:, 2 * TT:3 * TT], in0=PD[0:1, 0:TT], scalar1=1.0 / D,
                  scalar2=None, op0=ALU.mult), waits=[(sv, sv.n)], inc=sv)
              P.op("vector", lambda e: e.tensor_tensor(
                  out=mrow[:, 3 * TT:4 * TT], in0=mrow[:, 0:TT],
                  in1=mrow[:, 0:TT], op=ALU.mult), waits=[(sv, sv.n)], inc=sv)
              P.op("vector", lambda e: e.tensor_tensor(
                  out=mrow[:, 2 * TT:3 * TT], in0=mrow[:, 2 * TT:3 * TT],
                  in1=mrow[:, 3 * TT:4 * TT], op=ALU.subtract),
                  waits=[(sv, sv.n)], inc=sv)
              P.op("scalar", lambda e: e.activation(
                  out=mrow[:, 3 * TT:4 * TT], in_=mrow[:, 2 * TT:3 * TT],
                  func=AF.Ln, bias=EPSC[0:1, :]), waits=[(sv, sv.n)], inc=sa)
              P.op("scalar", lambda e: e.activation(
                  out=mrow[:, TT:2 * TT], in_=mrow[:, 3 * TT:4 * TT],
                  func=AF.Exp, scale=-0.5), waits=[(sa, sa.n)], inc=sa)
              SA_LNU = sa.n
              P.op("tensor", lambda e: e.matmul(
                  PE_[:, 0:TT], orow[0:1, :], mrow[0:1, 0:TT], start=True,
                  stop=True), waits=[(sa, SA_LNU), (sv, prev["sv_mid"])], inc=sp)
              P.op("tensor", lambda e: e.matmul(
                  PF[:, 0:TT], orow[0:1, :], mrow[0:1, TT:2 * TT], start=True,
                  stop=True), inc=sp)
              SP_BCU = sp.n
              P.op("vector", lambda e: e.tensor_copy(mbt[:], PE_[:, 0:TT]),
                   waits=[(sp, SP_BCU)], inc=sv)
              P.op("vector", lambda e: e.tensor_copy(rbt[:], PF[:, 0:TT]), inc=sv)
              SV_BCU = sv.n
              for c in range(8):
                  P.op("vector", lambda e, c=c: e.tensor_tensor(
                      out=x2t[:, c * TT:(c + 1) * TT], in0=xbt[:, c * TT:(c + 1) * TT],
                      in1=mbt[:], op=ALU.subtract),
                      waits=([(sv, SV_BCU), (sp, SP_STATU)] if c == 0
                             else [(sv, sv.n)]), inc=sv)
                  P.op("vector", lambda e, c=c: e.tensor_tensor(
                      out=xnt[:, c * TT:(c + 1) * TT], in0=x2t[:, c * TT:(c + 1) * TT],
                      in1=rbt[:], op=ALU.mult),
                      waits=[(sv, sv.n)] + ([(sp, sp_vend_cur)] if c == 0 else []),
                      inc=sv)
              return SP_STATU, SV_BCU, sv.n

          for t in range(NTIL):
              seq = t // TPS
              t0 = (t % TPS) * TT
              qx = (qx0, qx1)[t % 2]
              xs = x_sb[t % 2]

              if t == 0:
                  SP_STAT0, SV_BC0, SV_XN0 = emit_ln1_head(
                      0, X_LD[0], prev["sp_vend"], prev["sp_st2"])
                  prev = dict(prev, sp_stat0=SP_STAT0, sv_bc=SV_BC0,
                              sv_xn=SV_XN0)
              SV_XN = prev["sv_xn"]

              pq = (PC, PD)
              pdm = []
              for ei in range(16):
                  pp = pq[ei % 2]
                  for c in range(8):
                      w_ = []
                      if ei == 0 and c == 0:
                          w_ = [(sv, SV_XN)]
                      elif c == 0 and ei >= 2:
                          w_ = [(sv, pdm[ei - 2])]
                      P.op("tensor", lambda e, ei=ei, c=c, pp=pp: e.matmul(
                          pp[:, 0:TT],
                          waT[:, c * 3072 + ei * 128: c * 3072 + (ei + 1) * 128],
                          xnt[:, c * TT:(c + 1) * TT],
                          start=(c == 0), stop=(c == 7)),
                          waits=w_, inc=(sp if c == 7 else None))
                  P.op("vector", lambda e, ei=ei, pp=pp: e.tensor_copy(
                      qkt[:, ei * TT:(ei + 1) * TT], pp[:, 0:TT]),
                      waits=[(sp, sp.n)], inc=sv)
                  pdm.append(sv.n)
              for ts4 in range(TT // 128):
                  for eh in range(2):
                      gi_ = 16 + ts4 * 2 + eh
                      pp = pq[(ts4 * 2 + eh) % 2]
                      for c in range(8):
                          P.op("tensor", lambda e, ts4=ts4, eh=eh, c=c, pp=pp: e.matmul(
                              pp[:, :],
                              xnt[:, c * TT + ts4 * 128: c * TT + (ts4 + 1) * 128],
                              waT[:, c * 3072 + 2048 + eh * 512: c * 3072 + 2048 + (eh + 1) * 512],
                              start=(c == 0), stop=(c == 7)),
                              waits=([(sv, pdm[gi_ - 2])] if c == 0 else []),
                              inc=(sp if c == 7 else None))
                      P.op("vector", lambda e, ts4=ts4, eh=eh, pp=pp: e.tensor_copy(
                          v_sb[:, ts4 * 1024 + eh * 512: ts4 * 1024 + (eh + 1) * 512],
                          pp[:, :]), waits=[(sp, sp.n)], inc=sv)
                      pdm.append(sv.n)
              SV_QKV = sv.n
              SP_VEND = sp.n

              if t + 1 < NTIL:
                  u = t + 1
                  qxu = (qx0, qx1)[u % 2]
                  if u >= 2:
                      xsu = x_sb[u % 2]
                      for c in range(8):
                          P.op("sync", lambda e, c=c, u=u, xsu=xsu: e.dma_start(
                              out=xsu[:, c * TT:(c + 1) * TT],
                              in_=x_fm[u // TPS, c * 128:(c + 1) * 128,
                                       (u % TPS) * TT:(u % TPS) * TT + TT]),
                              waits=[(sv, prev["sv_mid"])] if c == 0 else [],
                              inc=qxu)
                      ldx_u = qxu.n
                  else:
                      ldx_u = X_LD[1]
                  guard = prev["sp_stat0"] if u == 1 else prev["sp_st2"]
                  SP_STATU, SV_BCU, SV_XNU = emit_ln1_head(
                      u, ldx_u, SP_VEND, guard)
                  SV_BCW = SV_BCU
              else:
                  SV_BCW = prev["sv_bc"]
                  SV_XNU = prev["sv_xn"]

              if attn_stage < 5:
                  prev = dict(prev, sv_mid=sv.n, sa_c2=sa.n, sp_st2=sp.n,
                              sv_v2=sv.n, sp_vend=SP_VEND)
                  continue
              CS2 = {(0, 0): 0, (0, 1): 256, (1, 0): 512, (1, 1): 768}
              NIT = WPT * 8
              mk = {"st": {}, "et": {-2: prev["sa_c2"], -1: prev["sa_c2"]},
                    "z": {}, "rc": {-2: prev["sa_c2"], -1: prev["sa_c2"]},
                    "av": {}, "bg": {}, "bge": {-2: prev["sa_c2"], -1: prev["sa_c2"]},
                    "y": {-2: SV_BCW, -1: SV_BCW, -3: SV_BCW, -4: SV_BCW}}

              def emit_st(k):
                  w, gi = divmod(k, 8)
                  pp = (PA, PB)[k % 2]
                  first = (k == 0)
                  for idx, (hp, jh) in enumerate(((0, 0), (0, 1), (1, 0), (1, 1))):
                      h = 2 * gi + hp
                      qc_ = h // 2
                      qb = (h % 2) * 64
                      w_ = []
                      if idx == 0:
                          w_ = [(sa, mk["et"][k - 2]), (sa, mk["rc"][k - 2])]
                          if first:
                              w_.append((sv, SV_QKV))
                      P.op("tensor", lambda e, pp=pp, qc_=qc_, qb=qb, jh=jh, w=w, hp=hp: e.matmul(
                          pp[:, CS2[(hp, jh)]:CS2[(hp, jh)] + 256],
                          qkt[qb:qb + 64, (8 + qc_) * TT + w * 256 + jh * 128:
                              (8 + qc_) * TT + w * 256 + (jh + 1) * 128],
                          qkt[qb:qb + 64, qc_ * TT + w * 256: qc_ * TT + (w + 1) * 256],
                          start=True, stop=True),
                          waits=w_, inc=(sp if idx == 3 else None))
                  mk["st"][k] = sp.n

              def emit_exp(k):
                  pp = (PA, PB)[k % 2]
                  eb = (k % 2) * 1024
                  P.op("scalar", lambda e, pp=pp, eb=eb: e.activation(
                      out=ett[:, eb:eb + 1024], in_=pp[:, :], func=AF.Exp, scale=0.125),
                      waits=[(sp, mk["st"][k]), (sp, mk["av"].get(k - 2, 0))], inc=sa)
                  mk["et"][k] = sa.n

              def emit_z(k):
                  zp = (PA, PB)[k % 2]
                  eb = (k % 2) * 1024
                  for idx, (hp, col) in enumerate(((0, 0), (1, 256))):
                      for jh in range(2):
                          w_ = []
                          if idx == 0 and jh == 0:
                              w_ = [(sa, mk["et"][k]), (sa, mk["rc"][k - 2])]
                          P.op("tensor", lambda e, zp=zp, col=col, eb=eb, hp=hp, jh=jh: e.matmul(
                              zp[0:1, col:col + 256], ONEB,
                              ett[:, eb + CS2[(hp, jh)]: eb + CS2[(hp, jh)] + 256],
                              start=(jh == 0), stop=(jh == 1)),
                              waits=w_, inc=(sp if (idx == 1 and jh == 1) else None))
                  mk["z"][k] = sp.n

              def emit_recip(k):
                  zp = (PA, PB)[k % 2]
                  sc = (k % 2) * 512
                  P.op("scalar", lambda e, zp=zp, sc=sc: e.activation(
                      out=zz[0:1, sc:sc + 256], in_=zp[0:1, 0:256], func=AF.Ln),
                      waits=[(sp, mk["z"][k]), (sp, mk["bg"].get(k - 2, 0))], inc=sa)
                  P.op("scalar", lambda e, zp=zp, sc=sc: e.activation(
                      out=zz[64:65, sc:sc + 256], in_=zp[0:1, 256:512], func=AF.Ln),
                      inc=sa)
                  mk["rc"][k] = sa.n

              def emit_av(k):
                  w, gi = divmod(k, 8)
                  eb = (k % 2) * 1024
                  yp = (PD, PE_)[k % 2]
                  for idx, hp in enumerate((0, 1)):
                      h = 2 * gi + hp
                      ob = hp * 64
                      for jh in range(2):
                          vt = w * 2 + jh
                          w_ = []
                          if idx == 0 and jh == 0:
                              w_ = [(sa, mk["et"][k]), (sv, mk["y"][k - 2])]
                          P.op("tensor", lambda e, ob=ob, h=h, jh=jh, vt=vt, eb=eb, yp=yp, hp=hp: e.matmul(
                              yp[ob:ob + 64, 0:256],
                              v_sb[:, vt * 1024 + h * 64: vt * 1024 + (h + 1) * 64],
                              ett[:, eb + CS2[(hp, jh)]: eb + CS2[(hp, jh)] + 256],
                              start=(jh == 0), stop=(jh == 1)),
                              waits=w_, inc=(sp if (idx == 1 and jh == 1) else None))
                  mk["av"][k] = sp.n

              def emit_bg(k):
                  sc = (k % 2) * 512
                  bp = (PC, PF)[k % 2]
                  P.op("tensor", lambda e, sc=sc, bp=bp: e.matmul(
                      bp[:, 0:256], e2f[0:65, :], zz[0:65, sc:sc + 256],
                      start=True, stop=True),
                      waits=[(sa, mk["rc"][k]), (sa, mk["bge"][k - 2])], inc=sp)
                  mk["bg"][k] = sp.n

              def emit_bge(k):
                  bc = (k % 2) * 256
                  bp = (PC, PF)[k % 2]
                  P.op("scalar", lambda e, bc=bc, bp=bp: e.activation(
                      out=bgt[:, bc:bc + 256], in_=bp[:, 0:256], func=AF.Exp,
                      scale=-1.0),
                      waits=[(sp, mk["bg"][k]), (sv, mk["y"][k - 2])], inc=sa)
                  mk["bge"][k] = sa.n

              def emit_y(k):
                  w, gi = divmod(k, 8)
                  yc = (k % 2) * 256
                  yp = (PD, PE_)[k % 2]
                  P.op("vector", lambda e, yc=yc, gi=gi, w=w, yp=yp: e.tensor_tensor(
                      out=y_sb[:, gi * TT + w * 256: gi * TT + (w + 1) * 256],
                      in0=yp[:, 0:256], in1=bgt[:, yc:yc + 256], op=ALU.mult),
                      waits=[(sp, mk["av"][k]), (sa, mk["bge"][k])], inc=sv)
                  mk["y"][k] = sv.n

              for k in range(NIT + 2):
                  if k < NIT:
                      emit_st(k)
                  if 1 <= k <= NIT:
                      emit_z(k - 1)
                  if k >= 2:
                      emit_av(k - 2)
                      emit_bg(k - 2)
                  if k < NIT:
                      emit_exp(k)
                  if 1 <= k <= NIT:
                      emit_recip(k - 1)
                  if k >= 2:
                      emit_bge(k - 2)
                      emit_y(k - 2)
              SV_Y = sv.n


              if attn_stage < 6:
                prev = dict(prev, sv_mid=sv.n, sa_c2=sa.n, sp_st2=sp.n,
                            sv_v2=sv.n, sp_vend=SP_VEND)
                continue
              prm = []
              for ot in range(8):
                  pp = (PE_, PF)[ot % 2]
                  for c in range(8):
                      w_ = []
                      if ot == 0 and c == 0:
                          w_ = [(sv, SV_Y)]
                      elif c == 0 and ot >= 2:
                          w_ = [(sv, prm[ot - 2])]
                      P.op("tensor", lambda e, ot=ot, c=c, pp=pp: e.matmul(
                          pp[:, :],
                          wpT[:, c * 1024 + ot * 128: c * 1024 + (ot + 1) * 128],
                          y_sb[:, c * TT:(c + 1) * TT],
                          start=(c == 0), stop=(c == 7)),
                          waits=w_, inc=(sp if c == 7 else None))
                  P.op("vector", lambda e, ot=ot, pp=pp, xs=xs: e.tensor_tensor(
                      out=midt[:, ot * TT:(ot + 1) * TT], in0=pp[:, :],
                      in1=xs[:, ot * TT:(ot + 1) * TT], op=ALU.add),
                      waits=[(sp, sp.n)] + ([(gmid, prev["gs_mid"]),
                                             (sa, prev["sa_c2"])] if ot == 0 else []),
                      inc=sv)
                  prm.append(sv.n)
              SV_MID = sv.n

              for c in range(8):
                  P.op("scalar", lambda e, c=c: e.copy(
                      out=xbt[:, c * TT:(c + 1) * TT], in_=midt[:, c * TT:(c + 1) * TT]),
                      waits=([(sv, SV_MID)] if c == 0 else []), inc=sa)
              for c in range(8):
                  P.op("scalar", lambda e, c=c: e.activation(
                      out=x2t[:, c * TT:(c + 1) * TT], in_=midt[:, c * TT:(c + 1) * TT],
                      func=AF.Square), inc=sa)
              SA_C2 = sa.n
              for c in range(8):
                  P.op("tensor", lambda e, c=c: e.matmul(
                      PA[0:1, 0:TT], ONEB, xbt[:, c * TT:(c + 1) * TT],
                      start=(c == 0), stop=(c == 7)),
                      waits=([(sa, SA_C2), (sv, sv.n)] if c == 0 else []),
                      inc=(sp if c == 7 else None))
              for c in range(8):
                  P.op("tensor", lambda e, c=c: e.matmul(
                      PB[0:1, 0:TT], ONEB, x2t[:, c * TT:(c + 1) * TT],
                      start=(c == 0), stop=(c == 7)), inc=(sp if c == 7 else None))
              SP_ST2 = sp.n
              P.op("vector", lambda e: e.tensor_scalar(
                  out=mrow2[:, 0:TT], in0=PA[0:1, 0:TT], scalar1=1.0 / D,
                  scalar2=None, op0=ALU.mult),
                  waits=[(sp, SP_ST2), (sv, sv.n), (gstat, prev["gs_stat"])],
                  inc=sv)
              P.op("vector", lambda e: e.tensor_scalar(
                  out=mrow2[:, 2 * TT:3 * TT], in0=PB[0:1, 0:TT], scalar1=1.0 / D,
                  scalar2=None, op0=ALU.mult), waits=[(sv, sv.n)], inc=sv)
              P.op("vector", lambda e: e.tensor_tensor(
                  out=mrow2[:, 3 * TT:4 * TT], in0=mrow2[:, 0:TT],
                  in1=mrow2[:, 0:TT], op=ALU.mult), waits=[(sv, sv.n)], inc=sv)
              P.op("vector", lambda e: e.tensor_tensor(
                  out=mrow2[:, 2 * TT:3 * TT], in0=mrow2[:, 2 * TT:3 * TT],
                  in1=mrow2[:, 3 * TT:4 * TT], op=ALU.subtract),
                  waits=[(sv, sv.n)], inc=sv)
              SV_V2 = sv.n
              P.op("scalar", lambda e: e.activation(
                  out=mrow2[:, 3 * TT:4 * TT], in_=mrow2[:, 2 * TT:3 * TT],
                  func=AF.Ln, bias=EPSC[0:1, :]), waits=[(sv, SV_V2)], inc=sa)
              P.op("scalar", lambda e: e.activation(
                  out=mrow2[:, TT:2 * TT], in_=mrow2[:, 3 * TT:4 * TT],
                  func=AF.Exp, scale=-0.5), waits=[(sa, sa.n)], inc=sa)
              SA_LN2 = sa.n

              tok0 = t * TT
              P.op("scalar", lambda e, tok0=tok0: e.dma_start(
                  out=stats_scr[0, tok0:tok0 + TT].unsqueeze(0), in_=mrow2[:, 0:TT]),
                  waits=[(sa, SA_LN2)], inc=gstat)
              P.op("scalar", lambda e, tok0=tok0: e.dma_start(
                  out=stats_scr[1, tok0:tok0 + TT].unsqueeze(0), in_=mrow2[:, TT:2 * TT]),
                  inc=gstat)
              GS_STAT = gstat.n
              for c in range(8):
                  P.op("scalar", lambda e, c=c, seq=seq, t0=t0: e.dma_start(
                      out=mid_scr[seq, c * 128:(c + 1) * 128, t0:t0 + TT],
                      in_=midt[:, c * TT:(c + 1) * TT]),
                      waits=[(sv, SV_MID)] if c == 0 else [], inc=gmid)
              GS_MID = gmid.n

              prev = dict(sp_st2=SP_ST2, sa_c2=SA_C2, gs_mid=GS_MID,
                          gs_stat=GS_STAT, sv_v2=SV_V2, sp_vend=SP_VEND,
                          sv_mid=SV_MID, sv_xn=SV_XNU, sv_bc=SV_BCW)

    SP_ATTN, SV_ATTN, SA_ATTN = sp.n, sv.n, sa.n
    GMID_ATTN, GSTAT_ATTN, GS_ATTN = gmid.n, gstat.n, gs.n
    wes.close()


    if phases == "pa":
        with contextlib.ExitStack() as des:
            df = des.enter_context(nc.sbuf_tensor("dfa", [128, 2048], F32))
            qd = SemCtr(des.enter_context(nc.semaphore("qda")), 16)
            for s_ in range(n_seq):
                for c in range(8):
                    P.op("sync", lambda e, s_=s_, c=c: e.dma_start(
                        out=df[:], in_=mid_scr[s_, c * 128:(c + 1) * 128, :]),
                        waits=[(gmid, gmid.n), (gout, gout.n)], inc=qd)
                    P.op("scalar", lambda e, s_=s_, c=c: e.dma_start(
                        out=out_fm[s_, c * 128:(c + 1) * 128, :], in_=df[:]),
                        waits=[(qd, qd.n)], inc=gout)
                    P.wait("sync", (gout, gout.n))
            P.wait("scalar", (gout, gout.n))

    with contextlib.ExitStack() as fes:
      if phases == "paf":
          fsb = lambda n, s, d: fes.enter_context(nc.sbuf_tensor(n, s, d))
          fps = lambda n, s, d: fes.enter_context(nc.psum_tensor(n, s, d))
          wex = fsb("wex", [128, 8 * HID], BF16)
          wcn = fsb("wcn", [128, 32 * 1024], BF16)
          mid2_ = [fsb(f"mid2{i}", [128, 8 * TT], F32) for i in range(2)]
          xn2 = fsb("xn2", [128, 8 * TT], BF16)
          g_sb = fsb("g_sb", [128, 32 * TT], BF16)
          o_sb = fsb("o_sb", [128, 2 * TT], F32)
          mb2 = fsb("mb2", [128, TT], BF16)
          rb2 = fsb("rb2", [128, TT], BF16)
          HP = [fps(f"HP{i}", [128, 512], F32) for i in range(2)]
          PO = [fps(f"PO{i}", [128, 512], F32) for i in range(4)]
          PBC = [fps(f"PBC{i}", [128, 512], F32) for i in range(2)]

          P.wait("sync", (gs, GS_ATTN), (gmid, GMID_ATTN), (gstat, GSTAT_ATTN))

          S = {"sv_evac": SV_ATTN}
          qf_t = {}
          sp_bc2 = {-1: SP_ATTN}
          sv_xn2 = {}
          gs_out_prev = 0

          def emit_loads(t, gs_out, qprev=None):
              seq = t // TPS
              t0 = (t % TPS) * TT
              tok0 = t * TT
              qft = (qf, qf2)[t % 2]
              mid2 = mid2_[t % 2]
              for c in range(8):
                  w_ = []
                  if c == 0:
                      w_ = [(sv, S["sv_evac"])]
                      if qprev is not None:
                          w_.append((qft, qprev))
                  P.op("sync", lambda e, c=c, seq=seq, t0=t0, mid2=mid2: e.dma_start(
                      out=mid2[:, c * TT:(c + 1) * TT],
                      in_=mid_scr[seq, c * 128:(c + 1) * 128, t0:t0 + TT]),
                      waits=w_, inc=qft)
              P.op("sync", lambda e, tok0=tok0: e.dma_start(
                  out=o_sb[0:1, TT:2 * TT],
                  in_=stats_scr[0, tok0:tok0 + TT].unsqueeze(0)),
                  waits=[(sp, sp_bc2[t - 1]), (gout, gs_out)], inc=qft)
              P.op("sync", lambda e, tok0=tok0: e.dma_start(
                  out=o_sb[32:33, TT:2 * TT],
                  in_=stats_scr[1, tok0:tok0 + TT].unsqueeze(0)),
                  inc=qft)
              qf_t[t] = qft.n

          def emit_head(t, sp_h):
              qft = (qf, qf2)[t % 2]
              mid2 = mid2_[t % 2]
              w_ = [(qft, qf_t[t])]
              if t == 0:
                  w_ += [(sa, SA_ATTN), (sv, SV_ATTN)]
              P.op("tensor", lambda e: e.matmul(
                  PBC[0][:, :], orow[0:1, :], o_sb[0:1, TT:2 * TT],
                  start=True, stop=True), waits=w_, inc=sp)
              P.op("tensor", lambda e: e.matmul(
                  PBC[1][:, :], orow[32:33, :], o_sb[32:33, TT:2 * TT],
                  start=True, stop=True), inc=sp)
              sp_bc2[t] = sp.n
              P.op("vector", lambda e: e.tensor_copy(mb2[:], PBC[0][:, :]),
                   waits=[(sp, sp_bc2[t])], inc=sv)
              P.op("vector", lambda e: e.tensor_copy(rb2[:], PBC[1][:, :]), inc=sv)
              SV_BC2 = sv.n
              for c in range(8):
                  if c == 0:
                      w_ = [(sv, SV_BC2)] + ([(sp, sp_h)] if sp_h else [])
                  else:
                      w_ = [(sv, sv.n)]
                  P.op("vector", lambda e, c=c, mid2=mid2: e.tensor_tensor(
                      out=PBC[1][:, :], in0=mid2[:, c * TT:(c + 1) * TT],
                      in1=mb2[:], op=ALU.subtract),
                      waits=w_, inc=sv)
                  P.op("vector", lambda e, c=c: e.tensor_tensor(
                      out=xn2[:, c * TT:(c + 1) * TT], in0=PBC[1][:, :],
                      in1=rb2[:], op=ALU.mult), waits=[(sv, sv.n)], inc=sv)
              sv_xn2[t] = sv.n

          emit_loads(0, 0)
          for c in range(8):
              P.op("sync", lambda e, c=c: e.dma_start(
                  out=wex[:, c * HID:(c + 1) * HID],
                  in_=wexp_scr[c * 128:(c + 1) * 128, :]),
                  waits=[(qf, qf_t[0])] if c == 0 else [], inc=qf)
          QF_WEX = qf.n
          wcv = wcon_scr.rearrange("(B jj s) c -> (B jj) s c", B=32, jj=4, s=32)
          for g in range(32):
              P.op("sync", lambda e, g=g: e.dma_start(
                  out=wcn[:, g * 1024:(g + 1) * 1024], in_=wcv[:, g, :]), inc=qf2)
          QF_WCN = qf2.n
          emit_head(0, None)
          for t in range(NTIL):
              seq = t // TPS
              t0 = (t % TPS) * TT
              qft = (qf, qf2)[t % 2]
              mid2 = mid2_[t % 2]

              gmk = []
              for fc in range(32):
                  pp = HP[fc % 2]
                  for c in range(8):
                      w_ = []
                      if fc == 0 and c == 0:
                          w_ = [(qft, qf_t[t]), (sv, sv_xn2[t])]
                          if t == 0:
                              w_.append((qf, QF_WEX))
                      elif c == 0 and fc >= 2:
                          w_ = [(sa, gmk[fc - 2])]
                      P.op("tensor", lambda e, fc=fc, c=c, pp=pp: e.matmul(
                          pp[:, :],
                          wex[:, c * HID + fc * 128: c * HID + (fc + 1) * 128],
                          xn2[:, c * TT:(c + 1) * TT],
                          start=(c == 0), stop=(c == 7)),
                          waits=w_, inc=(sp if c == 7 else None))
                  P.op("scalar", lambda e, fc=fc, pp=pp: e.activation(
                      out=g_sb[:, fc * TT:(fc + 1) * TT], in_=pp[:, :], func=AF.Gelu,
                      bias=biasg[:, fc:fc + 1]), waits=[(sp, sp.n)], inc=sa)
                  gmk.append(sa.n)
              SA_G = sa.n
              SP_H = sp.n

              if t + 1 < NTIL:
                  qprev = (QF_WCN if t + 1 == 1 else
                           QF_WEX if t + 1 == 2 else qf_t[t - 1])
                  emit_loads(t + 1, gs_out_prev, qprev)
                  emit_head(t + 1, SP_H)

              for oh in range(2):
                  for fc in range(32):
                      for osub in range(4):
                          ot = oh * 4 + osub
                          w_ = []
                          if fc == 0 and osub == 0:
                              w_ = [(sa, SA_G)] if oh == 0 else [(sv, sv.n)]
                              if t == 0 and oh == 0:
                                  w_.append((qf2, QF_WCN))
                          P.op("tensor", lambda e, fc=fc, ot=ot, osub=osub: e.matmul(
                              PO[osub][:, :],
                              wcn[:, fc * 1024 + ot * 128: fc * 1024 + (ot + 1) * 128],
                              g_sb[:, fc * TT:(fc + 1) * TT],
                              start=(fc == 0), stop=(fc == 31)),
                              waits=w_, inc=(sp if (fc == 31 and osub == 3) else None))
                  SP_O = sp.n
                  for osub in range(4):
                      ot = oh * 4 + osub
                      w_ = [(gout, gs_out_prev)] if osub % 2 == 0 else [(sv, sv.n)]
                      if osub == 0:
                          w_.append((sp, SP_O))
                      P.op("vector", lambda e, osub=osub, ot=ot, mid2=mid2: e.scalar_tensor_tensor(
                          out=o_sb[:, (osub % 2) * TT:((osub % 2) + 1) * TT],
                          in0=PO[osub][:, :],
                          scalar=bco[:, ot:ot + 1], in1=mid2[:, ot * TT:(ot + 1) * TT],
                          op0=ALU.add, op1=ALU.add),
                          waits=w_, inc=sv)
                      if osub % 2 == 1:
                          SV_O = sv.n
                          for oo in (osub - 1, osub):
                              ot2 = oh * 4 + oo
                              P.op("scalar", lambda e, oo=oo, ot2=ot2, seq=seq, t0=t0: e.dma_start(
                                  out=out_fm[seq, ot2 * 128:(ot2 + 1) * 128, t0:t0 + TT],
                                  in_=o_sb[:, (oo % 2) * TT:((oo % 2) + 1) * TT]),
                                  waits=[(sv, SV_O)] if oo == osub - 1 else [], inc=gout)
                          gs_out_prev = gout.n
              S["sv_evac"] = sv.n

          P.wait("scalar", (gout, gout.n))

    P.wait("sync", (qc, qc.n), (qx0, qx0.n), (qx1, qx1.n), (qp, qp.n),
           (qp2, qp2.n), (qf, qf.n), (qf2, qf2.n))
    P.wait("scalar", (gs, gs.n), (gso, gso.n), (gstat, gstat.n), (gmid, gmid.n),
           (gout, gout.n))

    with nc.allow_low_precision(
            reason="intentional bf16 operand rounding (matmul inputs / 1/Z "
                   "scale); all accumulation stays fp32 in PSUM"), \
         nc.Block() as block:
        @block.sync
        def _(e):
            for fn in P.q["sync"]:
                fn(e)

        @block.tensor
        def _(e):
            for fn in P.q["tensor"]:
                fn(e)

        @block.vector
        def _(e):
            for fn in P.q["vector"]:
                fn(e)

        @block.scalar
        def _(e):
            for fn in P.q["scalar"]:
                fn(e)

        @block.gpsimd
        def _(e):
            for fn in P.q["gpsimd"]:
                fn(e)

    es.close()
    return nc



def make_consts():
    cf = np.zeros((128, 132), np.float32)
    cf[:, 0:128] = np.eye(128, dtype=np.float32)
    cf[:, 128] = EPS
    cbm = np.zeros((128, 132), np.float32)
    cbm[:, 0:128] = np.eye(128)
    cbm[:, 128] = 1.0
    rf = np.zeros((4, 128), np.float32)
    rf[0, :] = 1.0
    rf[1, 0:64] = 1.0
    rf[2, 64:128] = 1.0
    rb2 = np.zeros((2, 128), np.float32)
    rb2[0, 0:64] = 1.0
    rb2[1, 64:128] = 1.0
    ef = np.zeros((128, 128), np.float32)
    ef[0, 0:64] = 1.0
    ef[64, 64:128] = 1.0
    cb2 = np.zeros((128, 4), np.float32)
    cb2[:, 0] = 1.0
    return (cf, cbm.astype(ml_dtypes.bfloat16), rf,
            rb2.astype(ml_dtypes.bfloat16), ef, cb2.astype(ml_dtypes.bfloat16))


def make_inputs(L0_attn_w_attn, L0_ffn_W_exp, L0_ffn_b_exp, L0_ffn_up_weights,
                L0_ffn_up_bias, L0_ffn_down_weights, L0_ffn_W_con, L0_ffn_b_con,
                L0_ffn_down_bias, L0_attn_w_proj):
    SA_u, SQ_u = build_stage_mats(np.asarray(L0_ffn_up_weights, np.float32))
    SA_d, SQ_d = build_stage_mats(np.asarray(L0_ffn_down_weights, np.float32))
    cfc, cbb, rf, rb2, ef, cb2 = make_consts()
    fmap = f_of_hprime()
    return {
        "w_attnT": np.ascontiguousarray(np.asarray(L0_attn_w_attn, np.float32).T),
        "w_projT": np.ascontiguousarray(np.asarray(L0_attn_w_proj, np.float32).T),
        "w_expT": np.ascontiguousarray(np.asarray(L0_ffn_W_exp, np.float32).T),
        "w_con": np.ascontiguousarray(np.asarray(L0_ffn_W_con, np.float32)),
        "b_exp": np.asarray(L0_ffn_b_exp, np.float32),
        "ub_perm": np.ascontiguousarray(np.asarray(L0_ffn_up_bias, np.float32)[fmap]),
        "b_con": np.asarray(L0_ffn_b_con, np.float32),
        "d_bias": np.asarray(L0_ffn_down_bias, np.float32),
        "SAu": np.ascontiguousarray(SA_u), "SQu": np.ascontiguousarray(SQ_u),
        "SAdT": np.ascontiguousarray(SA_d.transpose(0, 1, 3, 2)),
        "SQdT": np.ascontiguousarray(SQ_d.transpose(0, 1, 3, 2)),
        "cf_in": cfc, "cb_in": cbb, "rf_in": rf, "rb_in": rb2, "ef_in": ef,
        "cb2_in": cb2,
    }


def kernel(L0_attn_x, L0_attn_w_attn, L0_attn_w_proj, L0_ffn_W_exp, L0_ffn_b_exp,
           L0_ffn_up_weights, L0_ffn_up_bias, L0_ffn_down_weights, L0_ffn_W_con,
           L0_ffn_b_con, L0_ffn_down_bias):
    x = np.asarray(L0_attn_x, dtype=np.float32)
    B = x.shape[0]
    n_seq = B // N_CORES
    shared = make_inputs(L0_attn_w_attn, L0_ffn_W_exp, L0_ffn_b_exp,
                         L0_ffn_up_weights, L0_ffn_up_bias, L0_ffn_down_weights,
                         L0_ffn_W_con, L0_ffn_b_con, L0_ffn_down_bias,
                         L0_attn_w_proj)
    nc = build_program(n_seq=n_seq)
    in_maps = []
    for core in range(N_CORES):
        m = dict(shared)
        m["x_fm"] = np.ascontiguousarray(
            x[core * n_seq:(core + 1) * n_seq].transpose(0, 2, 1))
        in_maps.append(m)
    res = run_bass_kernel_spmd(nc, in_maps, core_ids=list(range(N_CORES)))
    outs = [np.ascontiguousarray(res.results[c]["out_fm"].transpose(0, 2, 1))
            for c in range(N_CORES)]
    return np.concatenate(outs, axis=0)


# version 18
# speedup vs baseline: 1.3202x; 1.0315x over previous
import contextlib
import numpy as np
import ml_dtypes
import concourse.bass as bass
import concourse.mybir as mybir
from concourse.bass_utils import run_bass_kernel_spmd

F32 = mybir.dt.float32
BF16 = mybir.dt.bfloat16
AF = mybir.ActivationFunctionType
ALU = mybir.AluOpType
AXC = mybir.AxisListType.C

N_CORES = 8
D = 1024
HID = 4096
NH = 16
WIN = 256
TSEQ = 2048
TT = 512
EPS = 1e-5



def build_stage_mats(W):
    SA = np.zeros((7, 32, 128, 128), np.float32)
    l = np.arange(128)
    for st in range(7):
        s = 1 << st
        q = l >> (st + 1)
        half = (l >> st) & 1
        r = l & (s - 1)
        kloc = q * s + r
        part = l + np.where(half == 0, s, -s)
        for B in range(32):
            k = B * 64 + kloc
            SA[st, B, l, l] = W[st, k, half, half]
            SA[st, B, l, part] = W[st, k, half, 1 - half]
    SQ = np.zeros((5, 32, 128, 128), np.float32)
    p = np.arange(128)
    Bp = p >> 2
    jj = p & 3
    for st in range(7, 12):
        s = 1 << st
        sb = 1 << (st - 7)
        for g in range(32):
            f = Bp * 128 + jj * 32 + g
            half = (f >> st) & 1
            k = (f >> (st + 1)) * s + (f & (s - 1))
            partner = p + np.where(half == 0, sb * 4, -sb * 4)
            SQ[st - 7, g, p, p] = W[st, k, half, half]
            SQ[st - 7, g, p, partner] = W[st, k, half, 1 - half]
    return SA, SQ


def f_of_hprime():
    hp = np.arange(HID)
    g = hp >> 7
    p = hp & 127
    return (p >> 2) * 128 + (p & 3) * 32 + g



ENGS = ("sync", "tensor", "vector", "scalar", "gpsimd")


class SemCtr:
    __slots__ = ("sem", "step", "n")

    def __init__(self, sem, step):
        self.sem = sem
        self.step = step
        self.n = 0


class Plan:
    def __init__(self):
        self.q = {e: [] for e in ENGS}

    def op(self, eng, build, waits=(), inc=None, inck=1):
        wl = tuple((sc.sem, v) for sc, v in waits if v > 0)
        amt = None
        sem = None
        if inc is not None:
            amt = inck * inc.step
            inc.n += amt
            sem = inc.sem

        def fn(e, wl=wl, build=build, amt=amt, sem=sem):
            for s, v in wl:
                e.wait_ge(s, v)
            inst = build(e)
            if amt is not None:
                inst.then_inc(sem, amt)
        self.q[eng].append(fn)

    def wait(self, eng, *waits):
        wl = tuple((sc.sem, v) for sc, v in waits if v > 0)

        def fn(e, wl=wl):
            for s, v in wl:
                e.wait_ge(s, v)
        self.q[eng].append(fn)



def build_program(n_seq=4, phases="paf", attn_tiles=None, attn_stage=9, win_stage=9, skip_prep=False, win_heads=4, win_wins=None, prep_stage=9):
    nc = bass.Bass("TRN2", target_bir_lowering=False)
    NTOK = n_seq * TSEQ
    NTIL = NTOK // TT
    TPS = TSEQ // TT
    WPT = TT // WIN

    dt = nc.dram_tensor
    x_fm = dt("x_fm", [n_seq, D, TSEQ], F32, kind="ExternalInput").ap()
    w_attnT = dt("w_attnT", [D, 3 * D], F32, kind="ExternalInput").ap()
    w_projT = dt("w_projT", [D, D], F32, kind="ExternalInput").ap()
    w_expT = dt("w_expT", [HID, D], F32, kind="ExternalInput").ap()
    w_con = dt("w_con", [HID, D], F32, kind="ExternalInput").ap()
    b_exp = dt("b_exp", [HID], F32, kind="ExternalInput").ap()
    ub_perm = dt("ub_perm", [HID], F32, kind="ExternalInput").ap()
    b_con = dt("b_con", [D], F32, kind="ExternalInput").ap()
    d_bias = dt("d_bias", [D], F32, kind="ExternalInput").ap()
    SAu = dt("SAu", [7, 32, 128, 128], F32, kind="ExternalInput").ap()
    SQu = dt("SQu", [5, 32, 128, 128], F32, kind="ExternalInput").ap()
    SAdT = dt("SAdT", [7, 32, 128, 128], F32, kind="ExternalInput").ap()
    SQdT = dt("SQdT", [5, 32, 128, 128], F32, kind="ExternalInput").ap()
    cf_in = dt("cf_in", [128, 132], F32, kind="ExternalInput").ap()
    cb_in = dt("cb_in", [128, 132], BF16, kind="ExternalInput").ap()
    cb2_in = dt("cb2_in", [128, 4], BF16, kind="ExternalInput").ap()
    rf_in = dt("rf_in", [4, 128], F32, kind="ExternalInput").ap()
    ef_in = dt("ef_in", [128, 128], F32, kind="ExternalInput").ap()
    rb_in = dt("rb_in", [2, 128], BF16, kind="ExternalInput").ap()

    out_fm = dt("out_fm", [n_seq, D, TSEQ], F32, kind="ExternalOutput").ap()

    mid_scr = dt("mid_scr", [n_seq, D, TSEQ], F32, kind="Internal").ap()
    p_scr = dt("p_scr", [HID, 1028], F32, kind="Internal").ap()
    wexp_scr = dt("wexp_scr", [D, HID], BF16, kind="Internal").ap()
    wcon_scr = dt("wcon_scr", [HID, D], BF16, kind="Internal").ap()
    stats_scr = dt("stats_scr", [2, NTOK], F32, kind="Internal").ap()

    es = contextlib.ExitStack()
    sb = lambda name, shape, dtp: es.enter_context(nc.sbuf_tensor(name, shape, dtp))
    sem = lambda name, step: SemCtr(es.enter_context(nc.semaphore(name)), step)

    cf = sb("cf", [128, 132], F32)
    cbc = sb("cbc", [128, 4], BF16)
    orow = sb("orow", [33, 128], F32)
    e2f = sb("e2f", [128, 128], F32)
    biasg = sb("biasg", [128, 32], F32)
    bex = sb("bex", [128, 32], F32)
    bco = sb("bco", [128, 8], F32)

    qc = sem("qc", 16)
    qx0 = sem("qx0", 16)
    qx1 = sem("qx1", 16)
    qp = sem("qp", 16)
    qp2 = sem("qp2", 16)
    gso = sem("gso", 16)
    qf = sem("qf", 16)
    qf2 = sem("qf2", 16)
    gs = sem("gs", 16)
    gstat = sem("gstat", 16)
    gmid = sem("gmid", 16)
    gout = sem("gout", 16)
    sp = sem("sp", 1)
    sv = sem("sv", 1)
    sa = sem("sa", 1)
    sg = sem("sg", 1)

    P = Plan()

    P.op("sync", lambda e: e.dma_start(out=cf[:], in_=cf_in[:, :]), inc=qc)
    P.op("sync", lambda e: e.dma_start(out=cbc[:], in_=cb2_in[:, :]), inc=qc)
    P.op("sync", lambda e: e.dma_start(out=orow[0:1, :], in_=rf_in[0:1, :]), inc=qc)
    P.op("sync", lambda e: e.dma_start(out=orow[32:33, :], in_=rf_in[0:1, :]), inc=qc)
    P.op("sync", lambda e: e.dma_start(out=e2f[:], in_=ef_in[:, :]), inc=qc)
    QC0 = qc.n
    ID = cf[:, 0:128]
    EPSC = cf[:, 128:129]
    ONEB = cbc[:, 0:1]

    wes = contextlib.ExitStack()
    waT = wes.enter_context(nc.sbuf_tensor("waT", [128, 8 * 3072], BF16))
    wpT = wes.enter_context(nc.sbuf_tensor("wpT", [128, 8 * 1024], BF16))
    x_sb = [wes.enter_context(nc.sbuf_tensor(f"x{i}", [128, 8 * TT], F32))
            for i in range(2)]
    qw = (qx0, qx1)
    cast_mark = [0, 0]
    for c in range(8):
        sl = c % 2
        P.op("sync", lambda e, c=c, sl=sl: e.dma_start(
            out=x_sb[sl][:, 0:3072], in_=w_attnT[c * 128:(c + 1) * 128, :]),
            waits=[(sv, cast_mark[sl])], inc=qw[sl])
        P.op("vector", lambda e, c=c, sl=sl: e.tensor_copy(
            waT[:, c * 3072:(c + 1) * 3072], x_sb[sl][:, 0:3072]),
            waits=[(qw[sl], qw[sl].n)], inc=sv)
        cast_mark[sl] = sv.n
    for c in range(8):
        sl = c % 2
        P.op("sync", lambda e, c=c, sl=sl: e.dma_start(
            out=x_sb[sl][:, 0:1024], in_=w_projT[c * 128:(c + 1) * 128, :]),
            waits=[(sv, cast_mark[sl])], inc=qw[sl])
        P.op("vector", lambda e, c=c, sl=sl: e.tensor_copy(
            wpT[:, c * 1024:(c + 1) * 1024], x_sb[sl][:, 0:1024]),
            waits=[(qw[sl], qw[sl].n)], inc=sv)
        cast_mark[sl] = sv.n
    SV_W = sv.n
    X_LD = [0, 0]
    for tpre in range(2):
        for c in range(8):
            P.op("sync", lambda e, tpre=tpre, c=c: e.dma_start(
                out=x_sb[tpre][:, c * TT:(c + 1) * TT],
                in_=x_fm[tpre // TPS, c * 128:(c + 1) * 128,
                         (tpre % TPS) * TT:(tpre % TPS) * TT + TT]),
                waits=[(sv, cast_mark[tpre])] if c == 0 else [], inc=qw[tpre])
        X_LD[tpre] = qw[tpre].n

    with contextlib.ExitStack() as pes:
      if not skip_prep:
        psb = lambda n, s, d: pes.enter_context(nc.sbuf_tensor(n, s, d))
        pps = lambda n, s, d: pes.enter_context(nc.psum_tensor(n, s, d))
        sall = [psb(f"sall{i}", [128, 32 * 128], F32) for i in range(2)]
        gall = [psb(f"gall{i}", [128, 32 * 128], F32) for i in range(2)]
        svin2 = [psb(f"svin{i}", [128, 1028], F32) for i in range(2)]
        svout2 = [psb(f"svout{i}", [128, 1028], F32) for i in range(2)]
        sv2b2 = [psb(f"sv2b{i}", [128, 1024], BF16) for i in range(2)]
        swx2 = [psb(f"swx{i}", [128, 1024], BF16) for i in range(2)]
        bx = psb("bx", [128, 96], F32)
        idbp = psb("idbp", [128, 128], BF16)
        P.op("sync", lambda e: e.dma_start(out=idbp[:], in_=cb_in[:, 0:128]), inc=qc)
        QC0b = qc.n
        QC0 = QC0b
        PP = [pps(f"PP{i}", [128, 512], F32) for i in range(3)]
        PQ2 = [pps(f"PQ2{i}", [128, 512], F32) for i in range(3)]
        PPT = [pps(f"PPT{i}", [128, 512], BF16) for i in range(2)]

        state = {"sall_mark": [0, 0], "gall_guard": 0, "ppmark": {},
                 "chain_end": 0}

        def batched_loop(stages_ap, n_st, reverse, emit_load, ncols, emit_out,
                         gslot_final_holder):
            order = list(range(n_st))
            if reverse:
                order = order[::-1]
            for i, j in enumerate(order):
                sslot = i % 2
                gsl = i % 2
                sv_ = stages_ap[j].rearrange("b p c -> p b c")
                qsl = (qp, qp2)[sslot]
                for part in range(8):
                    P.op("sync", lambda e, sv_=sv_, sslot=sslot, part=part: e.dma_start(
                        out=sall[sslot][:, :].rearrange("p (b c) -> p b c", b=32)[
                            :, part * 4:(part + 1) * 4, :],
                        in_=sv_[:, part * 4:(part + 1) * 4, :]),
                        waits=([(sp, state["sall_mark"][sslot]), (qc, QC0)]
                               if part == 0 else []), inc=qsl)
                ld = (qsl, qsl.n)
                for b in range(32):
                    tile = PP[b % 3][:, 0:128]
                    w_ = []
                    if b == 0:
                        w_ = [ld]
                        if i == 0:
                            w_.append((sv, sv.n))
                            w_.append((sa, sa.n))
                            w_.append((sp, state["gall_guard"]))
                            w_.append((qc, QC0))
                    pm = state["ppmark"].get(b % 3)
                    if pm is not None:
                        w_ = w_ + [pm]
                    if i == 0:
                        P.op("tensor", lambda e, b=b, tile=tile, sslot=sslot: e.matmul(
                            tile, sall[sslot][:, b * 128:(b + 1) * 128], ID,
                            start=True, stop=True), waits=w_, inc=sp)
                    else:
                        P.op("tensor", lambda e, b=b, tile=tile, sslot=sslot, gsl=gsl: e.matmul(
                            tile, sall[sslot][:, b * 128:(b + 1) * 128],
                            gall[1 - gsl][:, b * 128:(b + 1) * 128],
                            start=True, stop=True), waits=w_, inc=sp)
                    if b % 2 == 0:
                        P.op("vector", lambda e, b=b, tile=tile, gsl=gsl: e.tensor_copy(
                            gall[gsl][:, b * 128:(b + 1) * 128], tile),
                            waits=[(sp, sp.n)], inc=sv)
                        state["ppmark"][b % 3] = (sv, sv.n)
                    else:
                        P.op("scalar", lambda e, b=b, tile=tile, gsl=gsl: e.copy(
                            out=gall[gsl][:, b * 128:(b + 1) * 128], in_=tile),
                            waits=[(sp, sp.n)], inc=sa)
                        state["ppmark"][b % 3] = (sa, sa.n)
                state["sall_mark"][sslot] = sp.n
            gfin = (n_st - 1) % 2
            SV_CH, SA_CH = sv.n, sa.n
            state["chain_end"] = sp.n
            state.pop("apply_mm0", None)
            state.pop("apply_mm1", None)
            for b in range(32 if prep_stage >= 2 else 0):
                slot = b % 2
                qsl = (qp, qp2)[slot]
                emit_load(b, slot, qsl,
                          [(sp, state.get(f"apply_mm{slot}", state["chain_end"]))])
                ld = (qsl, qsl.n)
                splits = []
                c0 = 0
                for pi, pq in enumerate(PQ2):
                    if c0 >= ncols:
                        break
                    n = min(512, ncols - c0)
                    splits.append((pq, c0, n, slot))
                    c0 += n
                for k, (pq, c0, n, sl) in enumerate(splits):
                    w_ = []
                    if k == 0:
                        w_ = [ld, (sv, SV_CH), (sa, SA_CH)]
                        if state.get("pqmark"):
                            w_.append(state["pqmark"])
                    P.op("tensor", lambda e, pq=pq, c0=c0, n=n, b=b, sl=sl, gfin=gfin: e.matmul(
                        pq[:, (sl) * 0: n] if False else pq[:, 0:n],
                        gall[gfin][:, b * 128:(b + 1) * 128],
                        svin2[sl][:, c0:c0 + n], start=True, stop=True),
                        waits=w_, inc=(sp if k == len(splits) - 1 else None))
                state[f"apply_mm{slot}"] = sp.n
                for ki, (pq, c0, n, sl) in enumerate(splits):
                    w_ = [(sp, sp.n)]
                    if ki == 0 and state.get(f"outst{slot}"):
                        w_.append(state[f"outst{slot}"])
                    P.op("vector", lambda e, pq=pq, c0=c0, n=n, sl=sl: e.tensor_copy(
                        svout2[sl][:, c0:c0 + n], pq[:, 0:n]),
                        waits=w_, inc=sv)
                state["pqmark"] = (sv, sv.n)
                state[f"apply_ev{slot}"] = sv.n
                if prep_stage >= 3:
                    emit_out(b, slot)
            state["gall_guard"] = sp.n
            state["sall_mark"] = [sp.n, sp.n]

        def ld_upA(b, slot, qsl, w_):
            P.op("sync", lambda e, b=b, slot=slot: e.dma_start(
                out=svin2[slot][:, 0:1024], in_=w_expT[b * 128:(b + 1) * 128, :]),
                waits=w_, inc=qsl)
            P.op("sync", lambda e, b=b, slot=slot: e.dma_start(
                out=svin2[slot][:, 1024:1025],
                in_=b_exp[b * 128:(b + 1) * 128].unsqueeze(1)), inc=qsl)

        def out_upA(b, slot):
            gq = (gs, gso)[slot]
            P.op("scalar", lambda e, b=b, slot=slot: e.dma_start(
                out=p_scr[b * 128:(b + 1) * 128, 0:1025], in_=svout2[slot][:, 0:1025]),
                waits=[(sv, sv.n)], inc=gq)
            state[f"outst{slot}"] = (gq, gq.n)

        batched_loop(SAu, 7, True, ld_upA, 1025, out_upA, None)
        P.wait("sync", (gs, gs.n), (gso, gso.n))

        p_view = p_scr.rearrange("(B jj s) c -> (B jj) s c", B=32, jj=4, s=32)

        def ld_upQ(g, slot, qsl, w_):
            P.op("sync", lambda e, g=g, slot=slot: e.dma_start(
                out=svin2[slot][:, 0:1025], in_=p_view[:, g, 0:1025]),
                waits=w_, inc=qsl)

        def out_upQ(g, slot):
            P.op("vector", lambda e, g=g, slot=slot: e.tensor_copy(
                bex[:, g:g + 1], svout2[slot][:, 1024:1025]),
                waits=[(sv, sv.n)], inc=sv)
            P.op("vector", lambda e, slot=slot: e.tensor_copy(
                sv2b2[slot][:], svout2[slot][:, 0:1024]),
                waits=[(sv, sv.n), (sp, state.get(f"tp{slot}", 0))], inc=sv)
            SV_CAST = sv.n
            for ci in range(8):
                pt = PPT[ci % 2][:, 0:128]
                w_ = [(sv, SV_CAST)] if ci == 0 else []
                ptm = state.setdefault("ptmark", {}).get(ci % 2)
                if ptm is not None:
                    w_ = w_ + [ptm]
                else:
                    w_ = w_ + [(qc, QC0b)]
                P.op("tensor", lambda e, ci=ci, slot=slot, pt=pt: e.transpose(
                    pt, sv2b2[slot][:, ci * 128:(ci + 1) * 128], idbp[:]),
                    waits=w_, inc=sp)
                sw = state.get(f"swxst{slot}")
                P.op("scalar", lambda e, ci=ci, slot=slot, pt=pt: e.copy(
                    out=swx2[slot][:, ci * 128:(ci + 1) * 128], in_=pt),
                    waits=[(sp, sp.n)] + ([sw] if ci == 0 and sw else []), inc=sa)
                state["ptmark"][ci % 2] = (sa, sa.n)
            state[f"tp{slot}"] = sp.n
            SA_SWX = sa.n
            P.op("scalar", lambda e, g=g, slot=slot: e.dma_start(
                out=wexp_scr.rearrange("(cc p) f -> p (cc f)", p=128)[
                    :, 0:0] if False else wexp_scr[0:128, 0:128],
                in_=swx2[slot][:, 0:128]), waits=[], inc=gs) if False else None
            gq = (gs, gso)[slot]
            for ci in range(8):
                P.op("scalar", lambda e, ci=ci, g=g, slot=slot: e.dma_start(
                    out=wexp_scr[ci * 128:(ci + 1) * 128, g * 128:(g + 1) * 128],
                    in_=swx2[slot][:, ci * 128:(ci + 1) * 128]),
                    waits=[(sa, SA_SWX)] if ci == 0 else [], inc=gq)
            state[f"swxst{slot}"] = (gq, gq.n)
            state[f"outst{slot}"] = (sv, SV_CAST)

        batched_loop(SQu, 5, True, ld_upQ, 1025, out_upQ, None)
        P.wait("sync", (gs, gs.n), (gso, gso.n))

        w_view = w_con.rearrange("(B jj s) c -> (B jj) s c", B=32, jj=4, s=32)
        pv2 = p_scr.rearrange("(B jj s) c -> (B jj) s c", B=32, jj=4, s=32)

        def ld_dnQ(g, slot, qsl, w_):
            P.op("sync", lambda e, g=g, slot=slot: e.dma_start(
                out=svin2[slot][:, 0:1024], in_=w_view[:, g, :]),
                waits=w_, inc=qsl)

        def out_dnQ(g, slot):
            gq = (gs, gso)[slot]
            P.op("scalar", lambda e, g=g, slot=slot: e.dma_start(
                out=pv2[:, g, 0:1024], in_=svout2[slot][:, 0:1024]),
                waits=[(sv, sv.n)], inc=gq)
            state[f"outst{slot}"] = (gq, gq.n)

        batched_loop(SQdT, 5, False, ld_dnQ, 1024, out_dnQ, None)
        P.wait("sync", (gs, gs.n), (gso, gso.n))

        def ld_dnA(b, slot, qsl, w_):
            P.op("sync", lambda e, b=b, slot=slot: e.dma_start(
                out=svin2[slot][:, 0:1024], in_=p_scr[b * 128:(b + 1) * 128, 0:1024]),
                waits=w_, inc=qsl)

        def out_dnA(b, slot):
            wc = state.get(f"wcst{slot}")
            P.op("vector", lambda e, slot=slot: e.tensor_copy(
                sv2b2[slot][:], svout2[slot][:, 0:1024]),
                waits=[(sv, sv.n)] + ([wc] if wc else []), inc=sv)
            gq = (gs, gso)[slot]
            P.op("scalar", lambda e, b=b, slot=slot: e.dma_start(
                out=wcon_scr[b * 128:(b + 1) * 128, :], in_=sv2b2[slot][:]),
                waits=[(sv, sv.n)], inc=gq)
            state[f"wcst{slot}"] = (gq, gq.n)
            state[f"outst{slot}"] = (sv, sv.n)

        batched_loop(SAdT, 7, False, ld_dnA, 1024, out_dnA, None)
        P.wait("sync", (gs, gs.n), (gso, gso.n))

        uview = ub_perm.rearrange("(c p) -> c p", p=128)
        for c in range(32):
            P.op("sync", lambda e, c=c: e.dma_start(
                out=bx[:, 32 + c:33 + c], in_=uview[c, :].unsqueeze(1)), inc=qp)
        bcv = b_con.rearrange("(c p) -> c p", p=128)
        dbv = d_bias.rearrange("(c p) -> c p", p=128)
        for c in range(8):
            P.op("sync", lambda e, c=c: e.dma_start(
                out=bx[:, 64 + c:65 + c], in_=bcv[c, :].unsqueeze(1)), inc=qp)
            P.op("sync", lambda e, c=c: e.dma_start(
                out=bx[:, 72 + c:73 + c], in_=dbv[c, :].unsqueeze(1)), inc=qp)
        P.op("vector", lambda e: e.tensor_tensor(
            out=biasg[:], in0=bex[:], in1=bx[:, 32:64], op=ALU.add),
            waits=[(qp, qp.n), (sv, sv.n)], inc=sv)
        P.op("vector", lambda e: e.tensor_tensor(
            out=bco[:], in0=bx[:, 64:72], in1=bx[:, 72:80], op=ALU.add), inc=sv)

    SP_PREP, SV_PREP, SA_PREP, GS_PREP = sp.n, sv.n, sa.n, gs.n

    if "p" == phases:
        with contextlib.ExitStack() as des:
            db = des.enter_context(nc.sbuf_tensor("db", [128, 1024], BF16))
            df = des.enter_context(nc.sbuf_tensor("df", [128, 1024], F32))
            qd = SemCtr(des.enter_context(nc.semaphore("qd")), 16)
            for i in range(16):
                P.op("sync", lambda e, i=i: e.dma_start(
                    out=db[:], in_=wcon_scr[i * 128:(i + 1) * 128, :]),
                    waits=[(gs, gs.n), (sv, sv.n)], inc=qd)
                P.op("vector", lambda e, i=i: e.tensor_copy(df[:], db[:]),
                     waits=[(qd, qd.n), (gout, gout.n)], inc=sv)
                P.op("scalar", lambda e, i=i: e.dma_start(
                    out=out_fm[0, (i % 8) * 128:(i % 8 + 1) * 128,
                               (i // 8) * 1024:(i // 8 + 1) * 1024],
                    in_=df[:]), waits=[(sv, sv.n)], inc=gout)
                P.wait("sync", (gout, gout.n))
            P.wait("scalar", (gout, gout.n))
    attn_on = "a" in phases
    with contextlib.ExitStack() as aes:
      if attn_on:
          asb = lambda n, s, d: aes.enter_context(nc.sbuf_tensor(n, s, d))
          aps = lambda n, s, d: aes.enter_context(nc.psum_tensor(n, s, d))
          xbt = asb("xbt", [128, 8 * TT], BF16)
          x2t = asb("x2t", [128, 8 * TT], BF16)
          xnt = asb("xnt", [128, 8 * TT], BF16)
          qkt = asb("qkt", [128, 16 * TT], BF16)
          v_sb = asb("v_sb", [128, 8 * TT], BF16)
          y_sb = asb("y_sb", [128, 8 * TT], BF16)
          ett = asb("ett", [128, 2 * 1024], BF16)
          midt = asb("midt", [128, 8 * TT], F32)
          bgt = asb("bgt", [128, 512], F32)
          mrow = asb("mrow", [1, 4 * TT], F32)
          mrow2 = asb("mrow2", [1, 4 * TT], F32)
          zz = asb("zz", [128, 1024], F32)
          P.op("vector", lambda e: e.memset(zz[:], 0.0),
               waits=[(sv, sv.n)], inc=sv)
          mbt = asb("mbt", [128, TT], BF16)
          rbt = asb("rbt", [128, TT], BF16)
          PA = aps("PA", [128, 1024], F32)
          PB = aps("PB", [128, 1024], F32)
          PC = aps("PC", [128, 512], F32)
          PD = aps("PD", [128, 512], F32)
          PE_ = aps("PE_", [128, 512], F32)
          PF = aps("PF", [128, 512], F32)

          prev = dict(sp_st2=SP_PREP, sa_c2=SA_PREP, gs_mid=0,
                      gs_stat=0, sv_v2=SV_PREP, sp_vend=SP_PREP,
                      sv_mid=SV_W)

          def emit_ln1_head(u, ldx, sp_vend_cur, sp_guard):
              qxu = (qx0, qx1)[u % 2]
              xsu = x_sb[u % 2]
              for c in range(8):
                  w_ = ([(qxu, ldx), (sp, sp_guard), (sv, sv.n)]
                        if c == 0 else [])
                  P.op("vector", lambda e, c=c, xsu=xsu: e.tensor_copy(
                      xbt[:, c * TT:(c + 1) * TT], xsu[:, c * TT:(c + 1) * TT]),
                      waits=w_, inc=sv)
                  P.op("vector", lambda e, c=c, xsu=xsu: e.tensor_tensor(
                      out=x2t[:, c * TT:(c + 1) * TT],
                      in0=xsu[:, c * TT:(c + 1) * TT],
                      in1=xsu[:, c * TT:(c + 1) * TT], op=ALU.mult), inc=sv)
              SV_CASTU = sv.n
              for c in range(8):
                  P.op("tensor", lambda e, c=c: e.matmul(
                      PC[0:1, 0:TT], ONEB, xbt[:, c * TT:(c + 1) * TT],
                      start=(c == 0), stop=(c == 7)),
                      waits=([(sv, SV_CASTU)] if c == 0 else []),
                      inc=(sp if c == 7 else None))
              for c in range(8):
                  P.op("tensor", lambda e, c=c: e.matmul(
                      PD[0:1, 0:TT], ONEB, x2t[:, c * TT:(c + 1) * TT],
                      start=(c == 0), stop=(c == 7)), inc=(sp if c == 7 else None))
              SP_STATU = sp.n
              P.op("vector", lambda e: e.tensor_scalar(
                  out=mrow[:, 0:TT], in0=PC[0:1, 0:TT], scalar1=1.0 / D,
                  scalar2=None, op0=ALU.mult),
                  waits=[(sp, SP_STATU), (gstat, prev["gs_stat"])], inc=sv)
              P.op("vector", lambda e: e.tensor_scalar(
                  out=mrow[:, 2 * TT:3 * TT], in0=PD[0:1, 0:TT], scalar1=1.0 / D,
                  scalar2=None, op0=ALU.mult), waits=[(sv, sv.n)], inc=sv)
              P.op("vector", lambda e: e.tensor_tensor(
                  out=mrow[:, 3 * TT:4 * TT], in0=mrow[:, 0:TT],
                  in1=mrow[:, 0:TT], op=ALU.mult), waits=[(sv, sv.n)], inc=sv)
              P.op("vector", lambda e: e.tensor_tensor(
                  out=mrow[:, 2 * TT:3 * TT], in0=mrow[:, 2 * TT:3 * TT],
                  in1=mrow[:, 3 * TT:4 * TT], op=ALU.subtract),
                  waits=[(sv, sv.n)], inc=sv)
              P.op("scalar", lambda e: e.activation(
                  out=mrow[:, 3 * TT:4 * TT], in_=mrow[:, 2 * TT:3 * TT],
                  func=AF.Ln, bias=EPSC[0:1, :]), waits=[(sv, sv.n)], inc=sa)
              P.op("scalar", lambda e: e.activation(
                  out=mrow[:, TT:2 * TT], in_=mrow[:, 3 * TT:4 * TT],
                  func=AF.Exp, scale=-0.5), waits=[(sa, sa.n)], inc=sa)
              SA_LNU = sa.n
              P.op("tensor", lambda e: e.matmul(
                  PE_[:, 0:TT], orow[0:1, :], mrow[0:1, 0:TT], start=True,
                  stop=True), waits=[(sa, SA_LNU), (sv, prev["sv_mid"])], inc=sp)
              P.op("tensor", lambda e: e.matmul(
                  PF[:, 0:TT], orow[0:1, :], mrow[0:1, TT:2 * TT], start=True,
                  stop=True), inc=sp)
              SP_BCU = sp.n
              P.op("vector", lambda e: e.tensor_copy(mbt[:], PE_[:, 0:TT]),
                   waits=[(sp, SP_BCU)], inc=sv)
              P.op("vector", lambda e: e.tensor_copy(rbt[:], PF[:, 0:TT]), inc=sv)
              SV_BCU = sv.n
              for c in range(8):
                  P.op("vector", lambda e, c=c: e.tensor_tensor(
                      out=x2t[:, c * TT:(c + 1) * TT], in0=xbt[:, c * TT:(c + 1) * TT],
                      in1=mbt[:], op=ALU.subtract),
                      waits=([(sv, SV_BCU), (sp, SP_STATU)] if c == 0
                             else [(sv, sv.n)]), inc=sv)
                  P.op("vector", lambda e, c=c: e.tensor_tensor(
                      out=xnt[:, c * TT:(c + 1) * TT], in0=x2t[:, c * TT:(c + 1) * TT],
                      in1=rbt[:], op=ALU.mult),
                      waits=[(sv, sv.n)] + ([(sp, sp_vend_cur)] if c == 0 else []),
                      inc=sv)
              return SP_STATU, SV_BCU, sv.n

          for t in range(NTIL):
              seq = t // TPS
              t0 = (t % TPS) * TT
              qx = (qx0, qx1)[t % 2]
              xs = x_sb[t % 2]

              if t == 0:
                  SP_STAT0, SV_BC0, SV_XN0 = emit_ln1_head(
                      0, X_LD[0], prev["sp_vend"], prev["sp_st2"])
                  prev = dict(prev, sp_stat0=SP_STAT0, sv_bc=SV_BC0,
                              sv_xn=SV_XN0)
              SV_XN = prev["sv_xn"]

              pq = (PC, PD)
              pdm = []
              for ei in range(16):
                  pp = pq[ei % 2]
                  for c in range(8):
                      w_ = []
                      if ei == 0 and c == 0:
                          w_ = [(sv, SV_XN)]
                      elif c == 0 and ei >= 2:
                          w_ = [(sv, pdm[ei - 2])]
                      P.op("tensor", lambda e, ei=ei, c=c, pp=pp: e.matmul(
                          pp[:, 0:TT],
                          waT[:, c * 3072 + ei * 128: c * 3072 + (ei + 1) * 128],
                          xnt[:, c * TT:(c + 1) * TT],
                          start=(c == 0), stop=(c == 7)),
                          waits=w_, inc=(sp if c == 7 else None))
                  P.op("vector", lambda e, ei=ei, pp=pp: e.tensor_copy(
                      qkt[:, ei * TT:(ei + 1) * TT], pp[:, 0:TT]),
                      waits=[(sp, sp.n)], inc=sv)
                  pdm.append(sv.n)
              for ts4 in range(TT // 128):
                  for eh in range(2):
                      gi_ = 16 + ts4 * 2 + eh
                      pp = pq[(ts4 * 2 + eh) % 2]
                      for c in range(8):
                          P.op("tensor", lambda e, ts4=ts4, eh=eh, c=c, pp=pp: e.matmul(
                              pp[:, :],
                              xnt[:, c * TT + ts4 * 128: c * TT + (ts4 + 1) * 128],
                              waT[:, c * 3072 + 2048 + eh * 512: c * 3072 + 2048 + (eh + 1) * 512],
                              start=(c == 0), stop=(c == 7)),
                              waits=([(sv, pdm[gi_ - 2])] if c == 0 else []),
                              inc=(sp if c == 7 else None))
                      P.op("vector", lambda e, ts4=ts4, eh=eh, pp=pp: e.tensor_copy(
                          v_sb[:, ts4 * 1024 + eh * 512: ts4 * 1024 + (eh + 1) * 512],
                          pp[:, :]), waits=[(sp, sp.n)], inc=sv)
                      pdm.append(sv.n)
              SV_QKV = sv.n
              SP_VEND = sp.n

              if t + 1 < NTIL:
                  u = t + 1
                  qxu = (qx0, qx1)[u % 2]
                  if u >= 2:
                      xsu = x_sb[u % 2]
                      for c in range(8):
                          P.op("sync", lambda e, c=c, u=u, xsu=xsu: e.dma_start(
                              out=xsu[:, c * TT:(c + 1) * TT],
                              in_=x_fm[u // TPS, c * 128:(c + 1) * 128,
                                       (u % TPS) * TT:(u % TPS) * TT + TT]),
                              waits=[(sv, prev["sv_mid"])] if c == 0 else [],
                              inc=qxu)
                      ldx_u = qxu.n
                  else:
                      ldx_u = X_LD[1]
                  guard = prev["sp_stat0"] if u == 1 else prev["sp_st2"]
                  SP_STATU, SV_BCU, SV_XNU = emit_ln1_head(
                      u, ldx_u, SP_VEND, guard)
                  SV_BCW = SV_BCU
              else:
                  SV_BCW = prev["sv_bc"]
                  SV_XNU = prev["sv_xn"]

              if attn_stage < 5:
                  prev = dict(prev, sv_mid=sv.n, sa_c2=sa.n, sp_st2=sp.n,
                              sv_v2=sv.n, sp_vend=SP_VEND)
                  continue
              CS2 = {(0, 0): 0, (0, 1): 256, (1, 0): 512, (1, 1): 768}
              NIT = WPT * 8
              mk = {"st": {}, "st0": {}, "et": {-2: prev["sa_c2"], -1: prev["sa_c2"]},
                    "et2": {-2: prev["sa_c2"], -1: prev["sa_c2"]},
                    "z": {}, "rc": {-2: prev["sa_c2"], -1: prev["sa_c2"]},
                    "av": {}, "bg": {}, "bge": {-2: prev["sa_c2"], -1: prev["sa_c2"]},
                    "y": {-2: SV_BCW, -1: SV_BCW, -3: SV_BCW, -4: SV_BCW}}

              def emit_st(k):
                  w, gi = divmod(k, 8)
                  pp = (PA, PB)[k % 2]
                  first = (k == 0)
                  for idx, (hp, jh) in enumerate(((0, 0), (0, 1), (1, 0), (1, 1))):
                      h = 2 * gi + hp
                      qc_ = h // 2
                      qb = (h % 2) * 64
                      w_ = []
                      if idx == 0:
                          w_ = [(sa, mk["et"][k - 2]), (sa, mk["rc"][k - 2])]
                          if first:
                              w_.append((sv, SV_QKV))
                      elif idx == 2:
                          w_ = [(sa, mk["et2"][k - 2])]
                      P.op("tensor", lambda e, pp=pp, qc_=qc_, qb=qb, jh=jh, w=w, hp=hp: e.matmul(
                          pp[:, CS2[(hp, jh)]:CS2[(hp, jh)] + 256],
                          qkt[qb:qb + 64, (8 + qc_) * TT + w * 256 + jh * 128:
                              (8 + qc_) * TT + w * 256 + (jh + 1) * 128],
                          qkt[qb:qb + 64, qc_ * TT + w * 256: qc_ * TT + (w + 1) * 256],
                          start=True, stop=True),
                          waits=w_, inc=(sp if idx in (1, 3) else None))
                      if idx == 1:
                          mk["st0"][k] = sp.n
                  mk["st"][k] = sp.n

              def emit_exp(k):
                  pp = (PA, PB)[k % 2]
                  eb = (k % 2) * 1024
                  P.op("scalar", lambda e, pp=pp, eb=eb: e.activation(
                      out=ett[:, eb:eb + 512], in_=pp[:, 0:512], func=AF.Exp,
                      scale=0.125),
                      waits=[(sp, mk["st0"][k]), (sp, mk["av"].get(k - 2, 0))],
                      inc=sa)
                  mk["et"][k] = sa.n
                  P.op("scalar", lambda e, pp=pp, eb=eb: e.activation(
                      out=ett[:, eb + 512:eb + 1024], in_=pp[:, 512:1024],
                      func=AF.Exp, scale=0.125),
                      waits=[(sp, mk["st"][k])], inc=sa)
                  mk["et2"][k] = sa.n

              def emit_z(k):
                  zp = (PA, PB)[k % 2]
                  eb = (k % 2) * 1024
                  for idx, (hp, col) in enumerate(((0, 0), (1, 256))):
                      for jh in range(2):
                          w_ = []
                          if idx == 0 and jh == 0:
                              w_ = [(sa, mk["et"][k]), (sa, mk["rc"][k - 2])]
                          elif idx == 1 and jh == 0:
                              w_ = [(sa, mk["et2"][k])]
                          P.op("tensor", lambda e, zp=zp, col=col, eb=eb, hp=hp, jh=jh: e.matmul(
                              zp[0:1, col:col + 256], ONEB,
                              ett[:, eb + CS2[(hp, jh)]: eb + CS2[(hp, jh)] + 256],
                              start=(jh == 0), stop=(jh == 1)),
                              waits=w_, inc=(sp if (idx == 1 and jh == 1) else None))
                  mk["z"][k] = sp.n

              def emit_recip(k):
                  zp = (PA, PB)[k % 2]
                  sc = (k % 2) * 512
                  P.op("scalar", lambda e, zp=zp, sc=sc: e.activation(
                      out=zz[0:1, sc:sc + 256], in_=zp[0:1, 0:256], func=AF.Ln),
                      waits=[(sp, mk["z"][k]), (sp, mk["bg"].get(k - 2, 0))], inc=sa)
                  P.op("scalar", lambda e, zp=zp, sc=sc: e.activation(
                      out=zz[64:65, sc:sc + 256], in_=zp[0:1, 256:512], func=AF.Ln),
                      inc=sa)
                  mk["rc"][k] = sa.n

              def emit_av(k):
                  w, gi = divmod(k, 8)
                  eb = (k % 2) * 1024
                  yp = (PD, PE_)[k % 2]
                  for idx, hp in enumerate((0, 1)):
                      h = 2 * gi + hp
                      ob = hp * 64
                      for jh in range(2):
                          vt = w * 2 + jh
                          w_ = []
                          if idx == 0 and jh == 0:
                              w_ = [(sa, mk["et"][k]), (sv, mk["y"][k - 2])]
                          elif idx == 1 and jh == 0:
                              w_ = [(sa, mk["et2"][k])]
                          P.op("tensor", lambda e, ob=ob, h=h, jh=jh, vt=vt, eb=eb, yp=yp, hp=hp: e.matmul(
                              yp[ob:ob + 64, 0:256],
                              v_sb[:, vt * 1024 + h * 64: vt * 1024 + (h + 1) * 64],
                              ett[:, eb + CS2[(hp, jh)]: eb + CS2[(hp, jh)] + 256],
                              start=(jh == 0), stop=(jh == 1)),
                              waits=w_, inc=(sp if (idx == 1 and jh == 1) else None))
                  mk["av"][k] = sp.n

              def emit_bg(k):
                  sc = (k % 2) * 512
                  bp = (PC, PF)[k % 2]
                  P.op("tensor", lambda e, sc=sc, bp=bp: e.matmul(
                      bp[:, 0:256], e2f[0:65, :], zz[0:65, sc:sc + 256],
                      start=True, stop=True),
                      waits=[(sa, mk["rc"][k]), (sa, mk["bge"][k - 2])], inc=sp)
                  mk["bg"][k] = sp.n

              def emit_bge(k):
                  bc = (k % 2) * 256
                  bp = (PC, PF)[k % 2]
                  P.op("scalar", lambda e, bc=bc, bp=bp: e.activation(
                      out=bgt[:, bc:bc + 256], in_=bp[:, 0:256], func=AF.Exp,
                      scale=-1.0),
                      waits=[(sp, mk["bg"][k]), (sv, mk["y"][k - 2])], inc=sa)
                  mk["bge"][k] = sa.n

              def emit_y(k):
                  w, gi = divmod(k, 8)
                  yc = (k % 2) * 256
                  yp = (PD, PE_)[k % 2]
                  P.op("vector", lambda e, yc=yc, gi=gi, w=w, yp=yp: e.tensor_tensor(
                      out=y_sb[:, gi * TT + w * 256: gi * TT + (w + 1) * 256],
                      in0=yp[:, 0:256], in1=bgt[:, yc:yc + 256], op=ALU.mult),
                      waits=[(sp, mk["av"][k]), (sa, mk["bge"][k])], inc=sv)
                  mk["y"][k] = sv.n

              for k in range(NIT + 2):
                  if k < NIT:
                      emit_st(k)
                  if 1 <= k <= NIT:
                      emit_z(k - 1)
                  if k >= 2:
                      emit_av(k - 2)
                      emit_bg(k - 2)
                  if k < NIT:
                      emit_exp(k)
                  if 1 <= k <= NIT:
                      emit_recip(k - 1)
                  if k >= 2:
                      emit_bge(k - 2)
                      emit_y(k - 2)
              SV_Y = sv.n


              if attn_stage < 6:
                prev = dict(prev, sv_mid=sv.n, sa_c2=sa.n, sp_st2=sp.n,
                            sv_v2=sv.n, sp_vend=SP_VEND)
                continue
              prm = []
              for ot in range(8):
                  pp = (PE_, PF)[ot % 2]
                  for c in range(8):
                      w_ = []
                      if ot == 0 and c == 0:
                          w_ = [(sv, SV_Y)]
                      elif c == 0 and ot >= 2:
                          w_ = [(sv, prm[ot - 2])]
                      P.op("tensor", lambda e, ot=ot, c=c, pp=pp: e.matmul(
                          pp[:, :],
                          wpT[:, c * 1024 + ot * 128: c * 1024 + (ot + 1) * 128],
                          y_sb[:, c * TT:(c + 1) * TT],
                          start=(c == 0), stop=(c == 7)),
                          waits=w_, inc=(sp if c == 7 else None))
                  P.op("vector", lambda e, ot=ot, pp=pp, xs=xs: e.tensor_tensor(
                      out=midt[:, ot * TT:(ot + 1) * TT], in0=pp[:, :],
                      in1=xs[:, ot * TT:(ot + 1) * TT], op=ALU.add),
                      waits=[(sp, sp.n)] + ([(gmid, prev["gs_mid"]),
                                             (sa, prev["sa_c2"])] if ot == 0 else []),
                      inc=sv)
                  prm.append(sv.n)
              SV_MID = sv.n

              for c in range(8):
                  P.op("scalar", lambda e, c=c: e.copy(
                      out=xbt[:, c * TT:(c + 1) * TT], in_=midt[:, c * TT:(c + 1) * TT]),
                      waits=([(sv, SV_MID)] if c == 0 else []), inc=sa)
              for c in range(8):
                  P.op("scalar", lambda e, c=c: e.activation(
                      out=x2t[:, c * TT:(c + 1) * TT], in_=midt[:, c * TT:(c + 1) * TT],
                      func=AF.Square), inc=sa)
              SA_C2 = sa.n
              for c in range(8):
                  P.op("tensor", lambda e, c=c: e.matmul(
                      PA[0:1, 0:TT], ONEB, xbt[:, c * TT:(c + 1) * TT],
                      start=(c == 0), stop=(c == 7)),
                      waits=([(sa, SA_C2), (sv, sv.n)] if c == 0 else []),
                      inc=(sp if c == 7 else None))
              for c in range(8):
                  P.op("tensor", lambda e, c=c: e.matmul(
                      PB[0:1, 0:TT], ONEB, x2t[:, c * TT:(c + 1) * TT],
                      start=(c == 0), stop=(c == 7)), inc=(sp if c == 7 else None))
              SP_ST2 = sp.n
              P.op("vector", lambda e: e.tensor_scalar(
                  out=mrow2[:, 0:TT], in0=PA[0:1, 0:TT], scalar1=1.0 / D,
                  scalar2=None, op0=ALU.mult),
                  waits=[(sp, SP_ST2), (sv, sv.n), (gstat, prev["gs_stat"])],
                  inc=sv)
              P.op("vector", lambda e: e.tensor_scalar(
                  out=mrow2[:, 2 * TT:3 * TT], in0=PB[0:1, 0:TT], scalar1=1.0 / D,
                  scalar2=None, op0=ALU.mult), waits=[(sv, sv.n)], inc=sv)
              P.op("vector", lambda e: e.tensor_tensor(
                  out=mrow2[:, 3 * TT:4 * TT], in0=mrow2[:, 0:TT],
                  in1=mrow2[:, 0:TT], op=ALU.mult), waits=[(sv, sv.n)], inc=sv)
              P.op("vector", lambda e: e.tensor_tensor(
                  out=mrow2[:, 2 * TT:3 * TT], in0=mrow2[:, 2 * TT:3 * TT],
                  in1=mrow2[:, 3 * TT:4 * TT], op=ALU.subtract),
                  waits=[(sv, sv.n)], inc=sv)
              SV_V2 = sv.n
              P.op("scalar", lambda e: e.activation(
                  out=mrow2[:, 3 * TT:4 * TT], in_=mrow2[:, 2 * TT:3 * TT],
                  func=AF.Ln, bias=EPSC[0:1, :]), waits=[(sv, SV_V2)], inc=sa)
              P.op("scalar", lambda e: e.activation(
                  out=mrow2[:, TT:2 * TT], in_=mrow2[:, 3 * TT:4 * TT],
                  func=AF.Exp, scale=-0.5), waits=[(sa, sa.n)], inc=sa)
              SA_LN2 = sa.n

              tok0 = t * TT
              P.op("scalar", lambda e, tok0=tok0: e.dma_start(
                  out=stats_scr[0, tok0:tok0 + TT].unsqueeze(0), in_=mrow2[:, 0:TT]),
                  waits=[(sa, SA_LN2)], inc=gstat)
              P.op("scalar", lambda e, tok0=tok0: e.dma_start(
                  out=stats_scr[1, tok0:tok0 + TT].unsqueeze(0), in_=mrow2[:, TT:2 * TT]),
                  inc=gstat)
              GS_STAT = gstat.n
              for c in range(8):
                  P.op("scalar", lambda e, c=c, seq=seq, t0=t0: e.dma_start(
                      out=mid_scr[seq, c * 128:(c + 1) * 128, t0:t0 + TT],
                      in_=midt[:, c * TT:(c + 1) * TT]),
                      waits=[(sv, SV_MID)] if c == 0 else [], inc=gmid)
              GS_MID = gmid.n

              prev = dict(sp_st2=SP_ST2, sa_c2=SA_C2, gs_mid=GS_MID,
                          gs_stat=GS_STAT, sv_v2=SV_V2, sp_vend=SP_VEND,
                          sv_mid=SV_MID, sv_xn=SV_XNU, sv_bc=SV_BCW)

    SP_ATTN, SV_ATTN, SA_ATTN = sp.n, sv.n, sa.n
    GMID_ATTN, GSTAT_ATTN, GS_ATTN = gmid.n, gstat.n, gs.n
    wes.close()


    if phases == "pa":
        with contextlib.ExitStack() as des:
            df = des.enter_context(nc.sbuf_tensor("dfa", [128, 2048], F32))
            qd = SemCtr(des.enter_context(nc.semaphore("qda")), 16)
            for s_ in range(n_seq):
                for c in range(8):
                    P.op("sync", lambda e, s_=s_, c=c: e.dma_start(
                        out=df[:], in_=mid_scr[s_, c * 128:(c + 1) * 128, :]),
                        waits=[(gmid, gmid.n), (gout, gout.n)], inc=qd)
                    P.op("scalar", lambda e, s_=s_, c=c: e.dma_start(
                        out=out_fm[s_, c * 128:(c + 1) * 128, :], in_=df[:]),
                        waits=[(qd, qd.n)], inc=gout)
                    P.wait("sync", (gout, gout.n))
            P.wait("scalar", (gout, gout.n))

    with contextlib.ExitStack() as fes:
      if phases == "paf":
          fsb = lambda n, s, d: fes.enter_context(nc.sbuf_tensor(n, s, d))
          fps = lambda n, s, d: fes.enter_context(nc.psum_tensor(n, s, d))
          wex = fsb("wex", [128, 8 * HID], BF16)
          wcn = fsb("wcn", [128, 32 * 1024], BF16)
          mid2_ = [fsb(f"mid2{i}", [128, 8 * TT], F32) for i in range(2)]
          xn2 = fsb("xn2", [128, 8 * TT], BF16)
          g_sb = fsb("g_sb", [128, 32 * TT], BF16)
          o_sb = fsb("o_sb", [128, 2 * TT], F32)
          mb2 = fsb("mb2", [128, TT], BF16)
          rb2 = fsb("rb2", [128, TT], BF16)
          HP = [fps(f"HP{i}", [128, 512], F32) for i in range(2)]
          PO = [fps(f"PO{i}", [128, 512], F32) for i in range(4)]
          PBC = [fps(f"PBC{i}", [128, 512], F32) for i in range(2)]

          P.wait("sync", (gs, GS_ATTN), (gmid, GMID_ATTN), (gstat, GSTAT_ATTN))

          S = {"sv_evac": SV_ATTN}
          qf_t = {}
          sp_bc2 = {-1: SP_ATTN}
          sv_xn2 = {}
          gs_out_prev = 0

          def emit_loads(t, gs_out, qprev=None):
              seq = t // TPS
              t0 = (t % TPS) * TT
              tok0 = t * TT
              qft = (qf, qf2)[t % 2]
              mid2 = mid2_[t % 2]
              for c in range(8):
                  w_ = []
                  if c == 0:
                      w_ = [(sv, S["sv_evac"])]
                      if qprev is not None:
                          w_.append((qft, qprev))
                  P.op("sync", lambda e, c=c, seq=seq, t0=t0, mid2=mid2: e.dma_start(
                      out=mid2[:, c * TT:(c + 1) * TT],
                      in_=mid_scr[seq, c * 128:(c + 1) * 128, t0:t0 + TT]),
                      waits=w_, inc=qft)
              P.op("sync", lambda e, tok0=tok0: e.dma_start(
                  out=o_sb[0:1, TT:2 * TT],
                  in_=stats_scr[0, tok0:tok0 + TT].unsqueeze(0)),
                  waits=[(sp, sp_bc2[t - 1]), (gout, gs_out)], inc=qft)
              P.op("sync", lambda e, tok0=tok0: e.dma_start(
                  out=o_sb[32:33, TT:2 * TT],
                  in_=stats_scr[1, tok0:tok0 + TT].unsqueeze(0)),
                  inc=qft)
              qf_t[t] = qft.n

          def emit_head(t, sp_h):
              qft = (qf, qf2)[t % 2]
              mid2 = mid2_[t % 2]
              w_ = [(qft, qf_t[t])]
              if t == 0:
                  w_ += [(sa, SA_ATTN), (sv, SV_ATTN)]
              P.op("tensor", lambda e: e.matmul(
                  PBC[0][:, :], orow[0:1, :], o_sb[0:1, TT:2 * TT],
                  start=True, stop=True), waits=w_, inc=sp)
              P.op("tensor", lambda e: e.matmul(
                  PBC[1][:, :], orow[32:33, :], o_sb[32:33, TT:2 * TT],
                  start=True, stop=True), inc=sp)
              sp_bc2[t] = sp.n
              P.op("vector", lambda e: e.tensor_copy(mb2[:], PBC[0][:, :]),
                   waits=[(sp, sp_bc2[t])], inc=sv)
              P.op("vector", lambda e: e.tensor_copy(rb2[:], PBC[1][:, :]), inc=sv)
              SV_BC2 = sv.n
              for c in range(8):
                  if c == 0:
                      w_ = [(sv, SV_BC2)] + ([(sp, sp_h)] if sp_h else [])
                  else:
                      w_ = [(sv, sv.n)]
                  P.op("vector", lambda e, c=c, mid2=mid2: e.tensor_tensor(
                      out=PBC[1][:, :], in0=mid2[:, c * TT:(c + 1) * TT],
                      in1=mb2[:], op=ALU.subtract),
                      waits=w_, inc=sv)
                  P.op("vector", lambda e, c=c: e.tensor_tensor(
                      out=xn2[:, c * TT:(c + 1) * TT], in0=PBC[1][:, :],
                      in1=rb2[:], op=ALU.mult), waits=[(sv, sv.n)], inc=sv)
              sv_xn2[t] = sv.n

          emit_loads(0, 0)
          for c in range(8):
              P.op("sync", lambda e, c=c: e.dma_start(
                  out=wex[:, c * HID:(c + 1) * HID],
                  in_=wexp_scr[c * 128:(c + 1) * 128, :]),
                  waits=[(qf, qf_t[0])] if c == 0 else [], inc=qf)
          QF_WEX = qf.n
          wcv = wcon_scr.rearrange("(B jj s) c -> (B jj) s c", B=32, jj=4, s=32)
          for g in range(32):
              P.op("sync", lambda e, g=g: e.dma_start(
                  out=wcn[:, g * 1024:(g + 1) * 1024], in_=wcv[:, g, :]), inc=qf2)
          QF_WCN = qf2.n
          emit_head(0, None)
          for t in range(NTIL):
              seq = t // TPS
              t0 = (t % TPS) * TT
              qft = (qf, qf2)[t % 2]
              mid2 = mid2_[t % 2]

              gmk = []
              for fc in range(32):
                  pp = HP[fc % 2]
                  for c in range(8):
                      w_ = []
                      if fc == 0 and c == 0:
                          w_ = [(qft, qf_t[t]), (sv, sv_xn2[t])]
                          if t == 0:
                              w_.append((qf, QF_WEX))
                      elif c == 0 and fc >= 2:
                          w_ = [(sa, gmk[fc - 2])]
                      P.op("tensor", lambda e, fc=fc, c=c, pp=pp: e.matmul(
                          pp[:, :],
                          wex[:, c * HID + fc * 128: c * HID + (fc + 1) * 128],
                          xn2[:, c * TT:(c + 1) * TT],
                          start=(c == 0), stop=(c == 7)),
                          waits=w_, inc=(sp if c == 7 else None))
                  P.op("scalar", lambda e, fc=fc, pp=pp: e.activation(
                      out=g_sb[:, fc * TT:(fc + 1) * TT], in_=pp[:, :], func=AF.Gelu,
                      bias=biasg[:, fc:fc + 1]), waits=[(sp, sp.n)], inc=sa)
                  gmk.append(sa.n)
              SA_G = sa.n
              SP_H = sp.n

              if t + 1 < NTIL:
                  qprev = (QF_WCN if t + 1 == 1 else
                           QF_WEX if t + 1 == 2 else qf_t[t - 1])
                  emit_loads(t + 1, gs_out_prev, qprev)
                  emit_head(t + 1, SP_H)

              for oh in range(2):
                  for fc in range(32):
                      for osub in range(4):
                          ot = oh * 4 + osub
                          w_ = []
                          if fc == 0 and osub == 0:
                              w_ = [(sa, SA_G)] if oh == 0 else [(sv, sv.n)]
                              if t == 0 and oh == 0:
                                  w_.append((qf2, QF_WCN))
                          P.op("tensor", lambda e, fc=fc, ot=ot, osub=osub: e.matmul(
                              PO[osub][:, :],
                              wcn[:, fc * 1024 + ot * 128: fc * 1024 + (ot + 1) * 128],
                              g_sb[:, fc * TT:(fc + 1) * TT],
                              start=(fc == 0), stop=(fc == 31)),
                              waits=w_, inc=(sp if (fc == 31 and osub == 3) else None))
                  SP_O = sp.n
                  for osub in range(4):
                      ot = oh * 4 + osub
                      w_ = [(gout, gs_out_prev)] if osub % 2 == 0 else [(sv, sv.n)]
                      if osub == 0:
                          w_.append((sp, SP_O))
                      P.op("vector", lambda e, osub=osub, ot=ot, mid2=mid2: e.scalar_tensor_tensor(
                          out=o_sb[:, (osub % 2) * TT:((osub % 2) + 1) * TT],
                          in0=PO[osub][:, :],
                          scalar=bco[:, ot:ot + 1], in1=mid2[:, ot * TT:(ot + 1) * TT],
                          op0=ALU.add, op1=ALU.add),
                          waits=w_, inc=sv)
                      if osub % 2 == 1:
                          SV_O = sv.n
                          for oo in (osub - 1, osub):
                              ot2 = oh * 4 + oo
                              P.op("scalar", lambda e, oo=oo, ot2=ot2, seq=seq, t0=t0: e.dma_start(
                                  out=out_fm[seq, ot2 * 128:(ot2 + 1) * 128, t0:t0 + TT],
                                  in_=o_sb[:, (oo % 2) * TT:((oo % 2) + 1) * TT]),
                                  waits=[(sv, SV_O)] if oo == osub - 1 else [], inc=gout)
                          gs_out_prev = gout.n
              S["sv_evac"] = sv.n

          P.wait("scalar", (gout, gout.n))

    P.wait("sync", (qc, qc.n), (qx0, qx0.n), (qx1, qx1.n), (qp, qp.n),
           (qp2, qp2.n), (qf, qf.n), (qf2, qf2.n))
    P.wait("scalar", (gs, gs.n), (gso, gso.n), (gstat, gstat.n), (gmid, gmid.n),
           (gout, gout.n))

    with nc.allow_low_precision(
            reason="intentional bf16 operand rounding (matmul inputs / 1/Z "
                   "scale); all accumulation stays fp32 in PSUM"), \
         nc.Block() as block:
        @block.sync
        def _(e):
            for fn in P.q["sync"]:
                fn(e)

        @block.tensor
        def _(e):
            for fn in P.q["tensor"]:
                fn(e)

        @block.vector
        def _(e):
            for fn in P.q["vector"]:
                fn(e)

        @block.scalar
        def _(e):
            for fn in P.q["scalar"]:
                fn(e)

        @block.gpsimd
        def _(e):
            for fn in P.q["gpsimd"]:
                fn(e)

    es.close()
    return nc



def make_consts():
    cf = np.zeros((128, 132), np.float32)
    cf[:, 0:128] = np.eye(128, dtype=np.float32)
    cf[:, 128] = EPS
    cbm = np.zeros((128, 132), np.float32)
    cbm[:, 0:128] = np.eye(128)
    cbm[:, 128] = 1.0
    rf = np.zeros((4, 128), np.float32)
    rf[0, :] = 1.0
    rf[1, 0:64] = 1.0
    rf[2, 64:128] = 1.0
    rb2 = np.zeros((2, 128), np.float32)
    rb2[0, 0:64] = 1.0
    rb2[1, 64:128] = 1.0
    ef = np.zeros((128, 128), np.float32)
    ef[0, 0:64] = 1.0
    ef[64, 64:128] = 1.0
    cb2 = np.zeros((128, 4), np.float32)
    cb2[:, 0] = 1.0
    return (cf, cbm.astype(ml_dtypes.bfloat16), rf,
            rb2.astype(ml_dtypes.bfloat16), ef, cb2.astype(ml_dtypes.bfloat16))


def make_inputs(L0_attn_w_attn, L0_ffn_W_exp, L0_ffn_b_exp, L0_ffn_up_weights,
                L0_ffn_up_bias, L0_ffn_down_weights, L0_ffn_W_con, L0_ffn_b_con,
                L0_ffn_down_bias, L0_attn_w_proj):
    SA_u, SQ_u = build_stage_mats(np.asarray(L0_ffn_up_weights, np.float32))
    SA_d, SQ_d = build_stage_mats(np.asarray(L0_ffn_down_weights, np.float32))
    cfc, cbb, rf, rb2, ef, cb2 = make_consts()
    fmap = f_of_hprime()
    return {
        "w_attnT": np.ascontiguousarray(np.asarray(L0_attn_w_attn, np.float32).T),
        "w_projT": np.ascontiguousarray(np.asarray(L0_attn_w_proj, np.float32).T),
        "w_expT": np.ascontiguousarray(np.asarray(L0_ffn_W_exp, np.float32).T),
        "w_con": np.ascontiguousarray(np.asarray(L0_ffn_W_con, np.float32)),
        "b_exp": np.asarray(L0_ffn_b_exp, np.float32),
        "ub_perm": np.ascontiguousarray(np.asarray(L0_ffn_up_bias, np.float32)[fmap]),
        "b_con": np.asarray(L0_ffn_b_con, np.float32),
        "d_bias": np.asarray(L0_ffn_down_bias, np.float32),
        "SAu": np.ascontiguousarray(SA_u), "SQu": np.ascontiguousarray(SQ_u),
        "SAdT": np.ascontiguousarray(SA_d.transpose(0, 1, 3, 2)),
        "SQdT": np.ascontiguousarray(SQ_d.transpose(0, 1, 3, 2)),
        "cf_in": cfc, "cb_in": cbb, "rf_in": rf, "rb_in": rb2, "ef_in": ef,
        "cb2_in": cb2,
    }


def kernel(L0_attn_x, L0_attn_w_attn, L0_attn_w_proj, L0_ffn_W_exp, L0_ffn_b_exp,
           L0_ffn_up_weights, L0_ffn_up_bias, L0_ffn_down_weights, L0_ffn_W_con,
           L0_ffn_b_con, L0_ffn_down_bias):
    x = np.asarray(L0_attn_x, dtype=np.float32)
    B = x.shape[0]
    n_seq = B // N_CORES
    shared = make_inputs(L0_attn_w_attn, L0_ffn_W_exp, L0_ffn_b_exp,
                         L0_ffn_up_weights, L0_ffn_up_bias, L0_ffn_down_weights,
                         L0_ffn_W_con, L0_ffn_b_con, L0_ffn_down_bias,
                         L0_attn_w_proj)
    nc = build_program(n_seq=n_seq)
    in_maps = []
    for core in range(N_CORES):
        m = dict(shared)
        m["x_fm"] = np.ascontiguousarray(
            x[core * n_seq:(core + 1) * n_seq].transpose(0, 2, 1))
        in_maps.append(m)
    res = run_bass_kernel_spmd(nc, in_maps, core_ids=list(range(N_CORES)))
    outs = [np.ascontiguousarray(res.results[c]["out_fm"].transpose(0, 2, 1))
            for c in range(N_CORES)]
    return np.concatenate(outs, axis=0)
